# version 14
# baseline (speedup 1.0000x reference)
import numpy as np
import ml_dtypes
from contextlib import ExitStack

import concourse.bass as bass
import concourse.mybir as mybir
from concourse.bass_utils import run_bass_kernel_spmd

F32 = mybir.dt.float32
BF16 = mybir.dt.bfloat16
F8 = mybir.dt.float8e4
AF = mybir.ActivationFunctionType
ALU = mybir.AluOpType
AXX = mybir.AxisListType.X
PM = mybir.MatmulPerfMode

T, D, H, HKV, DH, G, WIN, J = 2048, 2048, 16, 4, 128, 4, 512, 8192
NT, ND, NJ, NC = 16, 16, 64, 8

ENGINES = ("sync", "scalar", "vector", "tensor", "gpsimd")


class Sem:
    __slots__ = ("h", "n")

    def __init__(self, h):
        self.h = h
        self.n = 0


class Sched:
    def __init__(self):
        self.ops = []
        self.own_sem = {}

    def emit(self, eng, fn):
        self.ops.append((eng, fn))

    def wait(self, eng, sem, n=None):
        tgt = sem.n if n is None else n
        if tgt <= 0:
            return
        self.emit(eng, lambda e, s=sem.h, t=tgt: e.wait_ge(s, t))

    def ins(self, eng, f, sem=None, inc=1, selfwait=True):
        own = self.own_sem.get(eng)
        if own is not None:
            if selfwait and own.n > 0:
                self.wait(eng, own)
            if sem is None:
                sem, inc = own, 1
        if sem is not None:
            sem.n += inc
        def fn(e, _f=f, _s=sem, _i=inc):
            r = _f(e)
            if _s is not None:
                r.then_inc(_s.h, _i)
        self.emit(eng, fn)

    def build(self, block):
        for ename in ENGINES:
            sec = getattr(block, ename)
            mine = [fn for (e, fn) in self.ops if e == ename]
            def body(eng, _my=mine):
                for fn in _my:
                    fn(eng)
            sec(body)


def build_program():
    nc = bass.Bass("TRN2", target_bir_lowering=False)

    dh = {}

    def din(name, shape, dt=F32):
        h = nc.dram_tensor(name, shape, dt, kind="ExternalInput")
        dh[name] = h
        return h.ap()

    x_d = din("x", [T, D])
    wqT_d = din("wqT", [D, H * DH])
    wkT_d = din("wkT", [D, HKV * DH])
    wvT_d = din("wvT", [D, HKV * DH])
    woT_d = din("woT", [H * DH, D])
    wfcT_d = din("wfcT", [D, J])
    wprT_d = din("wprT", [J, D])
    qn_d = din("qn", [1, DH])
    kn_d = din("kn", [1, DH])
    tqc_d = din("tqc", [T, DH])
    tqs_d = din("tqs", [T, DH])
    tkc_d = din("tkc", [T, DH])
    tks_d = din("tks", [T, DH])
    masks4_d = din("masks4", [4 * 128, 256], BF16)
    ones_bf_d = din("ones_bf", [128, 128], BF16)
    id_bf_d = din("id_bf", [128, 128], BF16)
    id_f32_d = din("id_f32", [128, 128], F32)
    onesr_d = din("onesr", [1, 128], F32)
    consts_d = din("consts", [128, 8], F32)
    zrow_d = din("zrow", [1, 8], F32)
    wslf_d = din("wslf", [D // NC, J])
    wslp_d = din("wslp", [J // NC, D])

    out_h = nc.dram_tensor("out", [T, D], F32, kind="ExternalOutput")
    out_d = out_h.ap()

    xmid_h = nc.dram_tensor("xmid_s", [T, D], F32)
    x2_h = nc.dram_tensor("x2_s", [T, D], F32)
    hT_h = nc.dram_tensor("hT_s", [NT * 128, J], BF16)
    ctx_h = nc.dram_tensor("ctx_s", [H * 128, T], BF16)
    xmid_d, x2_d, hT_d, ctx_d = xmid_h.ap(), x2_h.ap(), hT_h.ap(), ctx_h.ap()
    cc_in1 = nc.dram_tensor("cc_in1", [1, 8], F32).ap()
    cc_out1 = nc.dram_tensor("cc_out1", [1, 8], F32).ap()
    cc_in2 = nc.dram_tensor("cc_in2", [1, 8], F32).ap()
    cc_out2 = nc.dram_tensor("cc_out2", [1, 8], F32).ap()

    P = Sched()
    es = ExitStack()

    ALLSEMS = []

    def sem(name):
        s = Sem(es.enter_context(nc.semaphore(name)))
        ALLSEMS.append(s)
        return s

    def full_barrier():
        for _eng in ENGINES:
            for _s in ALLSEMS:
                P.wait(_eng, _s)

    sT = sem("s_pe")
    sA = sem("s_act")
    sV = sem("s_dve")
    sGc = sem("s_gpc")
    sCC = sem("s_cc")
    P.own_sem = {"vector": sV, "scalar": sA}

    dIN = sem("d_in")
    dX = [sem("d_x0"), sem("d_x1")]
    dTAB = sem("d_tab")
    dCTXL = [sem("d_ctxl0"), sem("d_ctxl1")]
    dXB5 = [sem("d_xb50"), sem("d_xb51")]
    dX2L = [sem("d_x2l0"), sem("d_x2l1")]
    dWFC = sem("d_wfc")
    dHSx = [sem("d_hs0"), sem("d_hs1")]
    dXM5 = [sem("d_xm50"), sem("d_xm51")]
    dOUT = [sem("d_out0"), sem("d_out1")]
    dWMX = [sem("d_wmx0")]
    dWKV = sem("d_wkv")
    dWQ = sem("d_wq")
    dCTXS = [sem("d_ctxs0"), sem("d_ctxs1")]
    dWO = [sem(f"d_wo{i}") for i in range(4)]
    dXMS = sem("d_xms")
    dX2S = sem("d_x2s")
    dWPR = sem("d_wpr")
    dHST = [sem(f"d_hst{i}") for i in range(4)]
    dCC = sem("d_cc")

    def sb(name, shape, dt, stk=es):
        return stk.enter_context(nc.sbuf_tensor(name, shape, dt))

    cons = sb("cons", [128, 8], F32)
    zrow_s = sb("zrow_s", [1, 8], F32)
    onesr = sb("onesr_s", [1, 128], F32)
    id_bf = sb("id_bf_s", [128, 128], BF16)
    id_f32 = sb("id_f32_s", [128, 128], F32)
    ones_bf = sb("ones_bf_s", [128, 128], BF16)
    masks4 = sb("masks4_s", [128, 4 * 256], BF16)
    qn_s = sb("qn_s", [1, DH], F32)
    kn_s = sb("kn_s", [1, DH], F32)
    wrow = sb("wrow", [1, 2 * DH], F32)
    wqB = sb("wqB", [128, DH], F32)
    wkB = sb("wkB", [128, DH], F32)
    STV = sb("STV", [128, 8 * 24], F32)
    STA = sb("STA", [128, 8 * 16], F32)
    RMX = sb("RMX", [128, 8 * 4], F32)
    SCL = sb("SCL", [128, 8], F32)
    SCL2 = sb("SCL2", [128, 8], F32)
    ROW = sb("ROWs", [1, 32], F32)
    ccs = sb("ccs_s", [1, 8], F32)
    tcol = sb("tcol", [128, 2], F32)
    trash = sb("trash", [128, 2048], BF16)

    def stv(i):
        return STV[:, 8 * i:8 * i + 1]

    def sta(i):
        return STA[:, 8 * i:8 * i + 1]

    def rmx(i):
        return RMX[:, 8 * i:8 * i + 1]

    for dst, src in [(cons, consts_d), (zrow_s, zrow_d), (onesr, onesr_d),
                     (id_bf, id_bf_d), (id_f32, id_f32_d), (ones_bf, ones_bf_d),
                     (qn_s, qn_d), (kn_s, kn_d)]:
        P.ins("sync", lambda e, d=dst, s=src: e.dma_start(out=d[:], in_=s), dIN, 16)
    P.ins("sync", lambda e: e.dma_start(
        out=bass.AP(masks4, 0, [[4 * 256, 128], [256, 4], [1, 256]]),
        in_=bass.AP(dh["masks4"], 0, [[256, 128], [128 * 256, 4], [1, 256]])), dIN, 16)
    P.ins("gpsimd", lambda e: e.memset(RMX[:], 0), sGc, 1)
    P.wait("vector", dIN)
    P.wait("scalar", dIN)
    P.wait("tensor", dIN)
    P.ins("vector", lambda e: e.tensor_scalar(wrow[0:1, 0:DH], qn_s[0:1, :], 1.0, 1.0,
                                              op0=ALU.mult, op1=ALU.add))
    P.ins("vector", lambda e: e.tensor_scalar(wrow[0:1, DH:2 * DH], kn_s[0:1, :], 1.0, 1.0,
                                              op0=ALU.mult, op1=ALU.add))

    with nc.psum_tensor("pinit", [128, 2 * DH], F32) as pinit:
        P.wait("tensor", sV)
        P.ins("tensor", lambda e: e.matmul(pinit[:, 0:DH], onesr[0:1, :], wrow[0:1, 0:DH],
                                           start=True, stop=True), sT)
        P.ins("tensor", lambda e: e.matmul(pinit[:, DH:2 * DH], onesr[0:1, :], wrow[0:1, DH:2 * DH],
                                           start=True, stop=True), sT)
        P.wait("scalar", sT)
        P.ins("scalar", lambda e: e.activation(out=wqB[:], in_=pinit[:, 0:DH], func=AF.Copy, scale=1.0))
        P.ins("scalar", lambda e: e.activation(out=wkB[:], in_=pinit[:, DH:2 * DH], func=AF.Copy, scale=1.0))

    ph_a = ExitStack()
    xnT = sb("xnT", [128, ND * T], BF16, ph_a)
    kT = sb("kT", [128, HKV * T], BF16, ph_a)
    v_sb = sb("v_sb", [128, HKV * T], BF16, ph_a)
    qTg = sb("qTg", [128, G * T], BF16, ph_a)
    tabc = sb("tabc", [128, NT * DH], F32, ph_a)
    tabs = sb("tabs", [128, NT * DH], F32, ph_a)
    xbuf = [sb(f"xbuf{i}", [128, D], F32, ph_a) for i in range(2)]
    xn_tok = sb("xn_tok", [128, D], BF16, ph_a)
    rop = [sb(f"rop{i}", [128, 3 * 128], F32, ph_a) for i in range(2)]
    qrot_bf = [sb(f"qrot{i}", [128, 128], BF16, ph_a) for i in range(2)]
    wkv_sb = sb("wkv_sb", [128, ND * 256], BF16, ph_a)
    wq_sb = sb("wq_sb", [128, ND * 512], BF16, ph_a)
    ctxs = [sb(f"ctxs{i}", [128, T], BF16, ph_a) for i in range(2)]
    expb = [sb(f"expb{i}", [128, 1536], BF16, ph_a) for i in range(2)]
    rzb = [sb(f"rzb{i}", [128, 256], F32, ph_a) for i in range(2)]
    wmx = [sb("wmx0", [128, 1024], F32, ph_a)]

    def xnT_t(dc, tt):
        return xnT[:, dc * T + tt * 128: dc * T + (tt + 1) * 128]

    def kT_t(hk, jt):
        return kT[:, hk * T + jt * 128: hk * T + (jt + 1) * 128]

    def v_t(hk, jt):
        return v_sb[:, hk * T + jt * 128: hk * T + (jt + 1) * 128]

    def qT_t(g, it):
        return qTg[:, g * T + it * 128: g * T + (it + 1) * 128]

    def tab_t(tab, tt):
        return tab[:, tt * 128:(tt + 1) * 128]

    def ln_stats(src_ap, sl, asl, scale_c=None, scale_c2=None):
        P.wait("scalar", sV)
        P.ins("vector", lambda e: e.reduce_sum(stv(sl), src_ap, axis=AXX))
        P.wait("scalar", sV)
        P.ins("scalar", lambda e: e.activation(out=trash[:], in_=src_ap, func=AF.Square,
                                               accum_out=sta(asl)))
        P.wait("vector", sA)
        if scale_c is None:
            P.ins("vector", lambda e: e.tensor_scalar(stv(sl), stv(sl), 1.0 / D, 0.0,
                                                      op0=ALU.mult, op1=ALU.add))
            P.ins("vector", lambda e: e.tensor_scalar(stv(sl + 1), sta(asl), 1.0 / D, 0.0,
                                                      op0=ALU.mult, op1=ALU.add))
        else:
            P.ins("vector", lambda e: e.tensor_scalar(stv(sl), stv(sl), scale_c, 1.0 / D,
                                                      op0=ALU.mult, op1=ALU.mult))
            P.ins("vector", lambda e: e.tensor_scalar(stv(sl + 1), sta(asl), scale_c2, 1.0 / D,
                                                      op0=ALU.mult, op1=ALU.mult))
        P.ins("vector", lambda e: e.tensor_scalar(stv(sl + 2), stv(sl), stv(sl), 0.0,
                                                  op0=ALU.mult, op1=ALU.add))
        P.ins("vector", lambda e: e.tensor_scalar(stv(sl + 2), stv(sl + 1), stv(sl + 2), 0.0,
                                                  op0=ALU.subtract, op1=ALU.add))
        P.wait("scalar", sV)
        P.ins("scalar", lambda e: e.activation(out=sta(asl + 1), in_=stv(sl + 2), func=AF.Sqrt,
                                               scale=1.0, bias=cons[:, 0:1]))
        P.wait("vector", sA)
        P.ins("vector", lambda e: e.reciprocal(stv(sl + 3), sta(asl + 1)))
        P.ins("vector", lambda e: e.tensor_scalar(stv(sl + 4), stv(sl), stv(sl + 3), -1.0,
                                                  op0=ALU.mult, op1=ALU.mult))

    full_barrier()
    P.wait("vector", sGc)
    wmax_tiles = []
    for (w_ap_, rows_, cols_, slot_) in [(wslf_d, D // NC, J, 2), (wslp_d, J // NC, D, 3)]:
        for r_ in range(rows_ // 128):
            for c_ in range(cols_ // 1024):
                wmax_tiles.append((w_ap_, r_, c_, slot_))
    wmax_state = {"i": 0, "tok": [0, 0]}

    def wmax_step(nstep=1):
        for _ in range(nstep):
            i = wmax_state["i"]
            if i >= len(wmax_tiles):
                return
            wmax_state["i"] += 1
            w_ap_, r_, c_, slot_ = wmax_tiles[i]
            wp = 0
            P.wait("gpsimd", sV, wmax_state["tok"][wp])
            P.ins("gpsimd", lambda e, w=w_ap_, rr=r_, cc=c_, p=wp: e.dma_start(
                out=wmx[p][:], in_=w[rr * 128:(rr + 1) * 128, cc * 1024:(cc + 1) * 1024]), dWMX[wp], 16)
            P.wait("vector", dWMX[wp])
            P.ins("vector", lambda e, p=wp: e.reduce_max(stv(15), wmx[p][:], axis=AXX,
                                                         apply_absolute_value=True))
            P.ins("vector", lambda e, s=slot_: e.tensor_scalar(rmx(s), rmx(s), stv(15), 0.0,
                                                               op0=ALU.max, op1=ALU.add))
            wmax_state["tok"][wp] = sV.n

    with nc.psum_tensor("ptr0", [128, 128], BF16) as ptr0, \
         nc.psum_tensor("ptr1", [128, 128], BF16) as ptr1:
        ptr = [ptr0, ptr1]
        for tt in range(NT):
            par = tt % 2
            wmax_step(2)
            P.wait("sync", sA)
            P.wait("sync", sV)
            P.ins("sync", lambda e, t=tt, p=par: e.dma_start(
                out=xbuf[p][:], in_=x_d[t * 128:(t + 1) * 128, :]), dX[par], 16)
            P.wait("vector", dX[par])
            P.wait("scalar", dX[par])
            ln_stats(xbuf[par][:], 0, 0)
            P.wait("scalar", sV)
            P.ins("scalar", lambda e, p=par: e.activation(out=xn_tok[:], in_=xbuf[p][:], func=AF.Identity,
                                                          scale=stv(3), bias=stv(4)))
            for dc in range(ND):
                tp = dc % 2
                P.wait("tensor", sA)
                P.ins("tensor", lambda e, d=dc, q=tp: e.transpose(
                    ptr[q][:], xn_tok[:, d * 128:(d + 1) * 128], id_bf[:]), sT)
                P.wait("scalar", sT)
                P.ins("scalar", lambda e, d=dc, t=tt, q=tp: e.activation(
                    out=xnT_t(d, t), in_=ptr[q][:], func=AF.Copy, scale=1.0))

    P.ins("sync", lambda e: e.dma_start(
        out=bass.AP(tabc, 0, [[NT * DH, 128], [DH, NT], [1, DH]]),
        in_=bass.AP(dh["tkc"], 0, [[DH, 128], [128 * DH, NT], [1, DH]])), dTAB, 16)
    P.ins("sync", lambda e: e.dma_start(
        out=bass.AP(tabs, 0, [[NT * DH, 128], [DH, NT], [1, DH]]),
        in_=bass.AP(dh["tks"], 0, [[DH, 128], [128 * DH, NT], [1, DH]])), dTAB, 16)
    P.wait("vector", dTAB)

    with nc.psum_tensor("pkv0", [128, 256], F32) as pkv0, \
         nc.psum_tensor("pkv1", [128, 256], F32) as pkv1, \
         nc.psum_tensor("ptk0", [128, 128], BF16) as ptk0, \
         nc.psum_tensor("ptk1", [128, 128], BF16) as ptk1:
        pkv = [pkv0, pkv1]
        ptk = [ptk0, ptk1]
        tokA = [0, 0]
        tokV = [0, 0]
        tokTr = [0, 0]
        tokEvK = [0, 0]
        for hk in range(HKV):
            P.wait("gpsimd", sT)
            P.ins("gpsimd", lambda e, h=hk: e.dma_start(
                out=bass.AP(wkv_sb, 0, [[ND * 256, 128], [256, ND], [1, 128]]),
                in_=bass.AP(dh["wkT"], h * 128, [[HKV * DH, 128], [128 * HKV * DH, ND], [1, 128]])), dWKV, 16)
            P.ins("gpsimd", lambda e, h=hk: e.dma_start(
                out=bass.AP(wkv_sb, 128, [[ND * 256, 128], [256, ND], [1, 128]]),
                in_=bass.AP(dh["wvT"], h * 128, [[HKV * DH, 128], [128 * HKV * DH, ND], [1, 128]])), dWKV, 16)
            for tt in range(NT):
                par = tt % 2
                wmax_step(2)
                P.wait("tensor", dWKV)
                P.wait("tensor", sA, tokA[par])
                P.wait("tensor", sV, tokV[par])
                for dc in range(ND):
                    P.ins("tensor", lambda e, d=dc, t=tt, p=par: e.matmul(
                        pkv[p][:], xnT_t(d, t), wkv_sb[:, d * 256:(d + 1) * 256],
                        start=(d == 0), stop=(d == ND - 1)), sT)
                tok_mm = sT.n
                P.wait("scalar", sT, tok_mm)
                P.wait("scalar", sV, tokV[par])
                P.ins("scalar", lambda e, p=par: e.activation(out=trash[:, 0:128], in_=pkv[p][:, 0:128],
                                                              func=AF.Square, accum_out=sta(8)), selfwait=False)
                P.ins("scalar", lambda e: e.activation(out=sta(12), in_=sta(8), func=AF.Sqrt,
                                                       scale=1.0 / DH, bias=cons[:, 1:2]))
                tok_std = sA.n
                P.wait("vector", sA, tok_std)
                P.ins("vector", lambda e: e.reciprocal(stv(16), sta(12)))
                P.ins("vector", lambda e, p=par: e.scalar_tensor_tensor(rop[p][:, 0:128], pkv[p][:, 0:128],
                                                                        stv(16), wkB[:],
                                                                        op0=ALU.mult, op1=ALU.mult))
                tokV[par] = sV.n
                P.wait("scalar", sV, tokV[par])
                P.ins("scalar", lambda e, h=hk, t=tt, p=par: e.activation(
                    out=v_t(h, t), in_=pkv[p][:, 128:256], func=AF.Copy, scale=1.0), selfwait=False)
                tokA[par] = sA.n
                P.wait("vector", sT, tokTr[par])
                P.ins("vector", lambda e, p=par, t=tt: e.tensor_tensor(
                    rop[p][:, 128:256], rop[p][:, 0:128], tab_t(tabc, t), op=ALU.mult))
                P.ins("vector", lambda e, p=par, t=tt: e.tensor_tensor(
                    rop[p][:, 256:320], rop[p][:, 64:128], tab_t(tabs, t)[:, 0:64], op=ALU.mult))
                P.ins("vector", lambda e, p=par, t=tt: e.tensor_tensor(
                    rop[p][:, 320:384], rop[p][:, 0:64], tab_t(tabs, t)[:, 64:128], op=ALU.mult))
                P.ins("vector", lambda e, p=par: e.tensor_tensor(
                    qrot_bf[p][:], rop[p][:, 128:256], rop[p][:, 256:384], op=ALU.add))
                tok_rot = sV.n
                P.wait("tensor", sV, tok_rot)
                P.wait("tensor", sA, tokEvK[par])
                P.ins("tensor", lambda e, p=par: e.transpose(ptk[p][:], qrot_bf[p][:], id_bf[:]), sT)
                tokTr[par] = sT.n
                P.wait("scalar", sT, tokTr[par])
                P.ins("scalar", lambda e, h=hk, t=tt, p=par: e.activation(
                    out=kT_t(h, t), in_=ptk[p][:], func=AF.Copy, scale=1.0), selfwait=False)
                tokEvK[par] = sA.n

    P.wait("sync", sV)
    P.ins("sync", lambda e: e.dma_start(
        out=bass.AP(tabc, 0, [[NT * DH, 128], [DH, NT], [1, DH]]),
        in_=bass.AP(dh["tqc"], 0, [[DH, 128], [128 * DH, NT], [1, DH]])), dTAB, 16)
    P.ins("sync", lambda e: e.dma_start(
        out=bass.AP(tabs, 0, [[NT * DH, 128], [DH, NT], [1, DH]]),
        in_=bass.AP(dh["tqs"], 0, [[DH, 128], [128 * DH, NT], [1, DH]])), dTAB, 16)
    P.wait("vector", dTAB)

    tokQa = [0, 0]
    tokQv = [0, 0]
    tokTrQ = [0, 0]
    tokEvQ = [0, 0]
    tE = [0, 0]
    tSc = [0, 0]
    tM = [0, 0]
    tZp = [0, 0]
    tCp = [0, 0]
    tRZp = [0, 0]
    tNp = [0, 0]
    for hk in range(HKV):
        P.wait("gpsimd", sT)
        P.ins("gpsimd", lambda e, h=hk: e.dma_start(
            out=bass.AP(wq_sb, 0, [[ND * 512, 128], [512, ND], [1, 512]]),
            in_=bass.AP(dh["wqT"], h * 512, [[H * DH, 128], [128 * H * DH, ND], [1, 512]])), dWQ, 16)
        with nc.psum_tensor(f"pq0_{hk}", [128, 512], F32) as pq0, \
             nc.psum_tensor(f"pq1_{hk}", [128, 512], F32) as pq1, \
             nc.psum_tensor(f"ptq0_{hk}", [128, 128], BF16) as ptq0, \
             nc.psum_tensor(f"ptq1_{hk}", [128, 128], BF16) as ptq1:
            pq = [pq0, pq1]
            ptq = [ptq0, ptq1]
            P.wait("tensor", sA)
            P.wait("tensor", sV)
            for tt in range(NT):
                par = tt % 2
                wmax_step(2)
                P.wait("tensor", dWQ)
                P.wait("tensor", sA, tokQa[par])
                P.wait("tensor", sV, tokQv[par])
                for dc in range(ND):
                    P.ins("tensor", lambda e, d=dc, t=tt, p=par: e.matmul(
                        pq[p][:], xnT_t(d, t), wq_sb[:, d * 512:(d + 1) * 512],
                        start=(d == 0), stop=(d == ND - 1)), sT)
                tok_mm = sT.n
                P.wait("scalar", sT, tok_mm)
                P.wait("scalar", sV, tokQv[par])
                for g in range(G):
                    P.ins("scalar", lambda e, gg=g, p=par: e.activation(
                        out=trash[:, gg * 128:(gg + 1) * 128], in_=pq[p][:, gg * 128:(gg + 1) * 128],
                        func=AF.Square, accum_out=sta(8 + gg)), selfwait=False)
                P.wait("scalar", sA)
                for g in range(G):
                    P.ins("scalar", lambda e, gg=g: e.activation(out=sta(12 + gg), in_=sta(8 + gg),
                                                                 func=AF.Sqrt, scale=1.0 / DH,
                                                                 bias=cons[:, 1:2]), selfwait=False)
                tok_std = sA.n
                P.wait("vector", sA, tok_std)
                for g in range(G):
                    P.ins("vector", lambda e, gg=g: e.reciprocal(stv(16 + gg), sta(12 + gg)), selfwait=False)
                for g in range(G):
                    gp = g % 2
                    P.wait("vector", sT, tokTrQ[gp])
                    P.ins("vector", lambda e, gg=g, p=par: e.scalar_tensor_tensor(
                        rop[0][:, 0:128], pq[p][:, gg * 128:(gg + 1) * 128], stv(16 + gg), wqB[:],
                        op0=ALU.mult, op1=ALU.mult))
                    P.ins("vector", lambda e, t=tt: e.tensor_tensor(
                        rop[0][:, 128:256], rop[0][:, 0:128], tab_t(tabc, t), op=ALU.mult))
                    P.ins("vector", lambda e, t=tt: e.tensor_tensor(
                        rop[0][:, 256:320], rop[0][:, 64:128], tab_t(tabs, t)[:, 0:64], op=ALU.mult))
                    P.ins("vector", lambda e, t=tt: e.tensor_tensor(
                        rop[0][:, 320:384], rop[0][:, 0:64], tab_t(tabs, t)[:, 64:128], op=ALU.mult))
                    P.ins("vector", lambda e, p=gp: e.tensor_tensor(
                        qrot_bf[p][:], rop[0][:, 128:256], rop[0][:, 256:384], op=ALU.add))
                    tok_rot = sV.n
                    P.wait("tensor", sV, tok_rot)
                    P.wait("tensor", sA, tokEvQ[gp])
                    P.ins("tensor", lambda e, p=gp: e.transpose(ptq[p][:], qrot_bf[p][:], id_bf[:]), sT)
                    tokTrQ[gp] = sT.n
                    P.wait("scalar", sT, tokTrQ[gp])
                    P.ins("scalar", lambda e, gg=g, t=tt, p=gp: e.activation(
                        out=qT_t(gg, t), in_=ptq[p][:], func=AF.Copy, scale=1.0), selfwait=False)
                    tokEvQ[gp] = sA.n
                tokQv[par] = sV.n
                tokQa[par] = sA.n
        with nc.psum_tensor(f"psS0_{hk}", [128, 1536], F32) as psS0, \
             nc.psum_tensor(f"psS1_{hk}", [128, 1536], F32) as psS1, \
             nc.psum_tensor(f"pcc0_{hk}", [128, 512], F32) as pcc0, \
             nc.psum_tensor(f"pcc1_{hk}", [128, 512], F32) as pcc1:
            psS = [psS0, psS1]
            pzzS = [psS0[:, 0:256], psS1[:, 0:256]]
            pccS = [pcc0[:, 0:256], pcc1[:, 0:256]]
            base = max(tokQa[0], tokQa[1])
            tE = [base, base]
            tSc = [0, 0]
            tM = [0, 0]
            tRZv = [0, 0]
            tNv = [0, 0]
            P.wait("tensor", sV)

            NR = NT // 2

            def jinfo(r):
                out = []
                for jt in range(max(0, 2 * r - 4), 2 * r + 2):
                    c = (2 * r - jt) * 128
                    slot = {-128: 0, 0: 1, 384: 2, 512: 3}.get(c)
                    out.append((jt, slot))
                return out

            def emit_scores(r, g):
                par = r % 2
                P.wait("tensor", sA, tE[par])
                P.wait("tensor", sV, tRZv[par])
                for jj, (jt, _) in enumerate(jinfo(r)):
                    P.ins("tensor", lambda e, p=par, jx=jj, hh=hk, j=jt, rr=r, gg=g: e.matmul(
                        psS[p][:, jx * 256:(jx + 1) * 256], kT_t(hh, j),
                        qTg[:, gg * T + rr * 256:(gg * T + (rr + 1) * 256)],
                        start=True, stop=True), sT)
                tSc[par] = sT.n

            def emit_exp(r):
                par = r % 2
                nj = len(jinfo(r))
                P.wait("scalar", sT, tSc[par])
                P.ins("scalar", lambda e, p=par, n=nj: e.activation(
                    out=expb[p][:, 0:n * 256], in_=psS[p][:, 0:n * 256], func=AF.Exp), selfwait=False)
                tE[par] = sA.n

            def emit_masks(r):
                par = r % 2
                P.wait("vector", sA, tE[par])
                first = True
                for jj, (jt, slot) in enumerate(jinfo(r)):
                    if slot is None:
                        continue
                    P.ins("vector", lambda e, p=par, jx=jj, m=slot: e.tensor_tensor(
                        expb[p][:, jx * 256:(jx + 1) * 256], expb[p][:, jx * 256:(jx + 1) * 256],
                        masks4[:, m * 256:(m + 1) * 256], op=ALU.mult), selfwait=first)
                    first = False
                tM[par] = sV.n

            def emit_zctx(r):
                par = r % 2
                ji = jinfo(r)
                nj = len(ji)
                P.wait("tensor", sV, max(tM[par], tRZv[par]))
                for jj in range(nj):
                    P.ins("tensor", lambda e, p=par, j=jj, n=nj: e.matmul(
                        pzzS[p], ones_bf[:], expb[p][:, j * 256:(j + 1) * 256],
                        start=(j == 0), stop=(j == n - 1)), sT)
                tz = sT.n
                P.wait("tensor", sV, tNv[par])
                for jj, (jt, _) in enumerate(ji):
                    P.ins("tensor", lambda e, p=par, hh=hk, j=jt, jx=jj, n=nj: e.matmul(
                        pccS[p], v_t(hh, j), expb[p][:, jx * 256:(jx + 1) * 256],
                        start=(jx == 0), stop=(jx == n - 1)), sT)
                return tz, sT.n

            for g in range(G):
                h = hk * G + g
                cpar = h % 2
                P.wait("vector", dCTXS[cpar])
                emit_scores(0, g)
                emit_exp(0)
                emit_masks(0)
                for r in range(NR):
                    par = r % 2
                    if r + 1 < NR:
                        emit_scores(r + 1, g)
                        emit_exp(r + 1)
                        emit_masks(r + 1)
                    tz, tc = emit_zctx(r)
                    P.wait("vector", sT, tz)
                    P.ins("vector", lambda e, p=par: e.reciprocal(rzb[p][:], pzzS[p]), selfwait=False)
                    tRZv[par] = sV.n
                    P.wait("vector", sT, tc)
                    P.ins("vector", lambda e, p=par, c=cpar, rr=r: e.tensor_tensor(
                        ctxs[c][:, rr * 256:(rr + 1) * 256], pccS[p], rzb[p][:], op=ALU.mult))
                    tNv[par] = sV.n
                P.wait("gpsimd", sV)
                P.ins("gpsimd", lambda e, hh=h, c=cpar: e.dma_start(
                    out=ctx_d[hh * 128:(hh + 1) * 128, :], in_=ctxs[c][:]), dCTXS[cpar], 16)
            tokQa = [max(tokQa[0], tE[0]), max(tokQa[1], tE[1])]

    wmax_step(len(wmax_tiles))

    full_barrier()
    ph_a.close()
    ph5 = ExitStack()
    wo_sb = sb("wo_sb", [128, ND * D], BF16, ph5)
    ctx_full = sb("ctx_full", [128, ND * T], BF16, ph5)
    xb5 = [sb(f"xb5_{i}", [128, D], F32, ph5) for i in range(2)]
    lnat = sb("lnat", [128, D], F32, ph5)
    xmid_sb = sb("xmid_sb", [128, D], F32, ph5)
    x2_sb = sb("x2_sb", [128, D], F32, ph5)

    wo_f = [sb(f"wo_f{i}", [128, D], F32, ph5) for i in range(2)]
    P.wait("sync", dCTXS[0])
    P.wait("sync", dCTXS[1])
    for hd in range(ND):
        P.ins("sync", lambda e, d=hd: e.dma_start(
            out=ctx_full[:, d * T:(d + 1) * T], in_=ctx_d[d * 128:(d + 1) * 128, :]), dCTXL[hd % 2], 16)
    tok_woc = [0] * ND
    tok_wof = [0, 0]
    for hd in range(ND):
        wp = hd % 2
        P.wait("sync", sA, tok_wof[wp])
        P.ins("sync", lambda e, d=hd, p=wp: e.dma_start(
            out=wo_f[p][:], in_=woT_d[d * 128:(d + 1) * 128, :]), dWO[hd % 4], 16)
        P.wait("scalar", dWO[hd % 4])
        P.ins("scalar", lambda e, d=hd, p=wp: e.activation(
            out=wo_sb[:, d * D:(d + 1) * D], in_=wo_f[p][:], func=AF.Copy, scale=1.0), selfwait=False)
        tok_woc[hd] = sA.n
        tok_wof[wp] = sA.n

    with nc.psum_tensor("po0", [128, D], F32) as po0, \
         nc.psum_tensor("po1", [128, D], F32) as po1:
        po = [po0, po1]
        for tt in range(NT):
            par = tt % 2
            P.wait("sync", sV)
            P.ins("sync", lambda e, t=tt, p=par: e.dma_start(
                out=xb5[p][:], in_=x_d[t * 128:(t + 1) * 128, :]), dXB5[par], 16)
            if tt == 0:
                P.wait("tensor", dCTXL[0])
                P.wait("tensor", dCTXL[1])
            P.wait("tensor", sA)
            P.wait("tensor", sV)
            for hd in range(ND):
                if tt == 0:
                    P.wait("tensor", sA, tok_woc[hd])
                for ns in range(4):
                    P.ins("tensor", lambda e, d=hd, n=ns, p=par, t=tt: e.matmul(
                        po[p][:, n * 512:(n + 1) * 512], ctx_full[:, d * T + t * 128: d * T + (t + 1) * 128],
                        wo_sb[:, d * D + n * 512: d * D + (n + 1) * 512],
                        start=(d == 0), stop=(d == ND - 1)), sT)
            P.wait("vector", sT)
            P.wait("scalar", sT)
            ln_stats(po[par][:], 0, 0)
            P.wait("scalar", sV)
            P.ins("scalar", lambda e, p=par: e.activation(out=lnat[:], in_=po[p][:], func=AF.Identity,
                                                          scale=stv(3), bias=stv(4)))
            P.wait("vector", sA)
            P.wait("vector", dXB5[par])
            P.wait("vector", dXMS)
            P.ins("vector", lambda e, p=par: e.tensor_tensor(xmid_sb[:], xb5[p][:], lnat[:], op=ALU.add))
            P.wait("gpsimd", sV)
            P.ins("gpsimd", lambda e, t=tt: e.dma_start(
                out=xmid_d[t * 128:(t + 1) * 128, :], in_=xmid_sb[:]), dXMS, 16)
            ln_stats(xmid_sb[:], 8, 4)
            P.wait("scalar", sV)
            P.wait("scalar", dX2S)
            P.ins("scalar", lambda e: e.activation(out=x2_sb[:], in_=xmid_sb[:], func=AF.Identity,
                                                   scale=stv(11), bias=stv(12)))
            P.wait("vector", sA)
            P.ins("vector", lambda e: e.reduce_max(stv(15), x2_sb[:], axis=AXX,
                                                   apply_absolute_value=True))
            P.ins("vector", lambda e: e.tensor_scalar(rmx(0), rmx(0), stv(15), 0.0,
                                                      op0=ALU.max, op1=ALU.add))
            P.wait("gpsimd", sA)
            P.ins("gpsimd", lambda e, t=tt: e.dma_start(
                out=x2_d[t * 128:(t + 1) * 128, :], in_=x2_sb[:]), dX2S, 16)
    full_barrier()
    ph5.close()

    def cross_part_max(col_ap, row_col):
        with nc.psum_tensor(f"pmx{row_col}", [128, 128], F32) as pmx:
            P.ins("vector", lambda e: e.tensor_scalar(tcol[:, 0:1], col_ap, 1.0, 0.0,
                                                      op0=ALU.mult, op1=ALU.add))
            P.wait("tensor", sV)
            P.ins("tensor", lambda e: e.transpose(pmx[0:1, 0:128], tcol[:, 0:1], id_f32[:]), sT)
            P.wait("vector", sT)
            P.ins("vector", lambda e, c=row_col: e.reduce_max(
                ROW[0:1, c:c + 1], pmx[0:1, 0:128], axis=AXX))

    cross_part_max(rmx(0), 0)
    cross_part_max(rmx(2), 1)
    cross_part_max(rmx(3), 2)

    P.ins("vector", lambda e: e.tensor_scalar(ccs[0:1, :], zrow_s[0:1, :], 0.0, 0.0,
                                              op0=ALU.mult, op1=ALU.add))
    P.ins("vector", lambda e: e.tensor_scalar(ccs[0:1, 0:1], ROW[0:1, 0:1], 1.0, 0.0,
                                              op0=ALU.mult, op1=ALU.add))
    P.ins("vector", lambda e: e.tensor_scalar(ccs[0:1, 1:2], ROW[0:1, 1:2], 1.0, 0.0,
                                              op0=ALU.mult, op1=ALU.add))
    P.ins("vector", lambda e: e.tensor_scalar(ccs[0:1, 2:3], ROW[0:1, 2:3], 1.0, 0.0,
                                              op0=ALU.mult, op1=ALU.add))
    P.wait("gpsimd", sV)
    P.ins("gpsimd", lambda e: e.dma_start(out=cc_in1, in_=ccs[0:1, :]), dCC, 16)
    P.wait("gpsimd", dCC)
    P.ins("gpsimd", lambda e: e.collective_compute(
        "AllReduce", ALU.max, replica_groups=[list(range(NC))],
        ins=[cc_in1], outs=[cc_out1]), sCC)
    P.wait("gpsimd", sCC)
    P.ins("gpsimd", lambda e: e.dma_start(out=ROW[0:1, 8:16], in_=cc_out1), dCC, 16)
    P.wait("vector", dCC)
    P.ins("vector", lambda e: e.tensor_scalar(ROW[0:1, 16:17], ROW[0:1, 8:9], 1.0 / 448.0, 1e-6,
                                              op0=ALU.mult, op1=ALU.max))
    P.ins("vector", lambda e: e.tensor_scalar(ROW[0:1, 17:18], ROW[0:1, 16:17], 2.0, 0.0,
                                              op0=ALU.mult, op1=ALU.add))
    P.ins("vector", lambda e: e.reciprocal(ROW[0:1, 17:18], ROW[0:1, 17:18]))
    P.ins("vector", lambda e: e.tensor_scalar(ROW[0:1, 18:19], ROW[0:1, 9:10], 1.0 / 448.0, 1e-6,
                                              op0=ALU.mult, op1=ALU.max))
    P.ins("vector", lambda e: e.tensor_scalar(ROW[0:1, 19:20], ROW[0:1, 18:19], 2.0, 0.0,
                                              op0=ALU.mult, op1=ALU.add))
    P.ins("vector", lambda e: e.reciprocal(ROW[0:1, 19:20], ROW[0:1, 19:20]))
    P.ins("vector", lambda e: e.tensor_scalar(ROW[0:1, 20:21], ROW[0:1, 16:17], ROW[0:1, 18:19], 4.0,
                                              op0=ALU.mult, op1=ALU.mult))
    P.ins("vector", lambda e: e.tensor_scalar(ROW[0:1, 21:22], ROW[0:1, 10:11], 1.0 / 448.0, 1e-6,
                                              op0=ALU.mult, op1=ALU.max))
    P.ins("vector", lambda e: e.tensor_scalar(ROW[0:1, 22:23], ROW[0:1, 21:22], 2.0, 0.0,
                                              op0=ALU.mult, op1=ALU.add))
    P.ins("vector", lambda e: e.reciprocal(ROW[0:1, 22:23], ROW[0:1, 22:23]))
    P.ins("vector", lambda e: e.tensor_scalar(ccs[0:1, 0:1], ROW[0:1, 17:18], 1.0, 0.0, op0=ALU.mult, op1=ALU.add))
    P.ins("vector", lambda e: e.tensor_scalar(ccs[0:1, 1:2], ROW[0:1, 20:21], 1.0, 0.0, op0=ALU.mult, op1=ALU.add))
    P.ins("vector", lambda e: e.tensor_scalar(ccs[0:1, 2:3], ROW[0:1, 19:20], 1.0, 0.0, op0=ALU.mult, op1=ALU.add))
    P.ins("vector", lambda e: e.tensor_scalar(ccs[0:1, 3:4], ROW[0:1, 22:23], 1.0, 0.0, op0=ALU.mult, op1=ALU.add))
    with nc.psum_tensor("pscl", [128, 8], F32) as pscl:
        P.wait("tensor", sV)
        P.ins("tensor", lambda e: e.matmul(pscl[:], onesr[0:1, :], ccs[0:1, :], start=True, stop=True), sT)
        P.wait("scalar", sT)
        P.ins("scalar", lambda e: e.activation(out=SCL[:], in_=pscl[:], func=AF.Copy, scale=1.0))

    ph_c = ExitStack()
    wpr8 = sb("wpr8", [128, NJ * D], F8, ph_c)
    ph_cm = ExitStack()
    x8T = sb("x8T", [128, ND * T], F8, ph_cm)

    with ExitStack() as pro, \
         nc.psum_tensor("ptc0", [128, 128], BF16) as ptc0, \
         nc.psum_tensor("ptc1", [128, 128], BF16) as ptc1:
        xbC = [sb(f"xbC{i}", [128, D], F32, pro) for i in range(2)]
        x8tok = sb("x8tok", [128, D], F8, pro)
        xqbf = sb("xqbf", [128, D], BF16, pro)
        ptc = [ptc0, ptc1]
        full_barrier()
        P.wait("sync", dX2S)
        tok_x8 = [0] * 4
        for tt in range(NT):
            par = tt % 2
            P.wait("sync", sA)
            P.ins("sync", lambda e, t=tt, p=par: e.dma_start(
                out=xbC[p][:], in_=x2_d[t * 128:(t + 1) * 128, :]), dX2L[par], 16)
            P.wait("scalar", dX2L[par])
            P.ins("scalar", lambda e, p=par: e.activation(out=x8tok[:], in_=xbC[p][:], func=AF.Copy,
                                                          scale=SCL[:, 0:1]))
            P.ins("scalar", lambda e: e.activation(out=xqbf[:], in_=x8tok[:], func=AF.Copy, scale=1.0))
            for dc in range(ND):
                tp = dc % 2
                P.wait("tensor", sA)
                P.ins("tensor", lambda e, d=dc, q=tp: e.transpose(
                    ptc[q][:], xqbf[:, d * 128:(d + 1) * 128], id_bf[:]), sT)
                P.wait("scalar", sT)
                P.ins("scalar", lambda e, d=dc, t=tt, q=tp: e.activation(
                    out=x8T[:, d * T + t * 128: d * T + (t + 1) * 128],
                    in_=ptc[q][:], func=AF.Copy, scale=1.0))
            if tt % 4 == 3:
                tok_x8[tt // 4] = sA.n

    full_barrier()
    wfc32 = sb("wfc32", [128, D], F32, ph_cm)
    wfc8 = [sb(f"wfc8_{i}", [128, D], F8, ph_cm) for i in range(2)]
    wpr32 = sb("wpr32_0", [128, D], F32, ph_cm)
    h_sb = [sb(f"h_sb{i}", [128, 512], BF16, ph_cm) for i in range(4)]
    r_sb = [sb(f"r_sb{i}", [128, 512], F32, ph_cm) for i in range(2)]

    with nc.psum_tensor("ph0", [128, 512], F32) as ph0, \
         nc.psum_tensor("ph1", [128, 512], F32) as ph1, \
         nc.psum_tensor("ph2", [128, 512], F32) as ph2, \
         nc.psum_tensor("ph3", [128, 512], F32) as ph3, \
         nc.psum_tensor("ph4", [128, 512], F32) as ph4, \
         nc.psum_tensor("ph5", [128, 512], F32) as ph5b, \
         nc.psum_tensor("ph6", [128, 512], F32) as ph6, \
         nc.psum_tensor("ph7", [128, 512], F32) as ph7:
        phh = [[ph0, ph1, ph2, ph3], [ph4, ph5b, ph6, ph7]]
        tok_quant = [0, 0]
        tok_quant_done = 0
        tok_wprq = 0
        tok_relu = [[0] * 4, [0] * 4]
        tok_mult = [[0, 0], [0, 0]]
        tok_mm = [[0] * 4, [0] * 4]
        for jc in range(NJ):
            par = jc % 2
            P.wait("sync", sA, tok_quant_done)
            P.ins("sync", lambda e, k=jc: e.dma_start(
                out=bass.AP(wfc32, 0, [[D, 128], [128, ND], [1, 128]]),
                in_=bass.AP(dh["wfcT"], k * 128, [[J, 128], [128 * J, ND], [1, 128]])), dWFC, 16)
            P.wait("scalar", dWFC)
            P.wait("scalar", sT, tok_mm[par][3])
            P.ins("scalar", lambda e, p=par: e.activation(out=wfc8[p][:], in_=wfc32[:], func=AF.Copy,
                                                          scale=SCL[:, 2:3]), selfwait=False)
            tok_quant[par] = sA.n
            tok_quant_done = sA.n
            P.wait("gpsimd", sA, tok_wprq)
            P.ins("gpsimd", lambda e, k=jc: e.dma_start(
                out=wpr32[:], in_=wprT_d[k * 128:(k + 1) * 128, :]), dWPR, 16)
            P.wait("scalar", dWPR)
            P.ins("scalar", lambda e, k=jc: e.activation(
                out=wpr8[:, k * D:(k + 1) * D], in_=wpr32[:], func=AF.Copy,
                scale=SCL[:, 3:4]), selfwait=False)
            tok_wprq = sA.n
            for ts in range(4):
                P.wait("tensor", sA, tok_quant[par])
                P.wait("tensor", sA, tok_relu[par][ts])
                for d2 in range(ND // 2):
                    P.ins("tensor", lambda e, d=d2, s=ts, p=par: e.matmul(
                        phh[p][s][:],
                        bass.AP(wfc8[p], d * 256, [[D, 128], [128, 2], [1, 128]]),
                        bass.AP(x8T, (2 * d) * T + s * 512, [[ND * T, 128], [T, 2], [1, 512]]),
                        start=(d == 0), stop=(d == ND // 2 - 1),
                        perf_mode=PM.DoubleRow), sT)
                tok_mm[par][ts] = sT.n
            for ts in range(4):
                rp = ts % 2
                P.wait("scalar", sT, tok_mm[par][ts])
                P.wait("scalar", sV, tok_mult[par][rp])
                P.ins("scalar", lambda e, s=ts, p=par, q=rp: e.activation(
                    out=r_sb[q][:], in_=phh[p][s][:], func=AF.Relu, scale=SCL[:, 1:2]), selfwait=False)
                tok_relu[par][ts] = sA.n
                P.wait("vector", sA, tok_relu[par][ts])
                P.wait("vector", dHST[ts])
                P.ins("vector", lambda e, s=ts, q=rp: e.tensor_tensor(
                    h_sb[s][:], r_sb[q][:], r_sb[q][:], op=ALU.mult), selfwait=False)
                tok_mult[par][rp] = sV.n
                P.ins("vector", lambda e, s=ts: e.reduce_max(stv(15), h_sb[s][:], axis=AXX))
                P.ins("vector", lambda e: e.tensor_scalar(rmx(1), rmx(1), stv(15), 0.0,
                                                          op0=ALU.max, op1=ALU.add))
                P.wait("gpsimd", sV)
                P.ins("gpsimd", lambda e, k=jc, s=ts: e.dma_start(
                    out=bass.AP(hT_h, (s * 4 * 128) * J + k * 128,
                                [[J, 128], [128 * J, 4], [1, 128]]),
                    in_=bass.AP(h_sb[s], 0, [[512, 128], [128, 4], [1, 128]])), dHST[ts], 16)

    cross_part_max(rmx(1), 3)
    P.ins("vector", lambda e: e.tensor_scalar(ccs[0:1, :], zrow_s[0:1, :], 0.0, 0.0,
                                              op0=ALU.mult, op1=ALU.add))
    P.ins("vector", lambda e: e.tensor_scalar(ccs[0:1, 0:1], ROW[0:1, 3:4], 1.0, 0.0,
                                              op0=ALU.mult, op1=ALU.add))
    P.wait("gpsimd", sV)
    P.ins("gpsimd", lambda e: e.dma_start(out=cc_in2, in_=ccs[0:1, :]), dCC, 16)
    P.wait("gpsimd", dCC)
    P.ins("gpsimd", lambda e: e.collective_compute(
        "AllReduce", ALU.max, replica_groups=[list(range(NC))],
        ins=[cc_in2], outs=[cc_out2]), sCC)
    P.wait("gpsimd", sCC)
    P.ins("gpsimd", lambda e: e.dma_start(out=ROW[0:1, 8:16], in_=cc_out2), dCC, 16)
    P.wait("vector", dCC)
    P.ins("vector", lambda e: e.tensor_scalar(ROW[0:1, 24:25], ROW[0:1, 8:9], 1.0 / 448.0, 1e-6,
                                              op0=ALU.mult, op1=ALU.max))
    P.ins("vector", lambda e: e.tensor_scalar(ROW[0:1, 25:26], ROW[0:1, 24:25], 2.0, 0.0,
                                              op0=ALU.mult, op1=ALU.add))
    P.ins("vector", lambda e: e.reciprocal(ROW[0:1, 25:26], ROW[0:1, 25:26]))
    P.ins("vector", lambda e: e.tensor_scalar(ROW[0:1, 26:27], ROW[0:1, 24:25], ROW[0:1, 21:22], 4.0,
                                              op0=ALU.mult, op1=ALU.mult))
    P.ins("vector", lambda e: e.tensor_scalar(ROW[0:1, 27:28], ROW[0:1, 26:27], ROW[0:1, 26:27], 0.0,
                                              op0=ALU.mult, op1=ALU.add))
    P.ins("vector", lambda e: e.tensor_scalar(ccs[0:1, 0:1], ROW[0:1, 25:26], 1.0, 0.0, op0=ALU.mult, op1=ALU.add))
    P.ins("vector", lambda e: e.tensor_scalar(ccs[0:1, 1:2], ROW[0:1, 26:27], 1.0, 0.0, op0=ALU.mult, op1=ALU.add))
    P.ins("vector", lambda e: e.tensor_scalar(ccs[0:1, 2:3], ROW[0:1, 27:28], 1.0, 0.0, op0=ALU.mult, op1=ALU.add))
    with nc.psum_tensor("pscl2", [128, 8], F32) as pscl2:
        P.wait("tensor", sV)
        P.ins("tensor", lambda e: e.matmul(pscl2[:], onesr[0:1, :], ccs[0:1, :], start=True, stop=True), sT)
        P.wait("scalar", sT)
        P.ins("scalar", lambda e: e.activation(out=SCL2[:], in_=pscl2[:], func=AF.Copy, scale=1.0))

    full_barrier()
    ph_cm.close()
    hf32 = [sb(f"hf32_{i}", [128, J // 2], BF16, ph_c) for i in range(2)]
    h8s = [sb(f"h8s{i}", [128, J], F8, ph_c) for i in range(2)]
    lnff = sb("lnff0", [128, D], F32, ph_c)
    xm5 = sb("xm5_0", [128, D], F32, ph_c)

    with nc.psum_tensor("pq0", [128, D], F32) as pq0, \
         nc.psum_tensor("pq1", [128, D], F32) as pq1:
        poD = [pq0, pq1]
        for s4 in dHST:
            P.wait("sync", s4)
        P.wait("sync", dXMS)
        tok_hq = [0, 0]
        tok_hq_done = [0, 0]
        tok_add = 0
        tok_out = 0
        tok_mmD = [0, 0]
        tok_lnff = [0, 0]
        tok_stats = [0, 0]
        for tt in range(NT):
            par = tt % 2
            for half in range(2):
                P.wait("gpsimd", sA, tok_hq_done[half])
                P.ins("gpsimd", lambda e, t=tt, hf=half: e.dma_start(
                    out=hf32[hf][:], in_=bass.AP(hT_h, (t * 128) * J + hf * (J // 2),
                                                 [[J, 128], [1, J // 2]])), dHSx[half], 16)
            for half in range(2):
                P.wait("scalar", dHSx[half])
                P.wait("scalar", sT, tok_mmD[par])
                P.ins("scalar", lambda e, p=par, hf=half: e.activation(
                    out=h8s[p][:, hf * (J // 2):(hf + 1) * (J // 2)], in_=hf32[hf][:], func=AF.Copy,
                    scale=SCL2[:, 0:1]), selfwait=False)
                tok_hq_done[half] = sA.n
            tok_hq[par] = sA.n
            P.wait("sync", sV, tok_add)
            P.ins("sync", lambda e, t=tt: e.dma_start(
                out=xm5[:], in_=xmid_d[t * 128:(t + 1) * 128, :]), dXM5[0], 16)
            P.wait("tensor", sA, max(tok_hq[par], tok_lnff[par]))
            P.wait("tensor", sV, tok_stats[par])
            for ns in range(4):
                for k2 in range(NJ // 2):
                    P.ins("tensor", lambda e, k=k2, n=ns, p=par: e.matmul(
                        poD[p][:, n * 512:(n + 1) * 512],
                        bass.AP(h8s[p], k * 256, [[J, 128], [128, 2], [1, 128]]),
                        bass.AP(wpr8, (2 * k) * D + n * 512, [[NJ * D, 128], [D, 2], [1, 512]]),
                        start=(k == 0), stop=(k == NJ // 2 - 1),
                        perf_mode=PM.DoubleRow), sT)
            tok_mmD[par] = sT.n
            P.wait("vector", sT, tok_mmD[par])
            P.wait("scalar", sT, tok_mmD[par])
            ln_stats(poD[par][:], 0, 0, scale_c=SCL2[:, 1:2], scale_c2=SCL2[:, 2:3])
            tok_stats[par] = sV.n + 1
            P.ins("vector", lambda e: e.tensor_scalar(stv(6), stv(3), SCL2[:, 1:2], 0.0,
                                                      op0=ALU.mult, op1=ALU.add))
            P.wait("scalar", sV)
            P.wait("scalar", dOUT[0])
            P.ins("scalar", lambda e, p=par: e.activation(out=lnff[:], in_=poD[p][:], func=AF.Identity,
                                                          scale=stv(6), bias=stv(4)), selfwait=False)
            tok_lnff[par] = sA.n
            P.wait("vector", sA)
            P.wait("vector", dXM5[0])
            P.ins("vector", lambda e: e.tensor_tensor(lnff[:], xm5[:], lnff[:], op=ALU.add))
            tok_add = sV.n
            P.wait("sync", sV, tok_add)
            P.ins("sync", lambda e, t=tt: e.dma_start(
                out=out_d[t * 128:(t + 1) * 128, :], in_=lnff[:]), dOUT[0], 16)

    P.wait("sync", dOUT[0])
    P.wait("sync", dOUT[1])
    P.wait("gpsimd", dOUT[0])
    P.wait("gpsimd", dOUT[1])

    with nc.Block() as block:
        P.build(block)
    ph_c.close()
    es.close()
    return nc


_CACHED = {}


def kernel(L0_attn_x, L0_attn_Wq, L0_attn_Wk, L0_attn_Wv, L0_attn_Wo,
           L0_attn_qn_w, L0_attn_kn_w, L0_ffn_w_fc, L0_ffn_w_proj):
    x = np.ascontiguousarray(np.asarray(L0_attn_x, dtype=np.float32))
    f32 = np.float32

    def t(a):
        return np.ascontiguousarray(np.asarray(a, dtype=np.float32).T)

    wqT, wkT, wvT = t(L0_attn_Wq), t(L0_attn_Wk), t(L0_attn_Wv)
    woT, wfcT, wprT = t(L0_attn_Wo), t(L0_ffn_w_fc), t(L0_ffn_w_proj)
    qn = np.ascontiguousarray(np.asarray(L0_attn_qn_w, f32).reshape(1, DH))
    kn = np.ascontiguousarray(np.asarray(L0_attn_kn_w, f32).reshape(1, DH))

    inv_freq = (f32(1.0) / f32(10000.0) ** (np.arange(0, DH, 2).astype(f32) / f32(DH))).astype(f32)
    pos = np.arange(T, dtype=f32)
    emb = np.concatenate([pos[:, None] * inv_freq[None, :]] * 2, axis=1).astype(f32)
    cos_t = np.cos(emb).astype(f32)
    sin_t = np.sin(emb).astype(f32)
    sgn = np.where(np.arange(DH) < 64, f32(-1.0), f32(1.0))
    qs = f32(DH ** -0.5)
    tqc = np.ascontiguousarray(cos_t * qs)
    tqs = np.ascontiguousarray(sin_t * sgn * qs)
    tkc = np.ascontiguousarray(cos_t)
    tks = np.ascontiguousarray(sin_t * sgn)

    jj_ = np.arange(128)[:, None]
    ii_ = np.arange(256)[None, :]
    m_list = []
    for c_ in (-128, 0, 384, 512):
        d_ = c_ + ii_ - jj_
        m_list.append(((d_ >= 0) & (d_ <= 512)).astype(ml_dtypes.bfloat16))
    masks4 = np.ascontiguousarray(np.concatenate(m_list, axis=0))
    ones_bfm = np.ones((128, 128), dtype=ml_dtypes.bfloat16)
    id_bfm = np.eye(128, dtype=ml_dtypes.bfloat16)
    id_f32m = np.eye(128, dtype=f32)
    onesrm = np.ones((1, 128), dtype=f32)
    consts = np.zeros((128, 8), f32)
    consts[:, 0] = 1e-5
    consts[:, 1] = 1e-6
    zrow = np.zeros((1, 8), f32)

    if "nc" not in _CACHED:
        _CACHED["nc"] = build_program()
    nc = _CACHED["nc"]

    shared = dict(wqT=wqT, wkT=wkT, wvT=wvT, woT=woT, wfcT=wfcT, wprT=wprT,
                  qn=qn, kn=kn, tqc=tqc, tqs=tqs, tkc=tkc, tks=tks,
                  masks4=masks4, ones_bf=ones_bfm, id_bf=id_bfm,
                  id_f32=id_f32m, onesr=onesrm, consts=consts, zrow=zrow)
    in_maps = [dict(shared, x=np.ascontiguousarray(x[b]),
                    wslf=np.ascontiguousarray(wfcT[b * (D // NC):(b + 1) * (D // NC), :]),
                    wslp=np.ascontiguousarray(wprT[b * (J // NC):(b + 1) * (J // NC), :]))
               for b in range(NC)]
    res = run_bass_kernel_spmd(nc, in_maps, core_ids=list(range(NC)))
    return np.stack([res.results[b]["out"] for b in range(NC)], axis=0)


# version 15
# speedup vs baseline: 1.0070x; 1.0070x over previous
import numpy as np
import ml_dtypes
from contextlib import ExitStack

import concourse.bass as bass
import concourse.mybir as mybir
from concourse.bass_utils import run_bass_kernel_spmd

F32 = mybir.dt.float32
BF16 = mybir.dt.bfloat16
F8 = mybir.dt.float8e4
AF = mybir.ActivationFunctionType
ALU = mybir.AluOpType
AXX = mybir.AxisListType.X
PM = mybir.MatmulPerfMode

T, D, H, HKV, DH, G, WIN, J = 2048, 2048, 16, 4, 128, 4, 512, 8192
NT, ND, NJ, NC = 16, 16, 64, 8

ENGINES = ("sync", "scalar", "vector", "tensor", "gpsimd")


class Sem:
    __slots__ = ("h", "n")

    def __init__(self, h):
        self.h = h
        self.n = 0


class Sched:
    def __init__(self):
        self.ops = []
        self.own_sem = {}

    def emit(self, eng, fn):
        self.ops.append((eng, fn))

    def wait(self, eng, sem, n=None):
        tgt = sem.n if n is None else n
        if tgt <= 0:
            return
        self.emit(eng, lambda e, s=sem.h, t=tgt: e.wait_ge(s, t))

    def ins(self, eng, f, sem=None, inc=1, selfwait=True):
        own = self.own_sem.get(eng)
        if own is not None:
            if selfwait and own.n > 0:
                self.wait(eng, own)
            if sem is None:
                sem, inc = own, 1
        if sem is not None:
            sem.n += inc
        def fn(e, _f=f, _s=sem, _i=inc):
            r = _f(e)
            if _s is not None:
                r.then_inc(_s.h, _i)
        self.emit(eng, fn)

    def build(self, block):
        for ename in ENGINES:
            sec = getattr(block, ename)
            mine = [fn for (e, fn) in self.ops if e == ename]
            def body(eng, _my=mine):
                for fn in _my:
                    fn(eng)
            sec(body)


def build_program():
    nc = bass.Bass("TRN2", target_bir_lowering=False)

    dh = {}

    def din(name, shape, dt=F32):
        h = nc.dram_tensor(name, shape, dt, kind="ExternalInput")
        dh[name] = h
        return h.ap()

    x_d = din("x", [T, D])
    wqT_d = din("wqT", [D, H * DH])
    wkT_d = din("wkT", [D, HKV * DH])
    wvT_d = din("wvT", [D, HKV * DH])
    woT_d = din("woT", [H * DH, D])
    wfcT_d = din("wfcT", [D, J])
    wprT_d = din("wprT", [J, D])
    qn_d = din("qn", [1, DH])
    kn_d = din("kn", [1, DH])
    tqc_d = din("tqc", [T, DH])
    tqs_d = din("tqs", [T, DH])
    tkc_d = din("tkc", [T, DH])
    tks_d = din("tks", [T, DH])
    masks4_d = din("masks4", [4 * 128, 256], BF16)
    ones_bf_d = din("ones_bf", [128, 128], BF16)
    id_bf_d = din("id_bf", [128, 128], BF16)
    id_f32_d = din("id_f32", [128, 128], F32)
    onesr_d = din("onesr", [1, 128], F32)
    consts_d = din("consts", [128, 8], F32)
    zrow_d = din("zrow", [1, 8], F32)
    wslf_d = din("wslf", [D // NC, J])
    wslp_d = din("wslp", [J // NC, D])

    out_h = nc.dram_tensor("out", [T, D], F32, kind="ExternalOutput")
    out_d = out_h.ap()

    xmid_h = nc.dram_tensor("xmid_s", [T, D], F32)
    x2_h = nc.dram_tensor("x2_s", [T, D], F32)
    hT_h = nc.dram_tensor("hT_s", [NT * 128, J], BF16)
    ctx_h = nc.dram_tensor("ctx_s", [H * 128, T], BF16)
    xmid_d, x2_d, hT_d, ctx_d = xmid_h.ap(), x2_h.ap(), hT_h.ap(), ctx_h.ap()
    cc_in1 = nc.dram_tensor("cc_in1", [1, 8], F32).ap()
    cc_out1 = nc.dram_tensor("cc_out1", [1, 8], F32).ap()
    cc_in2 = nc.dram_tensor("cc_in2", [1, 8], F32).ap()
    cc_out2 = nc.dram_tensor("cc_out2", [1, 8], F32).ap()

    P = Sched()
    es = ExitStack()

    ALLSEMS = []

    def sem(name):
        s = Sem(es.enter_context(nc.semaphore(name)))
        ALLSEMS.append(s)
        return s

    def full_barrier():
        for _eng in ENGINES:
            for _s in ALLSEMS:
                P.wait(_eng, _s)

    sT = sem("s_pe")
    sA = sem("s_act")
    sV = sem("s_dve")
    sGc = sem("s_gpc")
    sCC = sem("s_cc")
    P.own_sem = {"vector": sV, "scalar": sA}

    dIN = sem("d_in")
    dX = [sem("d_x0"), sem("d_x1")]
    dTAB = sem("d_tab")
    dCTXL = [sem("d_ctxl0"), sem("d_ctxl1")]
    dXB5 = [sem("d_xb50"), sem("d_xb51")]
    dX2L = [sem("d_x2l0"), sem("d_x2l1")]
    dWFC = sem("d_wfc")
    dHSx = [sem("d_hs0"), sem("d_hs1")]
    dXM5 = [sem("d_xm50"), sem("d_xm51")]
    dOUT = [sem("d_out0"), sem("d_out1")]
    dWMX = [sem("d_wmx0")]
    dWKV = sem("d_wkv")
    dWQ = sem("d_wq")
    dCTXS = [sem("d_ctxs0"), sem("d_ctxs1")]
    dWO = [sem(f"d_wo{i}") for i in range(4)]
    dXMS = sem("d_xms")
    dX2S = sem("d_x2s")
    dWPR = sem("d_wpr")
    dHST = [sem(f"d_hst{i}") for i in range(4)]
    dCC = sem("d_cc")

    def sb(name, shape, dt, stk=es):
        return stk.enter_context(nc.sbuf_tensor(name, shape, dt))

    cons = sb("cons", [128, 8], F32)
    zrow_s = sb("zrow_s", [1, 8], F32)
    onesr = sb("onesr_s", [1, 128], F32)
    id_bf = sb("id_bf_s", [128, 128], BF16)
    id_f32 = sb("id_f32_s", [128, 128], F32)
    ones_bf = sb("ones_bf_s", [128, 128], BF16)
    masks4 = sb("masks4_s", [128, 4 * 256], BF16)
    qn_s = sb("qn_s", [1, DH], F32)
    kn_s = sb("kn_s", [1, DH], F32)
    wrow = sb("wrow", [1, 2 * DH], F32)
    wqB = sb("wqB", [128, DH], F32)
    wkB = sb("wkB", [128, DH], F32)
    STV = sb("STV", [128, 8 * 24], F32)
    STA = sb("STA", [128, 8 * 16], F32)
    RMX = sb("RMX", [128, 8 * 4], F32)
    SCL = sb("SCL", [128, 8], F32)
    SCL2 = sb("SCL2", [128, 8], F32)
    ROW = sb("ROWs", [1, 32], F32)
    ccs = sb("ccs_s", [1, 8], F32)
    tcol = sb("tcol", [128, 2], F32)
    trash = sb("trash", [128, 2048], BF16)

    def stv(i):
        return STV[:, 8 * i:8 * i + 1]

    def sta(i):
        return STA[:, 8 * i:8 * i + 1]

    def rmx(i):
        return RMX[:, 8 * i:8 * i + 1]

    for dst, src in [(cons, consts_d), (zrow_s, zrow_d), (onesr, onesr_d),
                     (id_bf, id_bf_d), (id_f32, id_f32_d), (ones_bf, ones_bf_d),
                     (qn_s, qn_d), (kn_s, kn_d)]:
        P.ins("sync", lambda e, d=dst, s=src: e.dma_start(out=d[:], in_=s), dIN, 16)
    P.ins("sync", lambda e: e.dma_start(
        out=bass.AP(masks4, 0, [[4 * 256, 128], [256, 4], [1, 256]]),
        in_=bass.AP(dh["masks4"], 0, [[256, 128], [128 * 256, 4], [1, 256]])), dIN, 16)
    P.ins("gpsimd", lambda e: e.memset(RMX[:], 0), sGc, 1)
    P.wait("vector", dIN)
    P.wait("scalar", dIN)
    P.wait("tensor", dIN)
    P.ins("vector", lambda e: e.tensor_scalar(wrow[0:1, 0:DH], qn_s[0:1, :], 1.0, 1.0,
                                              op0=ALU.mult, op1=ALU.add))
    P.ins("vector", lambda e: e.tensor_scalar(wrow[0:1, DH:2 * DH], kn_s[0:1, :], 1.0, 1.0,
                                              op0=ALU.mult, op1=ALU.add))

    with nc.psum_tensor("pinit", [128, 2 * DH], F32) as pinit:
        P.wait("tensor", sV)
        P.ins("tensor", lambda e: e.matmul(pinit[:, 0:DH], onesr[0:1, :], wrow[0:1, 0:DH],
                                           start=True, stop=True), sT)
        P.ins("tensor", lambda e: e.matmul(pinit[:, DH:2 * DH], onesr[0:1, :], wrow[0:1, DH:2 * DH],
                                           start=True, stop=True), sT)
        P.wait("scalar", sT)
        P.ins("scalar", lambda e: e.activation(out=wqB[:], in_=pinit[:, 0:DH], func=AF.Copy, scale=1.0))
        P.ins("scalar", lambda e: e.activation(out=wkB[:], in_=pinit[:, DH:2 * DH], func=AF.Copy, scale=1.0))

    ph_a = ExitStack()
    xnT = sb("xnT", [128, ND * T], BF16, ph_a)
    kT = sb("kT", [128, HKV * T], BF16, ph_a)
    v_sb = sb("v_sb", [128, HKV * T], BF16, ph_a)
    qTg = sb("qTg", [128, G * T], BF16, ph_a)
    tabc = sb("tabc", [128, NT * DH], F32, ph_a)
    tabs = sb("tabs", [128, NT * DH], F32, ph_a)
    xbuf = [sb(f"xbuf{i}", [128, D], F32, ph_a) for i in range(2)]
    xn_tok = sb("xn_tok", [128, D], BF16, ph_a)
    rop = [sb(f"rop{i}", [128, 3 * 128], F32, ph_a) for i in range(2)]
    qrot_bf = [sb(f"qrot{i}", [128, 128], BF16, ph_a) for i in range(2)]
    wkv_sb = sb("wkv_sb", [128, ND * 256], BF16, ph_a)
    wq_sb = sb("wq_sb", [128, ND * 512], BF16, ph_a)
    ctxs = [sb(f"ctxs{i}", [128, T], BF16, ph_a) for i in range(2)]
    expb = [sb(f"expb{i}", [128, 1536], BF16, ph_a) for i in range(2)]
    rzb = [sb(f"rzb{i}", [128, 256], F32, ph_a) for i in range(2)]
    wmx = [sb("wmx0", [128, 1024], F32, ph_a)]

    def xnT_t(dc, tt):
        return xnT[:, dc * T + tt * 128: dc * T + (tt + 1) * 128]

    def kT_t(hk, jt):
        return kT[:, hk * T + jt * 128: hk * T + (jt + 1) * 128]

    def v_t(hk, jt):
        return v_sb[:, hk * T + jt * 128: hk * T + (jt + 1) * 128]

    def qT_t(g, it):
        return qTg[:, g * T + it * 128: g * T + (it + 1) * 128]

    def tab_t(tab, tt):
        return tab[:, tt * 128:(tt + 1) * 128]

    def ln_stats(src_ap, sl, asl, scale_c=None, scale_c2=None):
        P.wait("scalar", sV)
        P.ins("vector", lambda e: e.reduce_sum(stv(sl), src_ap, axis=AXX))
        P.wait("scalar", sV)
        P.ins("scalar", lambda e: e.activation(out=trash[:], in_=src_ap, func=AF.Square,
                                               accum_out=sta(asl)))
        P.wait("vector", sA)
        if scale_c is None:
            P.ins("vector", lambda e: e.tensor_scalar(stv(sl), stv(sl), 1.0 / D, 0.0,
                                                      op0=ALU.mult, op1=ALU.add))
            P.ins("vector", lambda e: e.tensor_scalar(stv(sl + 1), sta(asl), 1.0 / D, 0.0,
                                                      op0=ALU.mult, op1=ALU.add))
        else:
            P.ins("vector", lambda e: e.tensor_scalar(stv(sl), stv(sl), scale_c, 1.0 / D,
                                                      op0=ALU.mult, op1=ALU.mult))
            P.ins("vector", lambda e: e.tensor_scalar(stv(sl + 1), sta(asl), scale_c2, 1.0 / D,
                                                      op0=ALU.mult, op1=ALU.mult))
        P.ins("vector", lambda e: e.tensor_scalar(stv(sl + 2), stv(sl), stv(sl), 0.0,
                                                  op0=ALU.mult, op1=ALU.add))
        P.ins("vector", lambda e: e.tensor_scalar(stv(sl + 2), stv(sl + 1), stv(sl + 2), 0.0,
                                                  op0=ALU.subtract, op1=ALU.add))
        P.wait("scalar", sV)
        P.ins("scalar", lambda e: e.activation(out=sta(asl + 1), in_=stv(sl + 2), func=AF.Sqrt,
                                               scale=1.0, bias=cons[:, 0:1]))
        P.wait("vector", sA)
        P.ins("vector", lambda e: e.reciprocal(stv(sl + 3), sta(asl + 1)))
        P.ins("vector", lambda e: e.tensor_scalar(stv(sl + 4), stv(sl), stv(sl + 3), -1.0,
                                                  op0=ALU.mult, op1=ALU.mult))

    full_barrier()
    P.wait("vector", sGc)
    wmax_tiles = []
    for (w_ap_, rows_, cols_, slot_) in [(wslf_d, D // NC, J, 2), (wslp_d, J // NC, D, 3)]:
        for r_ in range(rows_ // 128):
            for c_ in range(cols_ // 1024):
                wmax_tiles.append((w_ap_, r_, c_, slot_))
    wmax_state = {"i": 0, "tok": [0, 0]}

    def wmax_step(nstep=1):
        for _ in range(nstep):
            i = wmax_state["i"]
            if i >= len(wmax_tiles):
                return
            wmax_state["i"] += 1
            w_ap_, r_, c_, slot_ = wmax_tiles[i]
            wp = 0
            P.wait("gpsimd", sV, wmax_state["tok"][wp])
            P.ins("gpsimd", lambda e, w=w_ap_, rr=r_, cc=c_, p=wp: e.dma_start(
                out=wmx[p][:], in_=w[rr * 128:(rr + 1) * 128, cc * 1024:(cc + 1) * 1024]), dWMX[wp], 16)
            P.wait("vector", dWMX[wp])
            P.ins("vector", lambda e, p=wp: e.reduce_max(stv(15), wmx[p][:], axis=AXX,
                                                         apply_absolute_value=True))
            P.ins("vector", lambda e, s=slot_: e.tensor_scalar(rmx(s), rmx(s), stv(15), 0.0,
                                                               op0=ALU.max, op1=ALU.add))
            wmax_state["tok"][wp] = sV.n

    with nc.psum_tensor("ptr0", [128, 128], BF16) as ptr0, \
         nc.psum_tensor("ptr1", [128, 128], BF16) as ptr1:
        ptr = [ptr0, ptr1]
        for tt in range(NT):
            par = tt % 2
            wmax_step(2)
            P.wait("sync", sA)
            P.wait("sync", sV)
            P.ins("sync", lambda e, t=tt, p=par: e.dma_start(
                out=xbuf[p][:], in_=x_d[t * 128:(t + 1) * 128, :]), dX[par], 16)
            P.wait("vector", dX[par])
            P.wait("scalar", dX[par])
            ln_stats(xbuf[par][:], 0, 0)
            P.wait("scalar", sV)
            P.ins("scalar", lambda e, p=par: e.activation(out=xn_tok[:], in_=xbuf[p][:], func=AF.Identity,
                                                          scale=stv(3), bias=stv(4)))
            for dc in range(ND):
                tp = dc % 2
                P.wait("tensor", sA)
                P.ins("tensor", lambda e, d=dc, q=tp: e.transpose(
                    ptr[q][:], xn_tok[:, d * 128:(d + 1) * 128], id_bf[:]), sT)
                P.wait("scalar", sT)
                P.ins("scalar", lambda e, d=dc, t=tt, q=tp: e.activation(
                    out=xnT_t(d, t), in_=ptr[q][:], func=AF.Copy, scale=1.0))

    P.ins("sync", lambda e: e.dma_start(
        out=bass.AP(tabc, 0, [[NT * DH, 128], [DH, NT], [1, DH]]),
        in_=bass.AP(dh["tkc"], 0, [[DH, 128], [128 * DH, NT], [1, DH]])), dTAB, 16)
    P.ins("sync", lambda e: e.dma_start(
        out=bass.AP(tabs, 0, [[NT * DH, 128], [DH, NT], [1, DH]]),
        in_=bass.AP(dh["tks"], 0, [[DH, 128], [128 * DH, NT], [1, DH]])), dTAB, 16)
    P.wait("vector", dTAB)

    with nc.psum_tensor("pkv0", [128, 256], F32) as pkv0, \
         nc.psum_tensor("pkv1", [128, 256], F32) as pkv1, \
         nc.psum_tensor("ptk0", [128, 128], BF16) as ptk0, \
         nc.psum_tensor("ptk1", [128, 128], BF16) as ptk1:
        pkv = [pkv0, pkv1]
        ptk = [ptk0, ptk1]
        tokA = [0, 0]
        tokV = [0, 0]
        tokTr = [0, 0]
        tokEvK = [0, 0]
        for hk in range(HKV):
            P.wait("gpsimd", sT)
            P.ins("gpsimd", lambda e, h=hk: e.dma_start(
                out=bass.AP(wkv_sb, 0, [[ND * 256, 128], [256, ND], [1, 128]]),
                in_=bass.AP(dh["wkT"], h * 128, [[HKV * DH, 128], [128 * HKV * DH, ND], [1, 128]])), dWKV, 16)
            P.ins("gpsimd", lambda e, h=hk: e.dma_start(
                out=bass.AP(wkv_sb, 128, [[ND * 256, 128], [256, ND], [1, 128]]),
                in_=bass.AP(dh["wvT"], h * 128, [[HKV * DH, 128], [128 * HKV * DH, ND], [1, 128]])), dWKV, 16)
            for tt in range(NT):
                par = tt % 2
                wmax_step(2)
                P.wait("tensor", dWKV)
                P.wait("tensor", sA, tokA[par])
                P.wait("tensor", sV, tokV[par])
                for dc in range(ND):
                    P.ins("tensor", lambda e, d=dc, t=tt, p=par: e.matmul(
                        pkv[p][:], xnT_t(d, t), wkv_sb[:, d * 256:(d + 1) * 256],
                        start=(d == 0), stop=(d == ND - 1)), sT)
                tok_mm = sT.n
                P.wait("scalar", sT, tok_mm)
                P.wait("scalar", sV, tokV[par])
                P.ins("scalar", lambda e, p=par: e.activation(out=trash[:, 0:128], in_=pkv[p][:, 0:128],
                                                              func=AF.Square, accum_out=sta(8)), selfwait=False)
                P.ins("scalar", lambda e: e.activation(out=sta(12), in_=sta(8), func=AF.Sqrt,
                                                       scale=1.0 / DH, bias=cons[:, 1:2]))
                tok_std = sA.n
                P.wait("vector", sA, tok_std)
                P.ins("vector", lambda e: e.reciprocal(stv(16), sta(12)))
                P.ins("vector", lambda e, p=par: e.scalar_tensor_tensor(rop[p][:, 0:128], pkv[p][:, 0:128],
                                                                        stv(16), wkB[:],
                                                                        op0=ALU.mult, op1=ALU.mult))
                tokV[par] = sV.n
                P.wait("scalar", sV, tokV[par])
                P.ins("scalar", lambda e, h=hk, t=tt, p=par: e.activation(
                    out=v_t(h, t), in_=pkv[p][:, 128:256], func=AF.Copy, scale=1.0), selfwait=False)
                tokA[par] = sA.n
                P.wait("vector", sT, tokTr[par])
                P.ins("vector", lambda e, p=par, t=tt: e.tensor_tensor(
                    rop[p][:, 128:256], rop[p][:, 0:128], tab_t(tabc, t), op=ALU.mult))
                P.ins("vector", lambda e, p=par, t=tt: e.tensor_tensor(
                    rop[p][:, 256:320], rop[p][:, 64:128], tab_t(tabs, t)[:, 0:64], op=ALU.mult))
                P.ins("vector", lambda e, p=par, t=tt: e.tensor_tensor(
                    rop[p][:, 320:384], rop[p][:, 0:64], tab_t(tabs, t)[:, 64:128], op=ALU.mult))
                P.ins("vector", lambda e, p=par: e.tensor_tensor(
                    qrot_bf[p][:], rop[p][:, 128:256], rop[p][:, 256:384], op=ALU.add))
                tok_rot = sV.n
                P.wait("tensor", sV, tok_rot)
                P.wait("tensor", sA, tokEvK[par])
                P.ins("tensor", lambda e, p=par: e.transpose(ptk[p][:], qrot_bf[p][:], id_bf[:]), sT)
                tokTr[par] = sT.n
                P.wait("scalar", sT, tokTr[par])
                P.ins("scalar", lambda e, h=hk, t=tt, p=par: e.activation(
                    out=kT_t(h, t), in_=ptk[p][:], func=AF.Copy, scale=1.0), selfwait=False)
                tokEvK[par] = sA.n

    P.wait("sync", sV)
    P.ins("sync", lambda e: e.dma_start(
        out=bass.AP(tabc, 0, [[NT * DH, 128], [DH, NT], [1, DH]]),
        in_=bass.AP(dh["tqc"], 0, [[DH, 128], [128 * DH, NT], [1, DH]])), dTAB, 16)
    P.ins("sync", lambda e: e.dma_start(
        out=bass.AP(tabs, 0, [[NT * DH, 128], [DH, NT], [1, DH]]),
        in_=bass.AP(dh["tqs"], 0, [[DH, 128], [128 * DH, NT], [1, DH]])), dTAB, 16)
    P.wait("vector", dTAB)

    tokQa = [0, 0]
    tokQv = [0, 0]
    tokTrQ = [0, 0]
    tokEvQ = [0, 0]
    tE = [0, 0]
    tSc = [0, 0]
    tM = [0, 0]
    tZp = [0, 0]
    tCp = [0, 0]
    tRZp = [0, 0]
    tNp = [0, 0]
    for hk in range(HKV):
        P.wait("gpsimd", sT)
        P.ins("gpsimd", lambda e, h=hk: e.dma_start(
            out=bass.AP(wq_sb, 0, [[ND * 512, 128], [512, ND], [1, 512]]),
            in_=bass.AP(dh["wqT"], h * 512, [[H * DH, 128], [128 * H * DH, ND], [1, 512]])), dWQ, 16)
        with nc.psum_tensor(f"pq0_{hk}", [128, 512], F32) as pq0, \
             nc.psum_tensor(f"pq1_{hk}", [128, 512], F32) as pq1, \
             nc.psum_tensor(f"ptq0_{hk}", [128, 128], BF16) as ptq0, \
             nc.psum_tensor(f"ptq1_{hk}", [128, 128], BF16) as ptq1:
            pq = [pq0, pq1]
            ptq = [ptq0, ptq1]
            P.wait("tensor", sA)
            P.wait("tensor", sV)
            for tt in range(NT):
                par = tt % 2
                wmax_step(2)
                P.wait("tensor", dWQ)
                P.wait("tensor", sA, tokQa[par])
                P.wait("tensor", sV, tokQv[par])
                for dc in range(ND):
                    P.ins("tensor", lambda e, d=dc, t=tt, p=par: e.matmul(
                        pq[p][:], xnT_t(d, t), wq_sb[:, d * 512:(d + 1) * 512],
                        start=(d == 0), stop=(d == ND - 1)), sT)
                tok_mm = sT.n
                P.wait("scalar", sT, tok_mm)
                P.wait("scalar", sV, tokQv[par])
                for g in range(G):
                    P.ins("scalar", lambda e, gg=g, p=par: e.activation(
                        out=trash[:, gg * 128:(gg + 1) * 128], in_=pq[p][:, gg * 128:(gg + 1) * 128],
                        func=AF.Square, accum_out=sta(8 + gg)), selfwait=False)
                P.wait("scalar", sA)
                for g in range(G):
                    P.ins("scalar", lambda e, gg=g: e.activation(out=sta(12 + gg), in_=sta(8 + gg),
                                                                 func=AF.Sqrt, scale=1.0 / DH,
                                                                 bias=cons[:, 1:2]), selfwait=False)
                tok_std = sA.n
                P.wait("vector", sA, tok_std)
                for g in range(G):
                    P.ins("vector", lambda e, gg=g: e.reciprocal(stv(16 + gg), sta(12 + gg)), selfwait=False)
                for g in range(G):
                    gp = g % 2
                    P.wait("vector", sT, tokTrQ[gp])
                    P.ins("vector", lambda e, gg=g, p=par: e.scalar_tensor_tensor(
                        rop[0][:, 0:128], pq[p][:, gg * 128:(gg + 1) * 128], stv(16 + gg), wqB[:],
                        op0=ALU.mult, op1=ALU.mult))
                    P.ins("vector", lambda e, t=tt: e.tensor_tensor(
                        rop[0][:, 128:256], rop[0][:, 0:128], tab_t(tabc, t), op=ALU.mult))
                    P.ins("vector", lambda e, t=tt: e.tensor_tensor(
                        rop[0][:, 256:320], rop[0][:, 64:128], tab_t(tabs, t)[:, 0:64], op=ALU.mult))
                    P.ins("vector", lambda e, t=tt: e.tensor_tensor(
                        rop[0][:, 320:384], rop[0][:, 0:64], tab_t(tabs, t)[:, 64:128], op=ALU.mult))
                    P.ins("vector", lambda e, p=gp: e.tensor_tensor(
                        qrot_bf[p][:], rop[0][:, 128:256], rop[0][:, 256:384], op=ALU.add))
                    tok_rot = sV.n
                    P.wait("tensor", sV, tok_rot)
                    P.wait("tensor", sA, tokEvQ[gp])
                    P.ins("tensor", lambda e, p=gp: e.transpose(ptq[p][:], qrot_bf[p][:], id_bf[:]), sT)
                    tokTrQ[gp] = sT.n
                    P.wait("scalar", sT, tokTrQ[gp])
                    P.ins("scalar", lambda e, gg=g, t=tt, p=gp: e.activation(
                        out=qT_t(gg, t), in_=ptq[p][:], func=AF.Copy, scale=1.0), selfwait=False)
                    tokEvQ[gp] = sA.n
                tokQv[par] = sV.n
                tokQa[par] = sA.n
        with nc.psum_tensor(f"psS0_{hk}", [128, 1536], F32) as psS0, \
             nc.psum_tensor(f"psS1_{hk}", [128, 1536], F32) as psS1, \
             nc.psum_tensor(f"pcc0_{hk}", [128, 512], F32) as pcc0, \
             nc.psum_tensor(f"pcc1_{hk}", [128, 512], F32) as pcc1:
            psS = [psS0, psS1]
            pzzS = [psS0[:, 0:256], psS1[:, 0:256]]
            pccS = [pcc0[:, 0:256], pcc1[:, 0:256]]
            base = max(tokQa[0], tokQa[1])
            tE = [base, base]
            tSc = [0, 0]
            tM = [0, 0]
            tRZv = [0, 0]
            tNv = [0, 0]
            P.wait("tensor", sV)

            NR = NT // 2

            def jinfo(r):
                out = []
                for jt in range(max(0, 2 * r - 4), 2 * r + 2):
                    c = (2 * r - jt) * 128
                    slot = {-128: 0, 0: 1, 384: 2, 512: 3}.get(c)
                    out.append((jt, slot))
                return out

            def emit_scores(r, g):
                par = r % 2
                P.wait("tensor", sA, tE[par])
                P.wait("tensor", sV, tRZv[par])
                for jj, (jt, _) in enumerate(jinfo(r)):
                    P.ins("tensor", lambda e, p=par, jx=jj, hh=hk, j=jt, rr=r, gg=g: e.matmul(
                        psS[p][:, jx * 256:(jx + 1) * 256], kT_t(hh, j),
                        qTg[:, gg * T + rr * 256:(gg * T + (rr + 1) * 256)],
                        start=True, stop=True), sT)
                tSc[par] = sT.n

            def emit_exp(r):
                par = r % 2
                nj = len(jinfo(r))
                P.wait("scalar", sT, tSc[par])
                P.ins("scalar", lambda e, p=par, n=nj: e.activation(
                    out=expb[p][:, 0:n * 256], in_=psS[p][:, 0:n * 256], func=AF.Exp), selfwait=False)
                tE[par] = sA.n

            def emit_masks(r):
                par = r % 2
                P.wait("gpsimd", sA, tE[par])
                for jj, (jt, slot) in enumerate(jinfo(r)):
                    if slot is None:
                        continue
                    P.ins("gpsimd", lambda e, p=par, jx=jj, m=slot: e.tensor_tensor(
                        expb[p][:, jx * 256:(jx + 1) * 256], expb[p][:, jx * 256:(jx + 1) * 256],
                        masks4[:, m * 256:(m + 1) * 256], op=ALU.mult), sGc)
                tM[par] = sGc.n

            def emit_zctx(r):
                par = r % 2
                ji = jinfo(r)
                nj = len(ji)
                P.wait("tensor", sGc, tM[par])
                P.wait("tensor", sV, tRZv[par])
                for jj in range(nj):
                    P.ins("tensor", lambda e, p=par, j=jj, n=nj: e.matmul(
                        pzzS[p], ones_bf[:], expb[p][:, j * 256:(j + 1) * 256],
                        start=(j == 0), stop=(j == n - 1)), sT)
                tz = sT.n
                P.wait("tensor", sV, tNv[par])
                for jj, (jt, _) in enumerate(ji):
                    P.ins("tensor", lambda e, p=par, hh=hk, j=jt, jx=jj, n=nj: e.matmul(
                        pccS[p], v_t(hh, j), expb[p][:, jx * 256:(jx + 1) * 256],
                        start=(jx == 0), stop=(jx == n - 1)), sT)
                return tz, sT.n

            for g in range(G):
                h = hk * G + g
                cpar = h % 2
                P.wait("vector", dCTXS[cpar])
                emit_scores(0, g)
                emit_exp(0)
                emit_masks(0)
                for r in range(NR):
                    par = r % 2
                    if r + 1 < NR:
                        emit_scores(r + 1, g)
                        emit_exp(r + 1)
                    tz, tc = emit_zctx(r)
                    P.wait("vector", sT, tz)
                    P.ins("vector", lambda e, p=par: e.reciprocal(rzb[p][:], pzzS[p]), selfwait=False)
                    tRZv[par] = sV.n
                    P.wait("vector", sT, tc)
                    P.ins("vector", lambda e, p=par, c=cpar, rr=r: e.tensor_tensor(
                        ctxs[c][:, rr * 256:(rr + 1) * 256], pccS[p], rzb[p][:], op=ALU.mult))
                    tNv[par] = sV.n
                    if r + 1 < NR:
                        emit_masks(r + 1)
                P.wait("gpsimd", sV)
                P.ins("gpsimd", lambda e, hh=h, c=cpar: e.dma_start(
                    out=ctx_d[hh * 128:(hh + 1) * 128, :], in_=ctxs[c][:]), dCTXS[cpar], 16)
            tokQa = [max(tokQa[0], tE[0]), max(tokQa[1], tE[1])]

    wmax_step(len(wmax_tiles))

    full_barrier()
    ph_a.close()
    ph5 = ExitStack()
    wo_sb = sb("wo_sb", [128, ND * D], BF16, ph5)
    ctx_full = sb("ctx_full", [128, ND * T], BF16, ph5)
    xb5 = [sb(f"xb5_{i}", [128, D], F32, ph5) for i in range(2)]
    lnat = sb("lnat", [128, D], F32, ph5)
    xmid_sb = sb("xmid_sb", [128, D], F32, ph5)
    x2_sb = sb("x2_sb", [128, D], F32, ph5)

    wo_f = [sb(f"wo_f{i}", [128, D], F32, ph5) for i in range(2)]
    P.wait("sync", dCTXS[0])
    P.wait("sync", dCTXS[1])
    for hd in range(ND):
        P.ins("sync", lambda e, d=hd: e.dma_start(
            out=ctx_full[:, d * T:(d + 1) * T], in_=ctx_d[d * 128:(d + 1) * 128, :]), dCTXL[hd % 2], 16)
    tok_woc = [0] * ND
    tok_wof = [0, 0]
    for hd in range(ND):
        wp = hd % 2
        P.wait("sync", sA, tok_wof[wp])
        P.ins("sync", lambda e, d=hd, p=wp: e.dma_start(
            out=wo_f[p][:], in_=woT_d[d * 128:(d + 1) * 128, :]), dWO[hd % 4], 16)
        P.wait("scalar", dWO[hd % 4])
        P.ins("scalar", lambda e, d=hd, p=wp: e.activation(
            out=wo_sb[:, d * D:(d + 1) * D], in_=wo_f[p][:], func=AF.Copy, scale=1.0), selfwait=False)
        tok_woc[hd] = sA.n
        tok_wof[wp] = sA.n

    with nc.psum_tensor("po0", [128, D], F32) as po0, \
         nc.psum_tensor("po1", [128, D], F32) as po1:
        po = [po0, po1]
        for tt in range(NT):
            par = tt % 2
            P.wait("sync", sV)
            P.ins("sync", lambda e, t=tt, p=par: e.dma_start(
                out=xb5[p][:], in_=x_d[t * 128:(t + 1) * 128, :]), dXB5[par], 16)
            if tt == 0:
                P.wait("tensor", dCTXL[0])
                P.wait("tensor", dCTXL[1])
            P.wait("tensor", sA)
            P.wait("tensor", sV)
            for hd in range(ND):
                if tt == 0:
                    P.wait("tensor", sA, tok_woc[hd])
                for ns in range(4):
                    P.ins("tensor", lambda e, d=hd, n=ns, p=par, t=tt: e.matmul(
                        po[p][:, n * 512:(n + 1) * 512], ctx_full[:, d * T + t * 128: d * T + (t + 1) * 128],
                        wo_sb[:, d * D + n * 512: d * D + (n + 1) * 512],
                        start=(d == 0), stop=(d == ND - 1)), sT)
            P.wait("vector", sT)
            P.wait("scalar", sT)
            ln_stats(po[par][:], 0, 0)
            P.wait("scalar", sV)
            P.ins("scalar", lambda e, p=par: e.activation(out=lnat[:], in_=po[p][:], func=AF.Identity,
                                                          scale=stv(3), bias=stv(4)))
            P.wait("vector", sA)
            P.wait("vector", dXB5[par])
            P.wait("vector", dXMS)
            P.ins("vector", lambda e, p=par: e.tensor_tensor(xmid_sb[:], xb5[p][:], lnat[:], op=ALU.add))
            P.wait("gpsimd", sV)
            P.ins("gpsimd", lambda e, t=tt: e.dma_start(
                out=xmid_d[t * 128:(t + 1) * 128, :], in_=xmid_sb[:]), dXMS, 16)
            ln_stats(xmid_sb[:], 8, 4)
            P.wait("scalar", sV)
            P.wait("scalar", dX2S)
            P.ins("scalar", lambda e: e.activation(out=x2_sb[:], in_=xmid_sb[:], func=AF.Identity,
                                                   scale=stv(11), bias=stv(12)))
            P.wait("vector", sA)
            P.ins("vector", lambda e: e.reduce_max(stv(15), x2_sb[:], axis=AXX,
                                                   apply_absolute_value=True))
            P.ins("vector", lambda e: e.tensor_scalar(rmx(0), rmx(0), stv(15), 0.0,
                                                      op0=ALU.max, op1=ALU.add))
            P.wait("gpsimd", sA)
            P.ins("gpsimd", lambda e, t=tt: e.dma_start(
                out=x2_d[t * 128:(t + 1) * 128, :], in_=x2_sb[:]), dX2S, 16)
    full_barrier()
    ph5.close()

    def cross_part_max(col_ap, row_col):
        with nc.psum_tensor(f"pmx{row_col}", [128, 128], F32) as pmx:
            P.ins("vector", lambda e: e.tensor_scalar(tcol[:, 0:1], col_ap, 1.0, 0.0,
                                                      op0=ALU.mult, op1=ALU.add))
            P.wait("tensor", sV)
            P.ins("tensor", lambda e: e.transpose(pmx[0:1, 0:128], tcol[:, 0:1], id_f32[:]), sT)
            P.wait("vector", sT)
            P.ins("vector", lambda e, c=row_col: e.reduce_max(
                ROW[0:1, c:c + 1], pmx[0:1, 0:128], axis=AXX))

    cross_part_max(rmx(0), 0)
    cross_part_max(rmx(2), 1)
    cross_part_max(rmx(3), 2)

    P.ins("vector", lambda e: e.tensor_scalar(ccs[0:1, :], zrow_s[0:1, :], 0.0, 0.0,
                                              op0=ALU.mult, op1=ALU.add))
    P.ins("vector", lambda e: e.tensor_scalar(ccs[0:1, 0:1], ROW[0:1, 0:1], 1.0, 0.0,
                                              op0=ALU.mult, op1=ALU.add))
    P.ins("vector", lambda e: e.tensor_scalar(ccs[0:1, 1:2], ROW[0:1, 1:2], 1.0, 0.0,
                                              op0=ALU.mult, op1=ALU.add))
    P.ins("vector", lambda e: e.tensor_scalar(ccs[0:1, 2:3], ROW[0:1, 2:3], 1.0, 0.0,
                                              op0=ALU.mult, op1=ALU.add))
    P.wait("gpsimd", sV)
    P.ins("gpsimd", lambda e: e.dma_start(out=cc_in1, in_=ccs[0:1, :]), dCC, 16)
    P.wait("gpsimd", dCC)
    P.ins("gpsimd", lambda e: e.collective_compute(
        "AllReduce", ALU.max, replica_groups=[list(range(NC))],
        ins=[cc_in1], outs=[cc_out1]), sCC)
    P.wait("gpsimd", sCC)
    P.ins("gpsimd", lambda e: e.dma_start(out=ROW[0:1, 8:16], in_=cc_out1), dCC, 16)
    P.wait("vector", dCC)
    P.ins("vector", lambda e: e.tensor_scalar(ROW[0:1, 16:17], ROW[0:1, 8:9], 1.0 / 448.0, 1e-6,
                                              op0=ALU.mult, op1=ALU.max))
    P.ins("vector", lambda e: e.tensor_scalar(ROW[0:1, 17:18], ROW[0:1, 16:17], 2.0, 0.0,
                                              op0=ALU.mult, op1=ALU.add))
    P.ins("vector", lambda e: e.reciprocal(ROW[0:1, 17:18], ROW[0:1, 17:18]))
    P.ins("vector", lambda e: e.tensor_scalar(ROW[0:1, 18:19], ROW[0:1, 9:10], 1.0 / 448.0, 1e-6,
                                              op0=ALU.mult, op1=ALU.max))
    P.ins("vector", lambda e: e.tensor_scalar(ROW[0:1, 19:20], ROW[0:1, 18:19], 2.0, 0.0,
                                              op0=ALU.mult, op1=ALU.add))
    P.ins("vector", lambda e: e.reciprocal(ROW[0:1, 19:20], ROW[0:1, 19:20]))
    P.ins("vector", lambda e: e.tensor_scalar(ROW[0:1, 20:21], ROW[0:1, 16:17], ROW[0:1, 18:19], 4.0,
                                              op0=ALU.mult, op1=ALU.mult))
    P.ins("vector", lambda e: e.tensor_scalar(ROW[0:1, 21:22], ROW[0:1, 10:11], 1.0 / 448.0, 1e-6,
                                              op0=ALU.mult, op1=ALU.max))
    P.ins("vector", lambda e: e.tensor_scalar(ROW[0:1, 22:23], ROW[0:1, 21:22], 2.0, 0.0,
                                              op0=ALU.mult, op1=ALU.add))
    P.ins("vector", lambda e: e.reciprocal(ROW[0:1, 22:23], ROW[0:1, 22:23]))
    P.ins("vector", lambda e: e.tensor_scalar(ccs[0:1, 0:1], ROW[0:1, 17:18], 1.0, 0.0, op0=ALU.mult, op1=ALU.add))
    P.ins("vector", lambda e: e.tensor_scalar(ccs[0:1, 1:2], ROW[0:1, 20:21], 1.0, 0.0, op0=ALU.mult, op1=ALU.add))
    P.ins("vector", lambda e: e.tensor_scalar(ccs[0:1, 2:3], ROW[0:1, 19:20], 1.0, 0.0, op0=ALU.mult, op1=ALU.add))
    P.ins("vector", lambda e: e.tensor_scalar(ccs[0:1, 3:4], ROW[0:1, 22:23], 1.0, 0.0, op0=ALU.mult, op1=ALU.add))
    with nc.psum_tensor("pscl", [128, 8], F32) as pscl:
        P.wait("tensor", sV)
        P.ins("tensor", lambda e: e.matmul(pscl[:], onesr[0:1, :], ccs[0:1, :], start=True, stop=True), sT)
        P.wait("scalar", sT)
        P.ins("scalar", lambda e: e.activation(out=SCL[:], in_=pscl[:], func=AF.Copy, scale=1.0))

    ph_c = ExitStack()
    wpr8 = sb("wpr8", [128, NJ * D], F8, ph_c)
    ph_cm = ExitStack()
    x8T = sb("x8T", [128, ND * T], F8, ph_cm)

    with ExitStack() as pro, \
         nc.psum_tensor("ptc0", [128, 128], BF16) as ptc0, \
         nc.psum_tensor("ptc1", [128, 128], BF16) as ptc1:
        xbC = [sb(f"xbC{i}", [128, D], F32, pro) for i in range(2)]
        x8tok = sb("x8tok", [128, D], F8, pro)
        xqbf = sb("xqbf", [128, D], BF16, pro)
        ptc = [ptc0, ptc1]
        full_barrier()
        P.wait("sync", dX2S)
        tok_x8 = [0] * 4
        for tt in range(NT):
            par = tt % 2
            P.wait("sync", sA)
            P.ins("sync", lambda e, t=tt, p=par: e.dma_start(
                out=xbC[p][:], in_=x2_d[t * 128:(t + 1) * 128, :]), dX2L[par], 16)
            P.wait("scalar", dX2L[par])
            P.ins("scalar", lambda e, p=par: e.activation(out=x8tok[:], in_=xbC[p][:], func=AF.Copy,
                                                          scale=SCL[:, 0:1]))
            P.ins("scalar", lambda e: e.activation(out=xqbf[:], in_=x8tok[:], func=AF.Copy, scale=1.0))
            for dc in range(ND):
                tp = dc % 2
                P.wait("tensor", sA)
                P.ins("tensor", lambda e, d=dc, q=tp: e.transpose(
                    ptc[q][:], xqbf[:, d * 128:(d + 1) * 128], id_bf[:]), sT)
                P.wait("scalar", sT)
                P.ins("scalar", lambda e, d=dc, t=tt, q=tp: e.activation(
                    out=x8T[:, d * T + t * 128: d * T + (t + 1) * 128],
                    in_=ptc[q][:], func=AF.Copy, scale=1.0))
            if tt % 4 == 3:
                tok_x8[tt // 4] = sA.n

    full_barrier()
    wfc32 = sb("wfc32", [128, D], F32, ph_cm)
    wfc8 = [sb(f"wfc8_{i}", [128, D], F8, ph_cm) for i in range(2)]
    wpr32 = sb("wpr32_0", [128, D], F32, ph_cm)
    h_sb = [sb(f"h_sb{i}", [128, 512], BF16, ph_cm) for i in range(4)]
    r_sb = [sb(f"r_sb{i}", [128, 512], F32, ph_cm) for i in range(2)]

    with nc.psum_tensor("ph0", [128, 512], F32) as ph0, \
         nc.psum_tensor("ph1", [128, 512], F32) as ph1, \
         nc.psum_tensor("ph2", [128, 512], F32) as ph2, \
         nc.psum_tensor("ph3", [128, 512], F32) as ph3, \
         nc.psum_tensor("ph4", [128, 512], F32) as ph4, \
         nc.psum_tensor("ph5", [128, 512], F32) as ph5b, \
         nc.psum_tensor("ph6", [128, 512], F32) as ph6, \
         nc.psum_tensor("ph7", [128, 512], F32) as ph7:
        phh = [[ph0, ph1, ph2, ph3], [ph4, ph5b, ph6, ph7]]
        tok_quant = [0, 0]
        tok_quant_done = 0
        tok_wprq = 0
        tok_relu = [[0] * 4, [0] * 4]
        tok_mult = [[0, 0], [0, 0]]
        tok_mm = [[0] * 4, [0] * 4]
        for jc in range(NJ):
            par = jc % 2
            P.wait("sync", sA, tok_quant_done)
            P.ins("sync", lambda e, k=jc: e.dma_start(
                out=bass.AP(wfc32, 0, [[D, 128], [128, ND], [1, 128]]),
                in_=bass.AP(dh["wfcT"], k * 128, [[J, 128], [128 * J, ND], [1, 128]])), dWFC, 16)
            P.wait("scalar", dWFC)
            P.wait("scalar", sT, tok_mm[par][3])
            P.ins("scalar", lambda e, p=par: e.activation(out=wfc8[p][:], in_=wfc32[:], func=AF.Copy,
                                                          scale=SCL[:, 2:3]), selfwait=False)
            tok_quant[par] = sA.n
            tok_quant_done = sA.n
            P.wait("gpsimd", sA, tok_wprq)
            P.ins("gpsimd", lambda e, k=jc: e.dma_start(
                out=wpr32[:], in_=wprT_d[k * 128:(k + 1) * 128, :]), dWPR, 16)
            P.wait("scalar", dWPR)
            P.ins("scalar", lambda e, k=jc: e.activation(
                out=wpr8[:, k * D:(k + 1) * D], in_=wpr32[:], func=AF.Copy,
                scale=SCL[:, 3:4]), selfwait=False)
            tok_wprq = sA.n
            for ts in range(4):
                P.wait("tensor", sA, tok_quant[par])
                P.wait("tensor", sA, tok_relu[par][ts])
                for d2 in range(ND // 2):
                    P.ins("tensor", lambda e, d=d2, s=ts, p=par: e.matmul(
                        phh[p][s][:],
                        bass.AP(wfc8[p], d * 256, [[D, 128], [128, 2], [1, 128]]),
                        bass.AP(x8T, (2 * d) * T + s * 512, [[ND * T, 128], [T, 2], [1, 512]]),
                        start=(d == 0), stop=(d == ND // 2 - 1),
                        perf_mode=PM.DoubleRow), sT)
                tok_mm[par][ts] = sT.n
            for ts in range(4):
                rp = ts % 2
                P.wait("scalar", sT, tok_mm[par][ts])
                P.wait("scalar", sV, tok_mult[par][rp])
                P.ins("scalar", lambda e, s=ts, p=par, q=rp: e.activation(
                    out=r_sb[q][:], in_=phh[p][s][:], func=AF.Relu, scale=SCL[:, 1:2]), selfwait=False)
                tok_relu[par][ts] = sA.n
                P.wait("vector", sA, tok_relu[par][ts])
                P.wait("vector", dHST[ts])
                P.ins("vector", lambda e, s=ts, q=rp: e.tensor_tensor(
                    h_sb[s][:], r_sb[q][:], r_sb[q][:], op=ALU.mult), selfwait=False)
                tok_mult[par][rp] = sV.n
                P.ins("vector", lambda e, s=ts: e.reduce_max(stv(15), h_sb[s][:], axis=AXX))
                P.ins("vector", lambda e: e.tensor_scalar(rmx(1), rmx(1), stv(15), 0.0,
                                                          op0=ALU.max, op1=ALU.add))
                P.wait("gpsimd", sV)
                P.ins("gpsimd", lambda e, k=jc, s=ts: e.dma_start(
                    out=bass.AP(hT_h, (s * 4 * 128) * J + k * 128,
                                [[J, 128], [128 * J, 4], [1, 128]]),
                    in_=bass.AP(h_sb[s], 0, [[512, 128], [128, 4], [1, 128]])), dHST[ts], 16)

    cross_part_max(rmx(1), 3)
    P.ins("vector", lambda e: e.tensor_scalar(ccs[0:1, :], zrow_s[0:1, :], 0.0, 0.0,
                                              op0=ALU.mult, op1=ALU.add))
    P.ins("vector", lambda e: e.tensor_scalar(ccs[0:1, 0:1], ROW[0:1, 3:4], 1.0, 0.0,
                                              op0=ALU.mult, op1=ALU.add))
    P.wait("gpsimd", sV)
    P.ins("gpsimd", lambda e: e.dma_start(out=cc_in2, in_=ccs[0:1, :]), dCC, 16)
    P.wait("gpsimd", dCC)
    P.ins("gpsimd", lambda e: e.collective_compute(
        "AllReduce", ALU.max, replica_groups=[list(range(NC))],
        ins=[cc_in2], outs=[cc_out2]), sCC)
    P.wait("gpsimd", sCC)
    P.ins("gpsimd", lambda e: e.dma_start(out=ROW[0:1, 8:16], in_=cc_out2), dCC, 16)
    P.wait("vector", dCC)
    P.ins("vector", lambda e: e.tensor_scalar(ROW[0:1, 24:25], ROW[0:1, 8:9], 1.0 / 448.0, 1e-6,
                                              op0=ALU.mult, op1=ALU.max))
    P.ins("vector", lambda e: e.tensor_scalar(ROW[0:1, 25:26], ROW[0:1, 24:25], 2.0, 0.0,
                                              op0=ALU.mult, op1=ALU.add))
    P.ins("vector", lambda e: e.reciprocal(ROW[0:1, 25:26], ROW[0:1, 25:26]))
    P.ins("vector", lambda e: e.tensor_scalar(ROW[0:1, 26:27], ROW[0:1, 24:25], ROW[0:1, 21:22], 4.0,
                                              op0=ALU.mult, op1=ALU.mult))
    P.ins("vector", lambda e: e.tensor_scalar(ROW[0:1, 27:28], ROW[0:1, 26:27], ROW[0:1, 26:27], 0.0,
                                              op0=ALU.mult, op1=ALU.add))
    P.ins("vector", lambda e: e.tensor_scalar(ccs[0:1, 0:1], ROW[0:1, 25:26], 1.0, 0.0, op0=ALU.mult, op1=ALU.add))
    P.ins("vector", lambda e: e.tensor_scalar(ccs[0:1, 1:2], ROW[0:1, 26:27], 1.0, 0.0, op0=ALU.mult, op1=ALU.add))
    P.ins("vector", lambda e: e.tensor_scalar(ccs[0:1, 2:3], ROW[0:1, 27:28], 1.0, 0.0, op0=ALU.mult, op1=ALU.add))
    with nc.psum_tensor("pscl2", [128, 8], F32) as pscl2:
        P.wait("tensor", sV)
        P.ins("tensor", lambda e: e.matmul(pscl2[:], onesr[0:1, :], ccs[0:1, :], start=True, stop=True), sT)
        P.wait("scalar", sT)
        P.ins("scalar", lambda e: e.activation(out=SCL2[:], in_=pscl2[:], func=AF.Copy, scale=1.0))

    full_barrier()
    ph_cm.close()
    hf32 = [sb(f"hf32_{i}", [128, J // 2], BF16, ph_c) for i in range(2)]
    h8s = [sb(f"h8s{i}", [128, J], F8, ph_c) for i in range(2)]
    lnff = sb("lnff0", [128, D], F32, ph_c)
    xm5 = sb("xm5_0", [128, D], F32, ph_c)

    with nc.psum_tensor("pq0", [128, D], F32) as pq0, \
         nc.psum_tensor("pq1", [128, D], F32) as pq1:
        poD = [pq0, pq1]
        for s4 in dHST:
            P.wait("sync", s4)
        P.wait("sync", dXMS)
        tok_hq = [0, 0]
        tok_hq_done = [0, 0]
        tok_add = 0
        tok_out = 0
        tok_mmD = [0, 0]
        tok_lnff = [0, 0]
        tok_stats = [0, 0]
        for tt in range(NT):
            par = tt % 2
            for half in range(2):
                P.wait("gpsimd", sA, tok_hq_done[half])
                P.ins("gpsimd", lambda e, t=tt, hf=half: e.dma_start(
                    out=hf32[hf][:], in_=bass.AP(hT_h, (t * 128) * J + hf * (J // 2),
                                                 [[J, 128], [1, J // 2]])), dHSx[half], 16)
            for half in range(2):
                P.wait("scalar", dHSx[half])
                P.wait("scalar", sT, tok_mmD[par])
                P.ins("scalar", lambda e, p=par, hf=half: e.activation(
                    out=h8s[p][:, hf * (J // 2):(hf + 1) * (J // 2)], in_=hf32[hf][:], func=AF.Copy,
                    scale=SCL2[:, 0:1]), selfwait=False)
                tok_hq_done[half] = sA.n
            tok_hq[par] = sA.n
            P.wait("sync", sV, tok_add)
            P.ins("sync", lambda e, t=tt: e.dma_start(
                out=xm5[:], in_=xmid_d[t * 128:(t + 1) * 128, :]), dXM5[0], 16)
            P.wait("tensor", sA, max(tok_hq[par], tok_lnff[par]))
            P.wait("tensor", sV, tok_stats[par])
            for ns in range(4):
                for k2 in range(NJ // 2):
                    P.ins("tensor", lambda e, k=k2, n=ns, p=par: e.matmul(
                        poD[p][:, n * 512:(n + 1) * 512],
                        bass.AP(h8s[p], k * 256, [[J, 128], [128, 2], [1, 128]]),
                        bass.AP(wpr8, (2 * k) * D + n * 512, [[NJ * D, 128], [D, 2], [1, 512]]),
                        start=(k == 0), stop=(k == NJ // 2 - 1),
                        perf_mode=PM.DoubleRow), sT)
            tok_mmD[par] = sT.n
            P.wait("vector", sT, tok_mmD[par])
            P.wait("scalar", sT, tok_mmD[par])
            ln_stats(poD[par][:], 0, 0, scale_c=SCL2[:, 1:2], scale_c2=SCL2[:, 2:3])
            tok_stats[par] = sV.n + 1
            P.ins("vector", lambda e: e.tensor_scalar(stv(6), stv(3), SCL2[:, 1:2], 0.0,
                                                      op0=ALU.mult, op1=ALU.add))
            P.wait("scalar", sV)
            P.wait("scalar", dOUT[0])
            P.ins("scalar", lambda e, p=par: e.activation(out=lnff[:], in_=poD[p][:], func=AF.Identity,
                                                          scale=stv(6), bias=stv(4)), selfwait=False)
            tok_lnff[par] = sA.n
            P.wait("vector", sA)
            P.wait("vector", dXM5[0])
            P.ins("vector", lambda e: e.tensor_tensor(lnff[:], xm5[:], lnff[:], op=ALU.add))
            tok_add = sV.n
            P.wait("sync", sV, tok_add)
            P.ins("sync", lambda e, t=tt: e.dma_start(
                out=out_d[t * 128:(t + 1) * 128, :], in_=lnff[:]), dOUT[0], 16)

    P.wait("sync", dOUT[0])
    P.wait("sync", dOUT[1])
    P.wait("gpsimd", dOUT[0])
    P.wait("gpsimd", dOUT[1])

    with nc.Block() as block:
        P.build(block)
    ph_c.close()
    es.close()
    return nc


_CACHED = {}


def kernel(L0_attn_x, L0_attn_Wq, L0_attn_Wk, L0_attn_Wv, L0_attn_Wo,
           L0_attn_qn_w, L0_attn_kn_w, L0_ffn_w_fc, L0_ffn_w_proj):
    x = np.ascontiguousarray(np.asarray(L0_attn_x, dtype=np.float32))
    f32 = np.float32

    def t(a):
        return np.ascontiguousarray(np.asarray(a, dtype=np.float32).T)

    wqT, wkT, wvT = t(L0_attn_Wq), t(L0_attn_Wk), t(L0_attn_Wv)
    woT, wfcT, wprT = t(L0_attn_Wo), t(L0_ffn_w_fc), t(L0_ffn_w_proj)
    qn = np.ascontiguousarray(np.asarray(L0_attn_qn_w, f32).reshape(1, DH))
    kn = np.ascontiguousarray(np.asarray(L0_attn_kn_w, f32).reshape(1, DH))

    inv_freq = (f32(1.0) / f32(10000.0) ** (np.arange(0, DH, 2).astype(f32) / f32(DH))).astype(f32)
    pos = np.arange(T, dtype=f32)
    emb = np.concatenate([pos[:, None] * inv_freq[None, :]] * 2, axis=1).astype(f32)
    cos_t = np.cos(emb).astype(f32)
    sin_t = np.sin(emb).astype(f32)
    sgn = np.where(np.arange(DH) < 64, f32(-1.0), f32(1.0))
    qs = f32(DH ** -0.5)
    tqc = np.ascontiguousarray(cos_t * qs)
    tqs = np.ascontiguousarray(sin_t * sgn * qs)
    tkc = np.ascontiguousarray(cos_t)
    tks = np.ascontiguousarray(sin_t * sgn)

    jj_ = np.arange(128)[:, None]
    ii_ = np.arange(256)[None, :]
    m_list = []
    for c_ in (-128, 0, 384, 512):
        d_ = c_ + ii_ - jj_
        m_list.append(((d_ >= 0) & (d_ <= 512)).astype(ml_dtypes.bfloat16))
    masks4 = np.ascontiguousarray(np.concatenate(m_list, axis=0))
    ones_bfm = np.ones((128, 128), dtype=ml_dtypes.bfloat16)
    id_bfm = np.eye(128, dtype=ml_dtypes.bfloat16)
    id_f32m = np.eye(128, dtype=f32)
    onesrm = np.ones((1, 128), dtype=f32)
    consts = np.zeros((128, 8), f32)
    consts[:, 0] = 1e-5
    consts[:, 1] = 1e-6
    zrow = np.zeros((1, 8), f32)

    if "nc" not in _CACHED:
        _CACHED["nc"] = build_program()
    nc = _CACHED["nc"]

    shared = dict(wqT=wqT, wkT=wkT, wvT=wvT, woT=woT, wfcT=wfcT, wprT=wprT,
                  qn=qn, kn=kn, tqc=tqc, tqs=tqs, tkc=tkc, tks=tks,
                  masks4=masks4, ones_bf=ones_bfm, id_bf=id_bfm,
                  id_f32=id_f32m, onesr=onesrm, consts=consts, zrow=zrow)
    in_maps = [dict(shared, x=np.ascontiguousarray(x[b]),
                    wslf=np.ascontiguousarray(wfcT[b * (D // NC):(b + 1) * (D // NC), :]),
                    wslp=np.ascontiguousarray(wprT[b * (J // NC):(b + 1) * (J // NC), :]))
               for b in range(NC)]
    res = run_bass_kernel_spmd(nc, in_maps, core_ids=list(range(NC)))
    return np.stack([res.results[b]["out"] for b in range(NC)], axis=0)


# version 16
# speedup vs baseline: 1.0100x; 1.0030x over previous
import numpy as np
import ml_dtypes
from contextlib import ExitStack

import concourse.bass as bass
import concourse.mybir as mybir
from concourse.bass_utils import run_bass_kernel_spmd

F32 = mybir.dt.float32
BF16 = mybir.dt.bfloat16
F8 = mybir.dt.float8e4
AF = mybir.ActivationFunctionType
ALU = mybir.AluOpType
AXX = mybir.AxisListType.X
PM = mybir.MatmulPerfMode

T, D, H, HKV, DH, G, WIN, J = 2048, 2048, 16, 4, 128, 4, 512, 8192
NT, ND, NJ, NC = 16, 16, 64, 8

ENGINES = ("sync", "scalar", "vector", "tensor", "gpsimd")


class Sem:
    __slots__ = ("h", "n")

    def __init__(self, h):
        self.h = h
        self.n = 0


class Sched:
    def __init__(self):
        self.ops = []
        self.own_sem = {}

    def emit(self, eng, fn):
        self.ops.append((eng, fn))

    def wait(self, eng, sem, n=None):
        tgt = sem.n if n is None else n
        if tgt <= 0:
            return
        self.emit(eng, lambda e, s=sem.h, t=tgt: e.wait_ge(s, t))

    def ins(self, eng, f, sem=None, inc=1, selfwait=True):
        own = self.own_sem.get(eng)
        if own is not None:
            if selfwait and own.n > 0:
                self.wait(eng, own)
            if sem is None:
                sem, inc = own, 1
        if sem is not None:
            sem.n += inc
        def fn(e, _f=f, _s=sem, _i=inc):
            r = _f(e)
            if _s is not None:
                r.then_inc(_s.h, _i)
        self.emit(eng, fn)

    def build(self, block):
        for ename in ENGINES:
            sec = getattr(block, ename)
            mine = [fn for (e, fn) in self.ops if e == ename]
            def body(eng, _my=mine):
                for fn in _my:
                    fn(eng)
            sec(body)


def build_program():
    nc = bass.Bass("TRN2", target_bir_lowering=False)

    dh = {}

    def din(name, shape, dt=F32):
        h = nc.dram_tensor(name, shape, dt, kind="ExternalInput")
        dh[name] = h
        return h.ap()

    x_d = din("x", [T, D])
    wqT_d = din("wqT", [D, H * DH])
    wkT_d = din("wkT", [D, HKV * DH])
    wvT_d = din("wvT", [D, HKV * DH])
    woT_d = din("woT", [H * DH, D])
    wfcT_d = din("wfcT", [D, J])
    wprT_d = din("wprT", [J, D])
    qn_d = din("qn", [1, DH])
    kn_d = din("kn", [1, DH])
    tqc_d = din("tqc", [T, DH])
    tqs_d = din("tqs", [T, DH])
    tkc_d = din("tkc", [T, DH])
    tks_d = din("tks", [T, DH])
    masks4_d = din("masks4", [4 * 128, 256], BF16)
    ones_bf_d = din("ones_bf", [128, 128], BF16)
    id_bf_d = din("id_bf", [128, 128], BF16)
    id_f32_d = din("id_f32", [128, 128], F32)
    onesr_d = din("onesr", [1, 128], F32)
    consts_d = din("consts", [128, 8], F32)
    zrow_d = din("zrow", [1, 8], F32)
    wslf_d = din("wslf", [D // NC, J])
    wslp_d = din("wslp", [J // NC, D])

    out_h = nc.dram_tensor("out", [T, D], F32, kind="ExternalOutput")
    out_d = out_h.ap()

    xmid_h = nc.dram_tensor("xmid_s", [T, D], F32)
    x2_h = nc.dram_tensor("x2_s", [T, D], F32)
    hT_h = nc.dram_tensor("hT_s", [NT * 128, J], BF16)
    ctx_h = nc.dram_tensor("ctx_s", [H * 128, T], BF16)
    xmid_d, x2_d, hT_d, ctx_d = xmid_h.ap(), x2_h.ap(), hT_h.ap(), ctx_h.ap()
    cc_in1 = nc.dram_tensor("cc_in1", [1, 8], F32).ap()
    cc_out1 = nc.dram_tensor("cc_out1", [1, 8], F32).ap()
    cc_in2 = nc.dram_tensor("cc_in2", [1, 8], F32).ap()
    cc_out2 = nc.dram_tensor("cc_out2", [1, 8], F32).ap()

    P = Sched()
    es = ExitStack()

    ALLSEMS = []

    def sem(name):
        s = Sem(es.enter_context(nc.semaphore(name)))
        ALLSEMS.append(s)
        return s

    def full_barrier():
        for _eng in ENGINES:
            for _s in ALLSEMS:
                P.wait(_eng, _s)

    sT = sem("s_pe")
    sA = sem("s_act")
    sV = sem("s_dve")
    sGc = sem("s_gpc")
    sCC = sem("s_cc")
    P.own_sem = {"vector": sV, "scalar": sA}

    dIN = sem("d_in")
    dX = [sem("d_x0"), sem("d_x1")]
    dTAB = sem("d_tab")
    dCTXL = [sem("d_ctxl0"), sem("d_ctxl1")]
    dXB5 = [sem("d_xb50"), sem("d_xb51")]
    dX2L = [sem("d_x2l0"), sem("d_x2l1")]
    dWFC = sem("d_wfc")
    dHSx = [sem("d_hs0"), sem("d_hs1")]
    dXM5 = [sem("d_xm50"), sem("d_xm51")]
    dOUT = [sem("d_out0"), sem("d_out1")]
    dWMX = [sem("d_wmx0")]
    dWKV = sem("d_wkv")
    dWQ = sem("d_wq")
    dCTXS = [sem("d_ctxs0"), sem("d_ctxs1")]
    dWO = [sem(f"d_wo{i}") for i in range(4)]
    dXMS = sem("d_xms")
    dX2S = sem("d_x2s")
    dWPR = sem("d_wpr")
    dHST = [sem(f"d_hst{i}") for i in range(4)]
    dCC = sem("d_cc")

    def sb(name, shape, dt, stk=es):
        return stk.enter_context(nc.sbuf_tensor(name, shape, dt))

    cons = sb("cons", [128, 8], F32)
    zrow_s = sb("zrow_s", [1, 8], F32)
    onesr = sb("onesr_s", [1, 128], F32)
    id_bf = sb("id_bf_s", [128, 128], BF16)
    id_f32 = sb("id_f32_s", [128, 128], F32)
    ones_bf = sb("ones_bf_s", [128, 128], BF16)
    masks4 = sb("masks4_s", [128, 4 * 256], BF16)
    qn_s = sb("qn_s", [1, DH], F32)
    kn_s = sb("kn_s", [1, DH], F32)
    wrow = sb("wrow", [1, 2 * DH], F32)
    wqB = sb("wqB", [128, DH], F32)
    wkB = sb("wkB", [128, DH], F32)
    STV = sb("STV", [128, 8 * 24], F32)
    STA = sb("STA", [128, 8 * 16], F32)
    RMX = sb("RMX", [128, 8 * 4], F32)
    SCL = sb("SCL", [128, 8], F32)
    SCL2 = sb("SCL2", [128, 8], F32)
    ROW = sb("ROWs", [1, 32], F32)
    ccs = sb("ccs_s", [1, 8], F32)
    tcol = sb("tcol", [128, 2], F32)
    trash = sb("trash", [128, 2048], BF16)

    def stv(i):
        return STV[:, 8 * i:8 * i + 1]

    def sta(i):
        return STA[:, 8 * i:8 * i + 1]

    def rmx(i):
        return RMX[:, 8 * i:8 * i + 1]

    for dst, src in [(cons, consts_d), (zrow_s, zrow_d), (onesr, onesr_d),
                     (id_bf, id_bf_d), (id_f32, id_f32_d), (ones_bf, ones_bf_d),
                     (qn_s, qn_d), (kn_s, kn_d)]:
        P.ins("sync", lambda e, d=dst, s=src: e.dma_start(out=d[:], in_=s), dIN, 16)
    P.ins("sync", lambda e: e.dma_start(
        out=bass.AP(masks4, 0, [[4 * 256, 128], [256, 4], [1, 256]]),
        in_=bass.AP(dh["masks4"], 0, [[256, 128], [128 * 256, 4], [1, 256]])), dIN, 16)
    P.ins("gpsimd", lambda e: e.memset(RMX[:], 0), sGc, 1)
    P.wait("vector", dIN)
    P.wait("scalar", dIN)
    P.wait("tensor", dIN)
    P.ins("vector", lambda e: e.tensor_scalar(wrow[0:1, 0:DH], qn_s[0:1, :], 1.0, 1.0,
                                              op0=ALU.mult, op1=ALU.add))
    P.ins("vector", lambda e: e.tensor_scalar(wrow[0:1, DH:2 * DH], kn_s[0:1, :], 1.0, 1.0,
                                              op0=ALU.mult, op1=ALU.add))

    with nc.psum_tensor("pinit", [128, 2 * DH], F32) as pinit:
        P.wait("tensor", sV)
        P.ins("tensor", lambda e: e.matmul(pinit[:, 0:DH], onesr[0:1, :], wrow[0:1, 0:DH],
                                           start=True, stop=True), sT)
        P.ins("tensor", lambda e: e.matmul(pinit[:, DH:2 * DH], onesr[0:1, :], wrow[0:1, DH:2 * DH],
                                           start=True, stop=True), sT)
        P.wait("scalar", sT)
        P.ins("scalar", lambda e: e.activation(out=wqB[:], in_=pinit[:, 0:DH], func=AF.Copy, scale=1.0))
        P.ins("scalar", lambda e: e.activation(out=wkB[:], in_=pinit[:, DH:2 * DH], func=AF.Copy, scale=1.0))

    ph_a = ExitStack()
    xnT = sb("xnT", [128, ND * T], BF16, ph_a)
    kT = sb("kT", [128, HKV * T], BF16, ph_a)
    v_sb = sb("v_sb", [128, HKV * T], BF16, ph_a)
    qTg = sb("qTg", [128, G * T], BF16, ph_a)
    tabc = sb("tabc", [128, NT * DH], F32, ph_a)
    tabs = sb("tabs", [128, NT * DH], F32, ph_a)
    xbuf = [sb(f"xbuf{i}", [128, D], F32, ph_a) for i in range(2)]
    xn_tok = sb("xn_tok", [128, D], BF16, ph_a)
    rop = [sb(f"rop{i}", [128, 3 * 128], F32, ph_a) for i in range(2)]
    qrot_bf = [sb(f"qrot{i}", [128, 128], BF16, ph_a) for i in range(2)]
    wkv_sb = sb("wkv_sb", [128, ND * 256], BF16, ph_a)
    wq_sb = sb("wq_sb", [128, ND * 512], BF16, ph_a)
    ctxs = [sb(f"ctxs{i}", [128, T], BF16, ph_a) for i in range(2)]
    expb = [sb(f"expb{i}", [128, 1536], BF16, ph_a) for i in range(2)]
    rzb = [sb(f"rzb{i}", [128, 256], F32, ph_a) for i in range(2)]
    wmx = [sb("wmx0", [128, 1024], F32, ph_a)]

    def xnT_t(dc, tt):
        return xnT[:, dc * T + tt * 128: dc * T + (tt + 1) * 128]

    def kT_t(hk, jt):
        return kT[:, hk * T + jt * 128: hk * T + (jt + 1) * 128]

    def v_t(hk, jt):
        return v_sb[:, hk * T + jt * 128: hk * T + (jt + 1) * 128]

    def qT_t(g, it):
        return qTg[:, g * T + it * 128: g * T + (it + 1) * 128]

    def tab_t(tab, tt):
        return tab[:, tt * 128:(tt + 1) * 128]

    def ln_stats(src_ap, sl, asl, scale_c=None, scale_c2=None):
        P.wait("scalar", sV)
        P.ins("vector", lambda e: e.reduce_sum(stv(sl), src_ap, axis=AXX))
        P.wait("scalar", sV)
        P.ins("scalar", lambda e: e.activation(out=trash[:], in_=src_ap, func=AF.Square,
                                               accum_out=sta(asl)))
        P.wait("vector", sA)
        if scale_c is None:
            P.ins("vector", lambda e: e.tensor_scalar(stv(sl), stv(sl), 1.0 / D, 0.0,
                                                      op0=ALU.mult, op1=ALU.add))
            P.ins("vector", lambda e: e.tensor_scalar(stv(sl + 1), sta(asl), 1.0 / D, 0.0,
                                                      op0=ALU.mult, op1=ALU.add))
        else:
            P.ins("vector", lambda e: e.tensor_scalar(stv(sl), stv(sl), scale_c, 1.0 / D,
                                                      op0=ALU.mult, op1=ALU.mult))
            P.ins("vector", lambda e: e.tensor_scalar(stv(sl + 1), sta(asl), scale_c2, 1.0 / D,
                                                      op0=ALU.mult, op1=ALU.mult))
        P.ins("vector", lambda e: e.tensor_scalar(stv(sl + 2), stv(sl), stv(sl), 0.0,
                                                  op0=ALU.mult, op1=ALU.add))
        P.ins("vector", lambda e: e.tensor_scalar(stv(sl + 2), stv(sl + 1), stv(sl + 2), 0.0,
                                                  op0=ALU.subtract, op1=ALU.add))
        P.wait("scalar", sV)
        P.ins("scalar", lambda e: e.activation(out=sta(asl + 1), in_=stv(sl + 2), func=AF.Sqrt,
                                               scale=1.0, bias=cons[:, 0:1]))
        P.wait("vector", sA)
        P.ins("vector", lambda e: e.reciprocal(stv(sl + 3), sta(asl + 1)))
        P.ins("vector", lambda e: e.tensor_scalar(stv(sl + 4), stv(sl), stv(sl + 3), -1.0,
                                                  op0=ALU.mult, op1=ALU.mult))

    full_barrier()
    P.wait("vector", sGc)
    wmax_tiles = []
    for (w_ap_, rows_, cols_, slot_) in [(wslf_d, D // NC, J, 2), (wslp_d, J // NC, D, 3)]:
        for r_ in range(rows_ // 128):
            for c_ in range(cols_ // 1024):
                wmax_tiles.append((w_ap_, r_, c_, slot_))
    wmax_state = {"i": 0, "tok": [0, 0]}

    def wmax_step(nstep=1):
        for _ in range(nstep):
            i = wmax_state["i"]
            if i >= len(wmax_tiles):
                return
            wmax_state["i"] += 1
            w_ap_, r_, c_, slot_ = wmax_tiles[i]
            wp = 0
            P.wait("gpsimd", sV, wmax_state["tok"][wp])
            P.ins("gpsimd", lambda e, w=w_ap_, rr=r_, cc=c_, p=wp: e.dma_start(
                out=wmx[p][:], in_=w[rr * 128:(rr + 1) * 128, cc * 1024:(cc + 1) * 1024]), dWMX[wp], 16)
            P.wait("vector", dWMX[wp])
            P.ins("vector", lambda e, p=wp: e.reduce_max(stv(15), wmx[p][:], axis=AXX,
                                                         apply_absolute_value=True))
            P.ins("vector", lambda e, s=slot_: e.tensor_scalar(rmx(s), rmx(s), stv(15), 0.0,
                                                               op0=ALU.max, op1=ALU.add))
            wmax_state["tok"][wp] = sV.n

    with nc.psum_tensor("ptr0", [128, 128], BF16) as ptr0, \
         nc.psum_tensor("ptr1", [128, 128], BF16) as ptr1:
        ptr = [ptr0, ptr1]
        for tt in range(NT):
            par = tt % 2
            wmax_step(2)
            P.wait("sync", sA)
            P.wait("sync", sV)
            P.ins("sync", lambda e, t=tt, p=par: e.dma_start(
                out=xbuf[p][:], in_=x_d[t * 128:(t + 1) * 128, :]), dX[par], 16)
            P.wait("vector", dX[par])
            P.wait("scalar", dX[par])
            ln_stats(xbuf[par][:], 0, 0)
            P.wait("scalar", sV)
            P.ins("scalar", lambda e, p=par: e.activation(out=xn_tok[:], in_=xbuf[p][:], func=AF.Identity,
                                                          scale=stv(3), bias=stv(4)))
            for dc in range(ND):
                tp = dc % 2
                P.wait("tensor", sA)
                P.ins("tensor", lambda e, d=dc, q=tp: e.transpose(
                    ptr[q][:], xn_tok[:, d * 128:(d + 1) * 128], id_bf[:]), sT)
                P.wait("scalar", sT)
                P.ins("scalar", lambda e, d=dc, t=tt, q=tp: e.activation(
                    out=xnT_t(d, t), in_=ptr[q][:], func=AF.Copy, scale=1.0))

    P.ins("sync", lambda e: e.dma_start(
        out=bass.AP(tabc, 0, [[NT * DH, 128], [DH, NT], [1, DH]]),
        in_=bass.AP(dh["tkc"], 0, [[DH, 128], [128 * DH, NT], [1, DH]])), dTAB, 16)
    P.ins("sync", lambda e: e.dma_start(
        out=bass.AP(tabs, 0, [[NT * DH, 128], [DH, NT], [1, DH]]),
        in_=bass.AP(dh["tks"], 0, [[DH, 128], [128 * DH, NT], [1, DH]])), dTAB, 16)
    P.wait("vector", dTAB)

    with nc.psum_tensor("pkv0", [128, 256], F32) as pkv0, \
         nc.psum_tensor("pkv1", [128, 256], F32) as pkv1, \
         nc.psum_tensor("ptk0", [128, 128], BF16) as ptk0, \
         nc.psum_tensor("ptk1", [128, 128], BF16) as ptk1:
        pkv = [pkv0, pkv1]
        ptk = [ptk0, ptk1]
        tokA = [0, 0]
        tokV = [0, 0]
        tokTr = [0, 0]
        tokEvK = [0, 0]
        for hk in range(HKV):
            P.wait("gpsimd", sT)
            P.ins("gpsimd", lambda e, h=hk: e.dma_start(
                out=bass.AP(wkv_sb, 0, [[ND * 256, 128], [256, ND], [1, 128]]),
                in_=bass.AP(dh["wkT"], h * 128, [[HKV * DH, 128], [128 * HKV * DH, ND], [1, 128]])), dWKV, 16)
            P.ins("gpsimd", lambda e, h=hk: e.dma_start(
                out=bass.AP(wkv_sb, 128, [[ND * 256, 128], [256, ND], [1, 128]]),
                in_=bass.AP(dh["wvT"], h * 128, [[HKV * DH, 128], [128 * HKV * DH, ND], [1, 128]])), dWKV, 16)
            for tt in range(NT):
                par = tt % 2
                wmax_step(2)
                P.wait("tensor", dWKV)
                P.wait("tensor", sA, tokA[par])
                P.wait("tensor", sV, tokV[par])
                for dc in range(ND):
                    P.ins("tensor", lambda e, d=dc, t=tt, p=par: e.matmul(
                        pkv[p][:], xnT_t(d, t), wkv_sb[:, d * 256:(d + 1) * 256],
                        start=(d == 0), stop=(d == ND - 1)), sT)
                tok_mm = sT.n
                P.wait("scalar", sT, tok_mm)
                P.wait("scalar", sV, tokV[par])
                P.ins("scalar", lambda e, p=par: e.activation(out=trash[:, 0:128], in_=pkv[p][:, 0:128],
                                                              func=AF.Square, accum_out=sta(8)), selfwait=False)
                P.ins("scalar", lambda e: e.activation(out=sta(12), in_=sta(8), func=AF.Sqrt,
                                                       scale=1.0 / DH, bias=cons[:, 1:2]))
                tok_std = sA.n
                P.wait("vector", sA, tok_std)
                P.ins("vector", lambda e: e.reciprocal(stv(16), sta(12)))
                P.ins("vector", lambda e, p=par: e.scalar_tensor_tensor(rop[p][:, 0:128], pkv[p][:, 0:128],
                                                                        stv(16), wkB[:],
                                                                        op0=ALU.mult, op1=ALU.mult))
                tokV[par] = sV.n
                P.wait("scalar", sV, tokV[par])
                P.ins("scalar", lambda e, h=hk, t=tt, p=par: e.activation(
                    out=v_t(h, t), in_=pkv[p][:, 128:256], func=AF.Copy, scale=1.0), selfwait=False)
                tokA[par] = sA.n
                P.wait("vector", sT, tokTr[par])
                P.ins("vector", lambda e, p=par, t=tt: e.tensor_tensor(
                    rop[p][:, 128:256], rop[p][:, 0:128], tab_t(tabc, t), op=ALU.mult))
                P.ins("vector", lambda e, p=par, t=tt: e.tensor_tensor(
                    rop[p][:, 256:320], rop[p][:, 64:128], tab_t(tabs, t)[:, 0:64], op=ALU.mult))
                P.ins("vector", lambda e, p=par, t=tt: e.tensor_tensor(
                    rop[p][:, 320:384], rop[p][:, 0:64], tab_t(tabs, t)[:, 64:128], op=ALU.mult))
                P.ins("vector", lambda e, p=par: e.tensor_tensor(
                    qrot_bf[p][:], rop[p][:, 128:256], rop[p][:, 256:384], op=ALU.add))
                tok_rot = sV.n
                P.wait("tensor", sV, tok_rot)
                P.wait("tensor", sA, tokEvK[par])
                P.ins("tensor", lambda e, p=par: e.transpose(ptk[p][:], qrot_bf[p][:], id_bf[:]), sT)
                tokTr[par] = sT.n
                P.wait("scalar", sT, tokTr[par])
                P.ins("scalar", lambda e, h=hk, t=tt, p=par: e.activation(
                    out=kT_t(h, t), in_=ptk[p][:], func=AF.Copy, scale=1.0), selfwait=False)
                tokEvK[par] = sA.n

    P.wait("sync", sV)
    P.ins("sync", lambda e: e.dma_start(
        out=bass.AP(tabc, 0, [[NT * DH, 128], [DH, NT], [1, DH]]),
        in_=bass.AP(dh["tqc"], 0, [[DH, 128], [128 * DH, NT], [1, DH]])), dTAB, 16)
    P.ins("sync", lambda e: e.dma_start(
        out=bass.AP(tabs, 0, [[NT * DH, 128], [DH, NT], [1, DH]]),
        in_=bass.AP(dh["tqs"], 0, [[DH, 128], [128 * DH, NT], [1, DH]])), dTAB, 16)
    P.wait("vector", dTAB)

    tokQa = [0, 0]
    tokQv = [0, 0]
    tokTrQ = [0, 0]
    tokEvQ = [0, 0]
    tE = [0, 0]
    tSc = [0, 0]
    tM = [0, 0]
    tZp = [0, 0]
    tCp = [0, 0]
    tRZp = [0, 0]
    tNp = [0, 0]
    for hk in range(HKV):
        P.wait("gpsimd", sT)
        P.ins("gpsimd", lambda e, h=hk: e.dma_start(
            out=bass.AP(wq_sb, 0, [[ND * 512, 128], [512, ND], [1, 512]]),
            in_=bass.AP(dh["wqT"], h * 512, [[H * DH, 128], [128 * H * DH, ND], [1, 512]])), dWQ, 16)
        with nc.psum_tensor(f"pq0_{hk}", [128, 512], F32) as pq0, \
             nc.psum_tensor(f"pq1_{hk}", [128, 512], F32) as pq1, \
             nc.psum_tensor(f"ptq0_{hk}", [128, 128], BF16) as ptq0, \
             nc.psum_tensor(f"ptq1_{hk}", [128, 128], BF16) as ptq1:
            pq = [pq0, pq1]
            ptq = [ptq0, ptq1]
            P.wait("tensor", sA)
            P.wait("tensor", sV)
            for tt in range(NT):
                par = tt % 2
                wmax_step(2)
                P.wait("tensor", dWQ)
                P.wait("tensor", sA, tokQa[par])
                P.wait("tensor", sV, tokQv[par])
                for dc in range(ND):
                    P.ins("tensor", lambda e, d=dc, t=tt, p=par: e.matmul(
                        pq[p][:], xnT_t(d, t), wq_sb[:, d * 512:(d + 1) * 512],
                        start=(d == 0), stop=(d == ND - 1)), sT)
                tok_mm = sT.n
                P.wait("scalar", sT, tok_mm)
                P.wait("scalar", sV, tokQv[par])
                for g in range(G):
                    P.ins("scalar", lambda e, gg=g, p=par: e.activation(
                        out=trash[:, gg * 128:(gg + 1) * 128], in_=pq[p][:, gg * 128:(gg + 1) * 128],
                        func=AF.Square, accum_out=sta(8 + gg)), selfwait=False)
                P.wait("scalar", sA)
                for g in range(G):
                    P.ins("scalar", lambda e, gg=g: e.activation(out=sta(12 + gg), in_=sta(8 + gg),
                                                                 func=AF.Sqrt, scale=1.0 / DH,
                                                                 bias=cons[:, 1:2]), selfwait=False)
                tok_std = sA.n
                P.wait("vector", sA, tok_std)
                for g in range(G):
                    P.ins("vector", lambda e, gg=g: e.reciprocal(stv(16 + gg), sta(12 + gg)), selfwait=False)
                for g in range(G):
                    gp = g % 2
                    P.wait("vector", sT, tokTrQ[gp])
                    P.ins("vector", lambda e, gg=g, p=par: e.scalar_tensor_tensor(
                        rop[0][:, 0:128], pq[p][:, gg * 128:(gg + 1) * 128], stv(16 + gg), wqB[:],
                        op0=ALU.mult, op1=ALU.mult))
                    P.ins("vector", lambda e, t=tt: e.tensor_tensor(
                        rop[0][:, 128:256], rop[0][:, 0:128], tab_t(tabc, t), op=ALU.mult))
                    P.ins("vector", lambda e, t=tt: e.tensor_tensor(
                        rop[0][:, 256:320], rop[0][:, 64:128], tab_t(tabs, t)[:, 0:64], op=ALU.mult))
                    P.ins("vector", lambda e, t=tt: e.tensor_tensor(
                        rop[0][:, 320:384], rop[0][:, 0:64], tab_t(tabs, t)[:, 64:128], op=ALU.mult))
                    P.ins("vector", lambda e, p=gp: e.tensor_tensor(
                        qrot_bf[p][:], rop[0][:, 128:256], rop[0][:, 256:384], op=ALU.add))
                    tok_rot = sV.n
                    P.wait("tensor", sV, tok_rot)
                    P.wait("tensor", sA, tokEvQ[gp])
                    P.ins("tensor", lambda e, p=gp: e.transpose(ptq[p][:], qrot_bf[p][:], id_bf[:]), sT)
                    tokTrQ[gp] = sT.n
                    P.wait("scalar", sT, tokTrQ[gp])
                    P.ins("scalar", lambda e, gg=g, t=tt, p=gp: e.activation(
                        out=qT_t(gg, t), in_=ptq[p][:], func=AF.Copy, scale=1.0), selfwait=False)
                    tokEvQ[gp] = sA.n
                tokQv[par] = sV.n
                tokQa[par] = sA.n
        with nc.psum_tensor(f"psS0_{hk}", [128, 1536], F32) as psS0, \
             nc.psum_tensor(f"psS1_{hk}", [128, 1536], F32) as psS1, \
             nc.psum_tensor(f"pzz_{hk}", [128, 256], F32) as pzz, \
             nc.psum_tensor(f"pcc_{hk}", [128, 256], F32) as pcc:
            psS = [psS0, psS1]
            base = max(tokQa[0], tokQa[1])
            tE = [base, base]
            tSc = [0, 0]
            tM = [0, 0]
            tZp = max(tRZp)
            tCp = 0
            tRZ1 = base
            tN1 = 0
            P.wait("tensor", sV)

            NR = NT // 2

            def jinfo(r):
                out = []
                for jt in range(max(0, 2 * r - 4), 2 * r + 2):
                    c = (2 * r - jt) * 128
                    slot = {-128: 0, 0: 1, 384: 2, 512: 3}.get(c)
                    out.append((jt, slot))
                return out

            def emit_scores(r, g):
                par = r % 2
                P.wait("tensor", sA, tE[par])
                for jj, (jt, _) in enumerate(jinfo(r)):
                    P.ins("tensor", lambda e, p=par, jx=jj, hh=hk, j=jt, rr=r, gg=g: e.matmul(
                        psS[p][:, jx * 256:(jx + 1) * 256], kT_t(hh, j),
                        qTg[:, gg * T + rr * 256:(gg * T + (rr + 1) * 256)],
                        start=True, stop=True), sT)
                tSc[par] = sT.n

            def emit_exp(r):
                par = r % 2
                nj = len(jinfo(r))
                P.wait("scalar", sT, tSc[par])
                P.ins("scalar", lambda e, p=par, n=nj: e.activation(
                    out=expb[p][:, 0:n * 256], in_=psS[p][:, 0:n * 256], func=AF.Exp), selfwait=False)
                tE[par] = sA.n

            def emit_masks(r):
                par = r % 2
                P.wait("vector", sA, tE[par])
                first = True
                for jj, (jt, slot) in enumerate(jinfo(r)):
                    if slot is None:
                        continue
                    P.ins("vector", lambda e, p=par, jx=jj, m=slot: e.tensor_tensor(
                        expb[p][:, jx * 256:(jx + 1) * 256], expb[p][:, jx * 256:(jx + 1) * 256],
                        masks4[:, m * 256:(m + 1) * 256], op=ALU.mult), selfwait=first)
                    first = False
                tM[par] = sV.n

            def emit_zctx(r):
                nonlocal_d = None
                par = r % 2
                ji = jinfo(r)
                nj = len(ji)
                P.wait("tensor", sV, tM[par])
                P.wait("tensor", sA, tRZ1)
                for jj in range(nj):
                    P.ins("tensor", lambda e, p=par, j=jj, n=nj: e.matmul(
                        pzz[:], ones_bf[:], expb[p][:, j * 256:(j + 1) * 256],
                        start=(j == 0), stop=(j == n - 1)), sT)
                tz = sT.n
                P.wait("tensor", sV, tN1)
                for jj, (jt, _) in enumerate(ji):
                    P.ins("tensor", lambda e, p=par, hh=hk, j=jt, jx=jj, n=nj: e.matmul(
                        pcc[:], v_t(hh, j), expb[p][:, jx * 256:(jx + 1) * 256],
                        start=(jx == 0), stop=(jx == n - 1)), sT)
                return tz, sT.n

            for g in range(G):
                h = hk * G + g
                cpar = h % 2
                P.wait("vector", dCTXS[cpar])
                emit_scores(0, g)
                emit_exp(0)
                emit_masks(0)
                for r in range(NR):
                    par = r % 2
                    if r + 1 < NR:
                        emit_scores(r + 1, g)
                        emit_exp(r + 1)
                        emit_masks(r + 1)
                    tz, tc = emit_zctx(r)
                    P.wait("scalar", sT, tz)
                    P.ins("scalar", lambda e, p=par: e.activation(out=rzb[p][:], in_=pzz[:],
                                                                  func=AF.Copy, scale=1.0), selfwait=False)
                    tRZ1 = sA.n
                    P.wait("vector", sA, tRZ1)
                    P.ins("vector", lambda e, p=par: e.reciprocal(rzb[p][:], rzb[p][:]), selfwait=False)
                    P.wait("vector", sT, tc)
                    P.ins("vector", lambda e, p=par, c=cpar, rr=r: e.tensor_tensor(
                        ctxs[c][:, rr * 256:(rr + 1) * 256], pcc[:], rzb[p][:], op=ALU.mult))
                    tN1 = sV.n
                P.wait("gpsimd", sV)
                P.ins("gpsimd", lambda e, hh=h, c=cpar: e.dma_start(
                    out=ctx_d[hh * 128:(hh + 1) * 128, :], in_=ctxs[c][:]), dCTXS[cpar], 16)
            tokQa = [max(tokQa[0], tE[0]), max(tokQa[1], tE[1])]

    wmax_step(len(wmax_tiles))

    full_barrier()
    ph_a.close()
    ph5 = ExitStack()
    wo_sb = sb("wo_sb", [128, ND * D], BF16, ph5)
    ctx_full = sb("ctx_full", [128, ND * T], BF16, ph5)
    xb5 = [sb(f"xb5_{i}", [128, D], F32, ph5) for i in range(2)]
    lnat = sb("lnat", [128, D], F32, ph5)
    xmid_sb = sb("xmid_sb", [128, D], F32, ph5)
    x2_sb = sb("x2_sb", [128, D], F32, ph5)

    wo_f = [sb(f"wo_f{i}", [128, D], F32, ph5) for i in range(2)]
    P.wait("sync", dCTXS[0])
    P.wait("sync", dCTXS[1])
    for hd in range(ND):
        P.ins("sync", lambda e, d=hd: e.dma_start(
            out=ctx_full[:, d * T:(d + 1) * T], in_=ctx_d[d * 128:(d + 1) * 128, :]), dCTXL[hd % 2], 16)
    tok_woc = [0] * ND
    tok_wof = [0, 0]
    for hd in range(ND):
        wp = hd % 2
        P.wait("sync", sA, tok_wof[wp])
        P.ins("sync", lambda e, d=hd, p=wp: e.dma_start(
            out=wo_f[p][:], in_=woT_d[d * 128:(d + 1) * 128, :]), dWO[hd % 4], 16)
        P.wait("scalar", dWO[hd % 4])
        P.ins("scalar", lambda e, d=hd, p=wp: e.activation(
            out=wo_sb[:, d * D:(d + 1) * D], in_=wo_f[p][:], func=AF.Copy, scale=1.0), selfwait=False)
        tok_woc[hd] = sA.n
        tok_wof[wp] = sA.n

    with nc.psum_tensor("po0", [128, D], F32) as po0, \
         nc.psum_tensor("po1", [128, D], F32) as po1:
        po = [po0, po1]
        for tt in range(NT):
            par = tt % 2
            P.wait("sync", sV)
            P.ins("sync", lambda e, t=tt, p=par: e.dma_start(
                out=xb5[p][:], in_=x_d[t * 128:(t + 1) * 128, :]), dXB5[par], 16)
            if tt == 0:
                P.wait("tensor", dCTXL[0])
                P.wait("tensor", dCTXL[1])
            P.wait("tensor", sA)
            P.wait("tensor", sV)
            for hd in range(ND):
                if tt == 0:
                    P.wait("tensor", sA, tok_woc[hd])
                for ns in range(4):
                    P.ins("tensor", lambda e, d=hd, n=ns, p=par, t=tt: e.matmul(
                        po[p][:, n * 512:(n + 1) * 512], ctx_full[:, d * T + t * 128: d * T + (t + 1) * 128],
                        wo_sb[:, d * D + n * 512: d * D + (n + 1) * 512],
                        start=(d == 0), stop=(d == ND - 1)), sT)
            P.wait("vector", sT)
            P.wait("scalar", sT)
            ln_stats(po[par][:], 0, 0)
            P.wait("scalar", sV)
            P.ins("scalar", lambda e, p=par: e.activation(out=lnat[:], in_=po[p][:], func=AF.Identity,
                                                          scale=stv(3), bias=stv(4)))
            P.wait("vector", sA)
            P.wait("vector", dXB5[par])
            P.wait("vector", dXMS)
            P.ins("vector", lambda e, p=par: e.tensor_tensor(xmid_sb[:], xb5[p][:], lnat[:], op=ALU.add))
            P.wait("gpsimd", sV)
            P.ins("gpsimd", lambda e, t=tt: e.dma_start(
                out=xmid_d[t * 128:(t + 1) * 128, :], in_=xmid_sb[:]), dXMS, 16)
            ln_stats(xmid_sb[:], 8, 4)
            P.wait("scalar", sV)
            P.wait("scalar", dX2S)
            P.ins("scalar", lambda e: e.activation(out=x2_sb[:], in_=xmid_sb[:], func=AF.Identity,
                                                   scale=stv(11), bias=stv(12)))
            P.wait("vector", sA)
            P.ins("vector", lambda e: e.reduce_max(stv(15), x2_sb[:], axis=AXX,
                                                   apply_absolute_value=True))
            P.ins("vector", lambda e: e.tensor_scalar(rmx(0), rmx(0), stv(15), 0.0,
                                                      op0=ALU.max, op1=ALU.add))
            P.wait("gpsimd", sA)
            P.ins("gpsimd", lambda e, t=tt: e.dma_start(
                out=x2_d[t * 128:(t + 1) * 128, :], in_=x2_sb[:]), dX2S, 16)
    full_barrier()
    ph5.close()

    def cross_part_max(col_ap, row_col):
        with nc.psum_tensor(f"pmx{row_col}", [128, 128], F32) as pmx:
            P.ins("vector", lambda e: e.tensor_scalar(tcol[:, 0:1], col_ap, 1.0, 0.0,
                                                      op0=ALU.mult, op1=ALU.add))
            P.wait("tensor", sV)
            P.ins("tensor", lambda e: e.transpose(pmx[0:1, 0:128], tcol[:, 0:1], id_f32[:]), sT)
            P.wait("vector", sT)
            P.ins("vector", lambda e, c=row_col: e.reduce_max(
                ROW[0:1, c:c + 1], pmx[0:1, 0:128], axis=AXX))

    cross_part_max(rmx(0), 0)
    cross_part_max(rmx(2), 1)
    cross_part_max(rmx(3), 2)

    P.ins("vector", lambda e: e.tensor_scalar(ccs[0:1, :], zrow_s[0:1, :], 0.0, 0.0,
                                              op0=ALU.mult, op1=ALU.add))
    P.ins("vector", lambda e: e.tensor_scalar(ccs[0:1, 0:1], ROW[0:1, 0:1], 1.0, 0.0,
                                              op0=ALU.mult, op1=ALU.add))
    P.ins("vector", lambda e: e.tensor_scalar(ccs[0:1, 1:2], ROW[0:1, 1:2], 1.0, 0.0,
                                              op0=ALU.mult, op1=ALU.add))
    P.ins("vector", lambda e: e.tensor_scalar(ccs[0:1, 2:3], ROW[0:1, 2:3], 1.0, 0.0,
                                              op0=ALU.mult, op1=ALU.add))
    P.wait("gpsimd", sV)
    P.ins("gpsimd", lambda e: e.dma_start(out=cc_in1, in_=ccs[0:1, :]), dCC, 16)
    P.wait("gpsimd", dCC)
    P.ins("gpsimd", lambda e: e.collective_compute(
        "AllReduce", ALU.max, replica_groups=[list(range(NC))],
        ins=[cc_in1], outs=[cc_out1]), sCC)
    P.wait("gpsimd", sCC)
    P.ins("gpsimd", lambda e: e.dma_start(out=ROW[0:1, 8:16], in_=cc_out1), dCC, 16)
    P.wait("vector", dCC)
    P.ins("vector", lambda e: e.tensor_scalar(ROW[0:1, 16:17], ROW[0:1, 8:9], 1.0 / 448.0, 1e-6,
                                              op0=ALU.mult, op1=ALU.max))
    P.ins("vector", lambda e: e.tensor_scalar(ROW[0:1, 17:18], ROW[0:1, 16:17], 2.0, 0.0,
                                              op0=ALU.mult, op1=ALU.add))
    P.ins("vector", lambda e: e.reciprocal(ROW[0:1, 17:18], ROW[0:1, 17:18]))
    P.ins("vector", lambda e: e.tensor_scalar(ROW[0:1, 18:19], ROW[0:1, 9:10], 1.0 / 448.0, 1e-6,
                                              op0=ALU.mult, op1=ALU.max))
    P.ins("vector", lambda e: e.tensor_scalar(ROW[0:1, 19:20], ROW[0:1, 18:19], 2.0, 0.0,
                                              op0=ALU.mult, op1=ALU.add))
    P.ins("vector", lambda e: e.reciprocal(ROW[0:1, 19:20], ROW[0:1, 19:20]))
    P.ins("vector", lambda e: e.tensor_scalar(ROW[0:1, 20:21], ROW[0:1, 16:17], ROW[0:1, 18:19], 4.0,
                                              op0=ALU.mult, op1=ALU.mult))
    P.ins("vector", lambda e: e.tensor_scalar(ROW[0:1, 21:22], ROW[0:1, 10:11], 1.0 / 448.0, 1e-6,
                                              op0=ALU.mult, op1=ALU.max))
    P.ins("vector", lambda e: e.tensor_scalar(ROW[0:1, 22:23], ROW[0:1, 21:22], 2.0, 0.0,
                                              op0=ALU.mult, op1=ALU.add))
    P.ins("vector", lambda e: e.reciprocal(ROW[0:1, 22:23], ROW[0:1, 22:23]))
    P.ins("vector", lambda e: e.tensor_scalar(ccs[0:1, 0:1], ROW[0:1, 17:18], 1.0, 0.0, op0=ALU.mult, op1=ALU.add))
    P.ins("vector", lambda e: e.tensor_scalar(ccs[0:1, 1:2], ROW[0:1, 20:21], 1.0, 0.0, op0=ALU.mult, op1=ALU.add))
    P.ins("vector", lambda e: e.tensor_scalar(ccs[0:1, 2:3], ROW[0:1, 19:20], 1.0, 0.0, op0=ALU.mult, op1=ALU.add))
    P.ins("vector", lambda e: e.tensor_scalar(ccs[0:1, 3:4], ROW[0:1, 22:23], 1.0, 0.0, op0=ALU.mult, op1=ALU.add))
    with nc.psum_tensor("pscl", [128, 8], F32) as pscl:
        P.wait("tensor", sV)
        P.ins("tensor", lambda e: e.matmul(pscl[:], onesr[0:1, :], ccs[0:1, :], start=True, stop=True), sT)
        P.wait("scalar", sT)
        P.ins("scalar", lambda e: e.activation(out=SCL[:], in_=pscl[:], func=AF.Copy, scale=1.0))

    ph_c = ExitStack()
    wpr8 = sb("wpr8", [128, NJ * D], F8, ph_c)
    ph_cm = ExitStack()
    x8T = sb("x8T", [128, ND * T], F8, ph_cm)
    wfc32 = sb("wfc32", [128, D], F32, ph_cm)
    wfc8 = [sb(f"wfc8_{i}", [128, D], F8, ph_cm) for i in range(2)]
    wpr32 = sb("wpr32_0", [128, D], F32, ph_cm)
    h_sb = [sb(f"h_sb{i}", [128, 512], BF16, ph_cm) for i in range(2)]
    r_sb = [sb(f"r_sb{i}", [128, 512], BF16, ph_cm) for i in range(2)]
    xbC = sb("xbC0", [128, D // 2], F32, ph_cm)
    x8tok = sb("x8tok", [128, D], F8, ph_cm)
    xqbf = sb("xqbf", [128, D], BF16, ph_cm)

    with nc.psum_tensor("ph0", [128, 512], F32) as ph0, \
         nc.psum_tensor("ph1", [128, 512], F32) as ph1, \
         nc.psum_tensor("ph2", [128, 512], F32) as ph2, \
         nc.psum_tensor("ph3", [128, 512], F32) as ph3, \
         nc.psum_tensor("ptc0", [128, 128], BF16) as ptc0, \
         nc.psum_tensor("ptc1", [128, 128], BF16) as ptc1:
        phh = [[ph0, ph1], [ph2, ph3]]
        ptc = [ptc0, ptc1]
        full_barrier()
        P.wait("sync", dX2S)
        tok_x8 = [0] * 4
        pro_state = {"q8": 0, "ptc": [0, 0], "xq": 0}

        def emit_pro_tile(tt):
            for hf in range(2):
                P.wait("sync", sA, pro_state["q8"])
                P.ins("sync", lambda e, t=tt, h=hf: e.dma_start(
                    out=xbC[:], in_=x2_d[t * 128:(t + 1) * 128, h * 1024:(h + 1) * 1024]),
                    dX2L[0], 16)
                P.wait("scalar", dX2L[0])
                P.ins("scalar", lambda e, h=hf: e.activation(
                    out=x8tok[:, h * 1024:(h + 1) * 1024], in_=xbC[:], func=AF.Copy,
                    scale=SCL[:, 0:1]), selfwait=False)
                pro_state["q8"] = sA.n
            P.wait("scalar", sT, pro_state["xq"])
            P.ins("scalar", lambda e: e.activation(out=xqbf[:], in_=x8tok[:], func=AF.Copy,
                                                   scale=1.0))
            tok_xqbf = sA.n
            for dc in range(ND):
                tp = dc % 2
                P.wait("tensor", sA, max(tok_xqbf, pro_state["ptc"][tp]))
                P.ins("tensor", lambda e, d=dc, q=tp: e.transpose(
                    ptc[q][:], xqbf[:, d * 128:(d + 1) * 128], id_bf[:]), sT)
                tok_tr = sT.n
                P.wait("scalar", sT, tok_tr)
                P.ins("scalar", lambda e, d=dc, t=tt, q=tp: e.activation(
                    out=x8T[:, d * T + t * 128: d * T + (t + 1) * 128],
                    in_=ptc[q][:], func=AF.Copy, scale=1.0), selfwait=False)
                pro_state["ptc"][tp] = sA.n
            pro_state["xq"] = sT.n

        tok_quant = [0, 0]
        tok_quant_done = 0
        tok_wprq = 0
        tok_relu = [[0] * 4, [0] * 4]
        tok_mult = [[0, 0], [0, 0]]
        tok_mm = [[0] * 4, [0] * 4]
        for jc in range(NJ):
            par = jc % 2
            P.wait("sync", sA, tok_quant_done)
            P.ins("sync", lambda e, k=jc: e.dma_start(
                out=bass.AP(wfc32, 0, [[D, 128], [128, ND], [1, 128]]),
                in_=bass.AP(dh["wfcT"], k * 128, [[J, 128], [128 * J, ND], [1, 128]])), dWFC, 16)
            P.wait("scalar", dWFC)
            P.wait("scalar", sT, tok_mm[par][3])
            P.ins("scalar", lambda e, p=par: e.activation(out=wfc8[p][:], in_=wfc32[:], func=AF.Copy,
                                                          scale=SCL[:, 2:3]), selfwait=False)
            tok_quant[par] = sA.n
            tok_quant_done = sA.n
            P.wait("gpsimd", sA, tok_wprq)
            P.ins("gpsimd", lambda e, k=jc: e.dma_start(
                out=wpr32[:], in_=wprT_d[k * 128:(k + 1) * 128, :]), dWPR, 16)
            P.wait("scalar", dWPR)
            P.ins("scalar", lambda e, k=jc: e.activation(
                out=wpr8[:, k * D:(k + 1) * D], in_=wpr32[:], func=AF.Copy,
                scale=SCL[:, 3:4]), selfwait=False)
            tok_wprq = sA.n
            for ts in range(4):
                rp = ts % 2
                if jc == 0:
                    for t4 in range(4):
                        emit_pro_tile(ts * 4 + t4)
                    tok_x8[ts] = sA.n
                P.wait("tensor", sA, max(tok_quant[par], tok_x8[ts]))
                P.wait("tensor", sA, tok_relu[par][(ts + 2) % 4])
                for d2 in range(ND // 2):
                    P.ins("tensor", lambda e, d=d2, s=ts, p=par: e.matmul(
                        phh[p][s % 2][:],
                        bass.AP(wfc8[p], d * 256, [[D, 128], [128, 2], [1, 128]]),
                        bass.AP(x8T, (2 * d) * T + s * 512, [[ND * T, 128], [T, 2], [1, 512]]),
                        start=(d == 0), stop=(d == ND // 2 - 1),
                        perf_mode=PM.DoubleRow), sT)
                tok_mm[par][ts] = sT.n
                P.wait("scalar", sT, tok_mm[par][ts])
                P.wait("scalar", sV, tok_mult[par][rp])
                P.ins("scalar", lambda e, s=ts, p=par, q=rp: e.activation(
                    out=r_sb[q][:], in_=phh[p][s % 2][:], func=AF.Relu, scale=SCL[:, 1:2]), selfwait=False)
                tok_relu[par][ts] = sA.n
                P.wait("vector", sA, tok_relu[par][ts])
                P.wait("vector", dHST[ts % 2])
                P.ins("vector", lambda e, s=ts, q=rp: e.tensor_tensor(
                    h_sb[s % 2][:], r_sb[q][:], r_sb[q][:], op=ALU.mult), selfwait=False)
                tok_mult[par][rp] = sV.n
                P.ins("vector", lambda e, s=ts: e.reduce_max(stv(15), h_sb[s % 2][:], axis=AXX))
                P.ins("vector", lambda e: e.tensor_scalar(rmx(1), rmx(1), stv(15), 0.0,
                                                          op0=ALU.max, op1=ALU.add))
                P.wait("gpsimd", sV)
                P.ins("gpsimd", lambda e, k=jc, s=ts: e.dma_start(
                    out=bass.AP(hT_h, (s * 4 * 128) * J + k * 128,
                                [[J, 128], [128 * J, 4], [1, 128]]),
                    in_=bass.AP(h_sb[s % 2], 0, [[512, 128], [128, 4], [1, 128]])), dHST[ts % 2], 16)

    cross_part_max(rmx(1), 3)
    P.ins("vector", lambda e: e.tensor_scalar(ccs[0:1, :], zrow_s[0:1, :], 0.0, 0.0,
                                              op0=ALU.mult, op1=ALU.add))
    P.ins("vector", lambda e: e.tensor_scalar(ccs[0:1, 0:1], ROW[0:1, 3:4], 1.0, 0.0,
                                              op0=ALU.mult, op1=ALU.add))
    P.wait("gpsimd", sV)
    P.ins("gpsimd", lambda e: e.dma_start(out=cc_in2, in_=ccs[0:1, :]), dCC, 16)
    P.wait("gpsimd", dCC)
    P.ins("gpsimd", lambda e: e.collective_compute(
        "AllReduce", ALU.max, replica_groups=[list(range(NC))],
        ins=[cc_in2], outs=[cc_out2]), sCC)
    P.wait("gpsimd", sCC)
    P.ins("gpsimd", lambda e: e.dma_start(out=ROW[0:1, 8:16], in_=cc_out2), dCC, 16)
    P.wait("vector", dCC)
    P.ins("vector", lambda e: e.tensor_scalar(ROW[0:1, 24:25], ROW[0:1, 8:9], 1.0 / 448.0, 1e-6,
                                              op0=ALU.mult, op1=ALU.max))
    P.ins("vector", lambda e: e.tensor_scalar(ROW[0:1, 25:26], ROW[0:1, 24:25], 2.0, 0.0,
                                              op0=ALU.mult, op1=ALU.add))
    P.ins("vector", lambda e: e.reciprocal(ROW[0:1, 25:26], ROW[0:1, 25:26]))
    P.ins("vector", lambda e: e.tensor_scalar(ROW[0:1, 26:27], ROW[0:1, 24:25], ROW[0:1, 21:22], 4.0,
                                              op0=ALU.mult, op1=ALU.mult))
    P.ins("vector", lambda e: e.tensor_scalar(ROW[0:1, 27:28], ROW[0:1, 26:27], ROW[0:1, 26:27], 0.0,
                                              op0=ALU.mult, op1=ALU.add))
    P.ins("vector", lambda e: e.tensor_scalar(ccs[0:1, 0:1], ROW[0:1, 25:26], 1.0, 0.0, op0=ALU.mult, op1=ALU.add))
    P.ins("vector", lambda e: e.tensor_scalar(ccs[0:1, 1:2], ROW[0:1, 26:27], 1.0, 0.0, op0=ALU.mult, op1=ALU.add))
    P.ins("vector", lambda e: e.tensor_scalar(ccs[0:1, 2:3], ROW[0:1, 27:28], 1.0, 0.0, op0=ALU.mult, op1=ALU.add))
    with nc.psum_tensor("pscl2", [128, 8], F32) as pscl2:
        P.wait("tensor", sV)
        P.ins("tensor", lambda e: e.matmul(pscl2[:], onesr[0:1, :], ccs[0:1, :], start=True, stop=True), sT)
        P.wait("scalar", sT)
        P.ins("scalar", lambda e: e.activation(out=SCL2[:], in_=pscl2[:], func=AF.Copy, scale=1.0))

    full_barrier()
    ph_cm.close()
    hf32 = [sb(f"hf32_{i}", [128, J // 2], BF16, ph_c) for i in range(2)]
    h8s = [sb(f"h8s{i}", [128, J], F8, ph_c) for i in range(2)]
    lnff = sb("lnff0", [128, D], F32, ph_c)
    xm5 = sb("xm5_0", [128, D], F32, ph_c)

    with nc.psum_tensor("pq0", [128, D], F32) as pq0, \
         nc.psum_tensor("pq1", [128, D], F32) as pq1:
        poD = [pq0, pq1]
        for s4 in dHST:
            P.wait("sync", s4)
        P.wait("sync", dXMS)
        tok_hq = [0, 0]
        tok_hq_done = [0, 0]
        tok_add = 0
        tok_out = 0
        tok_mmD = [0, 0]
        tok_lnff = [0, 0]
        tok_stats = [0, 0]
        for tt in range(NT):
            par = tt % 2
            for half in range(2):
                P.wait("gpsimd", sA, tok_hq_done[half])
                P.ins("gpsimd", lambda e, t=tt, hf=half: e.dma_start(
                    out=hf32[hf][:], in_=bass.AP(hT_h, (t * 128) * J + hf * (J // 2),
                                                 [[J, 128], [1, J // 2]])), dHSx[half], 16)
            for half in range(2):
                P.wait("scalar", dHSx[half])
                P.wait("scalar", sT, tok_mmD[par])
                P.ins("scalar", lambda e, p=par, hf=half: e.activation(
                    out=h8s[p][:, hf * (J // 2):(hf + 1) * (J // 2)], in_=hf32[hf][:], func=AF.Copy,
                    scale=SCL2[:, 0:1]), selfwait=False)
                tok_hq_done[half] = sA.n
            tok_hq[par] = sA.n
            P.wait("sync", sV, tok_add)
            P.ins("sync", lambda e, t=tt: e.dma_start(
                out=xm5[:], in_=xmid_d[t * 128:(t + 1) * 128, :]), dXM5[0], 16)
            P.wait("tensor", sA, max(tok_hq[par], tok_lnff[par]))
            P.wait("tensor", sV, tok_stats[par])
            for ns in range(4):
                for k2 in range(NJ // 2):
                    P.ins("tensor", lambda e, k=k2, n=ns, p=par: e.matmul(
                        poD[p][:, n * 512:(n + 1) * 512],
                        bass.AP(h8s[p], k * 256, [[J, 128], [128, 2], [1, 128]]),
                        bass.AP(wpr8, (2 * k) * D + n * 512, [[NJ * D, 128], [D, 2], [1, 512]]),
                        start=(k == 0), stop=(k == NJ // 2 - 1),
                        perf_mode=PM.DoubleRow), sT)
            tok_mmD[par] = sT.n
            P.wait("vector", sT, tok_mmD[par])
            P.wait("scalar", sT, tok_mmD[par])
            ln_stats(poD[par][:], 0, 0, scale_c=SCL2[:, 1:2], scale_c2=SCL2[:, 2:3])
            tok_stats[par] = sV.n + 1
            P.ins("vector", lambda e: e.tensor_scalar(stv(6), stv(3), SCL2[:, 1:2], 0.0,
                                                      op0=ALU.mult, op1=ALU.add))
            P.wait("scalar", sV)
            P.wait("scalar", dOUT[0])
            P.ins("scalar", lambda e, p=par: e.activation(out=lnff[:], in_=poD[p][:], func=AF.Identity,
                                                          scale=stv(6), bias=stv(4)), selfwait=False)
            tok_lnff[par] = sA.n
            P.wait("vector", sA)
            P.wait("vector", dXM5[0])
            P.ins("vector", lambda e: e.tensor_tensor(lnff[:], xm5[:], lnff[:], op=ALU.add))
            tok_add = sV.n
            P.wait("sync", sV, tok_add)
            P.ins("sync", lambda e, t=tt: e.dma_start(
                out=out_d[t * 128:(t + 1) * 128, :], in_=lnff[:]), dOUT[0], 16)

    P.wait("sync", dOUT[0])
    P.wait("sync", dOUT[1])
    P.wait("gpsimd", dOUT[0])
    P.wait("gpsimd", dOUT[1])

    with nc.Block() as block:
        P.build(block)
    ph_c.close()
    es.close()
    return nc


_CACHED = {}


def kernel(L0_attn_x, L0_attn_Wq, L0_attn_Wk, L0_attn_Wv, L0_attn_Wo,
           L0_attn_qn_w, L0_attn_kn_w, L0_ffn_w_fc, L0_ffn_w_proj):
    x = np.ascontiguousarray(np.asarray(L0_attn_x, dtype=np.float32))
    f32 = np.float32

    def t(a):
        return np.ascontiguousarray(np.asarray(a, dtype=np.float32).T)

    wqT, wkT, wvT = t(L0_attn_Wq), t(L0_attn_Wk), t(L0_attn_Wv)
    woT, wfcT, wprT = t(L0_attn_Wo), t(L0_ffn_w_fc), t(L0_ffn_w_proj)
    qn = np.ascontiguousarray(np.asarray(L0_attn_qn_w, f32).reshape(1, DH))
    kn = np.ascontiguousarray(np.asarray(L0_attn_kn_w, f32).reshape(1, DH))

    inv_freq = (f32(1.0) / f32(10000.0) ** (np.arange(0, DH, 2).astype(f32) / f32(DH))).astype(f32)
    pos = np.arange(T, dtype=f32)
    emb = np.concatenate([pos[:, None] * inv_freq[None, :]] * 2, axis=1).astype(f32)
    cos_t = np.cos(emb).astype(f32)
    sin_t = np.sin(emb).astype(f32)
    sgn = np.where(np.arange(DH) < 64, f32(-1.0), f32(1.0))
    qs = f32(DH ** -0.5)
    tqc = np.ascontiguousarray(cos_t * qs)
    tqs = np.ascontiguousarray(sin_t * sgn * qs)
    tkc = np.ascontiguousarray(cos_t)
    tks = np.ascontiguousarray(sin_t * sgn)

    jj_ = np.arange(128)[:, None]
    ii_ = np.arange(256)[None, :]
    m_list = []
    for c_ in (-128, 0, 384, 512):
        d_ = c_ + ii_ - jj_
        m_list.append(((d_ >= 0) & (d_ <= 512)).astype(ml_dtypes.bfloat16))
    masks4 = np.ascontiguousarray(np.concatenate(m_list, axis=0))
    ones_bfm = np.ones((128, 128), dtype=ml_dtypes.bfloat16)
    id_bfm = np.eye(128, dtype=ml_dtypes.bfloat16)
    id_f32m = np.eye(128, dtype=f32)
    onesrm = np.ones((1, 128), dtype=f32)
    consts = np.zeros((128, 8), f32)
    consts[:, 0] = 1e-5
    consts[:, 1] = 1e-6
    zrow = np.zeros((1, 8), f32)

    if "nc" not in _CACHED:
        _CACHED["nc"] = build_program()
    nc = _CACHED["nc"]

    shared = dict(wqT=wqT, wkT=wkT, wvT=wvT, woT=woT, wfcT=wfcT, wprT=wprT,
                  qn=qn, kn=kn, tqc=tqc, tqs=tqs, tkc=tkc, tks=tks,
                  masks4=masks4, ones_bf=ones_bfm, id_bf=id_bfm,
                  id_f32=id_f32m, onesr=onesrm, consts=consts, zrow=zrow)
    in_maps = [dict(shared, x=np.ascontiguousarray(x[b]),
                    wslf=np.ascontiguousarray(wfcT[b * (D // NC):(b + 1) * (D // NC), :]),
                    wslp=np.ascontiguousarray(wprT[b * (J // NC):(b + 1) * (J // NC), :]))
               for b in range(NC)]
    res = run_bass_kernel_spmd(nc, in_maps, core_ids=list(range(NC)))
    return np.stack([res.results[b]["out"] for b in range(NC)], axis=0)


# version 17
# speedup vs baseline: 1.0998x; 1.0888x over previous
import numpy as np
import ml_dtypes
from contextlib import ExitStack

import concourse.bass as bass
import concourse.mybir as mybir
from concourse.bass_utils import run_bass_kernel_spmd

F32 = mybir.dt.float32
BF16 = mybir.dt.bfloat16
F8 = mybir.dt.float8e4
AF = mybir.ActivationFunctionType
ALU = mybir.AluOpType
AXX = mybir.AxisListType.X
PM = mybir.MatmulPerfMode

T, D, H, HKV, DH, G, WIN, J = 2048, 2048, 16, 4, 128, 4, 512, 8192
NT, ND, NJ, NC = 16, 16, 64, 8

ENGINES = ("sync", "scalar", "vector", "tensor", "gpsimd")


class Sem:
    __slots__ = ("h", "n")

    def __init__(self, h):
        self.h = h
        self.n = 0


class Sched:
    def __init__(self):
        self.ops = []
        self.own_sem = {}

    def emit(self, eng, fn):
        self.ops.append((eng, fn))

    def wait(self, eng, sem, n=None):
        tgt = sem.n if n is None else n
        if tgt <= 0:
            return
        self.emit(eng, lambda e, s=sem.h, t=tgt: e.wait_ge(s, t))

    def ins(self, eng, f, sem=None, inc=1, selfwait=True):
        own = self.own_sem.get(eng)
        if own is not None:
            if selfwait and own.n > 0:
                self.wait(eng, own)
            if sem is None:
                sem, inc = own, 1
        if sem is not None:
            sem.n += inc
        def fn(e, _f=f, _s=sem, _i=inc):
            r = _f(e)
            if _s is not None:
                r.then_inc(_s.h, _i)
        self.emit(eng, fn)

    def build(self, block):
        for ename in ENGINES:
            sec = getattr(block, ename)
            mine = [fn for (e, fn) in self.ops if e == ename]
            def body(eng, _my=mine):
                for fn in _my:
                    fn(eng)
            sec(body)


def build_program():
    nc = bass.Bass("TRN2", target_bir_lowering=False)

    dh = {}

    def din(name, shape, dt=F32):
        h = nc.dram_tensor(name, shape, dt, kind="ExternalInput")
        dh[name] = h
        return h.ap()

    x_d = din("x", [T, D])
    wqT_d = din("wqT", [D, H * DH])
    wkT_d = din("wkT", [D, HKV * DH])
    wvT_d = din("wvT", [D, HKV * DH])
    woT_d = din("woT", [H * DH, D])
    wfcT_d = din("wfcT", [D, J])
    wprT_d = din("wprT", [J, D])
    qn_d = din("qn", [1, DH])
    kn_d = din("kn", [1, DH])
    tqc_d = din("tqc", [T, DH])
    tqs_d = din("tqs", [T, DH])
    tkc_d = din("tkc", [T, DH])
    tks_d = din("tks", [T, DH])
    masks4_d = din("masks4", [4 * 128, 256], BF16)
    ones_bf_d = din("ones_bf", [128, 128], BF16)
    id_bf_d = din("id_bf", [128, 128], BF16)
    id_f32_d = din("id_f32", [128, 128], F32)
    onesr_d = din("onesr", [1, 128], F32)
    consts_d = din("consts", [128, 8], F32)
    zrow_d = din("zrow", [1, 8], F32)
    wslf_d = din("wslf", [D // NC, J])
    wslp_d = din("wslp", [J // NC, D])

    out_h = nc.dram_tensor("out", [T, D], F32, kind="ExternalOutput")
    out_d = out_h.ap()

    xmid_h = nc.dram_tensor("xmid_s", [T, D], F32)
    x2_h = nc.dram_tensor("x2_s", [T, D], F32)
    hT_h = nc.dram_tensor("hT_s", [NT * 128, J], BF16)
    ctx_h = nc.dram_tensor("ctx_s", [H * 128, T], BF16)
    xmid_d, x2_d, hT_d, ctx_d = xmid_h.ap(), x2_h.ap(), hT_h.ap(), ctx_h.ap()
    cc_in1 = nc.dram_tensor("cc_in1", [1, 8], F32).ap()
    cc_out1 = nc.dram_tensor("cc_out1", [1, 8], F32).ap()
    cc_in2 = nc.dram_tensor("cc_in2", [1, 8], F32).ap()
    cc_out2 = nc.dram_tensor("cc_out2", [1, 8], F32).ap()

    P = Sched()
    es = ExitStack()

    ALLSEMS = []

    def sem(name):
        s = Sem(es.enter_context(nc.semaphore(name)))
        ALLSEMS.append(s)
        return s

    def full_barrier():
        for _eng in ENGINES:
            for _s in ALLSEMS:
                P.wait(_eng, _s)

    sT = sem("s_pe")
    sA = sem("s_act")
    sV = sem("s_dve")
    sGc = sem("s_gpc")
    sCC = sem("s_cc")
    P.own_sem = {"vector": sV, "scalar": sA}

    dIN = sem("d_in")
    dX = [sem("d_x0"), sem("d_x1")]
    dTAB = sem("d_tab")
    dCTXL = [sem("d_ctxl0"), sem("d_ctxl1")]
    dXB5 = [sem("d_xb50"), sem("d_xb51")]
    dX2L = [sem("d_x2l0"), sem("d_x2l1")]
    dWFC = sem("d_wfc")
    dHSx = [sem("d_hs0"), sem("d_hs1")]
    dXM5 = [sem("d_xm50"), sem("d_xm51")]
    dOUT = [sem("d_out0"), sem("d_out1")]
    dWMX = [sem("d_wmx0")]
    dWKV = sem("d_wkv")
    dWQ = sem("d_wq")
    dCTXS = [sem("d_ctxs0"), sem("d_ctxs1")]
    dWO = [sem(f"d_wo{i}") for i in range(4)]
    dXMS = sem("d_xms")
    dX2S = sem("d_x2s")
    dWPR = sem("d_wpr")
    dHST = [sem(f"d_hst{i}") for i in range(4)]
    dCC = sem("d_cc")

    def sb(name, shape, dt, stk=es):
        return stk.enter_context(nc.sbuf_tensor(name, shape, dt))

    cons = sb("cons", [128, 8], F32)
    zrow_s = sb("zrow_s", [1, 8], F32)
    onesr = sb("onesr_s", [1, 128], F32)
    id_bf = sb("id_bf_s", [128, 128], BF16)
    id_f32 = sb("id_f32_s", [128, 128], F32)
    ones_bf = sb("ones_bf_s", [128, 128], BF16)
    masks4 = sb("masks4_s", [128, 4 * 256], BF16)
    qn_s = sb("qn_s", [1, DH], F32)
    kn_s = sb("kn_s", [1, DH], F32)
    wrow = sb("wrow", [1, 2 * DH], F32)
    wqB = sb("wqB", [128, DH], F32)
    wkB = sb("wkB", [128, DH], F32)
    STV = sb("STV", [128, 8 * 24], F32)
    STA = sb("STA", [128, 8 * 16], F32)
    RMX = sb("RMX", [128, 8 * 4], F32)
    SCL = sb("SCL", [128, 8], F32)
    SCL2 = sb("SCL2", [128, 8], F32)
    ROW = sb("ROWs", [1, 32], F32)
    ccs = sb("ccs_s", [1, 8], F32)
    tcol = sb("tcol", [128, 2], F32)
    trash = sb("trash", [128, 2048], BF16)

    def stv(i):
        return STV[:, 8 * i:8 * i + 1]

    def sta(i):
        return STA[:, 8 * i:8 * i + 1]

    def rmx(i):
        return RMX[:, 8 * i:8 * i + 1]

    for dst, src in [(cons, consts_d), (zrow_s, zrow_d), (onesr, onesr_d),
                     (id_bf, id_bf_d), (id_f32, id_f32_d), (ones_bf, ones_bf_d),
                     (qn_s, qn_d), (kn_s, kn_d)]:
        P.ins("sync", lambda e, d=dst, s=src: e.dma_start(out=d[:], in_=s), dIN, 16)
    P.ins("sync", lambda e: e.dma_start(
        out=bass.AP(masks4, 0, [[4 * 256, 128], [256, 4], [1, 256]]),
        in_=bass.AP(dh["masks4"], 0, [[256, 128], [128 * 256, 4], [1, 256]])), dIN, 16)
    P.ins("gpsimd", lambda e: e.memset(RMX[:], 0), sGc, 1)
    P.wait("vector", dIN)
    P.wait("scalar", dIN)
    P.wait("tensor", dIN)
    P.ins("vector", lambda e: e.tensor_scalar(wrow[0:1, 0:DH], qn_s[0:1, :], 1.0, 1.0,
                                              op0=ALU.mult, op1=ALU.add))
    P.ins("vector", lambda e: e.tensor_scalar(wrow[0:1, DH:2 * DH], kn_s[0:1, :], 1.0, 1.0,
                                              op0=ALU.mult, op1=ALU.add))

    with nc.psum_tensor("pinit", [128, 2 * DH], F32) as pinit:
        P.wait("tensor", sV)
        P.ins("tensor", lambda e: e.matmul(pinit[:, 0:DH], onesr[0:1, :], wrow[0:1, 0:DH],
                                           start=True, stop=True), sT)
        P.ins("tensor", lambda e: e.matmul(pinit[:, DH:2 * DH], onesr[0:1, :], wrow[0:1, DH:2 * DH],
                                           start=True, stop=True), sT)
        P.wait("scalar", sT)
        P.ins("scalar", lambda e: e.activation(out=wqB[:], in_=pinit[:, 0:DH], func=AF.Copy, scale=1.0))
        P.ins("scalar", lambda e: e.activation(out=wkB[:], in_=pinit[:, DH:2 * DH], func=AF.Copy, scale=1.0))

    ph_a = ExitStack()
    xnT = sb("xnT", [128, ND * T], BF16, ph_a)
    kT = sb("kT", [128, HKV * T], BF16, ph_a)
    v_sb = sb("v_sb", [128, HKV * T], BF16, ph_a)
    qTg = sb("qTg", [128, G * T], BF16, ph_a)
    tabc = sb("tabc", [128, NT * DH], F32, ph_a)
    tabs = sb("tabs", [128, NT * DH], F32, ph_a)
    xbuf = [sb(f"xbuf{i}", [128, D], F32, ph_a) for i in range(2)]
    xn_tok = sb("xn_tok", [128, D], BF16, ph_a)
    xn_tok2 = [xn_tok, xn_tok]
    rop = [sb(f"rop{i}", [128, 3 * 128], F32, ph_a) for i in range(2)]
    qrot_bf = [sb(f"qrot{i}", [128, 128], BF16, ph_a) for i in range(2)]
    wkv_sb = sb("wkv_sb", [128, ND * 256], BF16, ph_a)
    wq_sb = sb("wq_sb", [128, ND * 512], BF16, ph_a)
    ctxs = [sb(f"ctxs{i}", [128, T], BF16, ph_a) for i in range(2)]
    expb = [sb(f"expb{i}", [128, 1536], BF16, ph_a) for i in range(2)]
    rzb = [sb(f"rzb{i}", [128, 256], F32, ph_a) for i in range(2)]
    wmx = [sb("wmx0", [128, 1024], F32, ph_a)]

    def xnT_t(dc, tt):
        return xnT[:, dc * T + tt * 128: dc * T + (tt + 1) * 128]

    def kT_t(hk, jt):
        return kT[:, hk * T + jt * 128: hk * T + (jt + 1) * 128]

    def v_t(hk, jt):
        return v_sb[:, hk * T + jt * 128: hk * T + (jt + 1) * 128]

    def qT_t(g, it):
        return qTg[:, g * T + it * 128: g * T + (it + 1) * 128]

    def tab_t(tab, tt):
        return tab[:, tt * 128:(tt + 1) * 128]

    def ln_stats(src_ap, sl, asl, scale_c=None, scale_c2=None):
        P.wait("scalar", sV)
        P.ins("vector", lambda e: e.reduce_sum(stv(sl), src_ap, axis=AXX))
        P.wait("scalar", sV)
        P.ins("scalar", lambda e: e.activation(out=trash[:], in_=src_ap, func=AF.Square,
                                               accum_out=sta(asl)))
        P.wait("vector", sA)
        if scale_c is None:
            P.ins("vector", lambda e: e.tensor_scalar(stv(sl), stv(sl), 1.0 / D, 0.0,
                                                      op0=ALU.mult, op1=ALU.add))
            P.ins("vector", lambda e: e.tensor_scalar(stv(sl + 1), sta(asl), 1.0 / D, 0.0,
                                                      op0=ALU.mult, op1=ALU.add))
        else:
            P.ins("vector", lambda e: e.tensor_scalar(stv(sl), stv(sl), scale_c, 1.0 / D,
                                                      op0=ALU.mult, op1=ALU.mult))
            P.ins("vector", lambda e: e.tensor_scalar(stv(sl + 1), sta(asl), scale_c2, 1.0 / D,
                                                      op0=ALU.mult, op1=ALU.mult))
        P.ins("vector", lambda e: e.tensor_scalar(stv(sl + 2), stv(sl), stv(sl), 0.0,
                                                  op0=ALU.mult, op1=ALU.add))
        P.ins("vector", lambda e: e.tensor_scalar(stv(sl + 2), stv(sl + 1), stv(sl + 2), 0.0,
                                                  op0=ALU.subtract, op1=ALU.add))
        P.wait("scalar", sV)
        P.ins("scalar", lambda e: e.activation(out=sta(asl + 1), in_=stv(sl + 2), func=AF.Sqrt,
                                               scale=1.0, bias=cons[:, 0:1]))
        P.wait("vector", sA)
        P.ins("vector", lambda e: e.reciprocal(stv(sl + 3), sta(asl + 1)))
        P.ins("vector", lambda e: e.tensor_scalar(stv(sl + 4), stv(sl), stv(sl + 3), -1.0,
                                                  op0=ALU.mult, op1=ALU.mult))

    full_barrier()
    P.wait("vector", sGc)
    wmax_tiles = []
    for (w_ap_, rows_, cols_, slot_) in [(wslf_d, D // NC, J, 2), (wslp_d, J // NC, D, 3)]:
        for r_ in range(rows_ // 128):
            for c_ in range(cols_ // 1024):
                wmax_tiles.append((w_ap_, r_, c_, slot_))
    wmax_state = {"i": 0, "tok": [0, 0]}

    def wmax_step(nstep=1):
        for _ in range(nstep):
            i = wmax_state["i"]
            if i >= len(wmax_tiles):
                return
            wmax_state["i"] += 1
            w_ap_, r_, c_, slot_ = wmax_tiles[i]
            wp = 0
            P.wait("gpsimd", sV, wmax_state["tok"][wp])
            P.ins("gpsimd", lambda e, w=w_ap_, rr=r_, cc=c_, p=wp: e.dma_start(
                out=wmx[p][:], in_=w[rr * 128:(rr + 1) * 128, cc * 1024:(cc + 1) * 1024]), dWMX[wp], 16)
            P.wait("vector", dWMX[wp])
            P.ins("vector", lambda e, p=wp: e.reduce_max(stv(15), wmx[p][:], axis=AXX,
                                                         apply_absolute_value=True))
            P.ins("vector", lambda e, s=slot_: e.tensor_scalar(rmx(s), rmx(s), stv(15), 0.0,
                                                               op0=ALU.max, op1=ALU.add))
            wmax_state["tok"][wp] = sV.n

    with nc.psum_tensor("ptr0", [128, 128], BF16) as ptr0, \
         nc.psum_tensor("ptr1", [128, 128], BF16) as ptr1:
        ptr = [ptr0, ptr1]
        tok_xn = [0, 0]
        tok_nrm = 0
        tok_ptr = [0, 0]
        for tt in range(NT):
            par = tt % 2
            wmax_step(2)
            P.wait("sync", sA)
            P.wait("sync", sV)
            P.ins("sync", lambda e, t=tt, p=par: e.dma_start(
                out=xbuf[p][:], in_=x_d[t * 128:(t + 1) * 128, :]), dX[par], 16)
            P.wait("vector", dX[par])
            P.wait("scalar", dX[par])
            ln_stats(xbuf[par][:], 0, 0)
            P.wait("scalar", sV)
            P.wait("scalar", sT, max(tok_xn))
            P.ins("scalar", lambda e, p=par: e.activation(out=xn_tok2[p][:], in_=xbuf[p][:],
                                                          func=AF.Identity,
                                                          scale=stv(3), bias=stv(4)))
            tok_nrm = sA.n
            for dc in range(ND):
                tp = dc % 2
                P.wait("tensor", sA, max(tok_nrm, tok_ptr[tp]))
                P.ins("tensor", lambda e, d=dc, q=tp, p=par: e.transpose(
                    ptr[q][:], xn_tok2[p][:, d * 128:(d + 1) * 128], id_bf[:]), sT)
                P.wait("scalar", sT)
                P.ins("scalar", lambda e, d=dc, t=tt, q=tp: e.activation(
                    out=xnT_t(d, t), in_=ptr[q][:], func=AF.Copy, scale=1.0))
                tok_ptr[tp] = sA.n
            tok_xn[par] = sT.n

    P.ins("sync", lambda e: e.dma_start(
        out=bass.AP(tabc, 0, [[NT * DH, 128], [DH, NT], [1, DH]]),
        in_=bass.AP(dh["tkc"], 0, [[DH, 128], [128 * DH, NT], [1, DH]])), dTAB, 16)
    P.ins("sync", lambda e: e.dma_start(
        out=bass.AP(tabs, 0, [[NT * DH, 128], [DH, NT], [1, DH]]),
        in_=bass.AP(dh["tks"], 0, [[DH, 128], [128 * DH, NT], [1, DH]])), dTAB, 16)
    P.wait("vector", dTAB)

    with nc.psum_tensor("pkv0", [128, 256], F32) as pkv0, \
         nc.psum_tensor("pkv1", [128, 256], F32) as pkv1, \
         nc.psum_tensor("ptk0", [128, 128], BF16) as ptk0, \
         nc.psum_tensor("ptk1", [128, 128], BF16) as ptk1:
        pkv = [pkv0, pkv1]
        ptk = [ptk0, ptk1]
        tokA = [sA.n, sA.n]
        tokV = [sV.n, sV.n]
        tokTr = [0, 0]
        tokEvK = [0, 0]
        for hk in range(HKV):
            P.wait("gpsimd", sT)
            P.ins("gpsimd", lambda e, h=hk: e.dma_start(
                out=bass.AP(wkv_sb, 0, [[ND * 256, 128], [256, ND], [1, 128]]),
                in_=bass.AP(dh["wkT"], h * 128, [[HKV * DH, 128], [128 * HKV * DH, ND], [1, 128]])), dWKV, 16)
            P.ins("gpsimd", lambda e, h=hk: e.dma_start(
                out=bass.AP(wkv_sb, 128, [[ND * 256, 128], [256, ND], [1, 128]]),
                in_=bass.AP(dh["wvT"], h * 128, [[HKV * DH, 128], [128 * HKV * DH, ND], [1, 128]])), dWKV, 16)
            for tt in range(NT):
                par = tt % 2
                wmax_step(2)
                P.wait("tensor", dWKV)
                P.wait("tensor", sA, tokA[par])
                P.wait("tensor", sV, tokV[par])
                for dc in range(ND):
                    P.ins("tensor", lambda e, d=dc, t=tt, p=par: e.matmul(
                        pkv[p][:], xnT_t(d, t), wkv_sb[:, d * 256:(d + 1) * 256],
                        start=(d == 0), stop=(d == ND - 1)), sT)
                tok_mm = sT.n
                P.wait("scalar", sT, tok_mm)
                P.wait("scalar", sV, tokV[par])
                P.ins("scalar", lambda e, p=par: e.activation(out=trash[:, 0:128], in_=pkv[p][:, 0:128],
                                                              func=AF.Square, accum_out=sta(8)), selfwait=False)
                P.ins("scalar", lambda e: e.activation(out=sta(12), in_=sta(8), func=AF.Sqrt,
                                                       scale=1.0 / DH, bias=cons[:, 1:2]))
                tok_std = sA.n
                P.wait("vector", sA, tok_std)
                P.ins("vector", lambda e: e.reciprocal(stv(16), sta(12)))
                P.ins("vector", lambda e, p=par: e.scalar_tensor_tensor(rop[p][:, 0:128], pkv[p][:, 0:128],
                                                                        stv(16), wkB[:],
                                                                        op0=ALU.mult, op1=ALU.mult))
                tokV[par] = sV.n
                P.wait("scalar", sV, tokV[par])
                P.ins("scalar", lambda e, h=hk, t=tt, p=par: e.activation(
                    out=v_t(h, t), in_=pkv[p][:, 128:256], func=AF.Copy, scale=1.0), selfwait=False)
                tokA[par] = sA.n
                P.wait("vector", sT, tokTr[par])
                P.ins("vector", lambda e, p=par, t=tt: e.tensor_tensor(
                    rop[p][:, 128:256], rop[p][:, 0:128], tab_t(tabc, t), op=ALU.mult))
                P.ins("vector", lambda e, p=par, t=tt: e.tensor_tensor(
                    rop[p][:, 256:320], rop[p][:, 64:128], tab_t(tabs, t)[:, 0:64], op=ALU.mult))
                P.ins("vector", lambda e, p=par, t=tt: e.tensor_tensor(
                    rop[p][:, 320:384], rop[p][:, 0:64], tab_t(tabs, t)[:, 64:128], op=ALU.mult))
                P.ins("vector", lambda e, p=par: e.tensor_tensor(
                    qrot_bf[p][:], rop[p][:, 128:256], rop[p][:, 256:384], op=ALU.add))
                tok_rot = sV.n
                P.wait("tensor", sV, tok_rot)
                P.wait("tensor", sA, tokEvK[par])
                P.ins("tensor", lambda e, p=par: e.transpose(ptk[p][:], qrot_bf[p][:], id_bf[:]), sT)
                tokTr[par] = sT.n
                P.wait("scalar", sT, tokTr[par])
                P.ins("scalar", lambda e, h=hk, t=tt, p=par: e.activation(
                    out=kT_t(h, t), in_=ptk[p][:], func=AF.Copy, scale=1.0), selfwait=False)
                tokEvK[par] = sA.n

    P.wait("sync", sV)
    P.ins("sync", lambda e: e.dma_start(
        out=bass.AP(tabc, 0, [[NT * DH, 128], [DH, NT], [1, DH]]),
        in_=bass.AP(dh["tqc"], 0, [[DH, 128], [128 * DH, NT], [1, DH]])), dTAB, 16)
    P.ins("sync", lambda e: e.dma_start(
        out=bass.AP(tabs, 0, [[NT * DH, 128], [DH, NT], [1, DH]]),
        in_=bass.AP(dh["tqs"], 0, [[DH, 128], [128 * DH, NT], [1, DH]])), dTAB, 16)
    P.wait("vector", dTAB)

    tokQa = [0, 0]
    tokQv = [0, 0]
    tokTrQ = [0, 0]
    tokEvQ = [0, 0]
    tE = [0, 0]
    tSc = [0, 0]
    tM = [0, 0]
    tZp = [0, 0]
    tCp = [0, 0]
    tRZp = [0, 0]
    tNp = [0, 0]
    for hk in range(HKV):
        P.wait("gpsimd", sT)
        P.ins("gpsimd", lambda e, h=hk: e.dma_start(
            out=bass.AP(wq_sb, 0, [[ND * 512, 128], [512, ND], [1, 512]]),
            in_=bass.AP(dh["wqT"], h * 512, [[H * DH, 128], [128 * H * DH, ND], [1, 512]])), dWQ, 16)
        with nc.psum_tensor(f"pq0_{hk}", [128, 512], F32) as pq0, \
             nc.psum_tensor(f"pq1_{hk}", [128, 512], F32) as pq1, \
             nc.psum_tensor(f"ptq0_{hk}", [128, 128], BF16) as ptq0, \
             nc.psum_tensor(f"ptq1_{hk}", [128, 128], BF16) as ptq1:
            pq = [pq0, pq1]
            ptq = [ptq0, ptq1]
            P.wait("tensor", sA)
            P.wait("tensor", sV)
            for tt in range(NT):
                par = tt % 2
                wmax_step(2)
                P.wait("tensor", dWQ)
                P.wait("tensor", sA, tokQa[par])
                P.wait("tensor", sV, tokQv[par])
                for dc in range(ND):
                    P.ins("tensor", lambda e, d=dc, t=tt, p=par: e.matmul(
                        pq[p][:], xnT_t(d, t), wq_sb[:, d * 512:(d + 1) * 512],
                        start=(d == 0), stop=(d == ND - 1)), sT)
                tok_mm = sT.n
                P.wait("scalar", sT, tok_mm)
                P.wait("scalar", sV, tokQv[par])
                for g in range(G):
                    P.ins("scalar", lambda e, gg=g, p=par: e.activation(
                        out=trash[:, gg * 128:(gg + 1) * 128], in_=pq[p][:, gg * 128:(gg + 1) * 128],
                        func=AF.Square, accum_out=sta(8 + gg)), selfwait=False)
                P.wait("scalar", sA)
                for g in range(G):
                    P.ins("scalar", lambda e, gg=g: e.activation(out=sta(12 + gg), in_=sta(8 + gg),
                                                                 func=AF.Sqrt, scale=1.0 / DH,
                                                                 bias=cons[:, 1:2]), selfwait=False)
                tok_std = sA.n
                P.wait("vector", sA, tok_std)
                for g in range(G):
                    P.ins("vector", lambda e, gg=g: e.reciprocal(stv(16 + gg), sta(12 + gg)), selfwait=False)
                for g in range(G):
                    gp = g % 2
                    P.wait("vector", sT, tokTrQ[gp])
                    P.ins("vector", lambda e, gg=g, p=par: e.scalar_tensor_tensor(
                        rop[0][:, 0:128], pq[p][:, gg * 128:(gg + 1) * 128], stv(16 + gg), wqB[:],
                        op0=ALU.mult, op1=ALU.mult))
                    P.ins("vector", lambda e, t=tt: e.tensor_tensor(
                        rop[0][:, 128:256], rop[0][:, 0:128], tab_t(tabc, t), op=ALU.mult))
                    P.ins("vector", lambda e, t=tt: e.tensor_tensor(
                        rop[0][:, 256:320], rop[0][:, 64:128], tab_t(tabs, t)[:, 0:64], op=ALU.mult))
                    P.ins("vector", lambda e, t=tt: e.tensor_tensor(
                        rop[0][:, 320:384], rop[0][:, 0:64], tab_t(tabs, t)[:, 64:128], op=ALU.mult))
                    P.ins("vector", lambda e, p=gp: e.tensor_tensor(
                        qrot_bf[p][:], rop[0][:, 128:256], rop[0][:, 256:384], op=ALU.add))
                    tok_rot = sV.n
                    P.wait("tensor", sV, tok_rot)
                    P.wait("tensor", sA, tokEvQ[gp])
                    P.ins("tensor", lambda e, p=gp: e.transpose(ptq[p][:], qrot_bf[p][:], id_bf[:]), sT)
                    tokTrQ[gp] = sT.n
                    P.wait("scalar", sT, tokTrQ[gp])
                    P.ins("scalar", lambda e, gg=g, t=tt, p=gp: e.activation(
                        out=qT_t(gg, t), in_=ptq[p][:], func=AF.Copy, scale=1.0), selfwait=False)
                    tokEvQ[gp] = sA.n
                tokQv[par] = sV.n
                tokQa[par] = sA.n
        with nc.psum_tensor(f"psS0_{hk}", [128, 1536], F32) as psS0, \
             nc.psum_tensor(f"psS1_{hk}", [128, 1536], F32) as psS1, \
             nc.psum_tensor(f"pzz_{hk}", [128, 256], F32) as pzz, \
             nc.psum_tensor(f"pcc_{hk}", [128, 256], F32) as pcc:
            psS = [psS0, psS1]
            base = max(tokQa[0], tokQa[1])
            tE = [base, base]
            tSc = [0, 0]
            tM = [0, 0]
            tZp = max(tRZp)
            tCp = 0
            tRZ1 = base
            tN1 = 0
            P.wait("tensor", sV)

            NR = NT // 2

            def jinfo(r):
                out = []
                for jt in range(max(0, 2 * r - 4), 2 * r + 2):
                    c = (2 * r - jt) * 128
                    slot = {-128: 0, 0: 1, 384: 2, 512: 3}.get(c)
                    out.append((jt, slot))
                return out

            def emit_scores(r, g):
                par = r % 2
                P.wait("tensor", sA, tE[par])
                for jj, (jt, _) in enumerate(jinfo(r)):
                    P.ins("tensor", lambda e, p=par, jx=jj, hh=hk, j=jt, rr=r, gg=g: e.matmul(
                        psS[p][:, jx * 256:(jx + 1) * 256], kT_t(hh, j),
                        qTg[:, gg * T + rr * 256:(gg * T + (rr + 1) * 256)],
                        start=True, stop=True), sT)
                tSc[par] = sT.n

            def emit_exp(r):
                par = r % 2
                nj = len(jinfo(r))
                P.wait("scalar", sT, tSc[par])
                P.ins("scalar", lambda e, p=par, n=nj: e.activation(
                    out=expb[p][:, 0:n * 256], in_=psS[p][:, 0:n * 256], func=AF.Exp), selfwait=False)
                tE[par] = sA.n

            def emit_masks(r):
                par = r % 2
                P.wait("vector", sA, tE[par])
                first = True
                for jj, (jt, slot) in enumerate(jinfo(r)):
                    if slot is None:
                        continue
                    P.ins("vector", lambda e, p=par, jx=jj, m=slot: e.tensor_tensor(
                        expb[p][:, jx * 256:(jx + 1) * 256], expb[p][:, jx * 256:(jx + 1) * 256],
                        masks4[:, m * 256:(m + 1) * 256], op=ALU.mult), selfwait=first)
                    first = False
                tM[par] = sV.n

            def emit_zctx(r):
                nonlocal_d = None
                par = r % 2
                ji = jinfo(r)
                nj = len(ji)
                P.wait("tensor", sV, tM[par])
                P.wait("tensor", sA, tRZ1)
                for jj in range(nj):
                    P.ins("tensor", lambda e, p=par, j=jj, n=nj: e.matmul(
                        pzz[:], ones_bf[:], expb[p][:, j * 256:(j + 1) * 256],
                        start=(j == 0), stop=(j == n - 1)), sT)
                tz = sT.n
                P.wait("tensor", sV, tN1)
                for jj, (jt, _) in enumerate(ji):
                    P.ins("tensor", lambda e, p=par, hh=hk, j=jt, jx=jj, n=nj: e.matmul(
                        pcc[:], v_t(hh, j), expb[p][:, jx * 256:(jx + 1) * 256],
                        start=(jx == 0), stop=(jx == n - 1)), sT)
                return tz, sT.n

            for g in range(G):
                h = hk * G + g
                cpar = h % 2
                P.wait("vector", dCTXS[cpar])
                emit_scores(0, g)
                emit_exp(0)
                emit_masks(0)
                for r in range(NR):
                    par = r % 2
                    if r + 1 < NR:
                        emit_scores(r + 1, g)
                        emit_exp(r + 1)
                        emit_masks(r + 1)
                    tz, tc = emit_zctx(r)
                    P.wait("scalar", sT, tz)
                    P.ins("scalar", lambda e, p=par: e.activation(out=rzb[p][:], in_=pzz[:],
                                                                  func=AF.Copy, scale=1.0), selfwait=False)
                    tRZ1 = sA.n
                    P.wait("vector", sA, tRZ1)
                    P.ins("vector", lambda e, p=par: e.reciprocal(rzb[p][:], rzb[p][:]), selfwait=False)
                    P.wait("vector", sT, tc)
                    P.ins("vector", lambda e, p=par, c=cpar, rr=r: e.tensor_tensor(
                        ctxs[c][:, rr * 256:(rr + 1) * 256], pcc[:], rzb[p][:], op=ALU.mult))
                    tN1 = sV.n
                P.wait("gpsimd", sV)
                P.ins("gpsimd", lambda e, hh=h, c=cpar: e.dma_start(
                    out=ctx_d[hh * 128:(hh + 1) * 128, :], in_=ctxs[c][:]), dCTXS[cpar], 16)
            tokQa = [max(tokQa[0], tE[0]), max(tokQa[1], tE[1])]

    wmax_step(len(wmax_tiles))

    full_barrier()
    ph_a.close()
    ph5 = ExitStack()
    wo_sb = sb("wo_sb", [128, ND * D], BF16, ph5)
    ctx_full = sb("ctx_full", [128, ND * T], BF16, ph5)
    xb5 = [sb(f"xb5_{i}", [128, D], F32, ph5) for i in range(2)]
    lnat = sb("lnat", [128, D], F32, ph5)
    xmid_sb = sb("xmid_sb", [128, D], F32, ph5)
    x2_sb = sb("x2_sb", [128, D], F32, ph5)

    wo_f = [sb(f"wo_f{i}", [128, D], F32, ph5) for i in range(2)]
    P.wait("sync", dCTXS[0])
    P.wait("sync", dCTXS[1])
    for hd in range(ND):
        P.ins("sync", lambda e, d=hd: e.dma_start(
            out=ctx_full[:, d * T:(d + 1) * T], in_=ctx_d[d * 128:(d + 1) * 128, :]), dCTXL[hd % 2], 16)
    tok_woc = [0] * ND
    tok_wof = [0, 0]
    for hd in range(ND):
        wp = hd % 2
        P.wait("sync", sA, tok_wof[wp])
        P.ins("sync", lambda e, d=hd, p=wp: e.dma_start(
            out=wo_f[p][:], in_=woT_d[d * 128:(d + 1) * 128, :]), dWO[hd % 4], 16)
        P.wait("scalar", dWO[hd % 4])
        P.ins("scalar", lambda e, d=hd, p=wp: e.activation(
            out=wo_sb[:, d * D:(d + 1) * D], in_=wo_f[p][:], func=AF.Copy, scale=1.0), selfwait=False)
        tok_woc[hd] = sA.n
        tok_wof[wp] = sA.n

    with nc.psum_tensor("po0", [128, D], F32) as po0, \
         nc.psum_tensor("po1", [128, D], F32) as po1:
        po = [po0, po1]
        tok_po_a = [0, 0]
        tok_po_v = [0, 0]
        for tt in range(NT):
            par = tt % 2
            P.wait("sync", sV)
            P.ins("sync", lambda e, t=tt, p=par: e.dma_start(
                out=xb5[p][:], in_=x_d[t * 128:(t + 1) * 128, :]), dXB5[par], 16)
            if tt == 0:
                P.wait("tensor", dCTXL[0])
                P.wait("tensor", dCTXL[1])
            P.wait("tensor", sA, tok_po_a[par])
            P.wait("tensor", sV, tok_po_v[par])
            for hd in range(ND):
                if tt == 0:
                    P.wait("tensor", sA, tok_woc[hd])
                for ns in range(4):
                    P.ins("tensor", lambda e, d=hd, n=ns, p=par, t=tt: e.matmul(
                        po[p][:, n * 512:(n + 1) * 512], ctx_full[:, d * T + t * 128: d * T + (t + 1) * 128],
                        wo_sb[:, d * D + n * 512: d * D + (n + 1) * 512],
                        start=(d == 0), stop=(d == ND - 1)), sT)
            P.wait("vector", sT)
            P.wait("scalar", sT)
            ln_stats(po[par][:], 0, 0)
            P.wait("scalar", sV)
            P.ins("scalar", lambda e, p=par: e.activation(out=lnat[:], in_=po[p][:], func=AF.Identity,
                                                          scale=stv(3), bias=stv(4)))
            tok_po_a[par] = sA.n
            tok_po_v[par] = sV.n
            P.wait("vector", sA)
            P.wait("vector", dXB5[par])
            P.wait("vector", dXMS)
            P.ins("vector", lambda e, p=par: e.tensor_tensor(xmid_sb[:], xb5[p][:], lnat[:], op=ALU.add))
            P.wait("gpsimd", sV)
            P.ins("gpsimd", lambda e, t=tt: e.dma_start(
                out=xmid_d[t * 128:(t + 1) * 128, :], in_=xmid_sb[:]), dXMS, 16)
            ln_stats(xmid_sb[:], 8, 4)
            P.wait("scalar", sV)
            P.wait("scalar", dX2S)
            P.ins("scalar", lambda e: e.activation(out=x2_sb[:], in_=xmid_sb[:], func=AF.Identity,
                                                   scale=stv(11), bias=stv(12)))
            P.wait("vector", sA)
            P.ins("vector", lambda e: e.reduce_max(stv(15), x2_sb[:], axis=AXX,
                                                   apply_absolute_value=True))
            P.ins("vector", lambda e: e.tensor_scalar(rmx(0), rmx(0), stv(15), 0.0,
                                                      op0=ALU.max, op1=ALU.add))
            P.wait("gpsimd", sA)
            P.ins("gpsimd", lambda e, t=tt: e.dma_start(
                out=x2_d[t * 128:(t + 1) * 128, :], in_=x2_sb[:]), dX2S, 16)
    full_barrier()
    ph5.close()

    def cross_part_max(col_ap, row_col):
        with nc.psum_tensor(f"pmx{row_col}", [128, 128], F32) as pmx:
            P.ins("vector", lambda e: e.tensor_scalar(tcol[:, 0:1], col_ap, 1.0, 0.0,
                                                      op0=ALU.mult, op1=ALU.add))
            P.wait("tensor", sV)
            P.ins("tensor", lambda e: e.transpose(pmx[0:1, 0:128], tcol[:, 0:1], id_f32[:]), sT)
            P.wait("vector", sT)
            P.ins("vector", lambda e, c=row_col: e.reduce_max(
                ROW[0:1, c:c + 1], pmx[0:1, 0:128], axis=AXX))

    cross_part_max(rmx(0), 0)
    cross_part_max(rmx(2), 1)
    cross_part_max(rmx(3), 2)

    P.ins("vector", lambda e: e.tensor_scalar(ccs[0:1, :], zrow_s[0:1, :], 0.0, 0.0,
                                              op0=ALU.mult, op1=ALU.add))
    P.ins("vector", lambda e: e.tensor_scalar(ccs[0:1, 0:1], ROW[0:1, 0:1], 1.0, 0.0,
                                              op0=ALU.mult, op1=ALU.add))
    P.ins("vector", lambda e: e.tensor_scalar(ccs[0:1, 1:2], ROW[0:1, 1:2], 1.0, 0.0,
                                              op0=ALU.mult, op1=ALU.add))
    P.ins("vector", lambda e: e.tensor_scalar(ccs[0:1, 2:3], ROW[0:1, 2:3], 1.0, 0.0,
                                              op0=ALU.mult, op1=ALU.add))
    P.wait("gpsimd", sV)
    P.ins("gpsimd", lambda e: e.dma_start(out=cc_in1, in_=ccs[0:1, :]), dCC, 16)
    P.wait("gpsimd", dCC)
    P.ins("gpsimd", lambda e: e.collective_compute(
        "AllReduce", ALU.max, replica_groups=[list(range(NC))],
        ins=[cc_in1], outs=[cc_out1]), sCC)
    P.wait("gpsimd", sCC)
    P.ins("gpsimd", lambda e: e.dma_start(out=ROW[0:1, 8:16], in_=cc_out1), dCC, 16)
    P.wait("vector", dCC)
    P.ins("vector", lambda e: e.tensor_scalar(ROW[0:1, 16:17], ROW[0:1, 8:9], 1.0 / 448.0, 1e-6,
                                              op0=ALU.mult, op1=ALU.max))
    P.ins("vector", lambda e: e.tensor_scalar(ROW[0:1, 17:18], ROW[0:1, 16:17], 2.0, 0.0,
                                              op0=ALU.mult, op1=ALU.add))
    P.ins("vector", lambda e: e.reciprocal(ROW[0:1, 17:18], ROW[0:1, 17:18]))
    P.ins("vector", lambda e: e.tensor_scalar(ROW[0:1, 18:19], ROW[0:1, 9:10], 1.0 / 448.0, 1e-6,
                                              op0=ALU.mult, op1=ALU.max))
    P.ins("vector", lambda e: e.tensor_scalar(ROW[0:1, 19:20], ROW[0:1, 18:19], 2.0, 0.0,
                                              op0=ALU.mult, op1=ALU.add))
    P.ins("vector", lambda e: e.reciprocal(ROW[0:1, 19:20], ROW[0:1, 19:20]))
    P.ins("vector", lambda e: e.tensor_scalar(ROW[0:1, 20:21], ROW[0:1, 16:17], ROW[0:1, 18:19], 4.0,
                                              op0=ALU.mult, op1=ALU.mult))
    P.ins("vector", lambda e: e.tensor_scalar(ROW[0:1, 21:22], ROW[0:1, 10:11], 1.0 / 448.0, 1e-6,
                                              op0=ALU.mult, op1=ALU.max))
    P.ins("vector", lambda e: e.tensor_scalar(ROW[0:1, 22:23], ROW[0:1, 21:22], 2.0, 0.0,
                                              op0=ALU.mult, op1=ALU.add))
    P.ins("vector", lambda e: e.reciprocal(ROW[0:1, 22:23], ROW[0:1, 22:23]))
    P.ins("vector", lambda e: e.tensor_scalar(ccs[0:1, 0:1], ROW[0:1, 17:18], 1.0, 0.0, op0=ALU.mult, op1=ALU.add))
    P.ins("vector", lambda e: e.tensor_scalar(ccs[0:1, 1:2], ROW[0:1, 20:21], 1.0, 0.0, op0=ALU.mult, op1=ALU.add))
    P.ins("vector", lambda e: e.tensor_scalar(ccs[0:1, 2:3], ROW[0:1, 19:20], 1.0, 0.0, op0=ALU.mult, op1=ALU.add))
    P.ins("vector", lambda e: e.tensor_scalar(ccs[0:1, 3:4], ROW[0:1, 22:23], 1.0, 0.0, op0=ALU.mult, op1=ALU.add))
    with nc.psum_tensor("pscl", [128, 8], F32) as pscl:
        P.wait("tensor", sV)
        P.ins("tensor", lambda e: e.matmul(pscl[:], onesr[0:1, :], ccs[0:1, :], start=True, stop=True), sT)
        P.wait("scalar", sT)
        P.ins("scalar", lambda e: e.activation(out=SCL[:], in_=pscl[:], func=AF.Copy, scale=1.0))

    ph_c = ExitStack()
    wpr8 = sb("wpr8", [128, NJ * D], F8, ph_c)
    ph_cm = ExitStack()
    x8T = sb("x8T", [128, ND * T], F8, ph_cm)

    with ExitStack() as pro, \
         nc.psum_tensor("ptc0", [128, 128], BF16) as ptc0, \
         nc.psum_tensor("ptc1", [128, 128], BF16) as ptc1:
        xbC = [sb(f"xbC{i}", [128, D], F32, pro) for i in range(2)]
        x8tok = sb("x8tok", [128, D], F8, pro)
        xqbf = sb("xqbf", [128, D], BF16, pro)
        ptc = [ptc0, ptc1]
        full_barrier()
        P.wait("sync", dX2S)
        tok_x8 = [0] * 4
        for tt in range(NT):
            par = tt % 2
            P.wait("sync", sA)
            P.ins("sync", lambda e, t=tt, p=par: e.dma_start(
                out=xbC[p][:], in_=x2_d[t * 128:(t + 1) * 128, :]), dX2L[par], 16)
            P.wait("scalar", dX2L[par])
            P.ins("scalar", lambda e, p=par: e.activation(out=x8tok[:], in_=xbC[p][:], func=AF.Copy,
                                                          scale=SCL[:, 0:1]))
            P.ins("scalar", lambda e: e.activation(out=xqbf[:], in_=x8tok[:], func=AF.Copy, scale=1.0))
            for dc in range(ND):
                tp = dc % 2
                P.wait("tensor", sA)
                P.ins("tensor", lambda e, d=dc, q=tp: e.transpose(
                    ptc[q][:], xqbf[:, d * 128:(d + 1) * 128], id_bf[:]), sT)
                P.wait("scalar", sT)
                P.ins("scalar", lambda e, d=dc, t=tt, q=tp: e.activation(
                    out=x8T[:, d * T + t * 128: d * T + (t + 1) * 128],
                    in_=ptc[q][:], func=AF.Copy, scale=1.0))
            if tt % 4 == 3:
                tok_x8[tt // 4] = sA.n

    full_barrier()
    wfc32 = sb("wfc32", [128, D], F32, ph_cm)
    wfc8 = [sb(f"wfc8_{i}", [128, D], F8, ph_cm) for i in range(2)]
    wpr32 = sb("wpr32_0", [128, D], F32, ph_cm)
    h_sb = [sb(f"h_sb{i}", [128, 512], BF16, ph_cm) for i in range(4)]
    r_sb = [sb(f"r_sb{i}", [128, 512], F32, ph_cm) for i in range(2)]

    with nc.psum_tensor("ph0", [128, 512], F32) as ph0, \
         nc.psum_tensor("ph1", [128, 512], F32) as ph1, \
         nc.psum_tensor("ph2", [128, 512], F32) as ph2, \
         nc.psum_tensor("ph3", [128, 512], F32) as ph3, \
         nc.psum_tensor("ph4", [128, 512], F32) as ph4, \
         nc.psum_tensor("ph5", [128, 512], F32) as ph5b, \
         nc.psum_tensor("ph6", [128, 512], F32) as ph6, \
         nc.psum_tensor("ph7", [128, 512], F32) as ph7:
        phh = [[ph0, ph1, ph2, ph3], [ph4, ph5b, ph6, ph7]]
        tok_quant = [0, 0]
        tok_quant_done = 0
        tok_wprq = 0
        tok_relu = [[0] * 4, [0] * 4]
        tok_mult = [[0, 0], [0, 0]]
        tok_mm = [[0] * 4, [0] * 4]
        for jc in range(NJ):
            par = jc % 2
            P.wait("sync", sA, tok_quant_done)
            P.ins("sync", lambda e, k=jc: e.dma_start(
                out=bass.AP(wfc32, 0, [[D, 128], [128, ND], [1, 128]]),
                in_=bass.AP(dh["wfcT"], k * 128, [[J, 128], [128 * J, ND], [1, 128]])), dWFC, 16)
            P.wait("scalar", dWFC)
            P.wait("scalar", sT, tok_mm[par][3])
            P.ins("scalar", lambda e, p=par: e.activation(out=wfc8[p][:], in_=wfc32[:], func=AF.Copy,
                                                          scale=SCL[:, 2:3]), selfwait=False)
            tok_quant[par] = sA.n
            tok_quant_done = sA.n
            P.wait("gpsimd", sA, tok_wprq)
            P.ins("gpsimd", lambda e, k=jc: e.dma_start(
                out=wpr32[:], in_=wprT_d[k * 128:(k + 1) * 128, :]), dWPR, 16)
            P.wait("scalar", dWPR)
            P.ins("scalar", lambda e, k=jc: e.activation(
                out=wpr8[:, k * D:(k + 1) * D], in_=wpr32[:], func=AF.Copy,
                scale=SCL[:, 3:4]), selfwait=False)
            tok_wprq = sA.n
            for ts in range(4):
                P.wait("tensor", sA, tok_quant[par])
                P.wait("tensor", sA, tok_relu[par][ts])
                for d2 in range(ND // 2):
                    P.ins("tensor", lambda e, d=d2, s=ts, p=par: e.matmul(
                        phh[p][s][:],
                        bass.AP(wfc8[p], d * 256, [[D, 128], [128, 2], [1, 128]]),
                        bass.AP(x8T, (2 * d) * T + s * 512, [[ND * T, 128], [T, 2], [1, 512]]),
                        start=(d == 0), stop=(d == ND // 2 - 1),
                        perf_mode=PM.DoubleRow), sT)
                tok_mm[par][ts] = sT.n
            for ts in range(4):
                rp = ts % 2
                P.wait("scalar", sT, tok_mm[par][ts])
                P.wait("scalar", sV, tok_mult[par][rp])
                P.ins("scalar", lambda e, s=ts, p=par, q=rp: e.activation(
                    out=r_sb[q][:], in_=phh[p][s][:], func=AF.Relu, scale=SCL[:, 1:2]), selfwait=False)
                tok_relu[par][ts] = sA.n
                P.wait("vector", sA, tok_relu[par][ts])
                P.wait("vector", dHST[ts])
                P.ins("vector", lambda e, s=ts, q=rp: e.tensor_tensor(
                    h_sb[s][:], r_sb[q][:], r_sb[q][:], op=ALU.mult), selfwait=False)
                tok_mult[par][rp] = sV.n
                P.ins("vector", lambda e, s=ts: e.reduce_max(stv(15), h_sb[s][:], axis=AXX))
                P.ins("vector", lambda e: e.tensor_scalar(rmx(1), rmx(1), stv(15), 0.0,
                                                          op0=ALU.max, op1=ALU.add))
                P.wait("gpsimd", sV)
                P.ins("gpsimd", lambda e, k=jc, s=ts: e.dma_start(
                    out=bass.AP(hT_h, (s * 4 * 128) * J + k * 128,
                                [[J, 128], [128 * J, 4], [1, 128]]),
                    in_=bass.AP(h_sb[s], 0, [[512, 128], [128, 4], [1, 128]])), dHST[ts], 16)

    cross_part_max(rmx(1), 3)
    P.ins("vector", lambda e: e.tensor_scalar(ccs[0:1, :], zrow_s[0:1, :], 0.0, 0.0,
                                              op0=ALU.mult, op1=ALU.add))
    P.ins("vector", lambda e: e.tensor_scalar(ccs[0:1, 0:1], ROW[0:1, 3:4], 1.0, 0.0,
                                              op0=ALU.mult, op1=ALU.add))
    P.wait("gpsimd", sV)
    P.ins("gpsimd", lambda e: e.dma_start(out=cc_in2, in_=ccs[0:1, :]), dCC, 16)
    P.wait("gpsimd", dCC)
    P.ins("gpsimd", lambda e: e.collective_compute(
        "AllReduce", ALU.max, replica_groups=[list(range(NC))],
        ins=[cc_in2], outs=[cc_out2]), sCC)
    P.wait("gpsimd", sCC)
    P.ins("gpsimd", lambda e: e.dma_start(out=ROW[0:1, 8:16], in_=cc_out2), dCC, 16)
    P.wait("vector", dCC)
    P.ins("vector", lambda e: e.tensor_scalar(ROW[0:1, 24:25], ROW[0:1, 8:9], 1.0 / 448.0, 1e-6,
                                              op0=ALU.mult, op1=ALU.max))
    P.ins("vector", lambda e: e.tensor_scalar(ROW[0:1, 25:26], ROW[0:1, 24:25], 2.0, 0.0,
                                              op0=ALU.mult, op1=ALU.add))
    P.ins("vector", lambda e: e.reciprocal(ROW[0:1, 25:26], ROW[0:1, 25:26]))
    P.ins("vector", lambda e: e.tensor_scalar(ROW[0:1, 26:27], ROW[0:1, 24:25], ROW[0:1, 21:22], 4.0,
                                              op0=ALU.mult, op1=ALU.mult))
    P.ins("vector", lambda e: e.tensor_scalar(ROW[0:1, 27:28], ROW[0:1, 26:27], ROW[0:1, 26:27], 0.0,
                                              op0=ALU.mult, op1=ALU.add))
    P.ins("vector", lambda e: e.tensor_scalar(ccs[0:1, 0:1], ROW[0:1, 25:26], 1.0, 0.0, op0=ALU.mult, op1=ALU.add))
    P.ins("vector", lambda e: e.tensor_scalar(ccs[0:1, 1:2], ROW[0:1, 26:27], 1.0, 0.0, op0=ALU.mult, op1=ALU.add))
    P.ins("vector", lambda e: e.tensor_scalar(ccs[0:1, 2:3], ROW[0:1, 27:28], 1.0, 0.0, op0=ALU.mult, op1=ALU.add))
    with nc.psum_tensor("pscl2", [128, 8], F32) as pscl2:
        P.wait("tensor", sV)
        P.ins("tensor", lambda e: e.matmul(pscl2[:], onesr[0:1, :], ccs[0:1, :], start=True, stop=True), sT)
        P.wait("scalar", sT)
        P.ins("scalar", lambda e: e.activation(out=SCL2[:], in_=pscl2[:], func=AF.Copy, scale=1.0))

    full_barrier()
    ph_cm.close()
    hf32 = [sb(f"hf32_{i}", [128, J // 2], BF16, ph_c) for i in range(2)]
    h8s = [sb(f"h8s{i}", [128, J], F8, ph_c) for i in range(2)]
    lnff = sb("lnff0", [128, D], F32, ph_c)
    xm5 = sb("xm5_0", [128, D], F32, ph_c)

    with nc.psum_tensor("pq0", [128, D], F32) as pq0, \
         nc.psum_tensor("pq1", [128, D], F32) as pq1:
        poD = [pq0, pq1]
        for s4 in dHST:
            P.wait("sync", s4)
        P.wait("sync", dXMS)
        tok_hq = [0, 0]
        tok_hq_done = [0, 0]
        tok_add = 0
        tok_out = 0
        tok_mmD = [0, 0]
        tok_lnff = [0, 0]
        tok_stats = [0, 0]
        for tt in range(NT):
            par = tt % 2
            for half in range(2):
                P.wait("gpsimd", sA, tok_hq_done[half])
                P.ins("gpsimd", lambda e, t=tt, hf=half: e.dma_start(
                    out=hf32[hf][:], in_=bass.AP(hT_h, (t * 128) * J + hf * (J // 2),
                                                 [[J, 128], [1, J // 2]])), dHSx[half], 16)
            for half in range(2):
                P.wait("scalar", dHSx[half])
                P.wait("scalar", sT, tok_mmD[par])
                P.ins("scalar", lambda e, p=par, hf=half: e.activation(
                    out=h8s[p][:, hf * (J // 2):(hf + 1) * (J // 2)], in_=hf32[hf][:], func=AF.Copy,
                    scale=SCL2[:, 0:1]), selfwait=False)
                tok_hq_done[half] = sA.n
            tok_hq[par] = sA.n
            P.wait("sync", sV, tok_add)
            P.ins("sync", lambda e, t=tt: e.dma_start(
                out=xm5[:], in_=xmid_d[t * 128:(t + 1) * 128, :]), dXM5[0], 16)
            P.wait("tensor", sA, max(tok_hq[par], tok_lnff[par]))
            P.wait("tensor", sV, tok_stats[par])
            for ns in range(4):
                for k2 in range(NJ // 2):
                    P.ins("tensor", lambda e, k=k2, n=ns, p=par: e.matmul(
                        poD[p][:, n * 512:(n + 1) * 512],
                        bass.AP(h8s[p], k * 256, [[J, 128], [128, 2], [1, 128]]),
                        bass.AP(wpr8, (2 * k) * D + n * 512, [[NJ * D, 128], [D, 2], [1, 512]]),
                        start=(k == 0), stop=(k == NJ // 2 - 1),
                        perf_mode=PM.DoubleRow), sT)
            tok_mmD[par] = sT.n
            P.wait("vector", sT, tok_mmD[par])
            P.wait("scalar", sT, tok_mmD[par])
            ln_stats(poD[par][:], 0, 0, scale_c=SCL2[:, 1:2], scale_c2=SCL2[:, 2:3])
            tok_stats[par] = sV.n + 1
            P.ins("vector", lambda e: e.tensor_scalar(stv(6), stv(3), SCL2[:, 1:2], 0.0,
                                                      op0=ALU.mult, op1=ALU.add))
            P.wait("scalar", sV)
            P.wait("scalar", dOUT[0])
            P.ins("scalar", lambda e, p=par: e.activation(out=lnff[:], in_=poD[p][:], func=AF.Identity,
                                                          scale=stv(6), bias=stv(4)), selfwait=False)
            tok_lnff[par] = sA.n
            P.wait("vector", sA)
            P.wait("vector", dXM5[0])
            P.ins("vector", lambda e: e.tensor_tensor(lnff[:], xm5[:], lnff[:], op=ALU.add))
            tok_add = sV.n
            P.wait("sync", sV, tok_add)
            P.ins("sync", lambda e, t=tt: e.dma_start(
                out=out_d[t * 128:(t + 1) * 128, :], in_=lnff[:]), dOUT[0], 16)

    P.wait("sync", dOUT[0])
    P.wait("sync", dOUT[1])
    P.wait("gpsimd", dOUT[0])
    P.wait("gpsimd", dOUT[1])

    with nc.Block() as block:
        P.build(block)
    ph_c.close()
    es.close()
    return nc


_CACHED = {}


def kernel(L0_attn_x, L0_attn_Wq, L0_attn_Wk, L0_attn_Wv, L0_attn_Wo,
           L0_attn_qn_w, L0_attn_kn_w, L0_ffn_w_fc, L0_ffn_w_proj):
    x = np.ascontiguousarray(np.asarray(L0_attn_x, dtype=np.float32))
    f32 = np.float32

    def t(a):
        return np.ascontiguousarray(np.asarray(a, dtype=np.float32).T)

    wqT, wkT, wvT = t(L0_attn_Wq), t(L0_attn_Wk), t(L0_attn_Wv)
    woT, wfcT, wprT = t(L0_attn_Wo), t(L0_ffn_w_fc), t(L0_ffn_w_proj)
    qn = np.ascontiguousarray(np.asarray(L0_attn_qn_w, f32).reshape(1, DH))
    kn = np.ascontiguousarray(np.asarray(L0_attn_kn_w, f32).reshape(1, DH))

    inv_freq = (f32(1.0) / f32(10000.0) ** (np.arange(0, DH, 2).astype(f32) / f32(DH))).astype(f32)
    pos = np.arange(T, dtype=f32)
    emb = np.concatenate([pos[:, None] * inv_freq[None, :]] * 2, axis=1).astype(f32)
    cos_t = np.cos(emb).astype(f32)
    sin_t = np.sin(emb).astype(f32)
    sgn = np.where(np.arange(DH) < 64, f32(-1.0), f32(1.0))
    qs = f32(DH ** -0.5)
    tqc = np.ascontiguousarray(cos_t * qs)
    tqs = np.ascontiguousarray(sin_t * sgn * qs)
    tkc = np.ascontiguousarray(cos_t)
    tks = np.ascontiguousarray(sin_t * sgn)

    jj_ = np.arange(128)[:, None]
    ii_ = np.arange(256)[None, :]
    m_list = []
    for c_ in (-128, 0, 384, 512):
        d_ = c_ + ii_ - jj_
        m_list.append(((d_ >= 0) & (d_ <= 512)).astype(ml_dtypes.bfloat16))
    masks4 = np.ascontiguousarray(np.concatenate(m_list, axis=0))
    ones_bfm = np.ones((128, 128), dtype=ml_dtypes.bfloat16)
    id_bfm = np.eye(128, dtype=ml_dtypes.bfloat16)
    id_f32m = np.eye(128, dtype=f32)
    onesrm = np.ones((1, 128), dtype=f32)
    consts = np.zeros((128, 8), f32)
    consts[:, 0] = 1e-5
    consts[:, 1] = 1e-6
    zrow = np.zeros((1, 8), f32)

    if "nc" not in _CACHED:
        _CACHED["nc"] = build_program()
    nc = _CACHED["nc"]

    shared = dict(wqT=wqT, wkT=wkT, wvT=wvT, woT=woT, wfcT=wfcT, wprT=wprT,
                  qn=qn, kn=kn, tqc=tqc, tqs=tqs, tkc=tkc, tks=tks,
                  masks4=masks4, ones_bf=ones_bfm, id_bf=id_bfm,
                  id_f32=id_f32m, onesr=onesrm, consts=consts, zrow=zrow)
    in_maps = [dict(shared, x=np.ascontiguousarray(x[b]),
                    wslf=np.ascontiguousarray(wfcT[b * (D // NC):(b + 1) * (D // NC), :]),
                    wslp=np.ascontiguousarray(wprT[b * (J // NC):(b + 1) * (J // NC), :]))
               for b in range(NC)]
    res = run_bass_kernel_spmd(nc, in_maps, core_ids=list(range(NC)))
    return np.stack([res.results[b]["out"] for b in range(NC)], axis=0)


# version 18
# speedup vs baseline: 1.1016x; 1.0016x over previous
import numpy as np
import ml_dtypes
from contextlib import ExitStack

import concourse.bass as bass
import concourse.mybir as mybir
from concourse.bass_utils import run_bass_kernel_spmd

F32 = mybir.dt.float32
BF16 = mybir.dt.bfloat16
F8 = mybir.dt.float8e4
AF = mybir.ActivationFunctionType
ALU = mybir.AluOpType
AXX = mybir.AxisListType.X
PM = mybir.MatmulPerfMode

T, D, H, HKV, DH, G, WIN, J = 2048, 2048, 16, 4, 128, 4, 512, 8192
NT, ND, NJ, NC = 16, 16, 64, 8

ENGINES = ("sync", "scalar", "vector", "tensor", "gpsimd")


class Sem:
    __slots__ = ("h", "n")

    def __init__(self, h):
        self.h = h
        self.n = 0


class Sched:
    def __init__(self):
        self.ops = []
        self.own_sem = {}

    def emit(self, eng, fn):
        self.ops.append((eng, fn))

    def wait(self, eng, sem, n=None):
        tgt = sem.n if n is None else n
        if tgt <= 0:
            return
        self.emit(eng, lambda e, s=sem.h, t=tgt: e.wait_ge(s, t))

    def ins(self, eng, f, sem=None, inc=1, selfwait=True):
        own = self.own_sem.get(eng)
        if own is not None:
            if selfwait and own.n > 0:
                self.wait(eng, own)
            if sem is None:
                sem, inc = own, 1
        if sem is not None:
            sem.n += inc
        def fn(e, _f=f, _s=sem, _i=inc):
            r = _f(e)
            if _s is not None:
                r.then_inc(_s.h, _i)
        self.emit(eng, fn)

    def build(self, block):
        for ename in ENGINES:
            sec = getattr(block, ename)
            mine = [fn for (e, fn) in self.ops if e == ename]
            def body(eng, _my=mine):
                for fn in _my:
                    fn(eng)
            sec(body)


def build_program():
    nc = bass.Bass("TRN2", target_bir_lowering=False)

    dh = {}

    def din(name, shape, dt=F32):
        h = nc.dram_tensor(name, shape, dt, kind="ExternalInput")
        dh[name] = h
        return h.ap()

    x_d = din("x", [T, D])
    wqT_d = din("wqT", [D, H * DH])
    wkT_d = din("wkT", [D, HKV * DH])
    wvT_d = din("wvT", [D, HKV * DH])
    woT_d = din("woT", [H * DH, D])
    wfcT_d = din("wfcT", [D, J])
    wprT_d = din("wprT", [J, D])
    qn_d = din("qn", [1, DH])
    kn_d = din("kn", [1, DH])
    tqc_d = din("tqc", [T, DH])
    tqs_d = din("tqs", [T, DH])
    tkc_d = din("tkc", [T, DH])
    tks_d = din("tks", [T, DH])
    masks4_d = din("masks4", [4 * 128, 256], BF16)
    ones_bf_d = din("ones_bf", [128, 128], BF16)
    id_bf_d = din("id_bf", [128, 128], BF16)
    id_f32_d = din("id_f32", [128, 128], F32)
    onesr_d = din("onesr", [1, 128], F32)
    consts_d = din("consts", [128, 8], F32)
    zrow_d = din("zrow", [1, 8], F32)
    wslf_d = din("wslf", [D // NC, J])
    wslp_d = din("wslp", [J // NC, D])

    out_h = nc.dram_tensor("out", [T, D], F32, kind="ExternalOutput")
    out_d = out_h.ap()

    xmid_h = nc.dram_tensor("xmid_s", [T, D], F32)
    x2_h = nc.dram_tensor("x2_s", [T, D], F32)
    hT_h = nc.dram_tensor("hT_s", [NT * 128, J], BF16)
    ctx_h = nc.dram_tensor("ctx_s", [H * 128, T], BF16)
    xmid_d, x2_d, hT_d, ctx_d = xmid_h.ap(), x2_h.ap(), hT_h.ap(), ctx_h.ap()
    cc_in1 = nc.dram_tensor("cc_in1", [1, 8], F32).ap()
    cc_out1 = nc.dram_tensor("cc_out1", [1, 8], F32).ap()
    cc_in2 = nc.dram_tensor("cc_in2", [1, 8], F32).ap()
    cc_out2 = nc.dram_tensor("cc_out2", [1, 8], F32).ap()

    P = Sched()
    es = ExitStack()

    ALLSEMS = []

    def sem(name):
        s = Sem(es.enter_context(nc.semaphore(name)))
        ALLSEMS.append(s)
        return s

    def full_barrier():
        for _eng in ENGINES:
            for _s in ALLSEMS:
                P.wait(_eng, _s)

    sT = sem("s_pe")
    sA = sem("s_act")
    sV = sem("s_dve")
    sGc = sem("s_gpc")
    sCC = sem("s_cc")
    P.own_sem = {"vector": sV, "scalar": sA}

    dIN = sem("d_in")
    dX = [sem("d_x0"), sem("d_x1")]
    dTAB = sem("d_tab")
    dCTXL = [sem("d_ctxl0"), sem("d_ctxl1")]
    dXB5 = [sem("d_xb50"), sem("d_xb51")]
    dX2L = [sem("d_x2l0"), sem("d_x2l1")]
    dWFC = sem("d_wfc")
    dHSx = [sem("d_hs0"), sem("d_hs1")]
    dXM5 = [sem("d_xm50"), sem("d_xm51")]
    dOUT = [sem("d_out0"), sem("d_out1")]
    dWMX = [sem("d_wmx0")]
    dWKV = sem("d_wkv")
    dWQ = sem("d_wq")
    dCTXS = [sem("d_ctxs0"), sem("d_ctxs1")]
    dWO = [sem(f"d_wo{i}") for i in range(4)]
    dXMS = sem("d_xms")
    dX2S = sem("d_x2s")
    dWPR = sem("d_wpr")
    dHST = [sem(f"d_hst{i}") for i in range(4)]
    dCC = sem("d_cc")

    def sb(name, shape, dt, stk=es):
        return stk.enter_context(nc.sbuf_tensor(name, shape, dt))

    cons = sb("cons", [128, 8], F32)
    zrow_s = sb("zrow_s", [1, 8], F32)
    onesr = sb("onesr_s", [1, 128], F32)
    id_bf = sb("id_bf_s", [128, 128], BF16)
    id_f32 = sb("id_f32_s", [128, 128], F32)
    ones_bf = sb("ones_bf_s", [128, 128], BF16)
    masks4 = sb("masks4_s", [128, 4 * 256], BF16)
    qn_s = sb("qn_s", [1, DH], F32)
    kn_s = sb("kn_s", [1, DH], F32)
    wrow = sb("wrow", [1, 2 * DH], F32)
    wqB = sb("wqB", [128, DH], F32)
    wkB = sb("wkB", [128, DH], F32)
    STV = sb("STV", [128, 8 * 24], F32)
    STA = sb("STA", [128, 8 * 16], F32)
    RMX = sb("RMX", [128, 8 * 4], F32)
    SCL = sb("SCL", [128, 8], F32)
    SCL2 = sb("SCL2", [128, 8], F32)
    ROW = sb("ROWs", [1, 32], F32)
    ccs = sb("ccs_s", [1, 8], F32)
    tcol = sb("tcol", [128, 2], F32)
    trash = sb("trash", [128, 2048], BF16)

    def stv(i):
        return STV[:, 8 * i:8 * i + 1]

    def sta(i):
        return STA[:, 8 * i:8 * i + 1]

    def rmx(i):
        return RMX[:, 8 * i:8 * i + 1]

    for dst, src in [(cons, consts_d), (zrow_s, zrow_d), (onesr, onesr_d),
                     (id_bf, id_bf_d), (id_f32, id_f32_d), (ones_bf, ones_bf_d),
                     (qn_s, qn_d), (kn_s, kn_d)]:
        P.ins("sync", lambda e, d=dst, s=src: e.dma_start(out=d[:], in_=s), dIN, 16)
    P.ins("sync", lambda e: e.dma_start(
        out=bass.AP(masks4, 0, [[4 * 256, 128], [256, 4], [1, 256]]),
        in_=bass.AP(dh["masks4"], 0, [[256, 128], [128 * 256, 4], [1, 256]])), dIN, 16)
    P.ins("gpsimd", lambda e: e.memset(RMX[:], 0), sGc, 1)
    P.wait("vector", dIN)
    P.wait("scalar", dIN)
    P.wait("tensor", dIN)
    P.ins("vector", lambda e: e.tensor_scalar(wrow[0:1, 0:DH], qn_s[0:1, :], 1.0, 1.0,
                                              op0=ALU.mult, op1=ALU.add))
    P.ins("vector", lambda e: e.tensor_scalar(wrow[0:1, DH:2 * DH], kn_s[0:1, :], 1.0, 1.0,
                                              op0=ALU.mult, op1=ALU.add))

    with nc.psum_tensor("pinit", [128, 2 * DH], F32) as pinit:
        P.wait("tensor", sV)
        P.ins("tensor", lambda e: e.matmul(pinit[:, 0:DH], onesr[0:1, :], wrow[0:1, 0:DH],
                                           start=True, stop=True), sT)
        P.ins("tensor", lambda e: e.matmul(pinit[:, DH:2 * DH], onesr[0:1, :], wrow[0:1, DH:2 * DH],
                                           start=True, stop=True), sT)
        P.wait("scalar", sT)
        P.ins("scalar", lambda e: e.activation(out=wqB[:], in_=pinit[:, 0:DH], func=AF.Copy, scale=1.0))
        P.ins("scalar", lambda e: e.activation(out=wkB[:], in_=pinit[:, DH:2 * DH], func=AF.Copy, scale=1.0))

    ph_a = ExitStack()
    xnT = sb("xnT", [128, ND * T], BF16, ph_a)
    kT = sb("kT", [128, HKV * T], BF16, ph_a)
    v_sb = sb("v_sb", [128, HKV * T], BF16, ph_a)
    qTg = sb("qTg", [128, G * T], BF16, ph_a)
    tabc = sb("tabc", [128, NT * DH], F32, ph_a)
    tabs = sb("tabs", [128, NT * DH], F32, ph_a)
    xbuf = [sb(f"xbuf{i}", [128, D], F32, ph_a) for i in range(2)]
    xn_tok = sb("xn_tok", [128, D], BF16, ph_a)
    xn_tok2 = [xn_tok, xn_tok]
    rop = [sb(f"rop{i}", [128, 3 * 128], F32, ph_a) for i in range(2)]
    qrot_bf = [sb(f"qrot{i}", [128, 128], BF16, ph_a) for i in range(2)]
    wkv_sb = sb("wkv_sb", [128, ND * 256], BF16, ph_a)
    wq_sb = sb("wq_sb", [128, ND * 512], BF16, ph_a)
    ctxs = [sb(f"ctxs{i}", [128, T], BF16, ph_a) for i in range(2)]
    expb = [sb(f"expb{i}", [128, 1536], BF16, ph_a) for i in range(2)]
    rzb = [sb(f"rzb{i}", [128, 256], F32, ph_a) for i in range(2)]
    wmx = [sb("wmx0", [128, 1024], F32, ph_a)]

    def xnT_t(dc, tt):
        return xnT[:, dc * T + tt * 128: dc * T + (tt + 1) * 128]

    def kT_t(hk, jt):
        return kT[:, hk * T + jt * 128: hk * T + (jt + 1) * 128]

    def v_t(hk, jt):
        return v_sb[:, hk * T + jt * 128: hk * T + (jt + 1) * 128]

    def qT_t(g, it):
        return qTg[:, g * T + it * 128: g * T + (it + 1) * 128]

    def tab_t(tab, tt):
        return tab[:, tt * 128:(tt + 1) * 128]

    def ln_stats(src_ap, sl, asl, scale_c=None, scale_c2=None):
        P.wait("scalar", sV)
        P.ins("vector", lambda e: e.reduce_sum(stv(sl), src_ap, axis=AXX))
        P.wait("scalar", sV)
        P.ins("scalar", lambda e: e.activation(out=trash[:], in_=src_ap, func=AF.Square,
                                               accum_out=sta(asl)))
        P.wait("vector", sA)
        if scale_c is None:
            P.ins("vector", lambda e: e.tensor_scalar(stv(sl), stv(sl), 1.0 / D, 0.0,
                                                      op0=ALU.mult, op1=ALU.add))
            P.ins("vector", lambda e: e.tensor_scalar(stv(sl + 1), sta(asl), 1.0 / D, 0.0,
                                                      op0=ALU.mult, op1=ALU.add))
        else:
            P.ins("vector", lambda e: e.tensor_scalar(stv(sl), stv(sl), scale_c, 1.0 / D,
                                                      op0=ALU.mult, op1=ALU.mult))
            P.ins("vector", lambda e: e.tensor_scalar(stv(sl + 1), sta(asl), scale_c2, 1.0 / D,
                                                      op0=ALU.mult, op1=ALU.mult))
        P.ins("vector", lambda e: e.tensor_scalar(stv(sl + 2), stv(sl), stv(sl), 0.0,
                                                  op0=ALU.mult, op1=ALU.add))
        P.ins("vector", lambda e: e.tensor_scalar(stv(sl + 2), stv(sl + 1), stv(sl + 2), 0.0,
                                                  op0=ALU.subtract, op1=ALU.add))
        P.wait("scalar", sV)
        P.ins("scalar", lambda e: e.activation(out=sta(asl + 1), in_=stv(sl + 2), func=AF.Sqrt,
                                               scale=1.0, bias=cons[:, 0:1]))
        P.wait("vector", sA)
        P.ins("vector", lambda e: e.reciprocal(stv(sl + 3), sta(asl + 1)))
        P.ins("vector", lambda e: e.tensor_scalar(stv(sl + 4), stv(sl), stv(sl + 3), -1.0,
                                                  op0=ALU.mult, op1=ALU.mult))

    full_barrier()
    P.wait("vector", sGc)
    wmax_tiles = []
    for (w_ap_, rows_, cols_, slot_) in [(wslf_d, D // NC, J, 2), (wslp_d, J // NC, D, 3)]:
        for r_ in range(rows_ // 128):
            for c_ in range(cols_ // 1024):
                wmax_tiles.append((w_ap_, r_, c_, slot_))
    wmax_state = {"i": 0, "tok": [0, 0]}

    def wmax_step(nstep=1):
        for _ in range(nstep):
            i = wmax_state["i"]
            if i >= len(wmax_tiles):
                return
            wmax_state["i"] += 1
            w_ap_, r_, c_, slot_ = wmax_tiles[i]
            wp = 0
            P.wait("gpsimd", sV, wmax_state["tok"][wp])
            P.ins("gpsimd", lambda e, w=w_ap_, rr=r_, cc=c_, p=wp: e.dma_start(
                out=wmx[p][:], in_=w[rr * 128:(rr + 1) * 128, cc * 1024:(cc + 1) * 1024]), dWMX[wp], 16)
            P.wait("vector", dWMX[wp])
            P.ins("vector", lambda e, p=wp: e.reduce_max(stv(15), wmx[p][:], axis=AXX,
                                                         apply_absolute_value=True))
            P.ins("vector", lambda e, s=slot_: e.tensor_scalar(rmx(s), rmx(s), stv(15), 0.0,
                                                               op0=ALU.max, op1=ALU.add))
            wmax_state["tok"][wp] = sV.n

    with nc.psum_tensor("ptr0", [128, 128], BF16) as ptr0, \
         nc.psum_tensor("ptr1", [128, 128], BF16) as ptr1:
        ptr = [ptr0, ptr1]
        tok_xn = [0, 0]
        tok_nrm = 0
        tok_ptr = [0, 0]
        for tt in range(NT):
            par = tt % 2
            wmax_step(2)
            P.wait("sync", sA)
            P.wait("sync", sV)
            P.ins("sync", lambda e, t=tt, p=par: e.dma_start(
                out=xbuf[p][:], in_=x_d[t * 128:(t + 1) * 128, :]), dX[par], 16)
            P.wait("vector", dX[par])
            P.wait("scalar", dX[par])
            ln_stats(xbuf[par][:], 0, 0)
            P.wait("scalar", sV)
            P.wait("scalar", sT, max(tok_xn))
            P.ins("scalar", lambda e, p=par: e.activation(out=xn_tok2[p][:], in_=xbuf[p][:],
                                                          func=AF.Identity,
                                                          scale=stv(3), bias=stv(4)))
            tok_nrm = sA.n
            for dc in range(ND):
                tp = dc % 2
                P.wait("tensor", sA, max(tok_nrm, tok_ptr[tp]))
                P.ins("tensor", lambda e, d=dc, q=tp, p=par: e.transpose(
                    ptr[q][:], xn_tok2[p][:, d * 128:(d + 1) * 128], id_bf[:]), sT)
                P.wait("scalar", sT)
                P.ins("scalar", lambda e, d=dc, t=tt, q=tp: e.activation(
                    out=xnT_t(d, t), in_=ptr[q][:], func=AF.Copy, scale=1.0))
                tok_ptr[tp] = sA.n
            tok_xn[par] = sT.n

    P.ins("sync", lambda e: e.dma_start(
        out=bass.AP(tabc, 0, [[NT * DH, 128], [DH, NT], [1, DH]]),
        in_=bass.AP(dh["tkc"], 0, [[DH, 128], [128 * DH, NT], [1, DH]])), dTAB, 16)
    P.ins("sync", lambda e: e.dma_start(
        out=bass.AP(tabs, 0, [[NT * DH, 128], [DH, NT], [1, DH]]),
        in_=bass.AP(dh["tks"], 0, [[DH, 128], [128 * DH, NT], [1, DH]])), dTAB, 16)
    P.wait("vector", dTAB)

    with nc.psum_tensor("pkv0", [128, 256], F32) as pkv0, \
         nc.psum_tensor("pkv1", [128, 256], F32) as pkv1, \
         nc.psum_tensor("ptk0", [128, 128], BF16) as ptk0, \
         nc.psum_tensor("ptk1", [128, 128], BF16) as ptk1:
        pkv = [pkv0, pkv1]
        ptk = [ptk0, ptk1]
        tokA = [sA.n, sA.n]
        tokV = [sV.n, sV.n]
        tokTr = [0, 0]
        tokEvK = [0, 0]
        for hk in range(HKV):
            P.wait("gpsimd", sT)
            P.ins("gpsimd", lambda e, h=hk: e.dma_start(
                out=bass.AP(wkv_sb, 0, [[ND * 256, 128], [256, ND], [1, 128]]),
                in_=bass.AP(dh["wkT"], h * 128, [[HKV * DH, 128], [128 * HKV * DH, ND], [1, 128]])), dWKV, 16)
            P.ins("gpsimd", lambda e, h=hk: e.dma_start(
                out=bass.AP(wkv_sb, 128, [[ND * 256, 128], [256, ND], [1, 128]]),
                in_=bass.AP(dh["wvT"], h * 128, [[HKV * DH, 128], [128 * HKV * DH, ND], [1, 128]])), dWKV, 16)
            for tt in range(NT):
                par = tt % 2
                wmax_step(2)
                P.wait("tensor", dWKV)
                P.wait("tensor", sA, tokA[par])
                P.wait("tensor", sV, tokV[par])
                for dc in range(ND):
                    P.ins("tensor", lambda e, d=dc, t=tt, p=par: e.matmul(
                        pkv[p][:], xnT_t(d, t), wkv_sb[:, d * 256:(d + 1) * 256],
                        start=(d == 0), stop=(d == ND - 1)), sT)
                tok_mm = sT.n
                P.wait("scalar", sT, tok_mm)
                P.wait("scalar", sV, tokV[par])
                P.ins("scalar", lambda e, p=par: e.activation(out=trash[:, 0:128], in_=pkv[p][:, 0:128],
                                                              func=AF.Square, accum_out=sta(8)), selfwait=False)
                P.ins("scalar", lambda e: e.activation(out=sta(12), in_=sta(8), func=AF.Sqrt,
                                                       scale=1.0 / DH, bias=cons[:, 1:2]))
                tok_std = sA.n
                P.wait("vector", sA, tok_std)
                P.ins("vector", lambda e: e.reciprocal(stv(16), sta(12)))
                P.ins("vector", lambda e, p=par: e.scalar_tensor_tensor(rop[p][:, 0:128], pkv[p][:, 0:128],
                                                                        stv(16), wkB[:],
                                                                        op0=ALU.mult, op1=ALU.mult))
                tokV[par] = sV.n
                P.wait("scalar", sV, tokV[par])
                P.ins("scalar", lambda e, h=hk, t=tt, p=par: e.activation(
                    out=v_t(h, t), in_=pkv[p][:, 128:256], func=AF.Copy, scale=1.0), selfwait=False)
                tokA[par] = sA.n
                P.wait("vector", sT, tokTr[par])
                P.ins("vector", lambda e, p=par, t=tt: e.tensor_tensor(
                    rop[p][:, 128:256], rop[p][:, 0:128], tab_t(tabc, t), op=ALU.mult))
                P.ins("vector", lambda e, p=par, t=tt: e.tensor_tensor(
                    rop[p][:, 256:320], rop[p][:, 64:128], tab_t(tabs, t)[:, 0:64], op=ALU.mult))
                P.ins("vector", lambda e, p=par, t=tt: e.tensor_tensor(
                    rop[p][:, 320:384], rop[p][:, 0:64], tab_t(tabs, t)[:, 64:128], op=ALU.mult))
                P.ins("vector", lambda e, p=par: e.tensor_tensor(
                    qrot_bf[p][:], rop[p][:, 128:256], rop[p][:, 256:384], op=ALU.add))
                tok_rot = sV.n
                P.wait("tensor", sV, tok_rot)
                P.wait("tensor", sA, tokEvK[par])
                P.ins("tensor", lambda e, p=par: e.transpose(ptk[p][:], qrot_bf[p][:], id_bf[:]), sT)
                tokTr[par] = sT.n
                P.wait("scalar", sT, tokTr[par])
                P.ins("scalar", lambda e, h=hk, t=tt, p=par: e.activation(
                    out=kT_t(h, t), in_=ptk[p][:], func=AF.Copy, scale=1.0), selfwait=False)
                tokEvK[par] = sA.n

    P.wait("sync", sV)
    P.ins("sync", lambda e: e.dma_start(
        out=bass.AP(tabc, 0, [[NT * DH, 128], [DH, NT], [1, DH]]),
        in_=bass.AP(dh["tqc"], 0, [[DH, 128], [128 * DH, NT], [1, DH]])), dTAB, 16)
    P.ins("sync", lambda e: e.dma_start(
        out=bass.AP(tabs, 0, [[NT * DH, 128], [DH, NT], [1, DH]]),
        in_=bass.AP(dh["tqs"], 0, [[DH, 128], [128 * DH, NT], [1, DH]])), dTAB, 16)
    P.wait("vector", dTAB)

    tokQa = [0, 0]
    tokQv = [0, 0]
    tokTrQ = [0, 0]
    tokEvQ = [0, 0]
    tE = [0, 0]
    tSc = [0, 0]
    tM = [0, 0]
    tZp = [0, 0]
    tCp = [0, 0]
    tRZp = [0, 0]
    tNp = [0, 0]
    for hk in range(HKV):
        P.wait("gpsimd", sT)
        P.ins("gpsimd", lambda e, h=hk: e.dma_start(
            out=bass.AP(wq_sb, 0, [[ND * 512, 128], [512, ND], [1, 512]]),
            in_=bass.AP(dh["wqT"], h * 512, [[H * DH, 128], [128 * H * DH, ND], [1, 512]])), dWQ, 16)
        with nc.psum_tensor(f"pq0_{hk}", [128, 512], F32) as pq0, \
             nc.psum_tensor(f"pq1_{hk}", [128, 512], F32) as pq1, \
             nc.psum_tensor(f"ptq0_{hk}", [128, 128], BF16) as ptq0, \
             nc.psum_tensor(f"ptq1_{hk}", [128, 128], BF16) as ptq1:
            pq = [pq0, pq1]
            ptq = [ptq0, ptq1]
            P.wait("tensor", sA)
            P.wait("tensor", sV)
            for tt in range(NT):
                par = tt % 2
                wmax_step(2)
                P.wait("tensor", dWQ)
                P.wait("tensor", sA, tokQa[par])
                P.wait("tensor", sV, tokQv[par])
                for dc in range(ND):
                    P.ins("tensor", lambda e, d=dc, t=tt, p=par: e.matmul(
                        pq[p][:], xnT_t(d, t), wq_sb[:, d * 512:(d + 1) * 512],
                        start=(d == 0), stop=(d == ND - 1)), sT)
                tok_mm = sT.n
                P.wait("scalar", sT, tok_mm)
                P.wait("scalar", sV, tokQv[par])
                for g in range(G):
                    P.ins("scalar", lambda e, gg=g, p=par: e.activation(
                        out=trash[:, gg * 128:(gg + 1) * 128], in_=pq[p][:, gg * 128:(gg + 1) * 128],
                        func=AF.Square, accum_out=sta(8 + gg)), selfwait=False)
                P.wait("scalar", sA)
                for g in range(G):
                    P.ins("scalar", lambda e, gg=g: e.activation(out=sta(12 + gg), in_=sta(8 + gg),
                                                                 func=AF.Sqrt, scale=1.0 / DH,
                                                                 bias=cons[:, 1:2]), selfwait=False)
                tok_std = sA.n
                P.wait("vector", sA, tok_std)
                for g in range(G):
                    P.ins("vector", lambda e, gg=g: e.reciprocal(stv(16 + gg), sta(12 + gg)), selfwait=False)
                for g in range(G):
                    gp = g % 2
                    P.wait("vector", sT, tokTrQ[gp])
                    P.ins("vector", lambda e, gg=g, p=par: e.scalar_tensor_tensor(
                        rop[0][:, 0:128], pq[p][:, gg * 128:(gg + 1) * 128], stv(16 + gg), wqB[:],
                        op0=ALU.mult, op1=ALU.mult))
                    P.ins("vector", lambda e, t=tt: e.tensor_tensor(
                        rop[0][:, 128:256], rop[0][:, 0:128], tab_t(tabc, t), op=ALU.mult))
                    P.ins("vector", lambda e, t=tt: e.tensor_tensor(
                        rop[0][:, 256:320], rop[0][:, 64:128], tab_t(tabs, t)[:, 0:64], op=ALU.mult))
                    P.ins("vector", lambda e, t=tt: e.tensor_tensor(
                        rop[0][:, 320:384], rop[0][:, 0:64], tab_t(tabs, t)[:, 64:128], op=ALU.mult))
                    P.ins("vector", lambda e, p=gp: e.tensor_tensor(
                        qrot_bf[p][:], rop[0][:, 128:256], rop[0][:, 256:384], op=ALU.add))
                    tok_rot = sV.n
                    P.wait("tensor", sV, tok_rot)
                    P.wait("tensor", sA, tokEvQ[gp])
                    P.ins("tensor", lambda e, p=gp: e.transpose(ptq[p][:], qrot_bf[p][:], id_bf[:]), sT)
                    tokTrQ[gp] = sT.n
                    P.wait("scalar", sT, tokTrQ[gp])
                    P.ins("scalar", lambda e, gg=g, t=tt, p=gp: e.activation(
                        out=qT_t(gg, t), in_=ptq[p][:], func=AF.Copy, scale=1.0), selfwait=False)
                    tokEvQ[gp] = sA.n
                tokQv[par] = sV.n
                tokQa[par] = sA.n
        with nc.psum_tensor(f"psS0_{hk}", [128, 1536], F32) as psS0, \
             nc.psum_tensor(f"psS1_{hk}", [128, 1536], F32) as psS1, \
             nc.psum_tensor(f"pzz_{hk}", [128, 256], F32) as pzz, \
             nc.psum_tensor(f"pcc_{hk}", [128, 256], F32) as pcc:
            psS = [psS0, psS1]
            base = max(tokQa[0], tokQa[1])
            tE = [base, base]
            tSc = [0, 0]
            tM = [0, 0]
            tZp = max(tRZp)
            tCp = 0
            tRZ1 = base
            tN1 = 0
            P.wait("tensor", sV)

            NR = NT // 2

            def jinfo(r):
                out = []
                for jt in range(max(0, 2 * r - 4), 2 * r + 2):
                    c = (2 * r - jt) * 128
                    slot = {-128: 0, 0: 1, 384: 2, 512: 3}.get(c)
                    out.append((jt, slot))
                return out

            def emit_scores(r, g):
                par = r % 2
                P.wait("tensor", sA, tE[par])
                for jj, (jt, _) in enumerate(jinfo(r)):
                    P.ins("tensor", lambda e, p=par, jx=jj, hh=hk, j=jt, rr=r, gg=g: e.matmul(
                        psS[p][:, jx * 256:(jx + 1) * 256], kT_t(hh, j),
                        qTg[:, gg * T + rr * 256:(gg * T + (rr + 1) * 256)],
                        start=True, stop=True), sT)
                tSc[par] = sT.n

            def emit_exp(r):
                par = r % 2
                nj = len(jinfo(r))
                P.wait("scalar", sT, tSc[par])
                P.ins("scalar", lambda e, p=par, n=nj: e.activation(
                    out=expb[p][:, 0:n * 256], in_=psS[p][:, 0:n * 256], func=AF.Exp), selfwait=False)
                tE[par] = sA.n

            def emit_masks(r):
                par = r % 2
                P.wait("vector", sA, tE[par])
                first = True
                for jj, (jt, slot) in enumerate(jinfo(r)):
                    if slot is None:
                        continue
                    P.ins("vector", lambda e, p=par, jx=jj, m=slot: e.tensor_tensor(
                        expb[p][:, jx * 256:(jx + 1) * 256], expb[p][:, jx * 256:(jx + 1) * 256],
                        masks4[:, m * 256:(m + 1) * 256], op=ALU.mult), selfwait=first)
                    first = False
                tM[par] = sV.n

            def emit_zctx(r):
                nonlocal_d = None
                par = r % 2
                ji = jinfo(r)
                nj = len(ji)
                P.wait("tensor", sV, tM[par])
                P.wait("tensor", sA, tRZ1)
                for jj in range(nj):
                    P.ins("tensor", lambda e, p=par, j=jj, n=nj: e.matmul(
                        pzz[:], ones_bf[:], expb[p][:, j * 256:(j + 1) * 256],
                        start=(j == 0), stop=(j == n - 1)), sT)
                tz = sT.n
                P.wait("tensor", sV, tN1)
                for jj, (jt, _) in enumerate(ji):
                    P.ins("tensor", lambda e, p=par, hh=hk, j=jt, jx=jj, n=nj: e.matmul(
                        pcc[:], v_t(hh, j), expb[p][:, jx * 256:(jx + 1) * 256],
                        start=(jx == 0), stop=(jx == n - 1)), sT)
                return tz, sT.n

            for g in range(G):
                h = hk * G + g
                cpar = h % 2
                P.wait("vector", dCTXS[cpar])
                emit_scores(0, g)
                emit_exp(0)
                emit_masks(0)
                for r in range(NR):
                    par = r % 2
                    if r + 1 < NR:
                        emit_scores(r + 1, g)
                        emit_exp(r + 1)
                        emit_masks(r + 1)
                    tz, tc = emit_zctx(r)
                    P.wait("scalar", sT, tz)
                    P.ins("scalar", lambda e, p=par: e.activation(out=rzb[p][:], in_=pzz[:],
                                                                  func=AF.Copy, scale=1.0), selfwait=False)
                    tRZ1 = sA.n
                    P.wait("vector", sA, tRZ1)
                    P.ins("vector", lambda e, p=par: e.reciprocal(rzb[p][:], rzb[p][:]), selfwait=False)
                    P.wait("vector", sT, tc)
                    P.ins("vector", lambda e, p=par, c=cpar, rr=r: e.tensor_tensor(
                        ctxs[c][:, rr * 256:(rr + 1) * 256], pcc[:], rzb[p][:], op=ALU.mult))
                    tN1 = sV.n
                P.wait("gpsimd", sV)
                P.ins("gpsimd", lambda e, hh=h, c=cpar: e.dma_start(
                    out=ctx_d[hh * 128:(hh + 1) * 128, :], in_=ctxs[c][:]), dCTXS[cpar], 16)
            tokQa = [max(tokQa[0], tE[0]), max(tokQa[1], tE[1])]

    wmax_step(len(wmax_tiles))

    full_barrier()
    ph_a.close()
    ph5 = ExitStack()
    wo_sb = sb("wo_sb", [128, ND * D], BF16, ph5)
    ctx_full = sb("ctx_full", [128, ND * T], BF16, ph5)
    xb5 = [sb(f"xb5_{i}", [128, D], F32, ph5) for i in range(2)]
    lnat = sb("lnat", [128, D], F32, ph5)
    xmid_sb = sb("xmid_sb", [128, D], F32, ph5)
    x2_sb = sb("x2_sb", [128, D], F32, ph5)

    wo_f = [sb(f"wo_f{i}", [128, D], F32, ph5) for i in range(2)]
    P.wait("sync", dCTXS[0])
    P.wait("sync", dCTXS[1])
    for hd in range(ND):
        P.ins("sync", lambda e, d=hd: e.dma_start(
            out=ctx_full[:, d * T:(d + 1) * T], in_=ctx_d[d * 128:(d + 1) * 128, :]), dCTXL[hd % 2], 16)
    tok_woc = [0] * ND
    tok_wof = [0, 0]
    for hd in range(ND):
        wp = hd % 2
        P.wait("sync", sA, tok_wof[wp])
        P.ins("sync", lambda e, d=hd, p=wp: e.dma_start(
            out=wo_f[p][:], in_=woT_d[d * 128:(d + 1) * 128, :]), dWO[hd % 4], 16)
        P.wait("scalar", dWO[hd % 4])
        P.ins("scalar", lambda e, d=hd, p=wp: e.activation(
            out=wo_sb[:, d * D:(d + 1) * D], in_=wo_f[p][:], func=AF.Copy, scale=1.0), selfwait=False)
        tok_woc[hd] = sA.n
        tok_wof[wp] = sA.n

    with nc.psum_tensor("po0", [128, D], F32) as po0, \
         nc.psum_tensor("po1", [128, D], F32) as po1:
        po = [po0, po1]
        tok_po_a = [0, 0]
        tok_po_v = [0, 0]
        for tt in range(NT):
            par = tt % 2
            P.wait("sync", sV)
            P.ins("sync", lambda e, t=tt, p=par: e.dma_start(
                out=xb5[p][:], in_=x_d[t * 128:(t + 1) * 128, :]), dXB5[par], 16)
            if tt == 0:
                P.wait("tensor", dCTXL[0])
                P.wait("tensor", dCTXL[1])
            P.wait("tensor", sA, tok_po_a[par])
            P.wait("tensor", sV, tok_po_v[par])
            for hd in range(ND):
                if tt == 0:
                    P.wait("tensor", sA, tok_woc[hd])
                for ns in range(4):
                    P.ins("tensor", lambda e, d=hd, n=ns, p=par, t=tt: e.matmul(
                        po[p][:, n * 512:(n + 1) * 512], ctx_full[:, d * T + t * 128: d * T + (t + 1) * 128],
                        wo_sb[:, d * D + n * 512: d * D + (n + 1) * 512],
                        start=(d == 0), stop=(d == ND - 1)), sT)
            P.wait("vector", sT)
            P.wait("scalar", sT)
            ln_stats(po[par][:], 0, 0)
            P.wait("scalar", sV)
            P.ins("scalar", lambda e, p=par: e.activation(out=lnat[:], in_=po[p][:], func=AF.Identity,
                                                          scale=stv(3), bias=stv(4)))
            tok_po_a[par] = sA.n
            tok_po_v[par] = sV.n
            P.wait("vector", sA)
            P.wait("vector", dXB5[par])
            P.wait("vector", dXMS)
            P.ins("vector", lambda e, p=par: e.tensor_tensor(xmid_sb[:], xb5[p][:], lnat[:], op=ALU.add))
            P.wait("gpsimd", sV)
            P.ins("gpsimd", lambda e, t=tt: e.dma_start(
                out=xmid_d[t * 128:(t + 1) * 128, :], in_=xmid_sb[:]), dXMS, 16)
            ln_stats(xmid_sb[:], 8, 4)
            P.wait("scalar", sV)
            P.wait("scalar", dX2S)
            P.ins("scalar", lambda e: e.activation(out=x2_sb[:], in_=xmid_sb[:], func=AF.Identity,
                                                   scale=stv(11), bias=stv(12)))
            P.wait("vector", sA)
            P.ins("vector", lambda e: e.reduce_max(stv(15), x2_sb[:], axis=AXX,
                                                   apply_absolute_value=True))
            P.ins("vector", lambda e: e.tensor_scalar(rmx(0), rmx(0), stv(15), 0.0,
                                                      op0=ALU.max, op1=ALU.add))
            P.wait("gpsimd", sA)
            P.ins("gpsimd", lambda e, t=tt: e.dma_start(
                out=x2_d[t * 128:(t + 1) * 128, :], in_=x2_sb[:]), dX2S, 16)
    full_barrier()
    ph5.close()

    def cross_part_max(col_ap, row_col):
        with nc.psum_tensor(f"pmx{row_col}", [128, 128], F32) as pmx:
            P.ins("vector", lambda e: e.tensor_scalar(tcol[:, 0:1], col_ap, 1.0, 0.0,
                                                      op0=ALU.mult, op1=ALU.add))
            P.wait("tensor", sV)
            P.ins("tensor", lambda e: e.transpose(pmx[0:1, 0:128], tcol[:, 0:1], id_f32[:]), sT)
            P.wait("vector", sT)
            P.ins("vector", lambda e, c=row_col: e.reduce_max(
                ROW[0:1, c:c + 1], pmx[0:1, 0:128], axis=AXX))

    cross_part_max(rmx(0), 0)
    cross_part_max(rmx(2), 1)
    cross_part_max(rmx(3), 2)

    P.ins("vector", lambda e: e.tensor_scalar(ccs[0:1, :], zrow_s[0:1, :], 0.0, 0.0,
                                              op0=ALU.mult, op1=ALU.add))
    P.ins("vector", lambda e: e.tensor_scalar(ccs[0:1, 0:1], ROW[0:1, 0:1], 1.0, 0.0,
                                              op0=ALU.mult, op1=ALU.add))
    P.ins("vector", lambda e: e.tensor_scalar(ccs[0:1, 1:2], ROW[0:1, 1:2], 1.0, 0.0,
                                              op0=ALU.mult, op1=ALU.add))
    P.ins("vector", lambda e: e.tensor_scalar(ccs[0:1, 2:3], ROW[0:1, 2:3], 1.0, 0.0,
                                              op0=ALU.mult, op1=ALU.add))
    P.wait("gpsimd", sV)
    P.ins("gpsimd", lambda e: e.dma_start(out=cc_in1, in_=ccs[0:1, :]), dCC, 16)
    P.wait("gpsimd", dCC)
    P.ins("gpsimd", lambda e: e.collective_compute(
        "AllReduce", ALU.max, replica_groups=[list(range(NC))],
        ins=[cc_in1], outs=[cc_out1]), sCC)
    P.wait("gpsimd", sCC)
    P.ins("gpsimd", lambda e: e.dma_start(out=ROW[0:1, 8:16], in_=cc_out1), dCC, 16)
    P.wait("vector", dCC)
    P.ins("vector", lambda e: e.tensor_scalar(ROW[0:1, 16:17], ROW[0:1, 8:9], 1.0 / 448.0, 1e-6,
                                              op0=ALU.mult, op1=ALU.max))
    P.ins("vector", lambda e: e.tensor_scalar(ROW[0:1, 17:18], ROW[0:1, 16:17], 2.0, 0.0,
                                              op0=ALU.mult, op1=ALU.add))
    P.ins("vector", lambda e: e.reciprocal(ROW[0:1, 17:18], ROW[0:1, 17:18]))
    P.ins("vector", lambda e: e.tensor_scalar(ROW[0:1, 18:19], ROW[0:1, 9:10], 1.0 / 448.0, 1e-6,
                                              op0=ALU.mult, op1=ALU.max))
    P.ins("vector", lambda e: e.tensor_scalar(ROW[0:1, 19:20], ROW[0:1, 18:19], 2.0, 0.0,
                                              op0=ALU.mult, op1=ALU.add))
    P.ins("vector", lambda e: e.reciprocal(ROW[0:1, 19:20], ROW[0:1, 19:20]))
    P.ins("vector", lambda e: e.tensor_scalar(ROW[0:1, 20:21], ROW[0:1, 16:17], ROW[0:1, 18:19], 4.0,
                                              op0=ALU.mult, op1=ALU.mult))
    P.ins("vector", lambda e: e.tensor_scalar(ROW[0:1, 21:22], ROW[0:1, 10:11], 1.0 / 448.0, 1e-6,
                                              op0=ALU.mult, op1=ALU.max))
    P.ins("vector", lambda e: e.tensor_scalar(ROW[0:1, 22:23], ROW[0:1, 21:22], 2.0, 0.0,
                                              op0=ALU.mult, op1=ALU.add))
    P.ins("vector", lambda e: e.reciprocal(ROW[0:1, 22:23], ROW[0:1, 22:23]))
    P.ins("vector", lambda e: e.tensor_scalar(ccs[0:1, 0:1], ROW[0:1, 17:18], 1.0, 0.0, op0=ALU.mult, op1=ALU.add))
    P.ins("vector", lambda e: e.tensor_scalar(ccs[0:1, 1:2], ROW[0:1, 20:21], 1.0, 0.0, op0=ALU.mult, op1=ALU.add))
    P.ins("vector", lambda e: e.tensor_scalar(ccs[0:1, 2:3], ROW[0:1, 19:20], 1.0, 0.0, op0=ALU.mult, op1=ALU.add))
    P.ins("vector", lambda e: e.tensor_scalar(ccs[0:1, 3:4], ROW[0:1, 22:23], 1.0, 0.0, op0=ALU.mult, op1=ALU.add))
    with nc.psum_tensor("pscl", [128, 8], F32) as pscl:
        P.wait("tensor", sV)
        P.ins("tensor", lambda e: e.matmul(pscl[:], onesr[0:1, :], ccs[0:1, :], start=True, stop=True), sT)
        P.wait("scalar", sT)
        P.ins("scalar", lambda e: e.activation(out=SCL[:], in_=pscl[:], func=AF.Copy, scale=1.0))

    ph_c = ExitStack()
    wpr8 = sb("wpr8", [128, NJ * D], F8, ph_c)
    ph_cm = ExitStack()
    x8T = sb("x8T", [128, ND * T], F8, ph_cm)

    with ExitStack() as pro, \
         nc.psum_tensor("ptc0", [128, 128], BF16) as ptc0, \
         nc.psum_tensor("ptc1", [128, 128], BF16) as ptc1:
        xbC = [sb(f"xbC{i}", [128, D], F32, pro) for i in range(2)]
        x8tok = sb("x8tok", [128, D], F8, pro)
        xqbf = sb("xqbf", [128, D], BF16, pro)
        ptc = [ptc0, ptc1]
        full_barrier()
        P.wait("sync", dX2S)
        tok_x8 = [0] * 4
        for tt in range(NT):
            par = tt % 2
            P.wait("sync", sA)
            P.ins("sync", lambda e, t=tt, p=par: e.dma_start(
                out=xbC[p][:], in_=x2_d[t * 128:(t + 1) * 128, :]), dX2L[par], 16)
            P.wait("scalar", dX2L[par])
            P.ins("scalar", lambda e, p=par: e.activation(out=x8tok[:], in_=xbC[p][:], func=AF.Copy,
                                                          scale=SCL[:, 0:1]))
            P.ins("scalar", lambda e: e.activation(out=xqbf[:], in_=x8tok[:], func=AF.Copy, scale=1.0))
            for dc in range(ND):
                tp = dc % 2
                P.wait("tensor", sA)
                P.ins("tensor", lambda e, d=dc, q=tp: e.transpose(
                    ptc[q][:], xqbf[:, d * 128:(d + 1) * 128], id_bf[:]), sT)
                P.wait("scalar", sT)
                P.ins("scalar", lambda e, d=dc, t=tt, q=tp: e.activation(
                    out=x8T[:, d * T + t * 128: d * T + (t + 1) * 128],
                    in_=ptc[q][:], func=AF.Copy, scale=1.0))
            if tt % 4 == 3:
                tok_x8[tt // 4] = sA.n

    full_barrier()
    wfc32 = sb("wfc32", [128, D], F32, ph_cm)
    wfc8 = [sb(f"wfc8_{i}", [128, D], F8, ph_cm) for i in range(2)]
    wpr32 = sb("wpr32_0", [128, D], F32, ph_cm)
    h_sb = [sb(f"h_sb{i}", [128, 512], BF16, ph_cm) for i in range(4)]
    r_sb = [sb(f"r_sb{i}", [128, 512], F32, ph_cm) for i in range(2)]

    with nc.psum_tensor("ph0", [128, 512], F32) as ph0, \
         nc.psum_tensor("ph1", [128, 512], F32) as ph1, \
         nc.psum_tensor("ph2", [128, 512], F32) as ph2, \
         nc.psum_tensor("ph3", [128, 512], F32) as ph3, \
         nc.psum_tensor("ph4", [128, 512], F32) as ph4, \
         nc.psum_tensor("ph5", [128, 512], F32) as ph5b, \
         nc.psum_tensor("ph6", [128, 512], F32) as ph6, \
         nc.psum_tensor("ph7", [128, 512], F32) as ph7:
        phh = [[ph0, ph1, ph2, ph3], [ph4, ph5b, ph6, ph7]]
        tok_quant = [0, 0]
        tok_quant_done = 0
        tok_wprq = 0
        tok_relu = [[0] * 4, [0] * 4]
        tok_mult = [[0, 0], [0, 0]]
        tok_mm = [[0] * 4, [0] * 4]
        for jc in range(NJ):
            par = jc % 2
            P.wait("sync", sA, tok_quant_done)
            P.ins("sync", lambda e, k=jc: e.dma_start(
                out=bass.AP(wfc32, 0, [[D, 128], [128, ND], [1, 128]]),
                in_=bass.AP(dh["wfcT"], k * 128, [[J, 128], [128 * J, ND], [1, 128]])), dWFC, 16)
            P.wait("scalar", dWFC)
            P.wait("scalar", sT, tok_mm[par][3])
            P.ins("scalar", lambda e, p=par: e.activation(out=wfc8[p][:], in_=wfc32[:], func=AF.Copy,
                                                          scale=SCL[:, 2:3]), selfwait=False)
            tok_quant[par] = sA.n
            tok_quant_done = sA.n
            P.wait("gpsimd", sA, tok_wprq)
            P.ins("gpsimd", lambda e, k=jc: e.dma_start(
                out=wpr32[:], in_=wprT_d[k * 128:(k + 1) * 128, :]), dWPR, 16)
            P.wait("scalar", dWPR)
            P.ins("scalar", lambda e, k=jc: e.activation(
                out=wpr8[:, k * D:(k + 1) * D], in_=wpr32[:], func=AF.Copy,
                scale=SCL[:, 3:4]), selfwait=False)
            tok_wprq = sA.n
            for ts in range(4):
                P.wait("tensor", sA, tok_quant[par])
                P.wait("tensor", sA, tok_relu[par][ts])
                for d2 in range(ND // 2):
                    P.ins("tensor", lambda e, d=d2, s=ts, p=par: e.matmul(
                        phh[p][s][:],
                        bass.AP(wfc8[p], d * 256, [[D, 128], [128, 2], [1, 128]]),
                        bass.AP(x8T, (2 * d) * T + s * 512, [[ND * T, 128], [T, 2], [1, 512]]),
                        start=(d == 0), stop=(d == ND // 2 - 1),
                        perf_mode=PM.DoubleRow), sT)
                tok_mm[par][ts] = sT.n
            for ts in range(4):
                rp = ts % 2
                P.wait("scalar", sT, tok_mm[par][ts])
                P.wait("scalar", sV, tok_mult[par][rp])
                P.ins("scalar", lambda e, s=ts, p=par, q=rp: e.activation(
                    out=r_sb[q][:], in_=phh[p][s][:], func=AF.Relu, scale=SCL[:, 1:2]), selfwait=False)
                tok_relu[par][ts] = sA.n
                P.wait("vector", sA, tok_relu[par][ts])
                P.wait("vector", dHST[ts])
                P.ins("vector", lambda e, s=ts, q=rp: e.tensor_tensor(
                    h_sb[s][:], r_sb[q][:], r_sb[q][:], op=ALU.mult), selfwait=False)
                tok_mult[par][rp] = sV.n
                P.ins("vector", lambda e, s=ts: e.reduce_max(stv(15), h_sb[s][:], axis=AXX))
                P.ins("vector", lambda e: e.tensor_scalar(rmx(1), rmx(1), stv(15), 0.0,
                                                          op0=ALU.max, op1=ALU.add))
                P.wait("gpsimd", sV)
                P.ins("gpsimd", lambda e, k=jc, s=ts: e.dma_start(
                    out=bass.AP(hT_h, (s * 4 * 128) * J + k * 128,
                                [[J, 128], [128 * J, 4], [1, 128]]),
                    in_=bass.AP(h_sb[s], 0, [[512, 128], [128, 4], [1, 128]])), dHST[ts], 16)

    cross_part_max(rmx(1), 3)
    P.ins("vector", lambda e: e.tensor_scalar(ccs[0:1, :], zrow_s[0:1, :], 0.0, 0.0,
                                              op0=ALU.mult, op1=ALU.add))
    P.ins("vector", lambda e: e.tensor_scalar(ccs[0:1, 0:1], ROW[0:1, 3:4], 1.0, 0.0,
                                              op0=ALU.mult, op1=ALU.add))
    P.wait("gpsimd", sV)
    P.ins("gpsimd", lambda e: e.dma_start(out=cc_in2, in_=ccs[0:1, :]), dCC, 16)
    P.wait("gpsimd", dCC)
    P.ins("gpsimd", lambda e: e.collective_compute(
        "AllReduce", ALU.max, replica_groups=[list(range(NC))],
        ins=[cc_in2], outs=[cc_out2]), sCC)
    P.wait("gpsimd", sCC)
    P.ins("gpsimd", lambda e: e.dma_start(out=ROW[0:1, 8:16], in_=cc_out2), dCC, 16)
    P.wait("vector", dCC)
    P.ins("vector", lambda e: e.tensor_scalar(ROW[0:1, 24:25], ROW[0:1, 8:9], 1.0 / 448.0, 1e-6,
                                              op0=ALU.mult, op1=ALU.max))
    P.ins("vector", lambda e: e.tensor_scalar(ROW[0:1, 25:26], ROW[0:1, 24:25], 2.0, 0.0,
                                              op0=ALU.mult, op1=ALU.add))
    P.ins("vector", lambda e: e.reciprocal(ROW[0:1, 25:26], ROW[0:1, 25:26]))
    P.ins("vector", lambda e: e.tensor_scalar(ROW[0:1, 26:27], ROW[0:1, 24:25], ROW[0:1, 21:22], 4.0,
                                              op0=ALU.mult, op1=ALU.mult))
    P.ins("vector", lambda e: e.tensor_scalar(ROW[0:1, 27:28], ROW[0:1, 26:27], ROW[0:1, 26:27], 0.0,
                                              op0=ALU.mult, op1=ALU.add))
    P.ins("vector", lambda e: e.tensor_scalar(ccs[0:1, 0:1], ROW[0:1, 25:26], 1.0, 0.0, op0=ALU.mult, op1=ALU.add))
    P.ins("vector", lambda e: e.tensor_scalar(ccs[0:1, 1:2], ROW[0:1, 26:27], 1.0, 0.0, op0=ALU.mult, op1=ALU.add))
    P.ins("vector", lambda e: e.tensor_scalar(ccs[0:1, 2:3], ROW[0:1, 27:28], 1.0, 0.0, op0=ALU.mult, op1=ALU.add))
    with nc.psum_tensor("pscl2", [128, 8], F32) as pscl2:
        P.wait("tensor", sV)
        P.ins("tensor", lambda e: e.matmul(pscl2[:], onesr[0:1, :], ccs[0:1, :], start=True, stop=True), sT)
        P.wait("scalar", sT)
        P.ins("scalar", lambda e: e.activation(out=SCL2[:], in_=pscl2[:], func=AF.Copy, scale=1.0))

    full_barrier()
    ph_cm.close()
    hf32 = [sb(f"hf32_{i}", [128, J // 2], BF16, ph_c) for i in range(2)]
    h8s = [sb(f"h8s{i}", [128, J], F8, ph_c) for i in range(2)]
    lnff = [sb(f"lnff{i}", [128, D], F32, ph_c) for i in range(2)]
    xm5 = [sb(f"xm5_{i}", [128, D], F32, ph_c) for i in range(2)]

    with nc.psum_tensor("pq0", [128, D], F32) as pq0, \
         nc.psum_tensor("pq1", [128, D], F32) as pq1:
        poD = [pq0, pq1]
        for s4 in dHST:
            P.wait("sync", s4)
        P.wait("sync", dXMS)
        tok_hq = [0, 0]
        tok_hq_done = [0, 0]
        tok_add = [0, 0]
        tok_mmD = [0, 0]
        tok_lnff = [0, 0]
        tok_stats = [0, 0]
        for tt in range(NT):
            par = tt % 2
            for half in range(2):
                P.wait("gpsimd", sA, tok_hq_done[half])
                P.ins("gpsimd", lambda e, t=tt, hf=half: e.dma_start(
                    out=hf32[hf][:], in_=bass.AP(hT_h, (t * 128) * J + hf * (J // 2),
                                                 [[J, 128], [1, J // 2]])), dHSx[half], 16)
            for half in range(2):
                P.wait("scalar", dHSx[half])
                P.wait("scalar", sT, tok_mmD[par])
                P.ins("scalar", lambda e, p=par, hf=half: e.activation(
                    out=h8s[p][:, hf * (J // 2):(hf + 1) * (J // 2)], in_=hf32[hf][:], func=AF.Copy,
                    scale=SCL2[:, 0:1]), selfwait=False)
                tok_hq_done[half] = sA.n
            tok_hq[par] = sA.n
            P.wait("sync", sV, tok_add[par])
            P.ins("sync", lambda e, t=tt, p=par: e.dma_start(
                out=xm5[p][:], in_=xmid_d[t * 128:(t + 1) * 128, :]), dXM5[par], 16)
            P.wait("tensor", sA, max(tok_hq[par], tok_lnff[par]))
            P.wait("tensor", sV, tok_stats[par])
            for ns in range(4):
                for k2 in range(NJ // 2):
                    P.ins("tensor", lambda e, k=k2, n=ns, p=par: e.matmul(
                        poD[p][:, n * 512:(n + 1) * 512],
                        bass.AP(h8s[p], k * 256, [[J, 128], [128, 2], [1, 128]]),
                        bass.AP(wpr8, (2 * k) * D + n * 512, [[NJ * D, 128], [D, 2], [1, 512]]),
                        start=(k == 0), stop=(k == NJ // 2 - 1),
                        perf_mode=PM.DoubleRow), sT)
            tok_mmD[par] = sT.n
            P.wait("vector", sT, tok_mmD[par])
            P.wait("scalar", sT, tok_mmD[par])
            ln_stats(poD[par][:], 0, 0, scale_c=SCL2[:, 1:2], scale_c2=SCL2[:, 2:3])
            tok_stats[par] = sV.n + 1
            P.ins("vector", lambda e: e.tensor_scalar(stv(6), stv(3), SCL2[:, 1:2], 0.0,
                                                      op0=ALU.mult, op1=ALU.add))
            P.wait("scalar", sV)
            P.wait("scalar", dOUT[par])
            P.ins("scalar", lambda e, p=par: e.activation(out=lnff[p][:], in_=poD[p][:], func=AF.Identity,
                                                          scale=stv(6), bias=stv(4)), selfwait=False)
            tok_lnff[par] = sA.n
            P.wait("vector", sA)
            P.wait("vector", dXM5[par])
            P.ins("vector", lambda e, p=par: e.tensor_tensor(lnff[p][:], xm5[p][:], lnff[p][:], op=ALU.add))
            tok_add[par] = sV.n
            P.wait("sync", sV, tok_add[par])
            P.ins("sync", lambda e, t=tt, p=par: e.dma_start(
                out=out_d[t * 128:(t + 1) * 128, :], in_=lnff[p][:]), dOUT[par], 16)

    P.wait("sync", dOUT[0])
    P.wait("sync", dOUT[1])
    P.wait("gpsimd", dOUT[0])
    P.wait("gpsimd", dOUT[1])

    with nc.Block() as block:
        P.build(block)
    ph_c.close()
    es.close()
    return nc


_CACHED = {}


def kernel(L0_attn_x, L0_attn_Wq, L0_attn_Wk, L0_attn_Wv, L0_attn_Wo,
           L0_attn_qn_w, L0_attn_kn_w, L0_ffn_w_fc, L0_ffn_w_proj):
    x = np.ascontiguousarray(np.asarray(L0_attn_x, dtype=np.float32))
    f32 = np.float32

    def t(a):
        return np.ascontiguousarray(np.asarray(a, dtype=np.float32).T)

    wqT, wkT, wvT = t(L0_attn_Wq), t(L0_attn_Wk), t(L0_attn_Wv)
    woT, wfcT, wprT = t(L0_attn_Wo), t(L0_ffn_w_fc), t(L0_ffn_w_proj)
    qn = np.ascontiguousarray(np.asarray(L0_attn_qn_w, f32).reshape(1, DH))
    kn = np.ascontiguousarray(np.asarray(L0_attn_kn_w, f32).reshape(1, DH))

    inv_freq = (f32(1.0) / f32(10000.0) ** (np.arange(0, DH, 2).astype(f32) / f32(DH))).astype(f32)
    pos = np.arange(T, dtype=f32)
    emb = np.concatenate([pos[:, None] * inv_freq[None, :]] * 2, axis=1).astype(f32)
    cos_t = np.cos(emb).astype(f32)
    sin_t = np.sin(emb).astype(f32)
    sgn = np.where(np.arange(DH) < 64, f32(-1.0), f32(1.0))
    qs = f32(DH ** -0.5)
    tqc = np.ascontiguousarray(cos_t * qs)
    tqs = np.ascontiguousarray(sin_t * sgn * qs)
    tkc = np.ascontiguousarray(cos_t)
    tks = np.ascontiguousarray(sin_t * sgn)

    jj_ = np.arange(128)[:, None]
    ii_ = np.arange(256)[None, :]
    m_list = []
    for c_ in (-128, 0, 384, 512):
        d_ = c_ + ii_ - jj_
        m_list.append(((d_ >= 0) & (d_ <= 512)).astype(ml_dtypes.bfloat16))
    masks4 = np.ascontiguousarray(np.concatenate(m_list, axis=0))
    ones_bfm = np.ones((128, 128), dtype=ml_dtypes.bfloat16)
    id_bfm = np.eye(128, dtype=ml_dtypes.bfloat16)
    id_f32m = np.eye(128, dtype=f32)
    onesrm = np.ones((1, 128), dtype=f32)
    consts = np.zeros((128, 8), f32)
    consts[:, 0] = 1e-5
    consts[:, 1] = 1e-6
    zrow = np.zeros((1, 8), f32)

    if "nc" not in _CACHED:
        _CACHED["nc"] = build_program()
    nc = _CACHED["nc"]

    shared = dict(wqT=wqT, wkT=wkT, wvT=wvT, woT=woT, wfcT=wfcT, wprT=wprT,
                  qn=qn, kn=kn, tqc=tqc, tqs=tqs, tkc=tkc, tks=tks,
                  masks4=masks4, ones_bf=ones_bfm, id_bf=id_bfm,
                  id_f32=id_f32m, onesr=onesrm, consts=consts, zrow=zrow)
    in_maps = [dict(shared, x=np.ascontiguousarray(x[b]),
                    wslf=np.ascontiguousarray(wfcT[b * (D // NC):(b + 1) * (D // NC), :]),
                    wslp=np.ascontiguousarray(wprT[b * (J // NC):(b + 1) * (J // NC), :]))
               for b in range(NC)]
    res = run_bass_kernel_spmd(nc, in_maps, core_ids=list(range(NC)))
    return np.stack([res.results[b]["out"] for b in range(NC)], axis=0)


# version 19 (final)
# speedup vs baseline: 1.1712x; 1.0632x over previous
import numpy as np
import ml_dtypes
from contextlib import ExitStack

import concourse.bass as bass
import concourse.mybir as mybir
from concourse.bass_utils import run_bass_kernel_spmd

F32 = mybir.dt.float32
BF16 = mybir.dt.bfloat16
F8 = mybir.dt.float8e4
AF = mybir.ActivationFunctionType
ALU = mybir.AluOpType
AXX = mybir.AxisListType.X
PM = mybir.MatmulPerfMode

T, D, H, HKV, DH, G, WIN, J = 2048, 2048, 16, 4, 128, 4, 512, 8192
NT, ND, NJ, NC = 16, 16, 64, 8

ENGINES = ("sync", "scalar", "vector", "tensor", "gpsimd")


class Sem:
    __slots__ = ("h", "n")

    def __init__(self, h):
        self.h = h
        self.n = 0


class Sched:
    def __init__(self):
        self.ops = []
        self.own_sem = {}

    def emit(self, eng, fn):
        self.ops.append((eng, fn))

    def wait(self, eng, sem, n=None):
        tgt = sem.n if n is None else n
        if tgt <= 0:
            return
        self.emit(eng, lambda e, s=sem.h, t=tgt: e.wait_ge(s, t))

    def ins(self, eng, f, sem=None, inc=1, selfwait=True):
        own = self.own_sem.get(eng)
        if own is not None:
            if selfwait and own.n > 0:
                self.wait(eng, own)
            if sem is None:
                sem, inc = own, 1
        if sem is not None:
            sem.n += inc
        def fn(e, _f=f, _s=sem, _i=inc):
            r = _f(e)
            if _s is not None:
                r.then_inc(_s.h, _i)
        self.emit(eng, fn)

    def build(self, block):
        for ename in ENGINES:
            sec = getattr(block, ename)
            mine = [fn for (e, fn) in self.ops if e == ename]
            def body(eng, _my=mine):
                for fn in _my:
                    fn(eng)
            sec(body)


def build_program():
    nc = bass.Bass("TRN2", target_bir_lowering=False)

    dh = {}

    def din(name, shape, dt=F32):
        h = nc.dram_tensor(name, shape, dt, kind="ExternalInput")
        dh[name] = h
        return h.ap()

    x_d = din("x", [T, D])
    wqT_d = din("wqT", [D, H * DH])
    wkT_d = din("wkT", [D, HKV * DH])
    wvT_d = din("wvT", [D, HKV * DH])
    woT_d = din("woT", [H * DH, D])
    wfcT_d = din("wfcT", [D, J])
    wprT_d = din("wprT", [J, D])
    qn_d = din("qn", [1, DH])
    kn_d = din("kn", [1, DH])
    tqc_d = din("tqc", [T, DH])
    tqs_d = din("tqs", [T, DH])
    tkc_d = din("tkc", [T, DH])
    tks_d = din("tks", [T, DH])
    masks4_d = din("masks4", [4 * 128, 256], BF16)
    ones_bf_d = din("ones_bf", [128, 128], BF16)
    id_bf_d = din("id_bf", [128, 128], BF16)
    id_f32_d = din("id_f32", [128, 128], F32)
    onesr_d = din("onesr", [1, 128], F32)
    consts_d = din("consts", [128, 8], F32)
    zrow_d = din("zrow", [1, 8], F32)
    wslf_d = din("wslf", [D // NC, J])
    wslp_d = din("wslp", [J // NC, D])

    out_h = nc.dram_tensor("out", [T, D], F32, kind="ExternalOutput")
    out_d = out_h.ap()

    xmid_h = nc.dram_tensor("xmid_s", [T, D], F32)
    x2_h = nc.dram_tensor("x2_s", [T, D], F32)
    hT_h = nc.dram_tensor("hT_s", [NT * 128, J], BF16)
    ctx_h = nc.dram_tensor("ctx_s", [H * 128, T], BF16)
    xmid_d, x2_d, hT_d, ctx_d = xmid_h.ap(), x2_h.ap(), hT_h.ap(), ctx_h.ap()
    cc_in1 = nc.dram_tensor("cc_in1", [1, 8], F32).ap()
    cc_out1 = nc.dram_tensor("cc_out1", [1, 8], F32).ap()
    cc_in2 = nc.dram_tensor("cc_in2", [1, 8], F32).ap()
    cc_out2 = nc.dram_tensor("cc_out2", [1, 8], F32).ap()

    P = Sched()
    es = ExitStack()

    ALLSEMS = []

    def sem(name):
        s = Sem(es.enter_context(nc.semaphore(name)))
        ALLSEMS.append(s)
        return s

    def full_barrier():
        for _eng in ENGINES:
            for _s in ALLSEMS:
                P.wait(_eng, _s)

    sT = sem("s_pe")
    sA = sem("s_act")
    sV = sem("s_dve")
    sGc = sem("s_gpc")
    sCC = sem("s_cc")
    P.own_sem = {"vector": sV, "scalar": sA}

    dIN = sem("d_in")
    dX = [sem("d_x0"), sem("d_x1")]
    dTAB = sem("d_tab")
    dCTXL = [sem("d_ctxl0"), sem("d_ctxl1")]
    dXB5 = [sem("d_xb50"), sem("d_xb51")]
    dX2L = [sem("d_x2l0"), sem("d_x2l1")]
    dWFC = sem("d_wfc")
    dHSx = [sem("d_hs0"), sem("d_hs1")]
    dXM5 = [sem("d_xm50"), sem("d_xm51")]
    dOUT = [sem("d_out0"), sem("d_out1")]
    dWMX = [sem("d_wmx0")]
    dWKV = sem("d_wkv")
    dWQ = sem("d_wq")
    dCTXS = [sem("d_ctxs0"), sem("d_ctxs1")]
    dWO = [sem(f"d_wo{i}") for i in range(4)]
    dXMS = sem("d_xms")
    dX2S = sem("d_x2s")
    dWPR = sem("d_wpr")
    dHST = [sem(f"d_hst{i}") for i in range(4)]
    dCC = sem("d_cc")

    def sb(name, shape, dt, stk=es):
        return stk.enter_context(nc.sbuf_tensor(name, shape, dt))

    cons = sb("cons", [128, 8], F32)
    zrow_s = sb("zrow_s", [1, 8], F32)
    onesr = sb("onesr_s", [1, 128], F32)
    id_bf = sb("id_bf_s", [128, 128], BF16)
    id_f32 = sb("id_f32_s", [128, 128], F32)
    ones_bf = sb("ones_bf_s", [128, 128], BF16)
    masks4 = sb("masks4_s", [128, 4 * 256], BF16)
    qn_s = sb("qn_s", [1, DH], F32)
    kn_s = sb("kn_s", [1, DH], F32)
    wrow = sb("wrow", [1, 2 * DH], F32)
    wqB = sb("wqB", [128, DH], F32)
    wkB = sb("wkB", [128, DH], F32)
    STV = sb("STV", [128, 8 * 24], F32)
    STA = sb("STA", [128, 8 * 16], F32)
    RMX = sb("RMX", [128, 8 * 4], F32)
    SCL = sb("SCL", [128, 8], F32)
    SCL2 = sb("SCL2", [128, 8], F32)
    ROW = sb("ROWs", [1, 32], F32)
    ccs = sb("ccs_s", [1, 8], F32)
    tcol = sb("tcol", [128, 2], F32)
    trash = sb("trash", [128, 2048], BF16)

    def stv(i):
        return STV[:, 8 * i:8 * i + 1]

    def sta(i):
        return STA[:, 8 * i:8 * i + 1]

    def rmx(i):
        return RMX[:, 8 * i:8 * i + 1]

    for dst, src in [(cons, consts_d), (zrow_s, zrow_d), (onesr, onesr_d),
                     (id_bf, id_bf_d), (id_f32, id_f32_d), (ones_bf, ones_bf_d),
                     (qn_s, qn_d), (kn_s, kn_d)]:
        P.ins("sync", lambda e, d=dst, s=src: e.dma_start(out=d[:], in_=s), dIN, 16)
    P.ins("sync", lambda e: e.dma_start(
        out=bass.AP(masks4, 0, [[4 * 256, 128], [256, 4], [1, 256]]),
        in_=bass.AP(dh["masks4"], 0, [[256, 128], [128 * 256, 4], [1, 256]])), dIN, 16)
    P.ins("gpsimd", lambda e: e.memset(RMX[:], 0), sGc, 1)
    P.wait("vector", dIN)
    P.wait("scalar", dIN)
    P.wait("tensor", dIN)
    P.ins("vector", lambda e: e.tensor_scalar(wrow[0:1, 0:DH], qn_s[0:1, :], 1.0, 1.0,
                                              op0=ALU.mult, op1=ALU.add))
    P.ins("vector", lambda e: e.tensor_scalar(wrow[0:1, DH:2 * DH], kn_s[0:1, :], 1.0, 1.0,
                                              op0=ALU.mult, op1=ALU.add))

    with nc.psum_tensor("pinit", [128, 2 * DH], F32) as pinit:
        P.wait("tensor", sV)
        P.ins("tensor", lambda e: e.matmul(pinit[:, 0:DH], onesr[0:1, :], wrow[0:1, 0:DH],
                                           start=True, stop=True), sT)
        P.ins("tensor", lambda e: e.matmul(pinit[:, DH:2 * DH], onesr[0:1, :], wrow[0:1, DH:2 * DH],
                                           start=True, stop=True), sT)
        P.wait("scalar", sT)
        P.ins("scalar", lambda e: e.activation(out=wqB[:], in_=pinit[:, 0:DH], func=AF.Copy, scale=1.0))
        P.ins("scalar", lambda e: e.activation(out=wkB[:], in_=pinit[:, DH:2 * DH], func=AF.Copy, scale=1.0))

    ph_a = ExitStack()
    xnT = sb("xnT", [128, ND * T], BF16, ph_a)
    kT = sb("kT", [128, HKV * T], BF16, ph_a)
    v_sb = sb("v_sb", [128, HKV * T], BF16, ph_a)
    qTg = sb("qTg", [128, G * T], BF16, ph_a)
    tabc = sb("tabc", [128, NT * DH], F32, ph_a)
    tabs = sb("tabs", [128, NT * DH], F32, ph_a)
    xbuf = [sb(f"xbuf{i}", [128, D], F32, ph_a) for i in range(2)]
    xn_tok = sb("xn_tok", [128, D], BF16, ph_a)
    xn_tok2 = [xn_tok, xn_tok]
    rop = [sb(f"rop{i}", [128, 3 * 128], F32, ph_a) for i in range(2)]
    qrot_bf = [sb(f"qrot{i}", [128, 128], BF16, ph_a) for i in range(2)]
    wkv_sb = sb("wkv_sb", [128, ND * 256], BF16, ph_a)
    wq_sb = sb("wq_sb", [128, ND * 512], BF16, ph_a)
    ctxs = [sb(f"ctxs{i}", [128, T], BF16, ph_a) for i in range(2)]
    expb = [sb(f"expb{i}", [128, 1536], BF16, ph_a) for i in range(2)]
    rzb = [sb(f"rzb{i}", [128, 256], F32, ph_a) for i in range(2)]
    wmx = [sb("wmx0", [128, 1024], F32, ph_a)]

    def xnT_t(dc, tt):
        return xnT[:, dc * T + tt * 128: dc * T + (tt + 1) * 128]

    def kT_t(hk, jt):
        return kT[:, hk * T + jt * 128: hk * T + (jt + 1) * 128]

    def v_t(hk, jt):
        return v_sb[:, hk * T + jt * 128: hk * T + (jt + 1) * 128]

    def qT_t(g, it):
        return qTg[:, g * T + it * 128: g * T + (it + 1) * 128]

    def tab_t(tab, tt):
        return tab[:, tt * 128:(tt + 1) * 128]

    def ln_stats(src_ap, sl, asl, scale_c=None, scale_c2=None):
        P.wait("scalar", sV)
        P.ins("vector", lambda e: e.reduce_sum(stv(sl), src_ap, axis=AXX))
        P.wait("scalar", sV)
        P.ins("scalar", lambda e: e.activation(out=trash[:], in_=src_ap, func=AF.Square,
                                               accum_out=sta(asl)))
        P.wait("vector", sA)
        if scale_c is None:
            P.ins("vector", lambda e: e.tensor_scalar(stv(sl), stv(sl), 1.0 / D, 0.0,
                                                      op0=ALU.mult, op1=ALU.add))
            P.ins("vector", lambda e: e.tensor_scalar(stv(sl + 1), sta(asl), 1.0 / D, 0.0,
                                                      op0=ALU.mult, op1=ALU.add))
        else:
            P.ins("vector", lambda e: e.tensor_scalar(stv(sl), stv(sl), scale_c, 1.0 / D,
                                                      op0=ALU.mult, op1=ALU.mult))
            P.ins("vector", lambda e: e.tensor_scalar(stv(sl + 1), sta(asl), scale_c2, 1.0 / D,
                                                      op0=ALU.mult, op1=ALU.mult))
        P.ins("vector", lambda e: e.tensor_scalar(stv(sl + 2), stv(sl), stv(sl), 0.0,
                                                  op0=ALU.mult, op1=ALU.add))
        P.ins("vector", lambda e: e.tensor_scalar(stv(sl + 2), stv(sl + 1), stv(sl + 2), 0.0,
                                                  op0=ALU.subtract, op1=ALU.add))
        P.wait("scalar", sV)
        P.ins("scalar", lambda e: e.activation(out=sta(asl + 1), in_=stv(sl + 2), func=AF.Sqrt,
                                               scale=1.0, bias=cons[:, 0:1]))
        P.wait("vector", sA)
        P.ins("vector", lambda e: e.reciprocal(stv(sl + 3), sta(asl + 1)))
        P.ins("vector", lambda e: e.tensor_scalar(stv(sl + 4), stv(sl), stv(sl + 3), -1.0,
                                                  op0=ALU.mult, op1=ALU.mult))

    full_barrier()
    P.wait("vector", sGc)
    wmax_tiles = []
    for (w_ap_, rows_, cols_, slot_) in [(wslf_d, D // NC, J, 2), (wslp_d, J // NC, D, 3)]:
        for r_ in range(rows_ // 128):
            for c_ in range(cols_ // 1024):
                wmax_tiles.append((w_ap_, r_, c_, slot_))
    wmax_state = {"i": 0, "tok": [0, 0]}

    def wmax_step(nstep=1):
        for _ in range(nstep):
            i = wmax_state["i"]
            if i >= len(wmax_tiles):
                return
            wmax_state["i"] += 1
            w_ap_, r_, c_, slot_ = wmax_tiles[i]
            wp = 0
            P.wait("gpsimd", sV, wmax_state["tok"][wp])
            P.ins("gpsimd", lambda e, w=w_ap_, rr=r_, cc=c_, p=wp: e.dma_start(
                out=wmx[p][:], in_=w[rr * 128:(rr + 1) * 128, cc * 1024:(cc + 1) * 1024]), dWMX[wp], 16)
            P.wait("vector", dWMX[wp])
            P.ins("vector", lambda e, p=wp: e.reduce_max(stv(15), wmx[p][:], axis=AXX,
                                                         apply_absolute_value=True))
            P.ins("vector", lambda e, s=slot_: e.tensor_scalar(rmx(s), rmx(s), stv(15), 0.0,
                                                               op0=ALU.max, op1=ALU.add))
            wmax_state["tok"][wp] = sV.n

    with nc.psum_tensor("ptr0", [128, 128], BF16) as ptr0, \
         nc.psum_tensor("ptr1", [128, 128], BF16) as ptr1:
        ptr = [ptr0, ptr1]
        tok_xn = [0, 0]
        tok_nrm = 0
        tok_ptr = [0, 0]
        for tt in range(NT):
            par = tt % 2
            wmax_step(2)
            P.wait("sync", sA)
            P.wait("sync", sV)
            P.ins("sync", lambda e, t=tt, p=par: e.dma_start(
                out=xbuf[p][:], in_=x_d[t * 128:(t + 1) * 128, :]), dX[par], 16)
            P.wait("vector", dX[par])
            P.wait("scalar", dX[par])
            ln_stats(xbuf[par][:], 0, 0)
            P.wait("scalar", sV)
            P.wait("scalar", sT, max(tok_xn))
            P.ins("scalar", lambda e, p=par: e.activation(out=xn_tok2[p][:], in_=xbuf[p][:],
                                                          func=AF.Identity,
                                                          scale=stv(3), bias=stv(4)))
            tok_nrm = sA.n
            for dc in range(ND):
                tp = dc % 2
                P.wait("tensor", sA, max(tok_nrm, tok_ptr[tp]))
                P.ins("tensor", lambda e, d=dc, q=tp, p=par: e.transpose(
                    ptr[q][:], xn_tok2[p][:, d * 128:(d + 1) * 128], id_bf[:]), sT)
                P.wait("scalar", sT)
                P.ins("scalar", lambda e, d=dc, t=tt, q=tp: e.activation(
                    out=xnT_t(d, t), in_=ptr[q][:], func=AF.Copy, scale=1.0))
                tok_ptr[tp] = sA.n
            tok_xn[par] = sT.n

    P.ins("sync", lambda e: e.dma_start(
        out=bass.AP(tabc, 0, [[NT * DH, 128], [DH, NT], [1, DH]]),
        in_=bass.AP(dh["tkc"], 0, [[DH, 128], [128 * DH, NT], [1, DH]])), dTAB, 16)
    P.ins("sync", lambda e: e.dma_start(
        out=bass.AP(tabs, 0, [[NT * DH, 128], [DH, NT], [1, DH]]),
        in_=bass.AP(dh["tks"], 0, [[DH, 128], [128 * DH, NT], [1, DH]])), dTAB, 16)
    P.wait("vector", dTAB)

    with nc.psum_tensor("pkv0", [128, 256], F32) as pkv0, \
         nc.psum_tensor("pkv1", [128, 256], F32) as pkv1, \
         nc.psum_tensor("ptk0", [128, 128], BF16) as ptk0, \
         nc.psum_tensor("ptk1", [128, 128], BF16) as ptk1:
        pkv = [pkv0, pkv1]
        ptk = [ptk0, ptk1]
        tokA = [sA.n, sA.n]
        tokV = [sV.n, sV.n]
        tokTr = [0, 0]
        tokEvK = [0, 0]
        for hk in range(HKV):
            P.wait("gpsimd", sT)
            P.ins("gpsimd", lambda e, h=hk: e.dma_start(
                out=bass.AP(wkv_sb, 0, [[ND * 256, 128], [256, ND], [1, 128]]),
                in_=bass.AP(dh["wkT"], h * 128, [[HKV * DH, 128], [128 * HKV * DH, ND], [1, 128]])), dWKV, 16)
            P.ins("gpsimd", lambda e, h=hk: e.dma_start(
                out=bass.AP(wkv_sb, 128, [[ND * 256, 128], [256, ND], [1, 128]]),
                in_=bass.AP(dh["wvT"], h * 128, [[HKV * DH, 128], [128 * HKV * DH, ND], [1, 128]])), dWKV, 16)
            for tt in range(NT):
                par = tt % 2
                wmax_step(2)
                P.wait("tensor", dWKV)
                P.wait("tensor", sA, tokA[par])
                P.wait("tensor", sV, tokV[par])
                for dc in range(ND):
                    P.ins("tensor", lambda e, d=dc, t=tt, p=par: e.matmul(
                        pkv[p][:], xnT_t(d, t), wkv_sb[:, d * 256:(d + 1) * 256],
                        start=(d == 0), stop=(d == ND - 1)), sT)
                tok_mm = sT.n
                P.wait("scalar", sT, tok_mm)
                P.wait("scalar", sV, tokV[par])
                P.ins("scalar", lambda e, p=par: e.activation(out=trash[:, 0:128], in_=pkv[p][:, 0:128],
                                                              func=AF.Square, accum_out=sta(8)), selfwait=False)
                P.ins("scalar", lambda e: e.activation(out=sta(12), in_=sta(8), func=AF.Sqrt,
                                                       scale=1.0 / DH, bias=cons[:, 1:2]))
                tok_std = sA.n
                P.wait("vector", sA, tok_std)
                P.ins("vector", lambda e: e.reciprocal(stv(16), sta(12)))
                P.ins("vector", lambda e, p=par: e.scalar_tensor_tensor(rop[p][:, 0:128], pkv[p][:, 0:128],
                                                                        stv(16), wkB[:],
                                                                        op0=ALU.mult, op1=ALU.mult))
                tokV[par] = sV.n
                P.wait("scalar", sV, tokV[par])
                P.ins("scalar", lambda e, h=hk, t=tt, p=par: e.activation(
                    out=v_t(h, t), in_=pkv[p][:, 128:256], func=AF.Copy, scale=1.0), selfwait=False)
                tokA[par] = sA.n
                P.wait("vector", sT, tokTr[par])
                P.ins("vector", lambda e, p=par, t=tt: e.tensor_tensor(
                    rop[p][:, 128:256], rop[p][:, 0:128], tab_t(tabc, t), op=ALU.mult))
                P.ins("vector", lambda e, p=par, t=tt: e.tensor_tensor(
                    rop[p][:, 256:320], rop[p][:, 64:128], tab_t(tabs, t)[:, 0:64], op=ALU.mult))
                P.ins("vector", lambda e, p=par, t=tt: e.tensor_tensor(
                    rop[p][:, 320:384], rop[p][:, 0:64], tab_t(tabs, t)[:, 64:128], op=ALU.mult))
                P.ins("vector", lambda e, p=par: e.tensor_tensor(
                    qrot_bf[p][:], rop[p][:, 128:256], rop[p][:, 256:384], op=ALU.add))
                tok_rot = sV.n
                P.wait("tensor", sV, tok_rot)
                P.wait("tensor", sA, tokEvK[par])
                P.ins("tensor", lambda e, p=par: e.transpose(ptk[p][:], qrot_bf[p][:], id_bf[:]), sT)
                tokTr[par] = sT.n
                P.wait("scalar", sT, tokTr[par])
                P.ins("scalar", lambda e, h=hk, t=tt, p=par: e.activation(
                    out=kT_t(h, t), in_=ptk[p][:], func=AF.Copy, scale=1.0), selfwait=False)
                tokEvK[par] = sA.n

    P.wait("sync", sV)
    P.ins("sync", lambda e: e.dma_start(
        out=bass.AP(tabc, 0, [[NT * DH, 128], [DH, NT], [1, DH]]),
        in_=bass.AP(dh["tqc"], 0, [[DH, 128], [128 * DH, NT], [1, DH]])), dTAB, 16)
    P.ins("sync", lambda e: e.dma_start(
        out=bass.AP(tabs, 0, [[NT * DH, 128], [DH, NT], [1, DH]]),
        in_=bass.AP(dh["tqs"], 0, [[DH, 128], [128 * DH, NT], [1, DH]])), dTAB, 16)
    P.wait("vector", dTAB)

    tokQa = [0, 0]
    tokQv = [0, 0]
    tokTrQ = [0, 0]
    tokEvQ = [0, 0]
    tE = [0, 0]
    tSc = [0, 0]
    tM = [0, 0]
    tZp = [0, 0]
    tCp = [0, 0]
    tRZp = [0, 0]
    tNp = [0, 0]
    for hk in range(HKV):
        P.wait("gpsimd", sT)
        P.ins("gpsimd", lambda e, h=hk: e.dma_start(
            out=bass.AP(wq_sb, 0, [[ND * 512, 128], [512, ND], [1, 512]]),
            in_=bass.AP(dh["wqT"], h * 512, [[H * DH, 128], [128 * H * DH, ND], [1, 512]])), dWQ, 16)
        with nc.psum_tensor(f"pq0_{hk}", [128, 512], F32) as pq0, \
             nc.psum_tensor(f"pq1_{hk}", [128, 512], F32) as pq1, \
             nc.psum_tensor(f"ptq0_{hk}", [128, 128], BF16) as ptq0, \
             nc.psum_tensor(f"ptq1_{hk}", [128, 128], BF16) as ptq1:
            pq = [pq0, pq1]
            ptq = [ptq0, ptq1]
            P.wait("tensor", sA)
            P.wait("tensor", sV)
            for tt in range(NT):
                par = tt % 2
                wmax_step(2)
                P.wait("tensor", dWQ)
                P.wait("tensor", sA, tokQa[par])
                P.wait("tensor", sV, tokQv[par])
                for dc in range(ND):
                    P.ins("tensor", lambda e, d=dc, t=tt, p=par: e.matmul(
                        pq[p][:], xnT_t(d, t), wq_sb[:, d * 512:(d + 1) * 512],
                        start=(d == 0), stop=(d == ND - 1)), sT)
                tok_mm = sT.n
                P.wait("scalar", sT, tok_mm)
                P.wait("scalar", sV, tokQv[par])
                for g in range(G):
                    P.ins("scalar", lambda e, gg=g, p=par: e.activation(
                        out=trash[:, gg * 128:(gg + 1) * 128], in_=pq[p][:, gg * 128:(gg + 1) * 128],
                        func=AF.Square, accum_out=sta(8 + gg)), selfwait=False)
                P.wait("scalar", sA)
                for g in range(G):
                    P.ins("scalar", lambda e, gg=g: e.activation(out=sta(12 + gg), in_=sta(8 + gg),
                                                                 func=AF.Sqrt, scale=1.0 / DH,
                                                                 bias=cons[:, 1:2]), selfwait=False)
                tok_std = sA.n
                P.wait("vector", sA, tok_std)
                for g in range(G):
                    P.ins("vector", lambda e, gg=g: e.reciprocal(stv(16 + gg), sta(12 + gg)), selfwait=False)
                for g in range(G):
                    gp = g % 2
                    P.wait("vector", sT, tokTrQ[gp])
                    P.ins("vector", lambda e, gg=g, p=par: e.scalar_tensor_tensor(
                        rop[0][:, 0:128], pq[p][:, gg * 128:(gg + 1) * 128], stv(16 + gg), wqB[:],
                        op0=ALU.mult, op1=ALU.mult))
                    P.ins("vector", lambda e, t=tt: e.tensor_tensor(
                        rop[0][:, 128:256], rop[0][:, 0:128], tab_t(tabc, t), op=ALU.mult))
                    P.ins("vector", lambda e, t=tt: e.tensor_tensor(
                        rop[0][:, 256:320], rop[0][:, 64:128], tab_t(tabs, t)[:, 0:64], op=ALU.mult))
                    P.ins("vector", lambda e, t=tt: e.tensor_tensor(
                        rop[0][:, 320:384], rop[0][:, 0:64], tab_t(tabs, t)[:, 64:128], op=ALU.mult))
                    P.ins("vector", lambda e, p=gp: e.tensor_tensor(
                        qrot_bf[p][:], rop[0][:, 128:256], rop[0][:, 256:384], op=ALU.add))
                    tok_rot = sV.n
                    P.wait("tensor", sV, tok_rot)
                    P.wait("tensor", sA, tokEvQ[gp])
                    P.ins("tensor", lambda e, p=gp: e.transpose(ptq[p][:], qrot_bf[p][:], id_bf[:]), sT)
                    tokTrQ[gp] = sT.n
                    P.wait("scalar", sT, tokTrQ[gp])
                    P.ins("scalar", lambda e, gg=g, t=tt, p=gp: e.activation(
                        out=qT_t(gg, t), in_=ptq[p][:], func=AF.Copy, scale=1.0), selfwait=False)
                    tokEvQ[gp] = sA.n
                tokQv[par] = sV.n
                tokQa[par] = sA.n
        with nc.psum_tensor(f"psS0_{hk}", [128, 1536], F32) as psS0, \
             nc.psum_tensor(f"psS1_{hk}", [128, 1536], F32) as psS1, \
             nc.psum_tensor(f"pzz_{hk}", [128, 256], F32) as pzz, \
             nc.psum_tensor(f"pcc_{hk}", [128, 256], F32) as pcc:
            psS = [psS0, psS1]
            base = max(tokQa[0], tokQa[1])
            tE = [base, base]
            tSc = [0, 0]
            tM = [0, 0]
            tZp = max(tRZp)
            tCp = 0
            tRZ1 = base
            tN1 = 0
            P.wait("tensor", sV)

            NR = NT // 2

            def jinfo(r):
                out = []
                for jt in range(max(0, 2 * r - 4), 2 * r + 2):
                    c = (2 * r - jt) * 128
                    slot = {-128: 0, 0: 1, 384: 2, 512: 3}.get(c)
                    out.append((jt, slot))
                return out

            def emit_scores(r, g):
                par = r % 2
                P.wait("tensor", sA, tE[par])
                for jj, (jt, _) in enumerate(jinfo(r)):
                    P.ins("tensor", lambda e, p=par, jx=jj, hh=hk, j=jt, rr=r, gg=g: e.matmul(
                        psS[p][:, jx * 256:(jx + 1) * 256], kT_t(hh, j),
                        qTg[:, gg * T + rr * 256:(gg * T + (rr + 1) * 256)],
                        start=True, stop=True), sT)
                tSc[par] = sT.n

            def emit_exp(r):
                par = r % 2
                nj = len(jinfo(r))
                P.wait("scalar", sT, tSc[par])
                P.ins("scalar", lambda e, p=par, n=nj: e.activation(
                    out=expb[p][:, 0:n * 256], in_=psS[p][:, 0:n * 256], func=AF.Exp), selfwait=False)
                tE[par] = sA.n

            def emit_masks(r):
                par = r % 2
                P.wait("vector", sA, tE[par])
                first = True
                for jj, (jt, slot) in enumerate(jinfo(r)):
                    if slot is None:
                        continue
                    P.ins("vector", lambda e, p=par, jx=jj, m=slot: e.tensor_tensor(
                        expb[p][:, jx * 256:(jx + 1) * 256], expb[p][:, jx * 256:(jx + 1) * 256],
                        masks4[:, m * 256:(m + 1) * 256], op=ALU.mult), selfwait=first)
                    first = False
                tM[par] = sV.n

            def emit_zctx(r):
                nonlocal_d = None
                par = r % 2
                ji = jinfo(r)
                nj = len(ji)
                P.wait("tensor", sV, tM[par])
                P.wait("tensor", sA, tRZ1)
                for jj in range(nj):
                    P.ins("tensor", lambda e, p=par, j=jj, n=nj: e.matmul(
                        pzz[:], ones_bf[:], expb[p][:, j * 256:(j + 1) * 256],
                        start=(j == 0), stop=(j == n - 1)), sT)
                tz = sT.n
                P.wait("tensor", sV, tN1)
                for jj, (jt, _) in enumerate(ji):
                    P.ins("tensor", lambda e, p=par, hh=hk, j=jt, jx=jj, n=nj: e.matmul(
                        pcc[:], v_t(hh, j), expb[p][:, jx * 256:(jx + 1) * 256],
                        start=(jx == 0), stop=(jx == n - 1)), sT)
                return tz, sT.n

            for g in range(G):
                h = hk * G + g
                cpar = h % 2
                P.wait("vector", dCTXS[cpar])
                emit_scores(0, g)
                emit_exp(0)
                emit_masks(0)
                for r in range(NR):
                    par = r % 2
                    if r + 1 < NR:
                        emit_scores(r + 1, g)
                        emit_exp(r + 1)
                        emit_masks(r + 1)
                    tz, tc = emit_zctx(r)
                    P.wait("scalar", sT, tz)
                    P.ins("scalar", lambda e, p=par: e.activation(out=rzb[p][:], in_=pzz[:],
                                                                  func=AF.Copy, scale=1.0), selfwait=False)
                    tRZ1 = sA.n
                    P.wait("vector", sA, tRZ1)
                    P.ins("vector", lambda e, p=par: e.reciprocal(rzb[p][:], rzb[p][:]), selfwait=False)
                    P.wait("vector", sT, tc)
                    P.ins("vector", lambda e, p=par, c=cpar, rr=r: e.tensor_tensor(
                        ctxs[c][:, rr * 256:(rr + 1) * 256], pcc[:], rzb[p][:], op=ALU.mult))
                    tN1 = sV.n
                P.wait("gpsimd", sV)
                P.ins("gpsimd", lambda e, hh=h, c=cpar: e.dma_start(
                    out=ctx_d[hh * 128:(hh + 1) * 128, :], in_=ctxs[c][:]), dCTXS[cpar], 16)
            tokQa = [max(tokQa[0], tE[0]), max(tokQa[1], tE[1])]

    wmax_step(len(wmax_tiles))

    full_barrier()
    ph_a.close()
    ph5 = ExitStack()
    wo_sb = sb("wo_sb", [128, ND * D], BF16, ph5)
    ctx_full = sb("ctx_full", [128, ND * T], BF16, ph5)
    xb5 = [sb(f"xb5_{i}", [128, D], F32, ph5) for i in range(2)]
    lnat = sb("lnat", [128, D], F32, ph5)
    xmid_sb = sb("xmid_sb", [128, D], F32, ph5)
    x2_sb = sb("x2_sb", [128, D], F32, ph5)

    wo_f = [sb(f"wo_f{i}", [128, D], F32, ph5) for i in range(2)]
    P.wait("sync", dCTXS[0])
    P.wait("sync", dCTXS[1])
    for hd in range(ND):
        P.ins("sync", lambda e, d=hd: e.dma_start(
            out=ctx_full[:, d * T:(d + 1) * T], in_=ctx_d[d * 128:(d + 1) * 128, :]), dCTXL[hd % 2], 16)
    tok_woc = [0] * ND
    tok_wof = [0, 0]
    for hd in range(ND):
        wp = hd % 2
        P.wait("sync", sA, tok_wof[wp])
        P.ins("sync", lambda e, d=hd, p=wp: e.dma_start(
            out=wo_f[p][:], in_=woT_d[d * 128:(d + 1) * 128, :]), dWO[hd % 4], 16)
        P.wait("scalar", dWO[hd % 4])
        P.ins("scalar", lambda e, d=hd, p=wp: e.activation(
            out=wo_sb[:, d * D:(d + 1) * D], in_=wo_f[p][:], func=AF.Copy, scale=1.0), selfwait=False)
        tok_woc[hd] = sA.n
        tok_wof[wp] = sA.n

    with nc.psum_tensor("po0", [128, D], F32) as po0, \
         nc.psum_tensor("po1", [128, D], F32) as po1:
        po = [po0, po1]
        tok_po_a = [0, 0]
        tok_po_v = [0, 0]
        for tt in range(NT):
            par = tt % 2
            P.wait("sync", sV)
            P.ins("sync", lambda e, t=tt, p=par: e.dma_start(
                out=xb5[p][:], in_=x_d[t * 128:(t + 1) * 128, :]), dXB5[par], 16)
            if tt == 0:
                P.wait("tensor", dCTXL[0])
                P.wait("tensor", dCTXL[1])
            P.wait("tensor", sA, tok_po_a[par])
            P.wait("tensor", sV, tok_po_v[par])
            for hd in range(ND):
                if tt == 0:
                    P.wait("tensor", sA, tok_woc[hd])
                for ns in range(4):
                    P.ins("tensor", lambda e, d=hd, n=ns, p=par, t=tt: e.matmul(
                        po[p][:, n * 512:(n + 1) * 512], ctx_full[:, d * T + t * 128: d * T + (t + 1) * 128],
                        wo_sb[:, d * D + n * 512: d * D + (n + 1) * 512],
                        start=(d == 0), stop=(d == ND - 1)), sT)
            P.wait("vector", sT)
            P.wait("scalar", sT)
            ln_stats(po[par][:], 0, 0)
            P.wait("scalar", sV)
            P.ins("scalar", lambda e, p=par: e.activation(out=lnat[:], in_=po[p][:], func=AF.Identity,
                                                          scale=stv(3), bias=stv(4)))
            tok_po_a[par] = sA.n
            tok_po_v[par] = sV.n
            P.wait("vector", sA)
            P.wait("vector", dXB5[par])
            P.wait("vector", dXMS)
            P.ins("vector", lambda e, p=par: e.tensor_tensor(xmid_sb[:], xb5[p][:], lnat[:], op=ALU.add))
            P.wait("gpsimd", sV)
            P.ins("gpsimd", lambda e, t=tt: e.dma_start(
                out=xmid_d[t * 128:(t + 1) * 128, :], in_=xmid_sb[:]), dXMS, 16)
            ln_stats(xmid_sb[:], 8, 4)
            P.wait("scalar", sV)
            P.wait("scalar", dX2S)
            P.ins("scalar", lambda e: e.activation(out=x2_sb[:], in_=xmid_sb[:], func=AF.Identity,
                                                   scale=stv(11), bias=stv(12)))
            P.wait("vector", sA)
            P.ins("vector", lambda e: e.reduce_max(stv(15), x2_sb[:], axis=AXX,
                                                   apply_absolute_value=True))
            P.ins("vector", lambda e: e.tensor_scalar(rmx(0), rmx(0), stv(15), 0.0,
                                                      op0=ALU.max, op1=ALU.add))
            P.wait("gpsimd", sA)
            P.ins("gpsimd", lambda e, t=tt: e.dma_start(
                out=x2_d[t * 128:(t + 1) * 128, :], in_=x2_sb[:]), dX2S, 16)
    full_barrier()
    ph5.close()

    def cross_part_max(col_ap, row_col):
        with nc.psum_tensor(f"pmx{row_col}", [128, 128], F32) as pmx:
            P.ins("vector", lambda e: e.tensor_scalar(tcol[:, 0:1], col_ap, 1.0, 0.0,
                                                      op0=ALU.mult, op1=ALU.add))
            P.wait("tensor", sV)
            P.ins("tensor", lambda e: e.transpose(pmx[0:1, 0:128], tcol[:, 0:1], id_f32[:]), sT)
            P.wait("vector", sT)
            P.ins("vector", lambda e, c=row_col: e.reduce_max(
                ROW[0:1, c:c + 1], pmx[0:1, 0:128], axis=AXX))

    cross_part_max(rmx(0), 0)
    cross_part_max(rmx(2), 1)
    cross_part_max(rmx(3), 2)

    P.ins("vector", lambda e: e.tensor_scalar(ccs[0:1, :], zrow_s[0:1, :], 0.0, 0.0,
                                              op0=ALU.mult, op1=ALU.add))
    P.ins("vector", lambda e: e.tensor_scalar(ccs[0:1, 0:1], ROW[0:1, 0:1], 1.0, 0.0,
                                              op0=ALU.mult, op1=ALU.add))
    P.ins("vector", lambda e: e.tensor_scalar(ccs[0:1, 1:2], ROW[0:1, 1:2], 1.0, 0.0,
                                              op0=ALU.mult, op1=ALU.add))
    P.ins("vector", lambda e: e.tensor_scalar(ccs[0:1, 2:3], ROW[0:1, 2:3], 1.0, 0.0,
                                              op0=ALU.mult, op1=ALU.add))
    P.wait("gpsimd", sV)
    P.ins("gpsimd", lambda e: e.dma_start(out=cc_in1, in_=ccs[0:1, :]), dCC, 16)
    P.wait("gpsimd", dCC)
    P.ins("gpsimd", lambda e: e.collective_compute(
        "AllReduce", ALU.max, replica_groups=[list(range(NC))],
        ins=[cc_in1], outs=[cc_out1]), sCC)
    P.wait("gpsimd", sCC)
    P.ins("gpsimd", lambda e: e.dma_start(out=ROW[0:1, 8:16], in_=cc_out1), dCC, 16)
    P.wait("vector", dCC)
    P.ins("vector", lambda e: e.tensor_scalar(ROW[0:1, 16:17], ROW[0:1, 8:9], 1.0 / 448.0, 1e-6,
                                              op0=ALU.mult, op1=ALU.max))
    P.ins("vector", lambda e: e.tensor_scalar(ROW[0:1, 17:18], ROW[0:1, 16:17], 2.0, 0.0,
                                              op0=ALU.mult, op1=ALU.add))
    P.ins("vector", lambda e: e.reciprocal(ROW[0:1, 17:18], ROW[0:1, 17:18]))
    P.ins("vector", lambda e: e.tensor_scalar(ROW[0:1, 18:19], ROW[0:1, 9:10], 1.0 / 448.0, 1e-6,
                                              op0=ALU.mult, op1=ALU.max))
    P.ins("vector", lambda e: e.tensor_scalar(ROW[0:1, 19:20], ROW[0:1, 18:19], 2.0, 0.0,
                                              op0=ALU.mult, op1=ALU.add))
    P.ins("vector", lambda e: e.reciprocal(ROW[0:1, 19:20], ROW[0:1, 19:20]))
    P.ins("vector", lambda e: e.tensor_scalar(ROW[0:1, 20:21], ROW[0:1, 16:17], ROW[0:1, 18:19], 4.0,
                                              op0=ALU.mult, op1=ALU.mult))
    P.ins("vector", lambda e: e.tensor_scalar(ROW[0:1, 21:22], ROW[0:1, 10:11], 1.0 / 448.0, 1e-6,
                                              op0=ALU.mult, op1=ALU.max))
    P.ins("vector", lambda e: e.tensor_scalar(ROW[0:1, 22:23], ROW[0:1, 21:22], 2.0, 0.0,
                                              op0=ALU.mult, op1=ALU.add))
    P.ins("vector", lambda e: e.reciprocal(ROW[0:1, 22:23], ROW[0:1, 22:23]))
    P.ins("vector", lambda e: e.tensor_scalar(ccs[0:1, 0:1], ROW[0:1, 17:18], 1.0, 0.0, op0=ALU.mult, op1=ALU.add))
    P.ins("vector", lambda e: e.tensor_scalar(ccs[0:1, 1:2], ROW[0:1, 20:21], 1.0, 0.0, op0=ALU.mult, op1=ALU.add))
    P.ins("vector", lambda e: e.tensor_scalar(ccs[0:1, 2:3], ROW[0:1, 19:20], 1.0, 0.0, op0=ALU.mult, op1=ALU.add))
    P.ins("vector", lambda e: e.tensor_scalar(ccs[0:1, 3:4], ROW[0:1, 22:23], 1.0, 0.0, op0=ALU.mult, op1=ALU.add))
    with nc.psum_tensor("pscl", [128, 8], F32) as pscl:
        P.wait("tensor", sV)
        P.ins("tensor", lambda e: e.matmul(pscl[:], onesr[0:1, :], ccs[0:1, :], start=True, stop=True), sT)
        P.wait("scalar", sT)
        P.ins("scalar", lambda e: e.activation(out=SCL[:], in_=pscl[:], func=AF.Copy, scale=1.0))

    ph_c = ExitStack()
    wpr8 = sb("wpr8", [128, NJ * D], F8, ph_c)
    ph_cm = ExitStack()
    x8T = sb("x8T", [128, ND * T], F8, ph_cm)

    with ExitStack() as pro, \
         nc.psum_tensor("ptc0", [128, 128], BF16) as ptc0, \
         nc.psum_tensor("ptc1", [128, 128], BF16) as ptc1:
        xbC = [sb(f"xbC{i}", [128, D], F32, pro) for i in range(2)]
        x8tok = sb("x8tok", [128, D], F8, pro)
        xqbf = sb("xqbf", [128, D], BF16, pro)
        ptc = [ptc0, ptc1]
        full_barrier()
        P.wait("sync", dX2S)
        tok_x8 = [0] * 4
        for tt in range(NT):
            par = tt % 2
            P.wait("sync", sA)
            P.ins("sync", lambda e, t=tt, p=par: e.dma_start(
                out=xbC[p][:], in_=x2_d[t * 128:(t + 1) * 128, :]), dX2L[par], 16)
            P.wait("scalar", dX2L[par])
            P.ins("scalar", lambda e, p=par: e.activation(out=x8tok[:], in_=xbC[p][:], func=AF.Copy,
                                                          scale=SCL[:, 0:1]))
            P.ins("scalar", lambda e: e.activation(out=xqbf[:], in_=x8tok[:], func=AF.Copy, scale=1.0))
            for dc in range(ND):
                tp = dc % 2
                P.wait("tensor", sA)
                P.ins("tensor", lambda e, d=dc, q=tp: e.transpose(
                    ptc[q][:], xqbf[:, d * 128:(d + 1) * 128], id_bf[:]), sT)
                P.wait("scalar", sT)
                P.ins("scalar", lambda e, d=dc, t=tt, q=tp: e.activation(
                    out=x8T[:, d * T + t * 128: d * T + (t + 1) * 128],
                    in_=ptc[q][:], func=AF.Copy, scale=1.0))
            if tt % 4 == 3:
                tok_x8[tt // 4] = sA.n

    full_barrier()
    wfc32 = sb("wfc32", [128, D], F32, ph_cm)
    wfc8 = [sb(f"wfc8_{i}", [128, D], F8, ph_cm) for i in range(2)]
    wpr32 = sb("wpr32_0", [128, D], F32, ph_cm)
    h_sb = [sb(f"h_sb{i}", [128, 512], BF16, ph_cm) for i in range(4)]
    r_sb = [sb(f"r_sb{i}", [128, 512], F32, ph_cm) for i in range(2)]

    with nc.psum_tensor("ph0", [128, 512], F32) as ph0, \
         nc.psum_tensor("ph1", [128, 512], F32) as ph1, \
         nc.psum_tensor("ph2", [128, 512], F32) as ph2, \
         nc.psum_tensor("ph3", [128, 512], F32) as ph3, \
         nc.psum_tensor("ph4", [128, 512], F32) as ph4, \
         nc.psum_tensor("ph5", [128, 512], F32) as ph5b, \
         nc.psum_tensor("ph6", [128, 512], F32) as ph6, \
         nc.psum_tensor("ph7", [128, 512], F32) as ph7:
        phh = [[ph0, ph1, ph2, ph3], [ph4, ph5b, ph6, ph7]]
        tok_quant = [0, 0]
        tok_quant_done = 0
        tok_wprq = 0
        tok_relu = [[0] * 4, [0] * 4]
        tok_mult = [0, 0]
        tok_mm = [[0] * 4, [0] * 4]
        for jc in range(NJ):
            par = jc % 2
            P.wait("sync", sA, tok_quant_done)
            P.ins("sync", lambda e, k=jc: e.dma_start(
                out=bass.AP(wfc32, 0, [[D, 128], [128, ND], [1, 128]]),
                in_=bass.AP(dh["wfcT"], k * 128, [[J, 128], [128 * J, ND], [1, 128]])), dWFC, 16)
            P.wait("scalar", dWFC)
            P.wait("scalar", sT, tok_mm[par][3])
            P.ins("scalar", lambda e, p=par: e.activation(out=wfc8[p][:], in_=wfc32[:], func=AF.Copy,
                                                          scale=SCL[:, 2:3]), selfwait=False)
            tok_quant[par] = sA.n
            tok_quant_done = sA.n
            P.ins("scalar", lambda e, k=jc: e.dma_start(
                out=wpr32[:], in_=wprT_d[k * 128:(k + 1) * 128, :]), dWPR, 16, selfwait=False)
            P.wait("scalar", dWPR)
            P.ins("scalar", lambda e, k=jc: e.activation(
                out=wpr8[:, k * D:(k + 1) * D], in_=wpr32[:], func=AF.Copy,
                scale=SCL[:, 3:4]), selfwait=False)
            tok_wprq = sA.n
            for ts in range(4):
                P.wait("tensor", sA, tok_quant[par])
                P.wait("tensor", sA, tok_relu[par][ts])
                for d2 in range(ND // 2):
                    P.ins("tensor", lambda e, d=d2, s=ts, p=par: e.matmul(
                        phh[p][s][:],
                        bass.AP(wfc8[p], d * 256, [[D, 128], [128, 2], [1, 128]]),
                        bass.AP(x8T, (2 * d) * T + s * 512, [[ND * T, 128], [T, 2], [1, 512]]),
                        start=(d == 0), stop=(d == ND // 2 - 1),
                        perf_mode=PM.DoubleRow), sT)
                tok_mm[par][ts] = sT.n
            for ts in range(4):
                rp = ts % 2
                P.wait("scalar", sT, tok_mm[par][ts])
                P.wait("scalar", sV, tok_mult[rp])
                P.ins("scalar", lambda e, s=ts, p=par, q=rp: e.activation(
                    out=r_sb[q][:], in_=phh[p][s][:], func=AF.Relu, scale=SCL[:, 1:2]), selfwait=False)
                tok_relu[par][ts] = sA.n
                P.wait("vector", sA, tok_relu[par][ts])
                P.wait("vector", dHST[ts])
                P.ins("vector", lambda e, s=ts, q=rp: e.tensor_tensor(
                    h_sb[s][:], r_sb[q][:], r_sb[q][:], op=ALU.mult), selfwait=False)
                tok_mult[rp] = sV.n
                P.ins("vector", lambda e, s=ts: e.reduce_max(stv(15), h_sb[s][:], axis=AXX))
                P.ins("vector", lambda e: e.tensor_scalar(rmx(1), rmx(1), stv(15), 0.0,
                                                          op0=ALU.max, op1=ALU.add))
                P.wait("gpsimd", sV)
                P.ins("gpsimd", lambda e, k=jc, s=ts: e.dma_start(
                    out=bass.AP(hT_h, (s * 4 * 128) * J + k * 128,
                                [[J, 128], [128 * J, 4], [1, 128]]),
                    in_=bass.AP(h_sb[s], 0, [[512, 128], [128, 4], [1, 128]])), dHST[ts], 16)

    cross_part_max(rmx(1), 3)
    P.ins("vector", lambda e: e.tensor_scalar(ccs[0:1, :], zrow_s[0:1, :], 0.0, 0.0,
                                              op0=ALU.mult, op1=ALU.add))
    P.ins("vector", lambda e: e.tensor_scalar(ccs[0:1, 0:1], ROW[0:1, 3:4], 1.0, 0.0,
                                              op0=ALU.mult, op1=ALU.add))
    P.wait("gpsimd", sV)
    P.ins("gpsimd", lambda e: e.dma_start(out=cc_in2, in_=ccs[0:1, :]), dCC, 16)
    P.wait("gpsimd", dCC)
    P.ins("gpsimd", lambda e: e.collective_compute(
        "AllReduce", ALU.max, replica_groups=[list(range(NC))],
        ins=[cc_in2], outs=[cc_out2]), sCC)
    P.wait("gpsimd", sCC)
    P.ins("gpsimd", lambda e: e.dma_start(out=ROW[0:1, 8:16], in_=cc_out2), dCC, 16)
    P.wait("vector", dCC)
    P.ins("vector", lambda e: e.tensor_scalar(ROW[0:1, 24:25], ROW[0:1, 8:9], 1.0 / 448.0, 1e-6,
                                              op0=ALU.mult, op1=ALU.max))
    P.ins("vector", lambda e: e.tensor_scalar(ROW[0:1, 25:26], ROW[0:1, 24:25], 2.0, 0.0,
                                              op0=ALU.mult, op1=ALU.add))
    P.ins("vector", lambda e: e.reciprocal(ROW[0:1, 25:26], ROW[0:1, 25:26]))
    P.ins("vector", lambda e: e.tensor_scalar(ROW[0:1, 26:27], ROW[0:1, 24:25], ROW[0:1, 21:22], 4.0,
                                              op0=ALU.mult, op1=ALU.mult))
    P.ins("vector", lambda e: e.tensor_scalar(ROW[0:1, 27:28], ROW[0:1, 26:27], ROW[0:1, 26:27], 0.0,
                                              op0=ALU.mult, op1=ALU.add))
    P.ins("vector", lambda e: e.tensor_scalar(ccs[0:1, 0:1], ROW[0:1, 25:26], 1.0, 0.0, op0=ALU.mult, op1=ALU.add))
    P.ins("vector", lambda e: e.tensor_scalar(ccs[0:1, 1:2], ROW[0:1, 26:27], 1.0, 0.0, op0=ALU.mult, op1=ALU.add))
    P.ins("vector", lambda e: e.tensor_scalar(ccs[0:1, 2:3], ROW[0:1, 27:28], 1.0, 0.0, op0=ALU.mult, op1=ALU.add))
    with nc.psum_tensor("pscl2", [128, 8], F32) as pscl2:
        P.wait("tensor", sV)
        P.ins("tensor", lambda e: e.matmul(pscl2[:], onesr[0:1, :], ccs[0:1, :], start=True, stop=True), sT)
        P.wait("scalar", sT)
        P.ins("scalar", lambda e: e.activation(out=SCL2[:], in_=pscl2[:], func=AF.Copy, scale=1.0))

    full_barrier()
    ph_cm.close()
    hf32 = [sb(f"hf32_{i}", [128, J // 2], BF16, ph_c) for i in range(2)]
    h8s = [sb(f"h8s{i}", [128, J], F8, ph_c) for i in range(2)]
    lnff = [sb(f"lnff{i}", [128, D], F32, ph_c) for i in range(2)]
    xm5 = [sb(f"xm5_{i}", [128, D], F32, ph_c) for i in range(2)]

    with nc.psum_tensor("pq0", [128, D], F32) as pq0, \
         nc.psum_tensor("pq1", [128, D], F32) as pq1:
        poD = [pq0, pq1]
        for s4 in dHST:
            P.wait("sync", s4)
        P.wait("sync", dXMS)
        tok_hq = [0, 0]
        tok_hq_done = [0, 0]
        tok_add = [0, 0]
        tok_mmD = [0, 0]
        tok_lnff = [0, 0]
        tok_stats = [0, 0]
        for tt in range(NT):
            par = tt % 2
            for half in range(2):
                P.wait("gpsimd", sA, tok_hq_done[half])
                P.ins("gpsimd", lambda e, t=tt, hf=half: e.dma_start(
                    out=hf32[hf][:], in_=bass.AP(hT_h, (t * 128) * J + hf * (J // 2),
                                                 [[J, 128], [1, J // 2]])), dHSx[half], 16)
            for half in range(2):
                P.wait("scalar", dHSx[half])
                P.wait("scalar", sT, tok_mmD[par])
                P.ins("scalar", lambda e, p=par, hf=half: e.activation(
                    out=h8s[p][:, hf * (J // 2):(hf + 1) * (J // 2)], in_=hf32[hf][:], func=AF.Copy,
                    scale=SCL2[:, 0:1]), selfwait=False)
                tok_hq_done[half] = sA.n
            tok_hq[par] = sA.n
            P.wait("sync", sV, tok_add[par])
            P.ins("sync", lambda e, t=tt, p=par: e.dma_start(
                out=xm5[p][:], in_=xmid_d[t * 128:(t + 1) * 128, :]), dXM5[par], 16)
            P.wait("tensor", sA, max(tok_hq[par], tok_lnff[par]))
            P.wait("tensor", sV, tok_stats[par])
            for ns in range(4):
                for k2 in range(NJ // 2):
                    P.ins("tensor", lambda e, k=k2, n=ns, p=par: e.matmul(
                        poD[p][:, n * 512:(n + 1) * 512],
                        bass.AP(h8s[p], k * 256, [[J, 128], [128, 2], [1, 128]]),
                        bass.AP(wpr8, (2 * k) * D + n * 512, [[NJ * D, 128], [D, 2], [1, 512]]),
                        start=(k == 0), stop=(k == NJ // 2 - 1),
                        perf_mode=PM.DoubleRow), sT)
            tok_mmD[par] = sT.n
            P.wait("vector", sT, tok_mmD[par])
            P.wait("scalar", sT, tok_mmD[par])
            ln_stats(poD[par][:], 0, 0, scale_c=SCL2[:, 1:2], scale_c2=SCL2[:, 2:3])
            tok_stats[par] = sV.n + 1
            P.ins("vector", lambda e: e.tensor_scalar(stv(6), stv(3), SCL2[:, 1:2], 0.0,
                                                      op0=ALU.mult, op1=ALU.add))
            P.wait("scalar", sV)
            P.wait("scalar", dOUT[par])
            P.ins("scalar", lambda e, p=par: e.activation(out=lnff[p][:], in_=poD[p][:], func=AF.Identity,
                                                          scale=stv(6), bias=stv(4)), selfwait=False)
            tok_lnff[par] = sA.n
            P.wait("vector", sA)
            P.wait("vector", dXM5[par])
            P.ins("vector", lambda e, p=par: e.tensor_tensor(lnff[p][:], xm5[p][:], lnff[p][:], op=ALU.add))
            tok_add[par] = sV.n
            P.wait("sync", sV, tok_add[par])
            P.ins("sync", lambda e, t=tt, p=par: e.dma_start(
                out=out_d[t * 128:(t + 1) * 128, :], in_=lnff[p][:]), dOUT[par], 16)

    P.wait("sync", dOUT[0])
    P.wait("sync", dOUT[1])
    P.wait("gpsimd", dOUT[0])
    P.wait("gpsimd", dOUT[1])

    with nc.Block() as block:
        P.build(block)
    ph_c.close()
    es.close()
    return nc


_CACHED = {}


def kernel(L0_attn_x, L0_attn_Wq, L0_attn_Wk, L0_attn_Wv, L0_attn_Wo,
           L0_attn_qn_w, L0_attn_kn_w, L0_ffn_w_fc, L0_ffn_w_proj):
    x = np.ascontiguousarray(np.asarray(L0_attn_x, dtype=np.float32))
    f32 = np.float32

    def t(a):
        return np.ascontiguousarray(np.asarray(a, dtype=np.float32).T)

    wqT, wkT, wvT = t(L0_attn_Wq), t(L0_attn_Wk), t(L0_attn_Wv)
    woT, wfcT, wprT = t(L0_attn_Wo), t(L0_ffn_w_fc), t(L0_ffn_w_proj)
    qn = np.ascontiguousarray(np.asarray(L0_attn_qn_w, f32).reshape(1, DH))
    kn = np.ascontiguousarray(np.asarray(L0_attn_kn_w, f32).reshape(1, DH))

    inv_freq = (f32(1.0) / f32(10000.0) ** (np.arange(0, DH, 2).astype(f32) / f32(DH))).astype(f32)
    pos = np.arange(T, dtype=f32)
    emb = np.concatenate([pos[:, None] * inv_freq[None, :]] * 2, axis=1).astype(f32)
    cos_t = np.cos(emb).astype(f32)
    sin_t = np.sin(emb).astype(f32)
    sgn = np.where(np.arange(DH) < 64, f32(-1.0), f32(1.0))
    qs = f32(DH ** -0.5)
    tqc = np.ascontiguousarray(cos_t * qs)
    tqs = np.ascontiguousarray(sin_t * sgn * qs)
    tkc = np.ascontiguousarray(cos_t)
    tks = np.ascontiguousarray(sin_t * sgn)

    jj_ = np.arange(128)[:, None]
    ii_ = np.arange(256)[None, :]
    m_list = []
    for c_ in (-128, 0, 384, 512):
        d_ = c_ + ii_ - jj_
        m_list.append(((d_ >= 0) & (d_ <= 512)).astype(ml_dtypes.bfloat16))
    masks4 = np.ascontiguousarray(np.concatenate(m_list, axis=0))
    ones_bfm = np.ones((128, 128), dtype=ml_dtypes.bfloat16)
    id_bfm = np.eye(128, dtype=ml_dtypes.bfloat16)
    id_f32m = np.eye(128, dtype=f32)
    onesrm = np.ones((1, 128), dtype=f32)
    consts = np.zeros((128, 8), f32)
    consts[:, 0] = 1e-5
    consts[:, 1] = 1e-6
    zrow = np.zeros((1, 8), f32)

    if "nc" not in _CACHED:
        _CACHED["nc"] = build_program()
    nc = _CACHED["nc"]

    shared = dict(wqT=wqT, wkT=wkT, wvT=wvT, woT=woT, wfcT=wfcT, wprT=wprT,
                  qn=qn, kn=kn, tqc=tqc, tqs=tqs, tkc=tkc, tks=tks,
                  masks4=masks4, ones_bf=ones_bfm, id_bf=id_bfm,
                  id_f32=id_f32m, onesr=onesrm, consts=consts, zrow=zrow)
    in_maps = [dict(shared, x=np.ascontiguousarray(x[b]),
                    wslf=np.ascontiguousarray(wfcT[b * (D // NC):(b + 1) * (D // NC), :]),
                    wslp=np.ascontiguousarray(wprT[b * (J // NC):(b + 1) * (J // NC), :]))
               for b in range(NC)]
    res = run_bass_kernel_spmd(nc, in_maps, core_ids=list(range(NC)))
    return np.stack([res.results[b]["out"] for b in range(NC)], axis=0)


# version 21
# speedup vs baseline: 1.1722x; 1.0008x over previous
import numpy as np
import ml_dtypes
from contextlib import ExitStack

import concourse.bass as bass
import concourse.mybir as mybir
from concourse.bass_utils import run_bass_kernel_spmd

F32 = mybir.dt.float32
BF16 = mybir.dt.bfloat16
F8 = mybir.dt.float8e4
AF = mybir.ActivationFunctionType
ALU = mybir.AluOpType
AXX = mybir.AxisListType.X
PM = mybir.MatmulPerfMode

T, D, H, HKV, DH, G, WIN, J = 2048, 2048, 16, 4, 128, 4, 512, 8192
NT, ND, NJ, NC = 16, 16, 64, 8

ENGINES = ("sync", "scalar", "vector", "tensor", "gpsimd")


class Sem:
    __slots__ = ("h", "n")

    def __init__(self, h):
        self.h = h
        self.n = 0


class Sched:
    def __init__(self):
        self.ops = []
        self.own_sem = {}

    def emit(self, eng, fn):
        self.ops.append((eng, fn))

    def wait(self, eng, sem, n=None):
        tgt = sem.n if n is None else n
        if tgt <= 0:
            return
        self.emit(eng, lambda e, s=sem.h, t=tgt: e.wait_ge(s, t))

    def ins(self, eng, f, sem=None, inc=1, selfwait=True):
        own = self.own_sem.get(eng)
        if own is not None:
            if selfwait and own.n > 0:
                self.wait(eng, own)
            if sem is None:
                sem, inc = own, 1
        if sem is not None:
            sem.n += inc
        def fn(e, _f=f, _s=sem, _i=inc):
            r = _f(e)
            if _s is not None:
                r.then_inc(_s.h, _i)
        self.emit(eng, fn)

    def build(self, block):
        for ename in ENGINES:
            sec = getattr(block, ename)
            mine = [fn for (e, fn) in self.ops if e == ename]
            def body(eng, _my=mine):
                for fn in _my:
                    fn(eng)
            sec(body)


def build_program():
    nc = bass.Bass("TRN2", target_bir_lowering=False)

    dh = {}

    def din(name, shape, dt=F32):
        h = nc.dram_tensor(name, shape, dt, kind="ExternalInput")
        dh[name] = h
        return h.ap()

    x_d = din("x", [T, D])
    wqT_d = din("wqT", [D, H * DH])
    wkT_d = din("wkT", [D, HKV * DH])
    wvT_d = din("wvT", [D, HKV * DH])
    woT_d = din("woT", [H * DH, D])
    wfcT_d = din("wfcT", [D, J])
    wprT_d = din("wprT", [J, D])
    qn_d = din("qn", [1, DH])
    kn_d = din("kn", [1, DH])
    tqc_d = din("tqc", [T, DH])
    tqs_d = din("tqs", [T, DH])
    tkc_d = din("tkc", [T, DH])
    tks_d = din("tks", [T, DH])
    masks4_d = din("masks4", [4 * 128, 256], BF16)
    ones_bf_d = din("ones_bf", [128, 128], BF16)
    id_bf_d = din("id_bf", [128, 128], BF16)
    id_f32_d = din("id_f32", [128, 128], F32)
    onesr_d = din("onesr", [1, 128], F32)
    consts_d = din("consts", [128, 8], F32)
    zrow_d = din("zrow", [1, 8], F32)
    wslf_d = din("wslf", [D // NC, J])
    wslp_d = din("wslp", [J // NC, D])

    out_h = nc.dram_tensor("out", [T, D], F32, kind="ExternalOutput")
    out_d = out_h.ap()

    xmid_h = nc.dram_tensor("xmid_s", [T, D], F32)
    x2_h = nc.dram_tensor("x2_s", [T, D], F32)
    hT_h = nc.dram_tensor("hT_s", [NT * 128, J], BF16)
    ctx_h = nc.dram_tensor("ctx_s", [H * 128, T], BF16)
    xmid_d, x2_d, hT_d, ctx_d = xmid_h.ap(), x2_h.ap(), hT_h.ap(), ctx_h.ap()
    cc_in1 = nc.dram_tensor("cc_in1", [1, 8], F32).ap()
    cc_out1 = nc.dram_tensor("cc_out1", [1, 8], F32).ap()
    cc_in2 = nc.dram_tensor("cc_in2", [1, 8], F32).ap()
    cc_out2 = nc.dram_tensor("cc_out2", [1, 8], F32).ap()

    P = Sched()
    es = ExitStack()

    ALLSEMS = []

    def sem(name):
        s = Sem(es.enter_context(nc.semaphore(name)))
        ALLSEMS.append(s)
        return s

    def full_barrier():
        for _eng in ENGINES:
            for _s in ALLSEMS:
                P.wait(_eng, _s)

    sT = sem("s_pe")
    sA = sem("s_act")
    sV = sem("s_dve")
    sGc = sem("s_gpc")
    sCC = sem("s_cc")
    P.own_sem = {"vector": sV, "scalar": sA}

    dIN = sem("d_in")
    dX = [sem("d_x0"), sem("d_x1")]
    dTAB = sem("d_tab")
    dCTXL = [sem("d_ctxl0"), sem("d_ctxl1")]
    dXB5 = [sem("d_xb50"), sem("d_xb51")]
    dX2L = [sem("d_x2l0"), sem("d_x2l1")]
    dWFC = sem("d_wfc")
    dHSx = [sem("d_hs0"), sem("d_hs1")]
    dXM5 = [sem("d_xm50"), sem("d_xm51")]
    dOUT = [sem("d_out0"), sem("d_out1")]
    dWMX = [sem("d_wmx0")]
    dWKV = sem("d_wkv")
    dWQ = sem("d_wq")
    dCTXS = [sem("d_ctxs0"), sem("d_ctxs1")]
    dWO = [sem(f"d_wo{i}") for i in range(4)]
    dXMS = sem("d_xms")
    dX2S = sem("d_x2s")
    dWPR = sem("d_wpr")
    dHST = [sem(f"d_hst{i}") for i in range(4)]
    dCC = sem("d_cc")

    def sb(name, shape, dt, stk=es):
        return stk.enter_context(nc.sbuf_tensor(name, shape, dt))

    cons = sb("cons", [128, 8], F32)
    zrow_s = sb("zrow_s", [1, 8], F32)
    onesr = sb("onesr_s", [1, 128], F32)
    id_bf = sb("id_bf_s", [128, 128], BF16)
    id_f32 = sb("id_f32_s", [128, 128], F32)
    ones_bf = sb("ones_bf_s", [128, 128], BF16)
    masks4 = sb("masks4_s", [128, 4 * 256], BF16)
    qn_s = sb("qn_s", [1, DH], F32)
    kn_s = sb("kn_s", [1, DH], F32)
    wrow = sb("wrow", [1, 2 * DH], F32)
    wqB = sb("wqB", [128, DH], F32)
    wkB = sb("wkB", [128, DH], F32)
    STV = sb("STV", [128, 8 * 24], F32)
    STA = sb("STA", [128, 8 * 16], F32)
    RMX = sb("RMX", [128, 8 * 4], F32)
    SCL = sb("SCL", [128, 8], F32)
    SCL2 = sb("SCL2", [128, 8], F32)
    ROW = sb("ROWs", [1, 32], F32)
    ccs = sb("ccs_s", [1, 8], F32)
    tcol = sb("tcol", [128, 2], F32)
    trash = sb("trash", [128, 2048], BF16)

    def stv(i):
        return STV[:, 8 * i:8 * i + 1]

    def sta(i):
        return STA[:, 8 * i:8 * i + 1]

    def rmx(i):
        return RMX[:, 8 * i:8 * i + 1]

    for dst, src in [(cons, consts_d), (zrow_s, zrow_d), (onesr, onesr_d),
                     (id_bf, id_bf_d), (id_f32, id_f32_d), (ones_bf, ones_bf_d),
                     (qn_s, qn_d), (kn_s, kn_d)]:
        P.ins("sync", lambda e, d=dst, s=src: e.dma_start(out=d[:], in_=s), dIN, 16)
    P.ins("sync", lambda e: e.dma_start(
        out=bass.AP(masks4, 0, [[4 * 256, 128], [256, 4], [1, 256]]),
        in_=bass.AP(dh["masks4"], 0, [[256, 128], [128 * 256, 4], [1, 256]])), dIN, 16)
    P.ins("gpsimd", lambda e: e.memset(RMX[:], 0), sGc, 1)
    P.wait("vector", dIN)
    P.wait("scalar", dIN)
    P.wait("tensor", dIN)
    P.ins("vector", lambda e: e.tensor_scalar(wrow[0:1, 0:DH], qn_s[0:1, :], 1.0, 1.0,
                                              op0=ALU.mult, op1=ALU.add))
    P.ins("vector", lambda e: e.tensor_scalar(wrow[0:1, DH:2 * DH], kn_s[0:1, :], 1.0, 1.0,
                                              op0=ALU.mult, op1=ALU.add))

    with nc.psum_tensor("pinit", [128, 2 * DH], F32) as pinit:
        P.wait("tensor", sV)
        P.ins("tensor", lambda e: e.matmul(pinit[:, 0:DH], onesr[0:1, :], wrow[0:1, 0:DH],
                                           start=True, stop=True), sT)
        P.ins("tensor", lambda e: e.matmul(pinit[:, DH:2 * DH], onesr[0:1, :], wrow[0:1, DH:2 * DH],
                                           start=True, stop=True), sT)
        P.wait("scalar", sT)
        P.ins("scalar", lambda e: e.activation(out=wqB[:], in_=pinit[:, 0:DH], func=AF.Copy, scale=1.0))
        P.ins("scalar", lambda e: e.activation(out=wkB[:], in_=pinit[:, DH:2 * DH], func=AF.Copy, scale=1.0))

    ph_a = ExitStack()
    xnT = sb("xnT", [128, ND * T], BF16, ph_a)
    kT = sb("kT", [128, HKV * T], BF16, ph_a)
    v_sb = sb("v_sb", [128, HKV * T], BF16, ph_a)
    qTg = sb("qTg", [128, G * T], BF16, ph_a)
    tabc = sb("tabc", [128, NT * DH], F32, ph_a)
    tabs = sb("tabs", [128, NT * DH], F32, ph_a)
    xbuf = [sb(f"xbuf{i}", [128, D], F32, ph_a) for i in range(2)]
    xn_tok = sb("xn_tok", [128, D], BF16, ph_a)
    xn_tok2 = [xn_tok, xn_tok]
    rop = [sb(f"rop{i}", [128, 3 * 128], F32, ph_a) for i in range(2)]
    qrot_bf = [sb(f"qrot{i}", [128, 128], BF16, ph_a) for i in range(2)]
    wkv_sb = sb("wkv_sb", [128, ND * 256], BF16, ph_a)
    wq_sb = sb("wq_sb", [128, ND * 512], BF16, ph_a)
    ctxs = [sb(f"ctxs{i}", [128, T], BF16, ph_a) for i in range(2)]
    expb = [sb(f"expb{i}", [128, 1536], BF16, ph_a) for i in range(2)]
    rzb = [sb(f"rzb{i}", [128, 256], F32, ph_a) for i in range(2)]
    wmx = [sb("wmx0", [128, 1024], F32, ph_a)]

    def xnT_t(dc, tt):
        return xnT[:, dc * T + tt * 128: dc * T + (tt + 1) * 128]

    def kT_t(hk, jt):
        return kT[:, hk * T + jt * 128: hk * T + (jt + 1) * 128]

    def v_t(hk, jt):
        return v_sb[:, hk * T + jt * 128: hk * T + (jt + 1) * 128]

    def qT_t(g, it):
        return qTg[:, g * T + it * 128: g * T + (it + 1) * 128]

    def tab_t(tab, tt):
        return tab[:, tt * 128:(tt + 1) * 128]

    def ln_stats(src_ap, sl, asl, scale_c=None, scale_c2=None):
        P.wait("scalar", sV)
        P.ins("vector", lambda e: e.reduce_sum(stv(sl), src_ap, axis=AXX))
        P.wait("scalar", sV)
        P.ins("scalar", lambda e: e.activation(out=trash[:], in_=src_ap, func=AF.Square,
                                               accum_out=sta(asl)))
        P.wait("vector", sA)
        if scale_c is None:
            P.ins("vector", lambda e: e.tensor_scalar(stv(sl), stv(sl), 1.0 / D, 0.0,
                                                      op0=ALU.mult, op1=ALU.add))
            P.ins("vector", lambda e: e.tensor_scalar(stv(sl + 1), sta(asl), 1.0 / D, 0.0,
                                                      op0=ALU.mult, op1=ALU.add))
        else:
            P.ins("vector", lambda e: e.tensor_scalar(stv(sl), stv(sl), scale_c, 1.0 / D,
                                                      op0=ALU.mult, op1=ALU.mult))
            P.ins("vector", lambda e: e.tensor_scalar(stv(sl + 1), sta(asl), scale_c2, 1.0 / D,
                                                      op0=ALU.mult, op1=ALU.mult))
        P.ins("vector", lambda e: e.tensor_scalar(stv(sl + 2), stv(sl), stv(sl), 0.0,
                                                  op0=ALU.mult, op1=ALU.add))
        P.ins("vector", lambda e: e.tensor_scalar(stv(sl + 2), stv(sl + 1), stv(sl + 2), 0.0,
                                                  op0=ALU.subtract, op1=ALU.add))
        P.wait("scalar", sV)
        P.ins("scalar", lambda e: e.activation(out=sta(asl + 1), in_=stv(sl + 2), func=AF.Sqrt,
                                               scale=1.0, bias=cons[:, 0:1]))
        P.wait("vector", sA)
        P.ins("vector", lambda e: e.reciprocal(stv(sl + 3), sta(asl + 1)))
        P.ins("vector", lambda e: e.tensor_scalar(stv(sl + 4), stv(sl), stv(sl + 3), -1.0,
                                                  op0=ALU.mult, op1=ALU.mult))

    full_barrier()
    P.wait("vector", sGc)
    wmax_tiles = []
    for (w_ap_, rows_, cols_, slot_) in [(wslf_d, D // NC, J, 2), (wslp_d, J // NC, D, 3)]:
        for r_ in range(rows_ // 128):
            for c_ in range(cols_ // 1024):
                wmax_tiles.append((w_ap_, r_, c_, slot_))
    wmax_state = {"i": 0, "tok": [0, 0]}

    def wmax_step(nstep=1):
        for _ in range(nstep):
            i = wmax_state["i"]
            if i >= len(wmax_tiles):
                return
            wmax_state["i"] += 1
            w_ap_, r_, c_, slot_ = wmax_tiles[i]
            wp = 0
            P.wait("gpsimd", sV, wmax_state["tok"][wp])
            P.ins("gpsimd", lambda e, w=w_ap_, rr=r_, cc=c_, p=wp: e.dma_start(
                out=wmx[p][:], in_=w[rr * 128:(rr + 1) * 128, cc * 1024:(cc + 1) * 1024]), dWMX[wp], 16)
            P.wait("vector", dWMX[wp])
            P.ins("vector", lambda e, p=wp: e.reduce_max(stv(15), wmx[p][:], axis=AXX,
                                                         apply_absolute_value=True))
            P.ins("vector", lambda e, s=slot_: e.tensor_scalar(rmx(s), rmx(s), stv(15), 0.0,
                                                               op0=ALU.max, op1=ALU.add))
            wmax_state["tok"][wp] = sV.n

    with nc.psum_tensor("ptr0", [128, 128], BF16) as ptr0, \
         nc.psum_tensor("ptr1", [128, 128], BF16) as ptr1:
        ptr = [ptr0, ptr1]
        tok_xn = [0, 0]
        tok_nrm = 0
        tok_ptr = [0, 0]
        for tt in range(NT):
            par = tt % 2
            wmax_step(2)
            P.wait("sync", sA)
            P.wait("sync", sV)
            P.ins("sync", lambda e, t=tt, p=par: e.dma_start(
                out=xbuf[p][:], in_=x_d[t * 128:(t + 1) * 128, :]), dX[par], 16)
            P.wait("vector", dX[par])
            P.wait("scalar", dX[par])
            ln_stats(xbuf[par][:], 0, 0)
            P.wait("scalar", sV)
            P.wait("scalar", sT, max(tok_xn))
            P.ins("scalar", lambda e, p=par: e.activation(out=xn_tok2[p][:], in_=xbuf[p][:],
                                                          func=AF.Identity,
                                                          scale=stv(3), bias=stv(4)))
            tok_nrm = sA.n
            for dc in range(ND):
                tp = dc % 2
                P.wait("tensor", sA, max(tok_nrm, tok_ptr[tp]))
                P.ins("tensor", lambda e, d=dc, q=tp, p=par: e.transpose(
                    ptr[q][:], xn_tok2[p][:, d * 128:(d + 1) * 128], id_bf[:]), sT)
                P.wait("scalar", sT)
                P.ins("scalar", lambda e, d=dc, t=tt, q=tp: e.activation(
                    out=xnT_t(d, t), in_=ptr[q][:], func=AF.Copy, scale=1.0))
                tok_ptr[tp] = sA.n
            tok_xn[par] = sT.n

    P.ins("sync", lambda e: e.dma_start(
        out=bass.AP(tabc, 0, [[NT * DH, 128], [DH, NT], [1, DH]]),
        in_=bass.AP(dh["tkc"], 0, [[DH, 128], [128 * DH, NT], [1, DH]])), dTAB, 16)
    P.ins("sync", lambda e: e.dma_start(
        out=bass.AP(tabs, 0, [[NT * DH, 128], [DH, NT], [1, DH]]),
        in_=bass.AP(dh["tks"], 0, [[DH, 128], [128 * DH, NT], [1, DH]])), dTAB, 16)
    P.wait("vector", dTAB)

    with nc.psum_tensor("pkv0", [128, 256], F32) as pkv0, \
         nc.psum_tensor("pkv1", [128, 256], F32) as pkv1, \
         nc.psum_tensor("ptk0", [128, 128], BF16) as ptk0, \
         nc.psum_tensor("ptk1", [128, 128], BF16) as ptk1:
        pkv = [pkv0, pkv1]
        ptk = [ptk0, ptk1]
        tokA = [sA.n, sA.n]
        tokV = [sV.n, sV.n]
        tokTr = [0, 0]
        tokEvK = [0, 0]
        for hk in range(HKV):
            P.wait("gpsimd", sT)
            P.ins("gpsimd", lambda e, h=hk: e.dma_start(
                out=bass.AP(wkv_sb, 0, [[ND * 256, 128], [256, ND], [1, 128]]),
                in_=bass.AP(dh["wkT"], h * 128, [[HKV * DH, 128], [128 * HKV * DH, ND], [1, 128]])), dWKV, 16)
            P.ins("gpsimd", lambda e, h=hk: e.dma_start(
                out=bass.AP(wkv_sb, 128, [[ND * 256, 128], [256, ND], [1, 128]]),
                in_=bass.AP(dh["wvT"], h * 128, [[HKV * DH, 128], [128 * HKV * DH, ND], [1, 128]])), dWKV, 16)
            for tt in range(NT):
                par = tt % 2
                wmax_step(2)
                P.wait("tensor", dWKV)
                P.wait("tensor", sA, tokA[par])
                P.wait("tensor", sV, tokV[par])
                for dc in range(ND):
                    P.ins("tensor", lambda e, d=dc, t=tt, p=par: e.matmul(
                        pkv[p][:], xnT_t(d, t), wkv_sb[:, d * 256:(d + 1) * 256],
                        start=(d == 0), stop=(d == ND - 1)), sT)
                tok_mm = sT.n
                P.wait("scalar", sT, tok_mm)
                P.wait("scalar", sV, tokV[par])
                P.ins("scalar", lambda e, p=par: e.activation(out=trash[:, 0:128], in_=pkv[p][:, 0:128],
                                                              func=AF.Square, accum_out=sta(8)), selfwait=False)
                P.ins("scalar", lambda e: e.activation(out=sta(12), in_=sta(8), func=AF.Sqrt,
                                                       scale=1.0 / DH, bias=cons[:, 1:2]))
                tok_std = sA.n
                P.wait("vector", sA, tok_std)
                P.ins("vector", lambda e: e.reciprocal(stv(16), sta(12)))
                P.ins("vector", lambda e, p=par: e.scalar_tensor_tensor(rop[p][:, 0:128], pkv[p][:, 0:128],
                                                                        stv(16), wkB[:],
                                                                        op0=ALU.mult, op1=ALU.mult))
                tokV[par] = sV.n
                P.wait("scalar", sV, tokV[par])
                P.ins("scalar", lambda e, h=hk, t=tt, p=par: e.activation(
                    out=v_t(h, t), in_=pkv[p][:, 128:256], func=AF.Copy, scale=1.0), selfwait=False)
                tokA[par] = sA.n
                P.wait("vector", sT, tokTr[par])
                P.ins("vector", lambda e, p=par, t=tt: e.tensor_tensor(
                    rop[p][:, 128:256], rop[p][:, 0:128], tab_t(tabc, t), op=ALU.mult))
                P.ins("vector", lambda e, p=par, t=tt: e.tensor_tensor(
                    rop[p][:, 256:320], rop[p][:, 64:128], tab_t(tabs, t)[:, 0:64], op=ALU.mult))
                P.ins("vector", lambda e, p=par, t=tt: e.tensor_tensor(
                    rop[p][:, 320:384], rop[p][:, 0:64], tab_t(tabs, t)[:, 64:128], op=ALU.mult))
                P.ins("vector", lambda e, p=par: e.tensor_tensor(
                    qrot_bf[p][:], rop[p][:, 128:256], rop[p][:, 256:384], op=ALU.add))
                tok_rot = sV.n
                P.wait("tensor", sV, tok_rot)
                P.wait("tensor", sA, tokEvK[par])
                P.ins("tensor", lambda e, p=par: e.transpose(ptk[p][:], qrot_bf[p][:], id_bf[:]), sT)
                tokTr[par] = sT.n
                P.wait("scalar", sT, tokTr[par])
                P.ins("scalar", lambda e, h=hk, t=tt, p=par: e.activation(
                    out=kT_t(h, t), in_=ptk[p][:], func=AF.Copy, scale=1.0), selfwait=False)
                tokEvK[par] = sA.n

    P.wait("sync", sV)
    P.ins("sync", lambda e: e.dma_start(
        out=bass.AP(tabc, 0, [[NT * DH, 128], [DH, NT], [1, DH]]),
        in_=bass.AP(dh["tqc"], 0, [[DH, 128], [128 * DH, NT], [1, DH]])), dTAB, 16)
    P.ins("sync", lambda e: e.dma_start(
        out=bass.AP(tabs, 0, [[NT * DH, 128], [DH, NT], [1, DH]]),
        in_=bass.AP(dh["tqs"], 0, [[DH, 128], [128 * DH, NT], [1, DH]])), dTAB, 16)
    P.wait("vector", dTAB)

    tokQa = [0, 0]
    tokQv = [0, 0]
    tokTrQ = [0, 0]
    tokEvQ = [0, 0]
    tE = [0, 0]
    tSc = [0, 0]
    tM = [0, 0]
    tZp = [0, 0]
    tCp = [0, 0]
    tRZp = [0, 0]
    tNp = [0, 0]
    for hk in range(HKV):
        P.wait("gpsimd", sT)
        P.ins("gpsimd", lambda e, h=hk: e.dma_start(
            out=bass.AP(wq_sb, 0, [[ND * 512, 128], [512, ND], [1, 512]]),
            in_=bass.AP(dh["wqT"], h * 512, [[H * DH, 128], [128 * H * DH, ND], [1, 512]])), dWQ, 16)
        with nc.psum_tensor(f"pq0_{hk}", [128, 512], F32) as pq0, \
             nc.psum_tensor(f"pq1_{hk}", [128, 512], F32) as pq1, \
             nc.psum_tensor(f"ptq0_{hk}", [128, 128], BF16) as ptq0, \
             nc.psum_tensor(f"ptq1_{hk}", [128, 128], BF16) as ptq1:
            pq = [pq0, pq1]
            ptq = [ptq0, ptq1]
            P.wait("tensor", sA)
            P.wait("tensor", sV)
            for tt in range(NT):
                par = tt % 2
                wmax_step(2)
                P.wait("tensor", dWQ)
                P.wait("tensor", sA, tokQa[par])
                P.wait("tensor", sV, tokQv[par])
                for dc in range(ND):
                    P.ins("tensor", lambda e, d=dc, t=tt, p=par: e.matmul(
                        pq[p][:], xnT_t(d, t), wq_sb[:, d * 512:(d + 1) * 512],
                        start=(d == 0), stop=(d == ND - 1)), sT)
                tok_mm = sT.n
                P.wait("scalar", sT, tok_mm)
                P.wait("scalar", sV, tokQv[par])
                for g in range(G):
                    P.ins("scalar", lambda e, gg=g, p=par: e.activation(
                        out=trash[:, gg * 128:(gg + 1) * 128], in_=pq[p][:, gg * 128:(gg + 1) * 128],
                        func=AF.Square, accum_out=sta(8 + gg)), selfwait=False)
                P.wait("scalar", sA)
                for g in range(G):
                    P.ins("scalar", lambda e, gg=g: e.activation(out=sta(12 + gg), in_=sta(8 + gg),
                                                                 func=AF.Sqrt, scale=1.0 / DH,
                                                                 bias=cons[:, 1:2]), selfwait=False)
                tok_std = sA.n
                P.wait("vector", sA, tok_std)
                for g in range(G):
                    P.ins("vector", lambda e, gg=g: e.reciprocal(stv(16 + gg), sta(12 + gg)), selfwait=False)
                for g in range(G):
                    gp = g % 2
                    P.wait("vector", sT, tokTrQ[gp])
                    P.ins("vector", lambda e, gg=g, p=par: e.scalar_tensor_tensor(
                        rop[0][:, 0:128], pq[p][:, gg * 128:(gg + 1) * 128], stv(16 + gg), wqB[:],
                        op0=ALU.mult, op1=ALU.mult))
                    P.ins("vector", lambda e, t=tt: e.tensor_tensor(
                        rop[0][:, 128:256], rop[0][:, 0:128], tab_t(tabc, t), op=ALU.mult))
                    P.ins("vector", lambda e, t=tt: e.tensor_tensor(
                        rop[0][:, 256:320], rop[0][:, 64:128], tab_t(tabs, t)[:, 0:64], op=ALU.mult))
                    P.ins("vector", lambda e, t=tt: e.tensor_tensor(
                        rop[0][:, 320:384], rop[0][:, 0:64], tab_t(tabs, t)[:, 64:128], op=ALU.mult))
                    P.ins("vector", lambda e, p=gp: e.tensor_tensor(
                        qrot_bf[p][:], rop[0][:, 128:256], rop[0][:, 256:384], op=ALU.add))
                    tok_rot = sV.n
                    P.wait("tensor", sV, tok_rot)
                    P.wait("tensor", sA, tokEvQ[gp])
                    P.ins("tensor", lambda e, p=gp: e.transpose(ptq[p][:], qrot_bf[p][:], id_bf[:]), sT)
                    tokTrQ[gp] = sT.n
                    P.wait("scalar", sT, tokTrQ[gp])
                    P.ins("scalar", lambda e, gg=g, t=tt, p=gp: e.activation(
                        out=qT_t(gg, t), in_=ptq[p][:], func=AF.Copy, scale=1.0), selfwait=False)
                    tokEvQ[gp] = sA.n
                tokQv[par] = sV.n
                tokQa[par] = sA.n
        with nc.psum_tensor(f"psS0_{hk}", [128, 1536], F32) as psS0, \
             nc.psum_tensor(f"psS1_{hk}", [128, 1536], F32) as psS1, \
             nc.psum_tensor(f"pzz_{hk}", [128, 256], F32) as pzz, \
             nc.psum_tensor(f"pcc_{hk}", [128, 256], F32) as pcc:
            psS = [psS0, psS1]
            base = max(tokQa[0], tokQa[1])
            tE = [base, base]
            tSc = [0, 0]
            tM = [0, 0]
            tZp = max(tRZp)
            tCp = 0
            tRZ1 = base
            tN1 = 0
            P.wait("tensor", sV)

            NR = NT // 2

            def jinfo(r):
                out = []
                for jt in range(max(0, 2 * r - 4), 2 * r + 2):
                    c = (2 * r - jt) * 128
                    slot = {-128: 0, 0: 1, 384: 2, 512: 3}.get(c)
                    out.append((jt, slot))
                return out

            def emit_scores(r, g):
                par = r % 2
                P.wait("tensor", sA, tE[par])
                for jj, (jt, _) in enumerate(jinfo(r)):
                    P.ins("tensor", lambda e, p=par, jx=jj, hh=hk, j=jt, rr=r, gg=g: e.matmul(
                        psS[p][:, jx * 256:(jx + 1) * 256], kT_t(hh, j),
                        qTg[:, gg * T + rr * 256:(gg * T + (rr + 1) * 256)],
                        start=True, stop=True), sT)
                tSc[par] = sT.n

            def emit_exp(r):
                par = r % 2
                nj = len(jinfo(r))
                P.wait("scalar", sT, tSc[par])
                P.ins("scalar", lambda e, p=par, n=nj: e.activation(
                    out=expb[p][:, 0:n * 256], in_=psS[p][:, 0:n * 256], func=AF.Exp), selfwait=False)
                tE[par] = sA.n

            def emit_masks(r):
                par = r % 2
                P.wait("vector", sA, tE[par])
                first = True
                for jj, (jt, slot) in enumerate(jinfo(r)):
                    if slot is None:
                        continue
                    P.ins("vector", lambda e, p=par, jx=jj, m=slot: e.tensor_tensor(
                        expb[p][:, jx * 256:(jx + 1) * 256], expb[p][:, jx * 256:(jx + 1) * 256],
                        masks4[:, m * 256:(m + 1) * 256], op=ALU.mult), selfwait=first)
                    first = False
                tM[par] = sV.n

            def emit_zctx(r):
                nonlocal_d = None
                par = r % 2
                ji = jinfo(r)
                nj = len(ji)
                P.wait("tensor", sV, tM[par])
                P.wait("tensor", sA, tRZ1)
                for jj in range(nj):
                    P.ins("tensor", lambda e, p=par, j=jj, n=nj: e.matmul(
                        pzz[:], ones_bf[:], expb[p][:, j * 256:(j + 1) * 256],
                        start=(j == 0), stop=(j == n - 1)), sT)
                tz = sT.n
                P.wait("tensor", sV, tN1)
                for jj, (jt, _) in enumerate(ji):
                    P.ins("tensor", lambda e, p=par, hh=hk, j=jt, jx=jj, n=nj: e.matmul(
                        pcc[:], v_t(hh, j), expb[p][:, jx * 256:(jx + 1) * 256],
                        start=(jx == 0), stop=(jx == n - 1)), sT)
                return tz, sT.n

            for g in range(G):
                h = hk * G + g
                cpar = h % 2
                P.wait("vector", dCTXS[cpar])
                emit_scores(0, g)
                emit_exp(0)
                emit_masks(0)
                for r in range(NR):
                    par = r % 2
                    if r + 1 < NR:
                        emit_scores(r + 1, g)
                        emit_exp(r + 1)
                        emit_masks(r + 1)
                    tz, tc = emit_zctx(r)
                    P.wait("scalar", sT, tz)
                    P.ins("scalar", lambda e, p=par: e.activation(out=rzb[p][:], in_=pzz[:],
                                                                  func=AF.Copy, scale=1.0), selfwait=False)
                    tRZ1 = sA.n
                    P.wait("vector", sA, tRZ1)
                    P.ins("vector", lambda e, p=par: e.reciprocal(rzb[p][:], rzb[p][:]), selfwait=False)
                    P.wait("vector", sT, tc)
                    P.ins("vector", lambda e, p=par, c=cpar, rr=r: e.tensor_tensor(
                        ctxs[c][:, rr * 256:(rr + 1) * 256], pcc[:], rzb[p][:], op=ALU.mult))
                    tN1 = sV.n
                P.wait("gpsimd", sV)
                P.ins("gpsimd", lambda e, hh=h, c=cpar: e.dma_start(
                    out=ctx_d[hh * 128:(hh + 1) * 128, :], in_=ctxs[c][:]), dCTXS[cpar], 16)
            tokQa = [max(tokQa[0], tE[0]), max(tokQa[1], tE[1])]

    wmax_step(len(wmax_tiles))

    full_barrier()
    ph_a.close()
    ph5 = ExitStack()
    wo_sb = sb("wo_sb", [128, ND * D], BF16, ph5)
    ctx_full = sb("ctx_full", [128, ND * T], BF16, ph5)
    xb5 = [sb(f"xb5_{i}", [128, D], F32, ph5) for i in range(2)]
    lnat = sb("lnat", [128, D], F32, ph5)
    xmid_sb = sb("xmid_sb", [128, D], F32, ph5)
    x2_sb = sb("x2_sb", [128, D], F32, ph5)

    wo_f = [sb(f"wo_f{i}", [128, D], F32, ph5) for i in range(2)]
    P.wait("sync", dCTXS[0])
    P.wait("sync", dCTXS[1])
    for hd in range(ND):
        P.ins("sync", lambda e, d=hd: e.dma_start(
            out=ctx_full[:, d * T:(d + 1) * T], in_=ctx_d[d * 128:(d + 1) * 128, :]), dCTXL[hd % 2], 16)
    tok_woc = [0] * ND
    tok_wof = [0, 0]
    for hd in range(ND):
        wp = hd % 2
        P.wait("sync", sA, tok_wof[wp])
        P.ins("sync", lambda e, d=hd, p=wp: e.dma_start(
            out=wo_f[p][:], in_=woT_d[d * 128:(d + 1) * 128, :]), dWO[hd % 4], 16)
        P.wait("scalar", dWO[hd % 4])
        P.ins("scalar", lambda e, d=hd, p=wp: e.activation(
            out=wo_sb[:, d * D:(d + 1) * D], in_=wo_f[p][:], func=AF.Copy, scale=1.0), selfwait=False)
        tok_woc[hd] = sA.n
        tok_wof[wp] = sA.n

    with nc.psum_tensor("po0", [128, D], F32) as po0, \
         nc.psum_tensor("po1", [128, D], F32) as po1:
        po = [po0, po1]
        tok_po_a = [0, 0]
        tok_po_v = [0, 0]
        for tt in range(NT):
            par = tt % 2
            P.wait("sync", sV)
            P.ins("sync", lambda e, t=tt, p=par: e.dma_start(
                out=xb5[p][:], in_=x_d[t * 128:(t + 1) * 128, :]), dXB5[par], 16)
            if tt == 0:
                P.wait("tensor", dCTXL[0])
                P.wait("tensor", dCTXL[1])
            P.wait("tensor", sA, tok_po_a[par])
            P.wait("tensor", sV, tok_po_v[par])
            for hd in range(ND):
                if tt == 0:
                    P.wait("tensor", sA, tok_woc[hd])
                for ns in range(4):
                    P.ins("tensor", lambda e, d=hd, n=ns, p=par, t=tt: e.matmul(
                        po[p][:, n * 512:(n + 1) * 512], ctx_full[:, d * T + t * 128: d * T + (t + 1) * 128],
                        wo_sb[:, d * D + n * 512: d * D + (n + 1) * 512],
                        start=(d == 0), stop=(d == ND - 1)), sT)
            P.wait("vector", sT)
            P.wait("scalar", sT)
            ln_stats(po[par][:], 0, 0)
            P.wait("scalar", sV)
            P.ins("scalar", lambda e, p=par: e.activation(out=lnat[:], in_=po[p][:], func=AF.Identity,
                                                          scale=stv(3), bias=stv(4)))
            tok_po_a[par] = sA.n
            tok_po_v[par] = sV.n
            P.wait("vector", sA)
            P.wait("vector", dXB5[par])
            P.wait("vector", dXMS)
            P.ins("vector", lambda e, p=par: e.tensor_tensor(xmid_sb[:], xb5[p][:], lnat[:], op=ALU.add))
            P.wait("gpsimd", sV)
            P.ins("gpsimd", lambda e, t=tt: e.dma_start(
                out=xmid_d[t * 128:(t + 1) * 128, :], in_=xmid_sb[:]), dXMS, 16)
            ln_stats(xmid_sb[:], 8, 4)
            P.wait("scalar", sV)
            P.wait("scalar", dX2S)
            P.ins("scalar", lambda e: e.activation(out=x2_sb[:], in_=xmid_sb[:], func=AF.Identity,
                                                   scale=stv(11), bias=stv(12)))
            P.ins("scalar", lambda e, t=tt: e.dma_start(
                out=x2_d[t * 128:(t + 1) * 128, :], in_=x2_sb[:]), dX2S, 16)
            P.wait("vector", sA)
            P.ins("vector", lambda e: e.reduce_max(stv(15), x2_sb[:], axis=AXX,
                                                   apply_absolute_value=True))
            P.ins("vector", lambda e: e.tensor_scalar(rmx(0), rmx(0), stv(15), 0.0,
                                                      op0=ALU.max, op1=ALU.add))
    full_barrier()
    ph5.close()

    def cross_part_max(col_ap, row_col):
        with nc.psum_tensor(f"pmx{row_col}", [128, 128], F32) as pmx:
            P.ins("vector", lambda e: e.tensor_scalar(tcol[:, 0:1], col_ap, 1.0, 0.0,
                                                      op0=ALU.mult, op1=ALU.add))
            P.wait("tensor", sV)
            P.ins("tensor", lambda e: e.transpose(pmx[0:1, 0:128], tcol[:, 0:1], id_f32[:]), sT)
            P.wait("vector", sT)
            P.ins("vector", lambda e, c=row_col: e.reduce_max(
                ROW[0:1, c:c + 1], pmx[0:1, 0:128], axis=AXX))

    cross_part_max(rmx(0), 0)
    cross_part_max(rmx(2), 1)
    cross_part_max(rmx(3), 2)

    P.ins("vector", lambda e: e.tensor_scalar(ccs[0:1, :], zrow_s[0:1, :], 0.0, 0.0,
                                              op0=ALU.mult, op1=ALU.add))
    P.ins("vector", lambda e: e.tensor_scalar(ccs[0:1, 0:1], ROW[0:1, 0:1], 1.0, 0.0,
                                              op0=ALU.mult, op1=ALU.add))
    P.ins("vector", lambda e: e.tensor_scalar(ccs[0:1, 1:2], ROW[0:1, 1:2], 1.0, 0.0,
                                              op0=ALU.mult, op1=ALU.add))
    P.ins("vector", lambda e: e.tensor_scalar(ccs[0:1, 2:3], ROW[0:1, 2:3], 1.0, 0.0,
                                              op0=ALU.mult, op1=ALU.add))
    P.wait("gpsimd", sV)
    P.ins("gpsimd", lambda e: e.dma_start(out=cc_in1, in_=ccs[0:1, :]), dCC, 16)
    P.wait("gpsimd", dCC)
    P.ins("gpsimd", lambda e: e.collective_compute(
        "AllReduce", ALU.max, replica_groups=[list(range(NC))],
        ins=[cc_in1], outs=[cc_out1]), sCC)
    P.wait("gpsimd", sCC)
    P.ins("gpsimd", lambda e: e.dma_start(out=ROW[0:1, 8:16], in_=cc_out1), dCC, 16)
    P.wait("vector", dCC)
    P.ins("vector", lambda e: e.tensor_scalar(ROW[0:1, 16:17], ROW[0:1, 8:9], 1.0 / 448.0, 1e-6,
                                              op0=ALU.mult, op1=ALU.max))
    P.ins("vector", lambda e: e.tensor_scalar(ROW[0:1, 17:18], ROW[0:1, 16:17], 2.0, 0.0,
                                              op0=ALU.mult, op1=ALU.add))
    P.ins("vector", lambda e: e.reciprocal(ROW[0:1, 17:18], ROW[0:1, 17:18]))
    P.ins("vector", lambda e: e.tensor_scalar(ROW[0:1, 18:19], ROW[0:1, 9:10], 1.0 / 448.0, 1e-6,
                                              op0=ALU.mult, op1=ALU.max))
    P.ins("vector", lambda e: e.tensor_scalar(ROW[0:1, 19:20], ROW[0:1, 18:19], 2.0, 0.0,
                                              op0=ALU.mult, op1=ALU.add))
    P.ins("vector", lambda e: e.reciprocal(ROW[0:1, 19:20], ROW[0:1, 19:20]))
    P.ins("vector", lambda e: e.tensor_scalar(ROW[0:1, 20:21], ROW[0:1, 16:17], ROW[0:1, 18:19], 4.0,
                                              op0=ALU.mult, op1=ALU.mult))
    P.ins("vector", lambda e: e.tensor_scalar(ROW[0:1, 21:22], ROW[0:1, 10:11], 1.0 / 448.0, 1e-6,
                                              op0=ALU.mult, op1=ALU.max))
    P.ins("vector", lambda e: e.tensor_scalar(ROW[0:1, 22:23], ROW[0:1, 21:22], 2.0, 0.0,
                                              op0=ALU.mult, op1=ALU.add))
    P.ins("vector", lambda e: e.reciprocal(ROW[0:1, 22:23], ROW[0:1, 22:23]))
    P.ins("vector", lambda e: e.tensor_scalar(ccs[0:1, 0:1], ROW[0:1, 17:18], 1.0, 0.0, op0=ALU.mult, op1=ALU.add))
    P.ins("vector", lambda e: e.tensor_scalar(ccs[0:1, 1:2], ROW[0:1, 20:21], 1.0, 0.0, op0=ALU.mult, op1=ALU.add))
    P.ins("vector", lambda e: e.tensor_scalar(ccs[0:1, 2:3], ROW[0:1, 19:20], 1.0, 0.0, op0=ALU.mult, op1=ALU.add))
    P.ins("vector", lambda e: e.tensor_scalar(ccs[0:1, 3:4], ROW[0:1, 22:23], 1.0, 0.0, op0=ALU.mult, op1=ALU.add))
    with nc.psum_tensor("pscl", [128, 8], F32) as pscl:
        P.wait("tensor", sV)
        P.ins("tensor", lambda e: e.matmul(pscl[:], onesr[0:1, :], ccs[0:1, :], start=True, stop=True), sT)
        P.wait("scalar", sT)
        P.ins("scalar", lambda e: e.activation(out=SCL[:], in_=pscl[:], func=AF.Copy, scale=1.0))

    ph_c = ExitStack()
    wpr8 = sb("wpr8", [128, NJ * D], F8, ph_c)
    ph_cm = ExitStack()
    x8T = sb("x8T", [128, ND * T], F8, ph_cm)

    with ExitStack() as pro, \
         nc.psum_tensor("ptc0", [128, 128], BF16) as ptc0, \
         nc.psum_tensor("ptc1", [128, 128], BF16) as ptc1:
        xbC = [sb(f"xbC{i}", [128, D], F32, pro) for i in range(2)]
        x8tok = sb("x8tok", [128, D], F8, pro)
        xqbf = sb("xqbf", [128, D], BF16, pro)
        ptc = [ptc0, ptc1]
        full_barrier()
        P.wait("sync", dX2S)
        tok_x8 = [0] * 4
        for tt in range(NT):
            par = tt % 2
            P.wait("sync", sA)
            P.ins("sync", lambda e, t=tt, p=par: e.dma_start(
                out=xbC[p][:], in_=x2_d[t * 128:(t + 1) * 128, :]), dX2L[par], 16)
            P.wait("scalar", dX2L[par])
            P.ins("scalar", lambda e, p=par: e.activation(out=x8tok[:], in_=xbC[p][:], func=AF.Copy,
                                                          scale=SCL[:, 0:1]))
            P.ins("scalar", lambda e: e.activation(out=xqbf[:], in_=x8tok[:], func=AF.Copy, scale=1.0))
            for dc in range(ND):
                tp = dc % 2
                P.wait("tensor", sA)
                P.ins("tensor", lambda e, d=dc, q=tp: e.transpose(
                    ptc[q][:], xqbf[:, d * 128:(d + 1) * 128], id_bf[:]), sT)
                P.wait("scalar", sT)
                P.ins("scalar", lambda e, d=dc, t=tt, q=tp: e.activation(
                    out=x8T[:, d * T + t * 128: d * T + (t + 1) * 128],
                    in_=ptc[q][:], func=AF.Copy, scale=1.0))
            if tt % 4 == 3:
                tok_x8[tt // 4] = sA.n

    full_barrier()
    wfc32 = sb("wfc32", [128, D], F32, ph_cm)
    wfc8 = [sb(f"wfc8_{i}", [128, D], F8, ph_cm) for i in range(2)]
    wpr32 = sb("wpr32_0", [128, D], F32, ph_cm)
    h_sb = [sb(f"h_sb{i}", [128, 512], BF16, ph_cm) for i in range(4)]
    r_sb = [sb(f"r_sb{i}", [128, 512], F32, ph_cm) for i in range(2)]

    with nc.psum_tensor("ph0", [128, 512], F32) as ph0, \
         nc.psum_tensor("ph1", [128, 512], F32) as ph1, \
         nc.psum_tensor("ph2", [128, 512], F32) as ph2, \
         nc.psum_tensor("ph3", [128, 512], F32) as ph3, \
         nc.psum_tensor("ph4", [128, 512], F32) as ph4, \
         nc.psum_tensor("ph5", [128, 512], F32) as ph5b, \
         nc.psum_tensor("ph6", [128, 512], F32) as ph6, \
         nc.psum_tensor("ph7", [128, 512], F32) as ph7:
        phh = [[ph0, ph1, ph2, ph3], [ph4, ph5b, ph6, ph7]]
        tok_quant = [0, 0]
        tok_quant_done = 0
        tok_wprq = 0
        tok_relu = [[0] * 4, [0] * 4]
        tok_mult = [0, 0]
        tok_mm = [[0] * 4, [0] * 4]
        for jc in range(NJ):
            par = jc % 2
            P.wait("sync", sA, tok_quant_done)
            P.ins("sync", lambda e, k=jc: e.dma_start(
                out=bass.AP(wfc32, 0, [[D, 128], [128, ND], [1, 128]]),
                in_=bass.AP(dh["wfcT"], k * 128, [[J, 128], [128 * J, ND], [1, 128]])), dWFC, 16)
            P.wait("scalar", dWFC)
            P.wait("scalar", sT, tok_mm[par][3])
            P.ins("scalar", lambda e, p=par: e.activation(out=wfc8[p][:], in_=wfc32[:], func=AF.Copy,
                                                          scale=SCL[:, 2:3]), selfwait=False)
            tok_quant[par] = sA.n
            tok_quant_done = sA.n
            P.ins("scalar", lambda e, k=jc: e.dma_start(
                out=wpr32[:], in_=wprT_d[k * 128:(k + 1) * 128, :]), dWPR, 16, selfwait=False)
            P.wait("scalar", dWPR)
            P.ins("scalar", lambda e, k=jc: e.activation(
                out=wpr8[:, k * D:(k + 1) * D], in_=wpr32[:], func=AF.Copy,
                scale=SCL[:, 3:4]), selfwait=False)
            tok_wprq = sA.n
            for ts in range(4):
                P.wait("tensor", sA, tok_quant[par])
                P.wait("tensor", sA, tok_relu[par][ts])
                for d2 in range(ND // 2):
                    P.ins("tensor", lambda e, d=d2, s=ts, p=par: e.matmul(
                        phh[p][s][:],
                        bass.AP(wfc8[p], d * 256, [[D, 128], [128, 2], [1, 128]]),
                        bass.AP(x8T, (2 * d) * T + s * 512, [[ND * T, 128], [T, 2], [1, 512]]),
                        start=(d == 0), stop=(d == ND // 2 - 1),
                        perf_mode=PM.DoubleRow), sT)
                tok_mm[par][ts] = sT.n
            for ts in range(4):
                rp = ts % 2
                P.wait("scalar", sT, tok_mm[par][ts])
                P.wait("scalar", sV, tok_mult[rp])
                P.ins("scalar", lambda e, s=ts, p=par, q=rp: e.activation(
                    out=r_sb[q][:], in_=phh[p][s][:], func=AF.Relu, scale=SCL[:, 1:2]), selfwait=False)
                tok_relu[par][ts] = sA.n
                P.wait("vector", sA, tok_relu[par][ts])
                P.wait("vector", dHST[ts])
                P.ins("vector", lambda e, s=ts, q=rp: e.tensor_tensor(
                    h_sb[s][:], r_sb[q][:], r_sb[q][:], op=ALU.mult), selfwait=False)
                tok_mult[rp] = sV.n
                P.ins("vector", lambda e, s=ts: e.reduce_max(stv(15), h_sb[s][:], axis=AXX))
                P.ins("vector", lambda e: e.tensor_scalar(rmx(1), rmx(1), stv(15), 0.0,
                                                          op0=ALU.max, op1=ALU.add))
                P.wait("gpsimd", sV)
                P.ins("gpsimd", lambda e, k=jc, s=ts: e.dma_start(
                    out=bass.AP(hT_h, (s * 4 * 128) * J + k * 128,
                                [[J, 128], [128 * J, 4], [1, 128]]),
                    in_=bass.AP(h_sb[s], 0, [[512, 128], [128, 4], [1, 128]])), dHST[ts], 16)

    cross_part_max(rmx(1), 3)
    P.ins("vector", lambda e: e.tensor_scalar(ccs[0:1, :], zrow_s[0:1, :], 0.0, 0.0,
                                              op0=ALU.mult, op1=ALU.add))
    P.ins("vector", lambda e: e.tensor_scalar(ccs[0:1, 0:1], ROW[0:1, 3:4], 1.0, 0.0,
                                              op0=ALU.mult, op1=ALU.add))
    P.wait("gpsimd", sV)
    P.ins("gpsimd", lambda e: e.dma_start(out=cc_in2, in_=ccs[0:1, :]), dCC, 16)
    P.wait("gpsimd", dCC)
    P.ins("gpsimd", lambda e: e.collective_compute(
        "AllReduce", ALU.max, replica_groups=[list(range(NC))],
        ins=[cc_in2], outs=[cc_out2]), sCC)
    P.wait("gpsimd", sCC)
    P.ins("gpsimd", lambda e: e.dma_start(out=ROW[0:1, 8:16], in_=cc_out2), dCC, 16)
    P.wait("vector", dCC)
    P.ins("vector", lambda e: e.tensor_scalar(ROW[0:1, 24:25], ROW[0:1, 8:9], 1.0 / 448.0, 1e-6,
                                              op0=ALU.mult, op1=ALU.max))
    P.ins("vector", lambda e: e.tensor_scalar(ROW[0:1, 25:26], ROW[0:1, 24:25], 2.0, 0.0,
                                              op0=ALU.mult, op1=ALU.add))
    P.ins("vector", lambda e: e.reciprocal(ROW[0:1, 25:26], ROW[0:1, 25:26]))
    P.ins("vector", lambda e: e.tensor_scalar(ROW[0:1, 26:27], ROW[0:1, 24:25], ROW[0:1, 21:22], 4.0,
                                              op0=ALU.mult, op1=ALU.mult))
    P.ins("vector", lambda e: e.tensor_scalar(ROW[0:1, 27:28], ROW[0:1, 26:27], ROW[0:1, 26:27], 0.0,
                                              op0=ALU.mult, op1=ALU.add))
    P.ins("vector", lambda e: e.tensor_scalar(ccs[0:1, 0:1], ROW[0:1, 25:26], 1.0, 0.0, op0=ALU.mult, op1=ALU.add))
    P.ins("vector", lambda e: e.tensor_scalar(ccs[0:1, 1:2], ROW[0:1, 26:27], 1.0, 0.0, op0=ALU.mult, op1=ALU.add))
    P.ins("vector", lambda e: e.tensor_scalar(ccs[0:1, 2:3], ROW[0:1, 27:28], 1.0, 0.0, op0=ALU.mult, op1=ALU.add))
    with nc.psum_tensor("pscl2", [128, 8], F32) as pscl2:
        P.wait("tensor", sV)
        P.ins("tensor", lambda e: e.matmul(pscl2[:], onesr[0:1, :], ccs[0:1, :], start=True, stop=True), sT)
        P.wait("scalar", sT)
        P.ins("scalar", lambda e: e.activation(out=SCL2[:], in_=pscl2[:], func=AF.Copy, scale=1.0))

    full_barrier()
    ph_cm.close()
    hf32 = [sb(f"hf32_{i}", [128, J // 2], BF16, ph_c) for i in range(2)]
    h8s = [sb(f"h8s{i}", [128, J], F8, ph_c) for i in range(2)]
    lnff = [sb(f"lnff{i}", [128, D], F32, ph_c) for i in range(2)]
    xm5 = [sb(f"xm5_{i}", [128, D], F32, ph_c) for i in range(2)]

    with nc.psum_tensor("pq0", [128, D], F32) as pq0, \
         nc.psum_tensor("pq1", [128, D], F32) as pq1:
        poD = [pq0, pq1]
        for s4 in dHST:
            P.wait("sync", s4)
        P.wait("sync", dXMS)
        tok_hq = [0, 0]
        tok_hq_done = [0, 0]
        tok_add = [0, 0]
        tok_mmD = [0, 0]
        tok_lnff = [0, 0]
        tok_stats = [0, 0]
        for tt in range(NT):
            par = tt % 2
            for half in range(2):
                P.wait("gpsimd", sA, tok_hq_done[half])
                P.ins("gpsimd", lambda e, t=tt, hf=half: e.dma_start(
                    out=hf32[hf][:], in_=bass.AP(hT_h, (t * 128) * J + hf * (J // 2),
                                                 [[J, 128], [1, J // 2]])), dHSx[half], 16)
            for half in range(2):
                P.wait("scalar", dHSx[half])
                P.wait("scalar", sT, tok_mmD[par])
                P.ins("scalar", lambda e, p=par, hf=half: e.activation(
                    out=h8s[p][:, hf * (J // 2):(hf + 1) * (J // 2)], in_=hf32[hf][:], func=AF.Copy,
                    scale=SCL2[:, 0:1]), selfwait=False)
                tok_hq_done[half] = sA.n
            tok_hq[par] = sA.n
            P.wait("sync", sV, tok_add[par])
            P.ins("sync", lambda e, t=tt, p=par: e.dma_start(
                out=xm5[p][:], in_=xmid_d[t * 128:(t + 1) * 128, :]), dXM5[par], 16)
            P.wait("tensor", sA, max(tok_hq[par], tok_lnff[par]))
            P.wait("tensor", sV, tok_stats[par])
            for ns in range(4):
                for k2 in range(NJ // 2):
                    P.ins("tensor", lambda e, k=k2, n=ns, p=par: e.matmul(
                        poD[p][:, n * 512:(n + 1) * 512],
                        bass.AP(h8s[p], k * 256, [[J, 128], [128, 2], [1, 128]]),
                        bass.AP(wpr8, (2 * k) * D + n * 512, [[NJ * D, 128], [D, 2], [1, 512]]),
                        start=(k == 0), stop=(k == NJ // 2 - 1),
                        perf_mode=PM.DoubleRow), sT)
            tok_mmD[par] = sT.n
            P.wait("vector", sT, tok_mmD[par])
            P.wait("scalar", sT, tok_mmD[par])
            ln_stats(poD[par][:], 0, 0, scale_c=SCL2[:, 1:2], scale_c2=SCL2[:, 2:3])
            tok_stats[par] = sV.n + 1
            P.ins("vector", lambda e: e.tensor_scalar(stv(6), stv(3), SCL2[:, 1:2], 0.0,
                                                      op0=ALU.mult, op1=ALU.add))
            P.wait("scalar", sV)
            P.wait("scalar", dOUT[par])
            P.ins("scalar", lambda e, p=par: e.activation(out=lnff[p][:], in_=poD[p][:], func=AF.Identity,
                                                          scale=stv(6), bias=stv(4)), selfwait=False)
            tok_lnff[par] = sA.n
            P.wait("vector", sA)
            P.wait("vector", dXM5[par])
            P.ins("vector", lambda e, p=par: e.tensor_tensor(lnff[p][:], xm5[p][:], lnff[p][:], op=ALU.add))
            tok_add[par] = sV.n
            P.wait("sync", sV, tok_add[par])
            P.ins("sync", lambda e, t=tt, p=par: e.dma_start(
                out=out_d[t * 128:(t + 1) * 128, :], in_=lnff[p][:]), dOUT[par], 16)

    P.wait("sync", dOUT[0])
    P.wait("sync", dOUT[1])
    P.wait("gpsimd", dOUT[0])
    P.wait("gpsimd", dOUT[1])

    with nc.Block() as block:
        P.build(block)
    ph_c.close()
    es.close()
    return nc


_CACHED = {}


def kernel(L0_attn_x, L0_attn_Wq, L0_attn_Wk, L0_attn_Wv, L0_attn_Wo,
           L0_attn_qn_w, L0_attn_kn_w, L0_ffn_w_fc, L0_ffn_w_proj):
    x = np.ascontiguousarray(np.asarray(L0_attn_x, dtype=np.float32))
    f32 = np.float32

    def t(a):
        return np.ascontiguousarray(np.asarray(a, dtype=np.float32).T)

    wqT, wkT, wvT = t(L0_attn_Wq), t(L0_attn_Wk), t(L0_attn_Wv)
    woT, wfcT, wprT = t(L0_attn_Wo), t(L0_ffn_w_fc), t(L0_ffn_w_proj)
    qn = np.ascontiguousarray(np.asarray(L0_attn_qn_w, f32).reshape(1, DH))
    kn = np.ascontiguousarray(np.asarray(L0_attn_kn_w, f32).reshape(1, DH))

    inv_freq = (f32(1.0) / f32(10000.0) ** (np.arange(0, DH, 2).astype(f32) / f32(DH))).astype(f32)
    pos = np.arange(T, dtype=f32)
    emb = np.concatenate([pos[:, None] * inv_freq[None, :]] * 2, axis=1).astype(f32)
    cos_t = np.cos(emb).astype(f32)
    sin_t = np.sin(emb).astype(f32)
    sgn = np.where(np.arange(DH) < 64, f32(-1.0), f32(1.0))
    qs = f32(DH ** -0.5)
    tqc = np.ascontiguousarray(cos_t * qs)
    tqs = np.ascontiguousarray(sin_t * sgn * qs)
    tkc = np.ascontiguousarray(cos_t)
    tks = np.ascontiguousarray(sin_t * sgn)

    jj_ = np.arange(128)[:, None]
    ii_ = np.arange(256)[None, :]
    m_list = []
    for c_ in (-128, 0, 384, 512):
        d_ = c_ + ii_ - jj_
        m_list.append(((d_ >= 0) & (d_ <= 512)).astype(ml_dtypes.bfloat16))
    masks4 = np.ascontiguousarray(np.concatenate(m_list, axis=0))
    ones_bfm = np.ones((128, 128), dtype=ml_dtypes.bfloat16)
    id_bfm = np.eye(128, dtype=ml_dtypes.bfloat16)
    id_f32m = np.eye(128, dtype=f32)
    onesrm = np.ones((1, 128), dtype=f32)
    consts = np.zeros((128, 8), f32)
    consts[:, 0] = 1e-5
    consts[:, 1] = 1e-6
    zrow = np.zeros((1, 8), f32)

    if "nc" not in _CACHED:
        _CACHED["nc"] = build_program()
    nc = _CACHED["nc"]

    shared = dict(wqT=wqT, wkT=wkT, wvT=wvT, woT=woT, wfcT=wfcT, wprT=wprT,
                  qn=qn, kn=kn, tqc=tqc, tqs=tqs, tkc=tkc, tks=tks,
                  masks4=masks4, ones_bf=ones_bfm, id_bf=id_bfm,
                  id_f32=id_f32m, onesr=onesrm, consts=consts, zrow=zrow)
    in_maps = [dict(shared, x=np.ascontiguousarray(x[b]),
                    wslf=np.ascontiguousarray(wfcT[b * (D // NC):(b + 1) * (D // NC), :]),
                    wslp=np.ascontiguousarray(wprT[b * (J // NC):(b + 1) * (J // NC), :]))
               for b in range(NC)]
    res = run_bass_kernel_spmd(nc, in_maps, core_ids=list(range(NC)))
    return np.stack([res.results[b]["out"] for b in range(NC)], axis=0)
